# version 06
# baseline (speedup 1.0000x reference)
import contextlib
import numpy as np

import concourse.bass as bass
import concourse.mybir as mybir
from concourse.bass_utils import run_bass_kernel_spmd

F32 = mybir.dt.float32
BF16 = mybir.dt.bfloat16
I32 = mybir.dt.int32
AF = mybir.ActivationFunctionType
ALU = mybir.AluOpType
AX = mybir.AxisListType

NTOK = 2048
NFULL = 4096
DM = 1024
DFF = 4096
NTC = NTOK // 128
KILL = -30000.0

DEV_TAPS = ()


def _rope_tables(h, dtype):
    i = np.arange(32, dtype=np.float64)
    inv = 1.0 / (10000.0 ** (2.0 * i / 64.0))
    pos = np.arange(2048 * h, 2048 * h + 2048, dtype=np.float64)
    rows = np.arange(128)
    fr = pos[None, :] * inv[(rows % 64) // 2][:, None]
    cos = np.cos(fr)
    sgn = np.where(rows % 2 == 0, -1.0, 1.0)[:, None]
    sin = np.sin(fr) * sgn
    return cos.astype(dtype), sin.astype(dtype)


_SHARED_CACHE = {}


def _prep_shared(inp):
    key = id(inp.get("L0_attn_Wq"))
    if _SHARED_CACHE.get("key") == key:
        return _SHARED_CACHE["d"]
    f32 = np.float32
    import ml_dtypes
    d = {}
    d["wq"] = np.ascontiguousarray(inp["L0_attn_Wq"], f32)
    d["wk"] = np.ascontiguousarray(inp["L0_attn_Wk"], f32)
    d["wv"] = np.ascontiguousarray(inp["L0_attn_Wv"], f32)
    d["wo"] = np.ascontiguousarray(inp["L0_attn_Wo"], f32)
    d["lamv"] = np.concatenate([inp["L0_attn_lq1"], inp["L0_attn_lk1"],
                                inp["L0_attn_lq2"], inp["L0_attn_lk2"]]).reshape(1, 256).astype(f32)
    d["lng"] = np.ascontiguousarray(inp["L0_attn_ln_g"].reshape(128, 1), f32)
    d["lnb"] = np.ascontiguousarray(inp["L0_attn_ln_b"].reshape(128, 1), f32)
    d["gw1T"] = np.ascontiguousarray(np.asarray(inp["L0_ffn_gate_w1"], f32).T)
    d["gb1T"] = np.ascontiguousarray(np.asarray(inp["L0_ffn_gate_b1"], f32).reshape(8, 128).T)
    d["gw2T"] = np.ascontiguousarray(np.asarray(inp["L0_ffn_gate_w2"], f32).T)
    d["gb2c"] = np.ascontiguousarray(np.asarray(inp["L0_ffn_gate_b2"], f32).reshape(8, 1))
    d["w1T"] = np.ascontiguousarray(np.asarray(inp["L0_ffn_W1"], f32).T)
    d["tw1T"] = np.ascontiguousarray(np.asarray(inp["L0_ffn_TW1"], f32).transpose(0, 2, 1))
    d["b1r"] = np.ascontiguousarray(np.asarray(inp["L0_ffn_b1"], f32).reshape(1, 4096))
    d["b1T"] = np.ascontiguousarray(np.asarray(inp["L0_ffn_b1"], f32).reshape(32, 128).T)
    d["tb1"] = np.ascontiguousarray(inp["L0_ffn_Tb1"], f32)
    d["w2T"] = np.ascontiguousarray(np.asarray(inp["L0_ffn_W2"], f32).T)
    d["tw2T"] = np.ascontiguousarray(np.asarray(inp["L0_ffn_TW2"], f32).transpose(0, 2, 1))
    d["b2r"] = np.ascontiguousarray(np.asarray(inp["L0_ffn_b2"], f32).reshape(1, 1024))
    d["tb2"] = np.ascontiguousarray(inp["L0_ffn_Tb2"], f32)
    d["wqkv"] = np.ascontiguousarray(inp["L1_attn_Wqkv"], f32)
    d["wout"] = np.ascontiguousarray(inp["L1_attn_Wout"], f32)
    d["boutr"] = np.ascontiguousarray(np.asarray(inp["L1_attn_bout"], f32).reshape(1, 1024))
    d["l1w1T"] = np.ascontiguousarray(np.asarray(inp["L1_ffn_w1"], f32).T)
    d["l1w2T"] = np.ascontiguousarray(np.asarray(inp["L1_ffn_w2"], f32).T)
    cw = np.asarray(inp["L1_ffn_conv_w"], f32)[:, 0, :]
    for j in range(3):
        d[f"cw{j}"] = np.ascontiguousarray(cw[:, j].reshape(32, 128).T)
    d["cb"] = np.ascontiguousarray(np.asarray(inp["L1_ffn_conv_b"], f32).reshape(32, 128).T)
    kk = np.arange(128)[:, None]
    qq = np.arange(512)[None, :]
    dmv = np.stack([(kk + 128 * i <= qq) for i in range(4)], 0).astype(f32)
    d["dmaskT"] = np.ascontiguousarray(dmv.reshape(4 * 128, 512)).astype(ml_dtypes.bfloat16)
    d["identT"] = np.eye(128, dtype=f32).astype(ml_dtypes.bfloat16)
    d["identF"] = np.eye(128, dtype=f32)
    d["hs2T"] = np.ascontiguousarray(
        np.stack([np.arange(128) < 64, np.arange(128) >= 64], 1).astype(f32)).astype(ml_dtypes.bfloat16)
    _SHARED_CACHE["key"] = key
    _SHARED_CACHE["d"] = d
    return d


def _prep_core(c, inp):
    b, h = c >> 1, c & 1
    own = slice(2048 * h, 2048 * h + 2048)
    oth = slice(2048 * (1 - h), 2048 * (1 - h) + 2048)
    f32 = np.float32
    x = np.asarray(inp["L0_attn_x"][b], f32)
    d = dict(_prep_shared(inp))
    d["x0"] = np.ascontiguousarray(np.concatenate([x[own], x[oth]], 0))
    cos, sin = _rope_tables(h, f32)
    d["ropec"] = cos
    d["ropes"] = sin
    d["parity"] = np.array([[h]], np.int32)
    d["killpb"] = np.full((128, 1), KILL * (1 - h), f32)
    d["halom"] = np.tile(np.array([[float(h), float(1 - h)]], f32), (128, 1))
    return d


class Sem:
    def __init__(self, h):
        self.h = h
        self.n = 0


class Em:
    def __init__(self):
        self.TH = {k: [] for k in "tvsgy"}

    def add(self, e, fn):
        self.TH[e].append(fn)

    def wait(self, e, sem, val=None):
        v = sem.n if val is None else val
        if v > 0:
            self.add(e, lambda E, h=sem.h, v=v: E.wait_ge(h, v))

    def run(self, e, fn, sem=None, k=1, selfwait=False):
        if sem is not None:
            if selfwait and sem.n > 0:
                v0 = sem.n
                self.add(e, lambda E, h=sem.h, v0=v0: E.wait_ge(h, v0))
            sem.n += k
            self.add(e, lambda E, fn=fn, h=sem.h, k=k: fn(E).then_inc(h, k))
        else:
            self.add(e, fn)


def build():
    nc = bass.Bass("TRN2", target_bir_lowering=False)
    es = contextlib.ExitStack()

    ap = lambda t: t.ap()
    x0 = ap(nc.dram_tensor("x0", [NFULL, DM], F32, kind="ExternalInput"))
    wq = ap(nc.dram_tensor("wq", [DM, DM], F32, kind="ExternalInput"))
    wk = ap(nc.dram_tensor("wk", [DM, DM], F32, kind="ExternalInput"))
    wv = ap(nc.dram_tensor("wv", [DM, DM], F32, kind="ExternalInput"))
    wo = ap(nc.dram_tensor("wo", [DM, DM], F32, kind="ExternalInput"))
    lamv = ap(nc.dram_tensor("lamv", [1, 256], F32, kind="ExternalInput"))
    lng = ap(nc.dram_tensor("lng", [128, 1], F32, kind="ExternalInput"))
    lnb = ap(nc.dram_tensor("lnb", [128, 1], F32, kind="ExternalInput"))
    gw1T = ap(nc.dram_tensor("gw1T", [DM, DM], F32, kind="ExternalInput"))
    gb1T = ap(nc.dram_tensor("gb1T", [128, 8], F32, kind="ExternalInput"))
    gw2T = ap(nc.dram_tensor("gw2T", [DM, 8], F32, kind="ExternalInput"))
    gb2c = ap(nc.dram_tensor("gb2c", [8, 1], F32, kind="ExternalInput"))
    w1T = ap(nc.dram_tensor("w1T", [DM, DFF], F32, kind="ExternalInput"))
    tw1T = ap(nc.dram_tensor("tw1T", [8, DM, DFF], F32, kind="ExternalInput"))
    b1r = ap(nc.dram_tensor("b1r", [1, DFF], F32, kind="ExternalInput"))
    b1T = ap(nc.dram_tensor("b1T", [128, 32], F32, kind="ExternalInput"))
    tb1 = ap(nc.dram_tensor("tb1", [8, DFF], F32, kind="ExternalInput"))
    w2T = ap(nc.dram_tensor("w2T", [DFF, DM], F32, kind="ExternalInput"))
    tw2T = ap(nc.dram_tensor("tw2T", [8, DFF, DM], F32, kind="ExternalInput"))
    b2r = ap(nc.dram_tensor("b2r", [1, DM], F32, kind="ExternalInput"))
    tb2 = ap(nc.dram_tensor("tb2", [8, DM], F32, kind="ExternalInput"))
    wqkv = ap(nc.dram_tensor("wqkv", [DM, 3 * DM], F32, kind="ExternalInput"))
    wout = ap(nc.dram_tensor("wout", [DM, DM], F32, kind="ExternalInput"))
    boutr = ap(nc.dram_tensor("boutr", [1, DM], F32, kind="ExternalInput"))
    l1w1T = ap(nc.dram_tensor("l1w1T", [DM, DFF], F32, kind="ExternalInput"))
    l1w2T = ap(nc.dram_tensor("l1w2T", [DFF, DM], F32, kind="ExternalInput"))
    cw0 = ap(nc.dram_tensor("cw0", [128, 32], F32, kind="ExternalInput"))
    cw1 = ap(nc.dram_tensor("cw1", [128, 32], F32, kind="ExternalInput"))
    cw2 = ap(nc.dram_tensor("cw2", [128, 32], F32, kind="ExternalInput"))
    cb = ap(nc.dram_tensor("cb", [128, 32], F32, kind="ExternalInput"))
    ropec = ap(nc.dram_tensor("ropec", [128, NTOK], F32, kind="ExternalInput"))
    ropes = ap(nc.dram_tensor("ropes", [128, NTOK], F32, kind="ExternalInput"))
    parity = ap(nc.dram_tensor("parity", [1, 1], I32, kind="ExternalInput"))
    killpb = ap(nc.dram_tensor("killpb", [128, 1], F32, kind="ExternalInput"))
    halom = ap(nc.dram_tensor("halom", [128, 2], F32, kind="ExternalInput"))
    dmaskT = ap(nc.dram_tensor("dmaskT", [4 * 128, 512], BF16, kind="ExternalInput"))
    identT = ap(nc.dram_tensor("identT", [128, 128], BF16, kind="ExternalInput"))
    identFT = ap(nc.dram_tensor("identF", [128, 128], F32, kind="ExternalInput"))
    hs2T = ap(nc.dram_tensor("hs2T", [128, 2], BF16, kind="ExternalInput"))
    out = ap(nc.dram_tensor("out", [NTOK, DM], F32, kind="ExternalOutput"))

    k_scr = ap(nc.dram_tensor("k_scr", [DM, NFULL], BF16))
    v_scr = ap(nc.dram_tensor("v_scr", [NFULL, DM], BF16))
    s_scr = ap(nc.dram_tensor("s_scr", [NTOK, DM], F32))
    x1_scr = ap(nc.dram_tensor("x1_scr", [NTOK, DM], F32))
    mw1_scr = ap(nc.dram_tensor("mw1_scr", [DM, DFF], BF16))
    mw2_scr = ap(nc.dram_tensor("mw2_scr", [DFF, DM], BF16))
    w1q_scr = ap(nc.dram_tensor("w1q_scr", [DM, DFF], BF16))
    w2q_scr = ap(nc.dram_tensor("w2q_scr", [DFF, DM], BF16))
    hs_scr = ap(nc.dram_tensor("hs_scr", [DFF, NTOK], BF16))
    kv_in = ap(nc.dram_tensor("kv_in", [2050, NTOK], BF16))
    kv_out = ap(nc.dram_tensor("kv_out", [4100, NTOK], BF16))
    halo_in = ap(nc.dram_tensor("halo_in", [2, DM], F32))
    halo_out = ap(nc.dram_tensor("halo_out", [4, DM], F32))
    taps = {}
    for t_ in DEV_TAPS:
        shp = {"xn0": [128, 8 * NTOK], "g8": [8, 1], "s_l0": [NTOK, DM],
               "x1": [NTOK, DM], "mw1": [DM, DFF], "s_l1": [NTOK, DM],
               "hs": [DFF, NTOK], "kq": [128, 16], "w1q": [DM, DFF]}[t_]
        dt_ = BF16 if t_ in ("mw1", "hs", "w1q") else F32
        taps[t_] = ap(nc.dram_tensor("tap_" + t_, shp, dt_, kind="ExternalOutput"))

    sb = lambda n, shp, dt: es.enter_context(nc.sbuf_tensor(n, shp, dt))
    xnF = sb("xnF", [128, 8 * NTOK], BF16)
    QF = sb("QF", [128, 8 * NTOK], BF16)
    POOL1 = sb("POOL1", [128, 32 * 512], BF16)
    POOL2 = sb("POOL2", [128, 4096], F32)
    POOL3 = sb("POOL3", [128, 4100], F32)
    wblk = sb("wblk", [128, 10 * 512], BF16)
    kxb = sb("kxb", [128, 2 * NFULL], BF16)
    eT = sb("eT", [128, 2048], BF16)
    oF = sb("oF", [128, 8 * 512], BF16)
    bcB = sb("bcB", [128, 1024], F32)
    bcC = sb("bcC", [128, 1024], F32)
    xti = sb("xti", [128, 2048], I32)
    mxT = sb("mxT", [128, NTOK], BF16)
    dmask = sb("dmask", [128, 4 * 512], BF16)
    rows = sb("rows", [4, 3104], F32)
    stat = sb("stat", [128, 64], F32)
    gwS = sb("gwS", [128, 64], BF16)
    ctap = sb("ctap", [128, 128], F32)
    ident = sb("ident", [128, 128], BF16)
    identF = sb("identFs", [128, 128], F32)
    hs2 = sb("hs2", [128, 2], BF16)
    onesb = sb("onesb", [128, 2], BF16)
    onesw = sb("onesw", [128, 128], BF16)
    biasP = sb("biasP", [128, 8], F32)
    biasPK = sb("biasPK", [128, 8], F32)
    onesf = sb("onesf", [128, 1], F32)
    onesr = sb("onesr", [1, 128], F32)
    mb1T = sb("mb1T", [128, 32], F32)
    biasH = sb("biasH", [128, 16], F32)
    biasHK = sb("biasHK", [128, 16], F32)
    q2m = sb("q2m", [2, 16], F32)
    k2m = sb("k2m", [2, 16], F32)
    lamS = sb("lamS", [128, 1], F32)
    klS = sb("klS", [128, 1], F32)
    hmS = sb("hmS", [128, 2], F32)
    sc1T = sb("sc1T", [128, 16], F32)
    c2T = sb("c2T", [128, 16], F32)
    xqh = sb("xqh", [128, 8], BF16)
    g8c = sb("g8c", [8, 1], F32)
    ps = es.enter_context(nc.psum_tensor("ps", [128, 4096], F32))

    kgrp = kxb
    vgrp = wblk
    uBlk = POOL1
    xqF = POOL1
    R_SUM = rows[0:1, 0:1024]
    R_RCP = rows[0:1, 1024:2048]
    R_MU = rows[0:1, 2048:2560]
    R_VAR = rows[0:1, 2560:3072]
    R_LAM = rows[0:1, 0:544]
    R_NEGB = rows[0:1, 0:16]
    R_G8R = rows[0:1, 16:24]
    R_B1 = rows[0:1, 24:152]
    R_MB2 = rows[0:1, 1024:2048]
    R_KN = rows[0:1, 0:80]
    R_KNOWN = rows[2:3, 1024:1040]
    R_HFIRST = rows[2:3, 0:1024]
    R_HLAST = rows[3:4, 0:1024]
    R_WM = rows[0:1, 3072:3080]
    R_HS = rows[0:1, 3080:3088]

    BK = lambda k: slice(512 * k, 512 * k + 512)

    MM = Sem(es.enter_context(nc.semaphore("MM")))
    SS = Sem(es.enter_context(nc.semaphore("SS")))
    VV = Sem(es.enter_context(nc.semaphore("VV")))
    CC = Sem(es.enter_context(nc.semaphore("CC")))
    STO = Sem(es.enter_context(nc.semaphore("STO")))
    WQA = Sem(es.enter_context(nc.semaphore("WQA")))
    WQB = Sem(es.enter_context(nc.semaphore("WQB")))
    LDW = Sem(es.enter_context(nc.semaphore("LDW")))
    LDO = Sem(es.enter_context(nc.semaphore("LDO")))
    MWA = Sem(es.enter_context(nc.semaphore("MWA")))
    MWB = Sem(es.enter_context(nc.semaphore("MWB")))
    LYC = Sem(es.enter_context(nc.semaphore("LYC")))
    LYX = Sem(es.enter_context(nc.semaphore("LYX")))
    LYW = Sem(es.enter_context(nc.semaphore("LYW")))
    LYO = Sem(es.enter_context(nc.semaphore("LYO")))
    LSWA = Sem(es.enter_context(nc.semaphore("LSWA")))
    LSWB = Sem(es.enter_context(nc.semaphore("LSWB")))
    LSKA = Sem(es.enter_context(nc.semaphore("LSKA")))
    LSKB = Sem(es.enter_context(nc.semaphore("LSKB")))
    LSS = Sem(es.enter_context(nc.semaphore("LSS")))
    LSX1 = Sem(es.enter_context(nc.semaphore("LSX1")))
    LSH = Sem(es.enter_context(nc.semaphore("LSH")))
    MSA = Sem(es.enter_context(nc.semaphore("MSA")))
    MSB = Sem(es.enter_context(nc.semaphore("MSB")))
    WVA = Sem(es.enter_context(nc.semaphore("WVA")))
    WVB = Sem(es.enter_context(nc.semaphore("WVB")))
    LWA = Sem(es.enter_context(nc.semaphore("LWA")))
    LWB = Sem(es.enter_context(nc.semaphore("LWB")))
    LW2A = Sem(es.enter_context(nc.semaphore("LW2A")))
    LW2B = Sem(es.enter_context(nc.semaphore("LW2B")))
    LOA = Sem(es.enter_context(nc.semaphore("LOA")))
    LOB = Sem(es.enter_context(nc.semaphore("LOB")))

    em = Em()
    ring_cons = {}

    def gdma(o, i, sem):
        em.run('g', lambda g, o=o, i=i: g.dma_start(out=o, in_=i), sem, 16)

    def godma(o, i, ns=False):
        em.wait('g', LDO)
        em.run('g', lambda g, o=o, i=i, ns=ns: g.dma_start(
            out=o, in_=i, allow_slow_non_contiguous=ns), LDO, 16)

    def ydma(o, i, sem):
        em.run('y', lambda y, o=o, i=i: y.dma_start(out=o, in_=i), sem, 16)

    def yodma(o, i):
        em.wait('y', LYO)
        em.run('y', lambda y, o=o, i=i: y.dma_start(out=o, in_=i), LYO, 16)

    def mm(o, l, r, st, sp, tp=None, sgc=False):
        em.run('t', lambda t, o=o, l=l, r=r, st=st, sp=sp, tp=tp, sgc=sgc:
               t.matmul(o, l, r, start=st, stop=sp, tile_position=tp,
                        skip_group_check=sgc), MM)

    def tr(o, i):
        mm(o, i, ident[:], True, True)

    def act(o, i, f, bias=0.0, scale=1.0, accum=None):
        em.run('s', lambda s, o=o, i=i, f=f, bias=bias, scale=scale, accum=accum:
               s.activation(out=o, in_=i, func=f, bias=bias, scale=scale,
                            accum_out=accum), SS, selfwait=True)

    def vop(name, *a, **kw):
        em.run('v', lambda v, name=name, a=a, kw=kw: getattr(v, name)(*a, **kw), VV,
               selfwait=True)

    def wt(e, sem, val=None):
        em.wait(e, sem, val)

    vop('memset', onesb[:], 1.0)
    vop('memset', onesf[:], 1.0)
    vop('memset', onesr[:], 1.0)
    vop('memset', q2m[:], 0.0)
    vop('memset', k2m[:], 0.0)
    vop('memset', stat[0:8, 20:22], 0.0)
    vop('memset', mxT[:], 0.0)
    vop('memset', onesw[:], 1.0)
    vop('memset', stat[:, 19:20], 1e-5)
    ydma(ident[:], identT, LYC)
    ydma(identF[:], identFT, LYC)
    ydma(hs2[:], hs2T, LYC)
    ydma(stat[:, 40:41], lng, LYC)
    ydma(stat[:, 41:42], lnb, LYC)
    ydma(klS[:], killpb, LYC)
    ydma(hmS[:], halom, LYC)
    ydma(ctap[:, 0:32], cw0, LYC)
    ydma(ctap[:, 32:64], cw1, LYC)
    ydma(ctap[:, 64:96], cw2, LYC)
    ydma(ctap[:, 96:128], cb, LYC)
    for j in range(4):
        ydma(dmask[:, 512 * j:512 * j + 512], dmaskT[128 * j:128 * j + 128, :], LYC)
    ydma(R_LAM[0:1, 0:256], lamv, LYC)
    lngS, lnbS = stat[:, 40:41], stat[:, 41:42]
    g8aP = stat[0:8, 21:22]

    lam_init = float(0.8 - 0.6 * np.exp(-0.3 * 12))
    wt('v', LYC)
    vop('tensor_mul', R_LAM[0:1, 256:320], R_LAM[0:1, 0:64], R_LAM[0:1, 64:128])
    vop('tensor_mul', R_LAM[0:1, 320:384], R_LAM[0:1, 128:192], R_LAM[0:1, 192:256])
    vop('reduce_sum', R_LAM[0:1, 384:385], R_LAM[0:1, 256:320], AX.X)
    vop('reduce_sum', R_LAM[0:1, 385:386], R_LAM[0:1, 320:384], AX.X)
    wt('s', VV)
    act(R_LAM[0:1, 386:388], R_LAM[0:1, 384:386], AF.Exp)
    wt('v', SS)
    vop('scalar_tensor_tensor', out=R_LAM[0:1, 388:389], in0=R_LAM[0:1, 387:388],
        scalar=-lam_init, in1=R_LAM[0:1, 386:387], op0=ALU.add, op1=ALU.subtract)
    wt('t', VV)
    mm(ps[:, BK(7)][:, 0:1], onesr[0:1, :], R_LAM[0:1, 388:389], True, True)
    wt('v', MM)
    vop('tensor_copy', lamS[:, 0:1], ps[:, BK(7)][:, 0:1])

    def wq_tiles(w):
        r, cdim = w.shape
        cs = min(cdim, 2048)
        return [(w[128 * i:128 * i + 128, cs * j:cs * j + cs], cs)
                for i in range(r // 128) for j in range(cdim // cs)]

    for wi, (wsrc, wdst) in enumerate([(l1w1T, w1q_scr), (l1w2T, w2q_scr)]):
        tiles = wq_tiles(wsrc)
        dtiles = wq_tiles(wdst)
        sumc = 43 + wi
        vop('memset', stat[:, sumc:sumc + 1], 0.0)
        for idx, (tin, cs) in enumerate(tiles):
            sl = idx % 2
            wqs = WQA if sl == 0 else WQB
            em.wait('g', VV, ring_cons.get(("wq", sl), 0))
            em.wait('g', (LSWA if sl == 0 else LSWB), ring_cons.get(("wqs", sl), 0))
            gdma(POOL1[:, 2048 * sl:2048 * sl + cs], tin, wqs)
            wt('v', wqs)
            vop('reduce_sum', stat[:, 47:48], POOL1[:, 2048 * sl:2048 * sl + cs], AX.X,
                apply_absolute_value=True)
            vop('tensor_add', stat[:, sumc:sumc + 1], stat[:, sumc:sumc + 1], stat[:, 47:48])
            ring_cons[("wq", sl)] = VV.n
        wt('t', VV)
        mm(ps[0:1, BK(7)][:, 0:1], onesf[:, 0:1], stat[:, sumc:sumc + 1], True, True)
        wt('v', MM)
        nel = float(wsrc.shape[0] * wsrc.shape[1])
        vop('tensor_scalar', out=R_WM[0:1, wi:wi + 1], in0=ps[0:1, BK(7)][:, 0:1],
            scalar1=1.0 / nel, scalar2=1e-5, op0=ALU.mult, op1=ALU.max)
        vop('reciprocal', R_WM[0:1, 2 + wi:3 + wi], R_WM[0:1, wi:wi + 1])
        wt('t', VV)
        mm(ps[:, BK(7)][:, 0:1], onesr[0:1, :], R_WM[0:1, 2 + wi:3 + wi], True, True)
        wt('v', MM)
        vop('tensor_copy', stat[:, 45 + wi * 10:46 + wi * 10], ps[:, BK(7)][:, 0:1])
        scol = stat[:, 45 + wi * 10:46 + wi * 10]
        for idx, (tin, cs) in enumerate(tiles):
            sl = idx % 2
            wqs = WQA if sl == 0 else WQB
            em.wait('g', VV, ring_cons.get(("wq", sl), 0))
            em.wait('g', (LSWA if sl == 0 else LSWB), ring_cons.get(("wqs", sl), 0))
            gdma(POOL1[:, 2048 * sl:2048 * sl + cs], tin, wqs)
            wt('v', wqs)
            src_ = POOL1[:, 2048 * sl:2048 * sl + cs]
            tpos = POOL1[:, 4096:4096 + cs]
            tneg = POOL1[:, 6144:6144 + cs]
            vop('tensor_scalar', out=tpos, in0=src_, scalar1=scol, scalar2=0.5,
                op0=ALU.mult, op1=ALU.is_ge)
            vop('tensor_scalar', out=tneg, in0=src_, scalar1=scol, scalar2=-0.5,
                op0=ALU.mult, op1=ALU.is_le)
            vop('tensor_sub', src_, tpos, tneg)
            ring_cons[("wq", sl)] = VV.n
            wt('y', VV)
            lsw = LSWA if sl == 0 else LSWB
            ydma(dtiles[idx][0], src_, lsw)
            ring_cons[("wqs", sl)] = lsw.n
    if "w1q" in taps:
        wt('g', LSWA)
        wt('g', LSWB)
        godma(taps["w1q"], w1q_scr)

    xt0 = POOL3[:, 0:1024]
    xt1 = POOL3[:, 1024:2048]
    xt2 = POOL3[:, 2048:3072]

    def ln_tile(src_rows, want_f32=False):
        wt('y', VV)
        ydma(xt0, src_rows, LYX)
        wt('v', LYX)
        vop('bn_stats', stat[:, 0:6], xt0[:, 0:512])
        vop('bn_stats', stat[:, 6:12], xt0[:, 512:1024])
        vop('bn_aggr', stat[:, 12:14], stat[:, 0:12])
        wt('s', VV)
        act(stat[:, 14:15], stat[:, 13:14], AF.Sqrt, bias=stat[:, 19:20])
        wt('v', SS)
        vop('reciprocal', stat[:, 15:16], stat[:, 14:15])
        vop('tensor_scalar', out=stat[:, 16:17], in0=stat[:, 12:13],
            scalar1=stat[:, 15:16], scalar2=-1.0, op0=ALU.mult, op1=ALU.mult)
        wt('v', MM)
        vop('tensor_scalar', out=eT[:, 0:1024], in0=xt0,
            scalar1=stat[:, 15:16], scalar2=stat[:, 16:17], op0=ALU.mult, op1=ALU.add)
        if want_f32:
            vop('tensor_scalar', out=xt2, in0=xt0,
                scalar1=stat[:, 15:16], scalar2=stat[:, 16:17], op0=ALU.mult, op1=ALU.add)

    def ln_transpose(dst_col_of_chunk, src=None):
        wt('t', VV)
        wt('t', SS)
        for j in range(8):
            tr(ps[:, j * 128:j * 128 + 128], (src or eT)[:, j * 128:j * 128 + 128])
        wt('s', MM)
        for j in range(8):
            act(dst_col_of_chunk(j), ps[:, j * 128:j * 128 + 128], AF.Copy)

    for tc in range(NTC):
        ln_tile(x0[128 * tc:128 * tc + 128, :])
        ln_transpose(lambda j, tc=tc: xnF[:, j * NTOK + tc * 128: j * NTOK + tc * 128 + 128])
    XNF_OWN_SS = SS.n

    def load_colblock(seg, w, c0, cw):
        em.wait('g', MM)
        for dinc in range(8):
            gdma(wblk[:, seg + dinc * cw: seg + dinc * cw + cw],
                 w[128 * dinc:128 * dinc + 128, c0:c0 + cw], LDW)

    def xsrc(tb, dinc, j0, jn):
        if tb < 4:
            return xnF[:, dinc * NTOK + tb * 512 + j0: dinc * NTOK + tb * 512 + j0 + jn]
        return kxb[:, dinc * 512 + j0: dinc * 512 + j0 + jn]

    kv_stage = [eT[:, 1024:1536], eT[:, 1536:2048]]
    sq_tile = oF[:, 512:1024]
    gb1S = stat[:, 48:56]
    yodma(gb1S, gb1T)
    for hc in range(8):
        godma(gwS[:, hc * 8:hc * 8 + 8], gw2T[128 * hc:128 * hc + 128, :])
    wt('s', LYO)
    wt('t', LDO)
    pwc = [0]
    WQ = [WQA, WQB]
    WV = [WVA, WVB]
    em.wait('g', VV)

    def pw_load(w, c0):
        sl = pwc[0] % 2
        pwc[0] += 1
        em.wait('g', MM, ring_cons.get(("pw", sl), 0))
        for dinc in range(8):
            gdma(wblk[:, 1024 * sl + dinc * 128: 1024 * sl + dinc * 128 + 128],
                 w[128 * dinc:128 * dinc + 128, c0:c0 + 128], WQ[sl])
        return sl

    for tb in range(8):
        if tb >= 4:
            for j in range(4):
                ln_tile(x0[2048 + 512 * (tb - 4) + 128 * j:
                           2048 + 512 * (tb - 4) + 128 * j + 128, :])
                ln_transpose(lambda dc, j=j: kxb[:, dc * 512 + j * 128: dc * 512 + j * 128 + 128])
        for which, wmat in (("k", wk), ("q", wq)):
            if which == "q" and tb >= 4:
                continue
            for dc in range(8):
                sl = pw_load(wmat, dc * 128)
                bank = 2 + (dc % 2)
                wt('t', WQ[sl])
                wt('t', SS)
                wt('t', VV)
                for dinc in range(8):
                    mm(ps[:, BK(bank)], wblk[:, 1024 * sl + dinc * 128: 1024 * sl + dinc * 128 + 128],
                       xsrc(tb, dinc, 0, 512), dinc == 0, dinc == 7)
                ring_cons[("pw", sl)] = MM.n
                wt('s', MM)
                wt('s', LSKA if dc % 2 == 0 else LSKB)
                stg = kv_stage[dc % 2]
                if which == "q":
                    act(QF[:, dc * NTOK + tb * 512: dc * NTOK + tb * 512 + 512],
                        ps[:, BK(bank)], AF.Copy, scale=0.125)
                    act(sq_tile, ps[:, BK(bank)], AF.Square, scale=0.125)
                else:
                    act(stg, ps[:, BK(bank)], AF.Copy)
                    act(sq_tile, ps[:, BK(bank)], AF.Square)
                    wt('y', SS)
                    ydma(k_scr[128 * dc:128 * dc + 128, 512 * tb:512 * tb + 512], stg,
                         LSKA if dc % 2 == 0 else LSKB)
                wt('t', SS)
                wt('t', VV)
                mm(ps[0:2, BK(5)], hs2[:], sq_tile, True, True)
                wt('v', MM)
                dstm = q2m if which == "q" else k2m
                vop('reduce_max', stat[0:2, 18:19], ps[0:2, BK(5)], AX.X)
                vop('tensor_max', dstm[0:2, dc:dc + 1], dstm[0:2, dc:dc + 1], stat[0:2, 18:19])
        vbanks = [0, 1, 6, 7]
        for col in range(2):
            for hc in range(8):
                wsl = hc % 2
                em.wait('g', MM, ring_cons.get(("wv", wsl), 0))
                gdma(wblk[:, 3072 + 512 * wsl: 3072 + 512 * wsl + 512],
                     wv[128 * hc:128 * hc + 128, col * 512: col * 512 + 512], WV[wsl])
                wt('t', WV[wsl])
                if hc == 0:
                    wt('t', SS)
                    wt('t', VV)
                for tc in range(4):
                    mm(ps[:, BK(vbanks[tc])], xsrc(tb, hc, tc * 128, 128),
                       wblk[:, 3072 + 512 * wsl: 3072 + 512 * wsl + 512],
                       hc == 0, hc == 7)
                ring_cons[("wv", wsl)] = MM.n
            wt('s', MM)
            for tc in range(4):
                stg = kv_stage[tc % 2]
                wt('s', LSKA if tc % 2 == 0 else LSKB)
                act(stg, ps[:, BK(vbanks[tc])], AF.Copy)
                wt('y', SS)
                ydma(v_scr[512 * tb + 128 * tc: 512 * tb + 128 * tc + 128,
                           col * 512: col * 512 + 512], stg,
                     LSKA if tc % 2 == 0 else LSKB)
        for hc in range(8):
            sl = pw_load(gw1T, hc * 128)
            bank = 2 + (hc % 2)
            wt('t', WQ[sl])
            wt('t', SS)
            for dinc in range(8):
                mm(ps[:, BK(bank)], wblk[:, 1024 * sl + dinc * 128: 1024 * sl + dinc * 128 + 128],
                   xsrc(tb, dinc, 0, 512), dinc == 0, dinc == 7)
            ring_cons[("pw", sl)] = MM.n
            wt('s', MM)
            act(oF[:, 0:512], ps[:, BK(bank)], AF.Relu, bias=gb1S[:, hc:hc + 1])
            wt('t', SS)
            mm(ps[0:8, BK(4)], gwS[:, hc * 8:hc * 8 + 8], oF[:, 0:512], hc == 0, hc == 7)
        wt('s', MM)
        act(bcC[0:8, 0:512], ps[0:8, BK(4)], AF.Copy, accum=stat[0:8, 20:21])
        wt('v', SS)
        vop('tensor_add', g8aP, g8aP, stat[0:8, 20:21])
    gb2S = stat[0:8, 57:58]
    yodma(gb2S, gb2c)
    wt('v', LYO)
    vop('tensor_scalar', out=g8c[0:8, 0:1], in0=g8aP, scalar1=1.0 / NFULL,
        scalar2=gb2S, op0=ALU.mult, op1=ALU.add)
    gBc = stat[:, 32:40]
    wt('t', VV)
    mm(ps[0:1, BK(5)][:, 0:8], g8c[0:8, 0:1], identF[0:8, 0:8], True, True)
    wt('v', MM)
    vop('tensor_copy', R_G8R, ps[0:1, BK(5)][:, 0:8])
    wt('t', VV)
    mm(ps[:, BK(6)][:, 0:8], onesr[0:1, :], R_G8R, True, True)
    wt('v', MM)
    vop('tensor_copy', gBc, ps[:, BK(6)][:, 0:8])
    vop('tensor_mul', q2m[0:2, 8:16], q2m[0:2, 0:8], k2m[0:2, 0:8])
    wt('s', VV)
    act(k2m[0:2, 8:16], q2m[0:2, 8:16], AF.Sqrt)
    wt('v', SS)
    vop('tensor_scalar', out=q2m[0:2, 8:16], in0=k2m[0:2, 8:16], scalar1=-0.5, op0=ALU.mult, scalar2=None)
    wt('g', VV)
    nb_v = R_NEGB.rearrange("o (a s) -> o a s", s=2)
    godma(nb_v[:, :, 0:1], q2m[0:1, 8:16], ns=True)
    godma(nb_v[:, :, 1:2], q2m[1:2, 8:16], ns=True)
    wt('t', LDO)
    mm(ps[:, BK(6)][:, 0:16], onesr[0:1, :], R_NEGB, True, True)
    wt('v', MM)
    vop('tensor_copy', biasH[:, 0:16], ps[:, BK(6)][:, 0:16])
    bp_v = biasH[:, 0:16].rearrange("p (a s) -> p a s", s=2)
    vop('tensor_tensor', biasP[:, 0:8], bp_v[:, :, 0:1], bp_v[:, :, 1:2], ALU.min)
    wt('t', VV)
    for c in range(32):
        em.wait('y', MM)
        ydma(R_B1, b1r[0:1, 128 * c:128 * c + 128], LYW)
        ydma(bcC[0:8, 0:128], tb1[:, 128 * c:128 * c + 128], LYW)
        wt('t', LYW)
        mm(ps[:, BK(5)][:, c:c + 1], bcC[0:8, 0:128], g8c[0:8, 0:1], True, False)
        mm(ps[:, BK(5)][:, c:c + 1], R_B1, onesf[0:1, 0:1], False, True)
    wt('v', MM)
    vop('tensor_copy', mb1T[:], ps[:, BK(5)][:, 0:32])
    yodma(R_MB2, b2r)
    for j in range(2):
        em.wait('y', MM)
        ydma(bcC[0:8, 128:640], tb2[:, 512 * j:512 * j + 512], LYW)
        wt('t', LYW)
        wt('t', VV)
        mm(ps[0:1, BK(6)], g8c[0:8, 0:1], bcC[0:8, 128:640], True, True)
        wt('v', MM)
        wt('v', LYO)
        vop('tensor_add', R_MB2[0:1, 512 * j:512 * j + 512],
            ps[0:1, BK(6)], R_MB2[0:1, 512 * j:512 * j + 512])
    for j in range(2):
        wt('t', VV)
        mm(ps[:, BK(6)], onesr[0:1, :], R_MB2[0:1, 512 * j:512 * j + 512], True, True)
        wt('v', MM)
        vop('tensor_copy', bcB[:, 512 * j:512 * j + 512], ps[:, BK(6)])
    if "g8" in taps:
        wt('g', VV)
        godma(taps["g8"], g8c[0:8, 0:1])
    if "kq" in taps:
        wt('g', VV)
        godma(taps["kq"], biasH[:, 0:16])
    if "xn0" in taps:
        wt('g', SS, XNF_OWN_SS)
        godma(taps["xn0"], xnF[:])

    mw_chunks = []
    for hc in range(8):
        for fcol in range(8):
            mw_chunks.append((w1T[128 * hc:128 * hc + 128, 512 * fcol:512 * fcol + 512],
                              [tw1T[e, 128 * hc:128 * hc + 128, 512 * fcol:512 * fcol + 512]
                               for e in range(8)],
                              mw1_scr[128 * hc:128 * hc + 128, 512 * fcol:512 * fcol + 512]))
    for fr in range(32):
        for col in range(2):
            mw_chunks.append((w2T[128 * fr:128 * fr + 128, 512 * col:512 * col + 512],
                              [tw2T[e, 128 * fr:128 * fr + 128, 512 * col:512 * col + 512]
                               for e in range(8)],
                              mw2_scr[128 * fr:128 * fr + 128, 512 * col:512 * col + 512]))
    mw_state = {"i": 0, "vfree": [0, 0], "yfree": [0, 0]}

    def emit_mw_chunk():
        i = mw_state["i"]
        if i >= len(mw_chunks):
            return
        mw_state["i"] = i + 1
        wsl, twl, dst = mw_chunks[i]
        sl = i % 2
        seg = 9 * 512 * sl
        mws = MWA if sl == 0 else MWB
        mss = MSA if sl == 0 else MSB
        em.wait('g', VV, mw_state["vfree"][sl])
        gdma(POOL1[:, seg:seg + 512], wsl, mws)
        for e in range(8):
            gdma(POOL1[:, seg + 512 * (e + 1):seg + 512 * (e + 2)], twl[e], mws)
        accb = mxT[:, 512 * sl:512 * sl + 512]
        wt('v', mws)
        em.wait('v', mss, mw_state["yfree"][sl])
        vop('scalar_tensor_tensor', out=accb, in0=POOL1[:, seg + 512:seg + 1024],
            scalar=gBc[:, 0:1], in1=POOL1[:, seg:seg + 512], op0=ALU.mult, op1=ALU.add)
        for e in range(1, 8):
            vop('scalar_tensor_tensor', out=accb,
                in0=POOL1[:, seg + 512 * (e + 1):seg + 512 * (e + 2)],
                scalar=gBc[:, e:e + 1], in1=accb, op0=ALU.mult, op1=ALU.add)
        mw_state["vfree"][sl] = VV.n
        wt('y', VV)
        ydma(dst, accb, mss)
        mw_state["yfree"][sl] = mss.n

    eseg = lambda st, i: eT[:, st * 2048 + i * 1024: st * 2048 + i * 1024 + 1024]
    of32 = POOL2[:, 0:512]
    otmp = POOL2[:, 512:1024]
    sevac = POOL2[:, 1024:1536]
    kbuf = [kxb[:, 0:4096], kxb[:, 4096:8192]]
    vbuf = [wblk[:, 0:4096], POOL1[:, 9216:13312]]
    WQ = [WQA, WQB]
    wt('s', LSKA)
    wt('s', LSKB)
    em.wait('g', VV)
    em.wait('g', LSWA)
    em.wait('g', LSWB)

    def l0_group_loads(g):
        if g >= 32:
            return
        vh = g % 8
        sl = g % 2
        em.wait('g', MM, ring_cons.get(("kv", sl), 0))
        em.wait('g', LSKA)
        em.wait('g', LSKB)
        gdma(kbuf[sl][:, 0:NFULL], k_scr[128 * vh:128 * vh + 128, :], WQ[sl])
        em.run('g', lambda g_, vh=vh, sl=sl: g_.dma_start(
            out=vbuf[sl][:, 0:NFULL].rearrange("p (c d) -> p c d", d=128),
            in_=v_scr[:, 128 * vh:128 * vh + 128].rearrange("(c p) d -> p c d", p=128)),
            WQ[sl], 16)

    l0_group_loads(0)
    for g in range(32):
        qb, vh = g // 8, g % 8
        sl = g % 2
        l0_group_loads(g + 1)
        kg, vg = kbuf[sl], vbuf[sl]

        def qrow(r0, r1, vh=vh, qb=qb):
            return QF[r0:r1, vh * NTOK + qb * 512: vh * NTOK + qb * 512 + 512]
        wt('t', WQ[sl])
        wt('t', VV)
        ss_after = {}
        pairbuf = lambda sp: eT[:, 0:2048] if sp == 0 else POOL3[:, 0:1024].bitcast(BF16)

        def scores2(p):
            if p >= 1:
                wt('t', SS, ss_after[p - 1])
            for j in range(2):
                c = 2 * p + j
                mm(ps[:, BK(2 * j)], kg[0:64, 128 * c:128 * c + 128],
                   qrow(0, 64), True, True, tp=(0, 0))
                mm(ps[:, BK(2 * j + 1)], kg[64:128, 128 * c:128 * c + 128],
                   qrow(64, 128), True, True, tp=(64, 0))
            wt('s', MM)
            act(pairbuf(p % 2), ps[:, 0:2048], AF.Exp, bias=biasP[:, vh:vh + 1])
            ss_after[p] = SS.n

        def av2(p, first, last):
            buf = pairbuf(p % 2)
            wt('t', SS, ss_after[p])
            for j in range(2):
                c = 2 * p + j
                fst = first and j == 0
                lst = last and j == 1
                e0 = buf[:, 1024 * j:1024 * j + 512]
                e1 = buf[:, 1024 * j + 512:1024 * j + 1024]
                mm(ps[0:128, BK(4)], vg[:, 128 * c:128 * c + 128], e0, fst, lst)
                mm(ps[0:128, BK(5)], vg[:, 128 * c:128 * c + 128], e1, fst, lst)
                mm(ps[:, BK(6)], onesw[:, 0:128], e0, fst, lst)
                mm(ps[:, BK(7)], onesw[:, 0:128], e1, fst, lst)

        scores2(0)
        for p in range(1, 16):
            scores2(p)
            av2(p - 1, p - 1 == 0, False)
        av2(15, False, True)
        ring_cons[("kv", sl)] = MM.n
        wt('v', MM)
        vop('reciprocal', bcC[:, 0:512], ps[:, BK(6)])
        vop('reciprocal', bcC[:, 512:1024], ps[:, BK(7)])
        vop('tensor_mul', of32, ps[:, BK(4)], bcC[:, 0:512])
        vop('scalar_tensor_tensor', out=otmp, in0=ps[:, BK(5)], scalar=lamS[:, 0:1],
            in1=bcC[:, 512:1024], op0=ALU.mult, op1=ALU.mult)
        vop('tensor_add', of32, of32, otmp)
        wt('s', VV)
        act(otmp, of32, AF.Square)
        wt('t', SS)
        mm(ps[64:65, BK(6)], onesf[:, 0:1], of32, True, True, tp=(0, 64))
        mm(ps[0:1, BK(1)], onesf[:, 0:1], otmp, True, True)
        wt('v', MM)
        vop('tensor_scalar', out=R_MU, in0=ps[64:65, BK(6)], scalar1=1.0 / 128,
            scalar2=None, op0=ALU.mult)
        vop('tensor_scalar', out=R_VAR, in0=ps[0:1, BK(1)], scalar1=1.0 / 128,
            scalar2=None, op0=ALU.mult)
        vop('tensor_mul', R_RCP[0:1, 0:512], R_MU, R_MU)
        vop('tensor_sub', R_VAR, R_VAR, R_RCP[0:1, 0:512])
        wt('s', VV)
        act(R_RCP[0:1, 512:1024], R_VAR, AF.Ln, bias=stat[0:1, 19:20])
        act(R_VAR, R_RCP[0:1, 512:1024], AF.Exp, scale=-0.5)
        wt('t', SS)
        mm(ps[:, BK(2)], onesr[0:1, :], R_MU, True, True)
        mm(ps[:, BK(3)], onesr[0:1, :], R_VAR, True, True)
        wt('v', MM)
        vop('tensor_copy', bcC[:, 0:512], ps[:, BK(2)])
        vop('tensor_copy', bcC[:, 512:1024], ps[:, BK(3)])
        vop('tensor_sub', of32, of32, bcC[:, 0:512])
        vop('tensor_mul', of32, of32, bcC[:, 512:1024])
        vop('tensor_scalar', out=oF[:, 512 * vh:512 * vh + 512], in0=of32,
            scalar1=lngS, scalar2=lnbS, op0=ALU.mult, op1=ALU.add)
        for _ in range(4):
            emit_mw_chunk()
        if vh == 7:
            for col in range(2):
                for vhh in range(8):
                    wsl = 4096 + 512 * (vhh % 2)
                    LO = LOA if vhh % 2 == 0 else LOB
                    em.wait('g', MM, ring_cons.get(("wo", vhh % 2), 0))
                    gdma(wblk[:, wsl:wsl + 512],
                         wo[128 * vhh:128 * vhh + 128, 512 * col:512 * col + 512], LO)
                    wt('t', LO)
                    if vhh == 0:
                        wt('t', VV)
                    for tc in range(4):
                        mm(ps[:, BK(tc)], oF[:, 512 * vhh + 128 * tc:512 * vhh + 128 * tc + 128],
                           wblk[:, wsl:wsl + 512], vhh == 0, vhh == 7)
                    ring_cons[("wo", vhh % 2)] = MM.n
                wt('v', MM)
                for tc in range(4):
                    wt('v', LSS)
                    vop('tensor_copy', sevac, ps[:, BK(tc)])
                    wt('y', VV)
                    ydma(s_scr[512 * qb + 128 * tc:512 * qb + 128 * tc + 128,
                               512 * col:512 * col + 512], sevac, LSS)
    if "s_l0" in taps:
        wt('g', LSS)
        godma(taps["s_l0"], s_scr)
    while mw_state["i"] < len(mw_chunks):
        emit_mw_chunk()
    if "mw1" in taps:
        wt('g', MSA)
        wt('g', MSB)
        godma(taps["mw1"], mw1_scr)

    for tb in range(4):
        for fc in range(32):
            sl = fc % 2
            LW = LWA if sl == 0 else LWB
            em.wait('y', MM, ring_cons.get(("mw1", sl), 0))
            if tb == 0 and fc == 0:
                em.wait('y', MSA)
                em.wait('y', MSB)
            for dinc in range(8):
                ydma(wblk[:, 1024 * sl + 128 * dinc:1024 * sl + 128 * dinc + 128],
                     mw1_scr[128 * dinc:128 * dinc + 128, fc * 128:fc * 128 + 128], LW)
            wt('t', LW)
            if fc < 2:
                wt('t', SS)
                wt('t', VV)
            else:
                wt('t', SS, ring_cons.get(("ge", sl), 0))
            for dinc in range(8):
                mm(ps[:, BK(sl)], wblk[:, 1024 * sl + 128 * dinc:1024 * sl + 128 * dinc + 128],
                   xnF[:, dinc * NTOK + tb * 512:dinc * NTOK + tb * 512 + 512],
                   dinc == 0, dinc == 7)
            ring_cons[("mw1", sl)] = MM.n
            wt('s', MM)
            act(uBlk[:, fc * 512:fc * 512 + 512], ps[:, BK(sl)], AF.Gelu_apprx_tanh,
                bias=mb1T[:, fc:fc + 1])
            ring_cons[("ge", sl)] = SS.n
        for fc in range(32):
            sl = fc % 2
            LW2 = LW2A if sl == 0 else LW2B
            em.wait('y', MM, ring_cons.get(("mw2", sl), 0))
            ydma(wblk[:, 2048 + 1024 * sl:2048 + 1024 * sl + 512],
                 mw2_scr[fc * 128:fc * 128 + 128, 0:512], LW2)
            ydma(wblk[:, 2048 + 1024 * sl + 512:2048 + 1024 * sl + 1024],
                 mw2_scr[fc * 128:fc * 128 + 128, 512:1024], LW2)
            wt('t', LW2)
            if fc == 0:
                wt('t', SS)
                wt('t', VV)
            for tc in range(4):
                for col in range(2):
                    mm(ps[:, BK(tc * 2 + col)],
                       uBlk[:, fc * 512 + tc * 128:fc * 512 + tc * 128 + 128],
                       wblk[:, 2048 + 1024 * sl + 512 * col:2048 + 1024 * sl + 512 * col + 512],
                       fc == 0, fc == 31)
            ring_cons[("mw2", sl)] = MM.n
        for tc in range(4):
            wt('y', VV)
            wt('y', LSS)
            ydma(xt0, x0[tb * 512 + tc * 128:tb * 512 + tc * 128 + 128, :], LYX)
            ydma(xt1, s_scr[tb * 512 + tc * 128:tb * 512 + tc * 128 + 128, :], LYX)
            for col in range(2):
                wt('v', MM)
                wt('v', LYX)
                wt('v', LSX1)
                vop('tensor_add', POOL2[:, 0:512], ps[:, BK(tc * 2 + col)],
                    bcB[:, 512 * col:512 * col + 512])
                wt('s', VV)
                act(POOL2[:, 512:1024], POOL2[:, 0:512], AF.Silu)
                wt('v', SS)
                vop('tensor_mul', POOL2[:, 1024:1536], POOL2[:, 512:1024],
                    xt1[:, 512 * col:512 * col + 512])
                vop('tensor_add', POOL2[:, 1536:2048], POOL2[:, 1024:1536],
                    xt0[:, 512 * col:512 * col + 512])
                wt('y', VV)
                ydma(x1_scr[tb * 512 + tc * 128:tb * 512 + tc * 128 + 128,
                            512 * col:512 * col + 512], POOL2[:, 1536:2048], LSX1)
    if "x1" in taps:
        wt('g', LSX1)
        godma(taps["x1"], x1_scr)

    wt('y', LSX1)
    for tc in range(NTC):
        wt('v', LDO)
        ln_tile(x1_scr[128 * tc:128 * tc + 128, :], want_f32=True)
        vop('reduce_max', stat[:, 22:23], xt2, AX.X, apply_absolute_value=True)
        vop('tensor_scalar_max', stat[:, 23:24], stat[:, 22:23], 1e-5)
        vop('reciprocal', stat[:, 24:25], stat[:, 23:24])
        vop('tensor_scalar', out=sc1T[:, tc:tc + 1], in0=stat[:, 23:24],
            scalar1=1.0 / 127.0, scalar2=None, op0=ALU.mult)
        vop('tensor_scalar', out=xt0, in0=xt2, scalar1=stat[:, 24:25], scalar2=127.0,
            op0=ALU.mult, op1=ALU.mult)
        vop('tensor_copy', xti[:, 0:1024], xt0)
        wt('v', MM)
        vop('tensor_copy', eT[:, 1024:2048], xti[:, 0:1024])
        if tc == 0:
            wt('g', VV)
            godma(R_HFIRST, xt2[0:1, :])
        if tc == NTC - 1:
            wt('g', VV)
            godma(R_HLAST, xt2[127:128, :])
        ln_transpose(lambda j, tc=tc: xnF[:, j * NTOK + tc * 128: j * NTOK + tc * 128 + 128])
        ln_transpose(lambda j, tc=tc: xqF[:, j * NTOK + tc * 128: j * NTOK + tc * 128 + 128],
                     src=eT[:, 1024:2048])
        if tc == 0:
            pass
    godma(halo_in[0:1, :], R_HFIRST)
    godma(halo_in[1:2, :], R_HLAST)
    wt('g', LDO)
    em.run('g', lambda g: g.collective_compute(
        "AllGather", ALU.bypass, replica_groups=[[0, 1], [2, 3], [4, 5], [6, 7]],
        ins=[halo_in], outs=[halo_out]), CC, 1)

    em.wait('g', MM)
    hfc = POOL3[:, 0:2050]
    hf2 = POOL3[:, 2050 + 0:2050 + 2048]
    ropeC = POOL2[:, 0:2048]
    ropeS = POOL2[:, 2048:4096]
    yodma(ropeC, ropec)
    yodma(ropeS, ropes)
    wt('v', LYO)
    vop('memset', q2m[:], 0.0)
    vop('memset', k2m[:], 0.0)
    P3A = POOL3[:, 0:2048]
    P3B = POOL3[:, 2048:4096]
    swap_mask = [i ^ 1 for i in range(32)]
    kstage = eT[:, 0:2048]
    ksq = mxT[:, 0:2048]

    def l1_qk_chunk(which, dc):
        base = 0 if which == "q" else 1024
        sl = pw_load(wqkv, base + dc * 128)
        for tb in range(4):
            bank = 2 + (tb % 2)
            wt('t', WQ[sl])
            wt('t', SS)
            wt('t', VV)
            for dinc in range(8):
                mm(ps[:, BK(bank)], wblk[:, 1024 * sl + dinc * 128:1024 * sl + dinc * 128 + 128],
                   xnF[:, dinc * NTOK + tb * 512:dinc * NTOK + tb * 512 + 512],
                   dinc == 0, dinc == 7)
            if tb == 3:
                ring_cons[("pw", sl)] = MM.n
            wt('s', MM)
            act(P3A[:, tb * 512:tb * 512 + 512], ps[:, BK(bank)], AF.Copy,
                scale=(0.125 if which == "q" else 1.0))
        wt('v', SS)
        vop('stream_shuffle', P3B, P3A, swap_mask)
        vop('tensor_mul', P3B, P3B, ropeS)
        vop('tensor_mul', P3A, P3A, ropeC)
        dst = (QF[:, dc * NTOK:dc * NTOK + NTOK] if which == "q" else kstage)
        if which == "k":
            wt('v', LSKA)
        vop('tensor_add', dst, P3A, P3B)
        wt('s', VV)
        act(ksq, dst, AF.Square)
        dstm = q2m if which == "q" else k2m
        for j in range(4):
            wt('t', SS)
            wt('t', VV)
            mm(ps[0:2, BK(5)], hs2[:], ksq[:, j * 512:j * 512 + 512], True, True)
            wt('v', MM)
            vop('reduce_max', stat[0:2, 18:19], ps[0:2, BK(5)], AX.X)
            vop('tensor_max', dstm[0:2, dc:dc + 1], dstm[0:2, dc:dc + 1], stat[0:2, 18:19])
        if which == "k":
            wt('y', VV)
            ydma(kv_in[128 * dc:128 * dc + 128, :], kstage, LSKA)

    for dc in range(8):
        l1_qk_chunk("q", dc)
        l1_qk_chunk("k", dc)
    for col in range(2):
        for tcg in range(4):
            for hc in range(8):
                wsl = hc % 2
                em.wait('g', MM, ring_cons.get(("wv", wsl), 0))
                gdma(wblk[:, 3072 + 512 * wsl:3072 + 512 * wsl + 512],
                     wqkv[128 * hc:128 * hc + 128, 2048 + col * 512:2048 + col * 512 + 512], WV[wsl])
                wt('t', WV[wsl])
                if hc == 0:
                    wt('t', SS)
                    wt('t', VV)
                for tc4 in range(4):
                    tc = tcg * 4 + tc4
                    mm(ps[:, BK([0, 1, 6, 7][tc4])],
                       xnF[:, hc * NTOK + tc * 128:hc * NTOK + tc * 128 + 128],
                       wblk[:, 3072 + 512 * wsl:3072 + 512 * wsl + 512],
                       hc == 0, hc == 7)
                ring_cons[("wv", wsl)] = MM.n
            wt('s', MM)
            for tc4 in range(4):
                tc = tcg * 4 + tc4
                t0 = tc * 128
                stg = kv_stage[tc4 % 2]
                wt('s', LSKA if tc4 % 2 == 0 else LSKB)
                act(stg, ps[:, BK([0, 1, 6, 7][tc4])], AF.Copy)
                wt('y', SS)
                ydma(kv_in[1024 + (t0 % 1024):1024 + (t0 % 1024) + 128,
                           (t0 // 1024) * 1024 + col * 512:(t0 // 1024) * 1024 + col * 512 + 512],
                     stg, LSKA if tc4 % 2 == 0 else LSKB)
    vop('memset', hf2[0:2, 0:2048], 0.0)
    wt('g', VV)
    godma(kv_in[2048:2050, :], hf2[0:2, 0:2048])
    knb = R_KNOWN.rearrange("o (a s) -> o a s", s=2)
    godma(knb[:, :, 0:1], k2m[0:1, 0:8], ns=True)
    godma(knb[:, :, 1:2], k2m[1:2, 0:8], ns=True)
    godma(kv_in[2049:2050, 0:16], R_KNOWN)
    wt('g', LSKA)
    wt('g', LSKB)
    wt('g', LDO)
    em.run('g', lambda g: g.collective_compute(
        "AllGather", ALU.bypass, replica_groups=[[0, 1], [2, 3], [4, 5], [6, 7]],
        ins=[kv_in], outs=[kv_out]), CC, 1)

    pari = sb("pari", [1, 1], I32)
    yodma(pari[:], parity)
    prow = rows[0:1, 2048:3072]
    _halo_lyo = LYO.n
    _halo_ldo = LDO.n

    def _halo_switch(g):
        g.wait_ge(CC.h, 1)
        g.wait_ge(LYO.h, _halo_lyo)
        g.wait_ge(LDO.h, _halo_ldo)
        with g.register() as r:
            g.load(r, pari[0:1, 0:1])
            idx = g.snap(r, min_val=0, max_val=1)
            for i in g.Switch(idx, 2):
                src = halo_out[2:3, :] if i == 0 else halo_out[1:2, :]
                g.dma_start(out=prow, in_=src).then_inc(LDO.h, 16)
    LDO.n += 16
    em.add('g', _halo_switch)

    vop('memset', mxT[:], 0.0)
    vop('memset', onesw[:], 1.0)
    wt('t', SS)
    wt('t', VV)
    for tc in range(16):
        mm(ps[0:1, tc * 128:tc * 128 + 128], sc1T[:, tc:tc + 1], identF[:], True, True)
    wt('v', MM)
    vop('tensor_copy', rows[0:1, 0:2048], ps[0:1, 0:2048])
    vop('tensor_scalar', out=rows[0:1, 0:2048], in0=rows[0:1, 0:2048],
        scalar1=R_WM[0:1, 0:1], scalar2=None, op0=ALU.mult)
    wt('t', VV)
    for j in range(4):
        mm(ps[:, 2048 + 512 * j:2560 + 512 * j], onesr[0:1, :],
           rows[0:1, 512 * j:512 * j + 512], True, True)
    wt('v', MM)
    cB = POOL2[:, 0:2048]
    vop('tensor_copy', cB, ps[:, 2048:4096])
    wt('v', LDO)
    vop('reduce_max', R_HS[0:1, 0:1], prow, AX.X, apply_absolute_value=True)
    vop('tensor_scalar_max', R_HS[0:1, 1:2], R_HS[0:1, 0:1], 1e-5)
    vop('reciprocal', R_HS[0:1, 2:3], R_HS[0:1, 1:2])
    wt('t', VV)
    for hc in range(8):
        mm(ps[:, BK(6)][:, hc:hc + 1], prow[0:1, 128 * hc:128 * hc + 128],
           onesf[0:1, 0:1], True, True)
    mm(ps[:, BK(7)][:, 0:1], onesr[0:1, :], R_HS[0:1, 2:3], True, True)
    mm(ps[:, BK(7)][:, 1:2], onesr[0:1, :], R_HS[0:1, 1:2], True, True)
    wt('v', MM)
    vop('tensor_copy', stat[:, 25:26], ps[:, BK(7)][:, 0:1])
    vop('tensor_copy', stat[:, 26:27], ps[:, BK(7)][:, 1:2])
    vop('tensor_scalar', out=xt0[:, 0:8], in0=ps[:, BK(6)][:, 0:8],
        scalar1=stat[:, 25:26], scalar2=127.0, op0=ALU.mult, op1=ALU.mult)
    vop('tensor_copy', xti[:, 0:8], xt0[:, 0:8])
    vop('tensor_copy', xqh[:, 0:8], xti[:, 0:8])
    wt('t', VV)
    mm(ps[:, BK(7)][:, 2:3], onesr[0:1, :], R_WM[0:1, 0:1], True, True)
    wt('v', MM)
    vop('tensor_copy', stat[:, 29:30], ps[:, BK(7)][:, 2:3])
    vop('tensor_scalar', out=stat[:, 28:29], in0=stat[:, 26:27],
        scalar1=stat[:, 29:30], scalar2=1.0 / 127.0, op0=ALU.mult, op1=ALU.mult)
    for fc in range(32):
        sl = fc % 2
        LW = LWA if sl == 0 else LWB
        em.wait('y', MM, ring_cons.get(("w1q", sl), 0))
        for dinc in range(8):
            ydma(wblk[:, 1024 * sl + 128 * dinc:1024 * sl + 128 * dinc + 128],
                 w1q_scr[128 * dinc:128 * dinc + 128, fc * 128:fc * 128 + 128], LW)
        wt('t', LW)
        wt('t', VV)
        for dinc in range(8):
            for b4 in range(4):
                mm(ps[:, BK(b4)], wblk[:, 1024 * sl + 128 * dinc:1024 * sl + 128 * dinc + 128],
                   xqF[:, dinc * NTOK + b4 * 512:dinc * NTOK + b4 * 512 + 512],
                   dinc == 0, dinc == 7)
            mm(ps[:, BK(4)][:, 0:8], wblk[:, 1024 * sl + 128 * dinc:1024 * sl + 128 * dinc + 128],
               xqh[:, 0:8], dinc == 0, dinc == 7)
        ring_cons[("w1q", sl)] = MM.n
        wt('v', MM)
        for b4 in range(4):
            vop('tensor_mul', hfc[:, 1 + 512 * b4:1 + 512 * b4 + 512], ps[:, BK(b4)],
                cB[:, 512 * b4:512 * b4 + 512])
        vop('tensor_scalar', out=xt0[:, 8:9], in0=ps[:, BK(4)][:, 0:1],
            scalar1=stat[:, 28:29], scalar2=None, op0=ALU.mult)
        vop('tensor_scalar', out=hfc[:, 0:1], in0=xt0[:, 8:9], scalar1=hmS[:, 0:1],
            scalar2=None, op0=ALU.mult)
        vop('tensor_scalar', out=hfc[:, 2049:2050], in0=xt0[:, 8:9], scalar1=hmS[:, 1:2],
            scalar2=None, op0=ALU.mult)
        vop('tensor_scalar', out=hf2, in0=hfc[:, 0:2048], scalar1=ctap[:, fc:fc + 1],
            scalar2=ctap[:, 96 + fc:97 + fc], op0=ALU.mult, op1=ALU.add)
        vop('scalar_tensor_tensor', out=hf2, in0=hfc[:, 1:2049],
            scalar=ctap[:, 32 + fc:33 + fc], in1=hf2, op0=ALU.mult, op1=ALU.add)
        vop('scalar_tensor_tensor', out=hf2, in0=hfc[:, 2:2050],
            scalar=ctap[:, 64 + fc:65 + fc], in1=hf2, op0=ALU.mult, op1=ALU.add)
        wt('s', VV)
        wt('s', LSH)
        act(kxb[:, 0:2048], hf2, AF.Silu)
        act(kxb[:, 2048:4096], kxb[:, 0:2048], AF.Abs)
        wt('v', SS)
        vop('tensor_max', mxT[:, 0:2048], mxT[:, 0:2048], kxb[:, 2048:4096])
        wt('y', VV)
        ydma(hs_scr[128 * fc:128 * fc + 128, :], kxb[:, 0:2048], LSH)
    if "hs" in taps:
        wt('g', LSH)
        godma(taps["hs"], hs_scr)

    ph9_ldo = LDO.n
    ph9_mm = MM.n

    def _kv_switch(g):
        g.wait_ge(CC.h, 2)
        g.wait_ge(LDO.h, ph9_ldo)
        g.wait_ge(MM.h, ph9_mm)
        with g.register() as r:
            g.load(r, pari[0:1, 0:1])
            idx = g.snap(r, min_val=0, max_val=1)
            for i in g.Switch(idx, 2):
                own, oth = (0, 2050) if i == 0 else (2050, 0)
                g.dma_start(out=k_scr[:, 0:2048], in_=kv_out[own:own + 1024, :]).then_inc(LDO.h, 16)
                g.dma_start(out=k_scr[:, 2048:4096], in_=kv_out[oth:oth + 1024, :]).then_inc(LDO.h, 16)
                g.dma_start(out=v_scr[0:1024, :], in_=kv_out[own + 1024:own + 2048, 0:1024]).then_inc(LDO.h, 16)
                g.dma_start(out=v_scr[1024:2048, :], in_=kv_out[own + 1024:own + 2048, 1024:2048]).then_inc(LDO.h, 16)
                g.dma_start(out=v_scr[2048:3072, :], in_=kv_out[oth + 1024:oth + 2048, 0:1024]).then_inc(LDO.h, 16)
                g.dma_start(out=v_scr[3072:4096, :], in_=kv_out[oth + 1024:oth + 2048, 1024:2048]).then_inc(LDO.h, 16)
                g.dma_start(out=R_KN[0:1, 16:32], in_=kv_out[oth + 2049:oth + 2050, 0:16]).then_inc(LDO.h, 16)
                g.dma_start(out=R_KN[0:1, 0:16], in_=kv_out[own + 2049:own + 2050, 0:16]).then_inc(LDO.h, 16)
    LDO.n += 8 * 16
    em.add('g', _kv_switch)
    KVSW_LD = LDO.n

    wt('v', LDO, KVSW_LD)
    qnb = R_KN[0:1, 32:48].rearrange("o (a s) -> o a s", s=2)
    godma(qnb[:, :, 0:1], q2m[0:1, 0:8], ns=True)
    godma(qnb[:, :, 1:2], q2m[1:2, 0:8], ns=True)
    wt('v', LDO)
    vop('tensor_max', R_KN[0:1, 48:64], R_KN[0:1, 0:16], R_KN[0:1, 16:32])
    vop('tensor_mul', R_KN[0:1, 48:64], R_KN[0:1, 48:64], R_KN[0:1, 32:48])
    wt('s', VV)
    act(R_KN[0:1, 64:80], R_KN[0:1, 48:64], AF.Sqrt)
    wt('v', SS)
    vop('tensor_scalar', out=R_KN[0:1, 64:80], in0=R_KN[0:1, 64:80], scalar1=-0.5,
        scalar2=None, op0=ALU.mult)
    wt('t', VV)
    mm(ps[:, BK(6)][:, 0:16], onesr[0:1, :], R_KN[0:1, 64:80], True, True)
    wt('v', MM)
    vop('tensor_copy', biasH[:, 0:16], ps[:, BK(6)][:, 0:16])
    vop('tensor_scalar', out=biasHK[:, 0:16], in0=biasH[:, 0:16], scalar1=klS[:, 0:1],
        scalar2=None, op0=ALU.add)
    bp1_v = biasH[:, 0:16].rearrange("p (a s) -> p a s", s=2)
    vop('tensor_tensor', biasP[:, 0:8], bp1_v[:, :, 0:1], bp1_v[:, :, 1:2], ALU.min)
    vop('tensor_scalar', out=biasPK[:, 0:8], in0=biasP[:, 0:8], scalar1=klS[:, 0:1],
        scalar2=None, op0=ALU.add)
    yodma(R_MB2, boutr)
    wt('t', LYO)
    for j in range(2):
        wt('t', VV)
        mm(ps[:, BK(6)], onesr[0:1, :], R_MB2[0:1, 512 * j:512 * j + 512], True, True)
        wt('v', MM)
        vop('tensor_copy', bcB[:, 512 * j:512 * j + 512], ps[:, BK(6)])
    kbuf = [kxb[:, 0:4096], kxb[:, 4096:8192]]
    vbuf1 = [POOL1[:, 8064:12224], POOL1[:, 12224:16384]]
    gseq = []
    for qb in range(4):
        for hg in range(8):
            gseq.append((qb, hg))
    wt('v', MM)
    vop('memset', POOL1[:, 8064:16384], 1.0)
    wt('s', LSKA)
    wt('s', LSKB)
    em.wait('g', VV)
    em.wait('g', MM)

    def l1_group_loads(g):
        if g >= 32:
            return
        qb, hg = gseq[g]
        sl = g % 2
        em.wait('g', MM, ring_cons.get(("kv1", sl), 0))
        em.wait('g', LSH)
        em.wait('g', LDO, KVSW_LD)
        gdma(kbuf[sl][:, 0:NFULL], k_scr[128 * hg:128 * hg + 128, :], WQ[sl])
        vv_ = vbuf1[sl][:, 0:4160].rearrange("p (c w) -> p c w", w=130)
        em.run('g', lambda g_, hg=hg, vv_=vv_: g_.dma_start(
            out=vv_[:, :, 0:64],
            in_=v_scr[:, 128 * hg:128 * hg + 64].rearrange("(c p) d -> p c d", p=128)),
            WQ[sl], 16)
        em.run('g', lambda g_, hg=hg, vv_=vv_: g_.dma_start(
            out=vv_[:, :, 65:129],
            in_=v_scr[:, 128 * hg + 64:128 * hg + 128].rearrange("(c p) d -> p c d", p=128)),
            WQ[sl], 16)

    l1_group_loads(0)
    for g in range(32):
        qb, hg = gseq[g]
        sl = g % 2
        seq = list(range(0, 4 * qb + 4)) + list(range(16, 32))
        nch = len(seq)
        diag0 = 4 * qb
        l1_group_loads(g + 1)
        kg, vg = kbuf[sl], vbuf1[sl]

        def qrow1(r0, r1, hg=hg, qb=qb):
            return QF[r0:r1, hg * NTOK + qb * 512: hg * NTOK + qb * 512 + 512]
        wt('t', WQ[sl])
        wt('t', VV)
        ss_after = {}
        vv_after = {}
        eA = lambda st, hh: eT[:, (2 * st + hh) * 512:(2 * st + hh) * 512 + 512]

        def scores1(cix):
            c = seq[cix]
            isB = c >= 16
            isdiag = (not isB) and (c >= diag0)
            bsrcP = biasPK if isB else biasP
            if cix >= 2:
                wt('t', SS, ss_after[cix - 2])
                if (cix - 2) in vv_after:
                    wt('t', VV, vv_after[cix - 2])
            st = cix % 2
            mm(ps[:, BK(2 * st)], kg[0:64, 128 * c:128 * c + 128],
               qrow1(0, 64), True, True, tp=(0, 0))
            mm(ps[:, BK(2 * st + 1)], kg[64:128, 128 * c:128 * c + 128],
               qrow1(64, 128), True, True, tp=(64, 0))
            wt('s', MM)
            act(eT[:, 1024 * st:1024 * st + 1024], ps[:, 1024 * st:1024 * st + 1024],
                AF.Exp, bias=bsrcP[:, hg:hg + 1])
            ss_after[cix] = SS.n
            if isdiag:
                di = c - diag0
                wt('v', SS, ss_after[cix])
                vop('tensor_mul', eA(st, 0), eA(st, 0), dmask[:, 512 * di:512 * di + 512])
                vop('tensor_mul', eA(st, 1), eA(st, 1), dmask[:, 512 * di:512 * di + 512])
                vv_after[cix] = VV.n

        def av1(cix, first, last):
            c = seq[cix]
            st = cix % 2
            wt('t', SS, ss_after[cix])
            if cix in vv_after:
                wt('t', VV, vv_after[cix])
            mm(ps[0:65, BK(4)], vg[:, 130 * c:130 * c + 65], eA(st, 0), first, last)
            mm(ps[64:128, BK(5)], vg[:, 130 * c + 65:130 * c + 129], eA(st, 1),
               first, last, tp=(0, 64), sgc=True)
            mm(ps[0:1, BK(5)], onesb[:, 1:2], eA(st, 1), False, last, tp=(0, 0), sgc=True)

        scores1(0)
        for cix in range(1, nch):
            scores1(cix)
            av1(cix - 1, cix - 1 == 0, False)
        av1(nch - 1, nch == 1, True)
        ring_cons[("kv1", sl)] = MM.n
        wt('v', MM)
        vop('tensor_copy', R_SUM[0:1, 0:512], ps[64:65, BK(4)])
        vop('tensor_copy', R_SUM[0:1, 512:1024], ps[0:1, BK(5)])
        vop('reciprocal', R_RCP, R_SUM)
        wt('t', VV)
        wt('t', SS)
        mm(ps[:, BK(6)], onesr[0:1, :], R_RCP[0:1, 0:512], True, True)
        mm(ps[:, BK(7)], onesr[0:1, :], R_RCP[0:1, 512:1024], True, True)
        wt('v', MM)
        vop('tensor_copy', bcC[:, 0:512], ps[:, BK(6)])
        vop('tensor_copy', bcC[:, 512:1024], ps[:, BK(7)])
        vop('tensor_mul', oF[0:64, 512 * hg:512 * hg + 512], ps[0:64, BK(4)],
            bcC[0:64, 0:512])
        vop('tensor_mul', oF[64:128, 512 * hg:512 * hg + 512], ps[64:128, BK(5)],
            bcC[64:128, 512:1024])
        if hg == 7:
            for col in range(2):
                for vhh in range(8):
                    wsl_ = 4096 + 512 * (vhh % 2)
                    LO = LOA if vhh % 2 == 0 else LOB
                    em.wait('g', MM, ring_cons.get(("wo", vhh % 2), 0))
                    gdma(wblk[:, wsl_:wsl_ + 512],
                         wout[128 * vhh:128 * vhh + 128, 512 * col:512 * col + 512], LO)
                    wt('t', LO)
                    if vhh == 0:
                        wt('t', VV)
                    for tc in range(4):
                        mm(ps[:, BK(tc)], oF[:, 512 * vhh + 128 * tc:512 * vhh + 128 * tc + 128],
                           wblk[:, wsl_:wsl_ + 512], vhh == 0, vhh == 7)
                    ring_cons[("wo", vhh % 2)] = MM.n
                wt('v', MM)
                for tc in range(4):
                    wt('v', LSS)
                    vop('tensor_add', POOL2[:, 1024:1536], ps[:, BK(tc)],
                        bcB[:, 512 * col:512 * col + 512])
                    wt('y', VV)
                    ydma(s_scr[512 * qb + 128 * tc:512 * qb + 128 * tc + 128,
                               512 * col:512 * col + 512], POOL2[:, 1024:1536], LSS)
    if "s_l1" in taps:
        wt('g', LSS)
        godma(taps["s_l1"], s_scr)

    s2b = stat[:, 26:27]
    wt('t', VV)
    mm(ps[:, BK(7)][:, 2:3], onesr[0:1, :], R_WM[0:1, 1:2], True, True)
    wt('v', MM)
    vop('tensor_copy', s2b, ps[:, BK(7)][:, 2:3])
    for tg in range(4):
        wt('t', VV)
        for j in range(4):
            tc = tg * 4 + j
            tr(ps[:, 2048 + 128 * j:2048 + 128 * j + 128], mxT[:, 128 * tc:128 * tc + 128])
        wt('v', MM)
        for j in range(4):
            tc = tg * 4 + j
            vop('reduce_max', stat[:, 22:23], ps[:, 2048 + 128 * j:2048 + 128 * j + 128], AX.X)
            vop('tensor_scalar_max', stat[:, 23:24], stat[:, 22:23], 1e-5)
            vop('tensor_scalar', out=c2T[:, tc:tc + 1], in0=stat[:, 23:24],
                scalar1=s2b, scalar2=1.0 / 127.0, op0=ALU.mult, op1=ALU.mult)
            vop('reciprocal', stat[:, 24:25], stat[:, 23:24])
            vop('tensor_scalar', out=sc1T[:, tc:tc + 1], in0=stat[:, 24:25],
                scalar1=127.0, scalar2=None, op0=ALU.mult)
    wt('t', VV)
    for tc in range(16):
        mm(ps[0:1, tc * 128:tc * 128 + 128], sc1T[:, tc:tc + 1], identF[:], True, True)
    wt('v', MM)
    vop('tensor_copy', rows[0:1, 0:2048], ps[0:1, 0:2048])
    wt('t', VV)
    for j in range(4):
        mm(ps[:, 2048 + 512 * j:2560 + 512 * j], onesr[0:1, :],
           rows[0:1, 512 * j:512 * j + 512], True, True)
    wt('v', MM)
    wt('v', LSS)
    sc2B = POOL2[:, 0:2048]
    vop('tensor_copy', sc2B, ps[:, 2048:4096])
    for fc in range(32):
        wt('y', VV)
        wt('y', LSH)
        ydma(kxb[:, 0:2048], hs_scr[128 * fc:128 * fc + 128, :], LYW)
        wt('v', LYW)
        vop('tensor_tensor', xti[:, 0:2048], kxb[:, 0:2048], sc2B, ALU.mult)
        vop('tensor_copy', kxb[:, 2048:4096], xti[:, 0:2048])
        wt('y', VV)
        ydma(hs_scr[128 * fc:128 * fc + 128, :], kxb[:, 2048:4096], LSH)
    for tb in range(4):
        for fc in range(32):
            sl = fc % 2
            LW = LWA if sl == 0 else LWB
            LW2 = LW2A if sl == 0 else LW2B
            em.wait('y', MM, ring_cons.get(("bl2", sl), 0))
            if tb == 0 and fc == 0:
                em.wait('y', LSH)
            ydma(wblk[:, 1024 * sl:1024 * sl + 512],
                 hs_scr[128 * fc:128 * fc + 128, 512 * tb:512 * tb + 512], LW)
            ydma(wblk[:, 2048 + 1024 * sl:2048 + 1024 * sl + 512],
                 w2q_scr[128 * fc:128 * fc + 128, 0:512], LW2)
            ydma(wblk[:, 2048 + 1024 * sl + 512:2048 + 1024 * sl + 1024],
                 w2q_scr[128 * fc:128 * fc + 128, 512:1024], LW2)
            wt('t', LW)
            wt('t', LW2)
            if fc == 0:
                wt('t', SS)
                wt('t', VV)
            for tc in range(4):
                for col in range(2):
                    mm(ps[:, BK(tc * 2 + col)],
                       wblk[:, 1024 * sl + 128 * tc:1024 * sl + 128 * tc + 128],
                       wblk[:, 2048 + 1024 * sl + 512 * col:2048 + 1024 * sl + 512 * col + 512],
                       fc == 0, fc == 31)
            ring_cons[("bl2", sl)] = MM.n
        for tc in range(4):
            wt('y', VV)
            wt('y', LSS)
            wt('y', LSX1)
            ydma(xt0, x1_scr[tb * 512 + tc * 128:tb * 512 + tc * 128 + 128, :], LYX)
            ydma(xt1, s_scr[tb * 512 + tc * 128:tb * 512 + tc * 128 + 128, :], LYX)
            for col in range(2):
                wt('s', MM)
                wt('s', VV)
                act(POOL2[:, 2048:2560], ps[:, BK(tc * 2 + col)], AF.Silu,
                    scale=c2T[:, tb * 4 + tc:tb * 4 + tc + 1])
                wt('v', SS)
                wt('v', LYX)
                wt('v', STO)
                vop('tensor_mul', POOL2[:, 2560:3072], POOL2[:, 2048:2560],
                    xt1[:, 512 * col:512 * col + 512])
                vop('tensor_add', POOL2[:, 3072:3584], POOL2[:, 2560:3072],
                    xt0[:, 512 * col:512 * col + 512])
                wt('y', VV)
                em.run('y', lambda y, tb=tb, tc=tc, col=col: y.dma_start(
                    out=out[tb * 512 + tc * 128:tb * 512 + tc * 128 + 128,
                            512 * col:512 * col + 512],
                    in_=POOL2[:, 3072:3584]), STO, 16)
    em.add('g', lambda g: g.wait_ge(STO.h, STO.n))
    em.add('y', lambda y: y.wait_ge(STO.h, STO.n))

    with nc.Block() as block:
        @block.tensor
        def _(t):
            for f in em.TH['t']:
                f(t)

        @block.vector
        def _(v):
            for f in em.TH['v']:
                f(v)

        @block.scalar
        def _(s):
            for f in em.TH['s']:
                f(s)

        @block.gpsimd
        def _(g):
            for f in em.TH['g']:
                f(g)

        @block.sync
        def _(y):
            for f in em.TH['y']:
                f(y)

    es.close()
    return nc


_BUILT = {}


def kernel(**inputs):
    in_maps = [_prep_core(c, inputs) for c in range(8)]
    nc = build()
    res = run_bass_kernel_spmd(nc, in_maps, core_ids=list(range(8)))
    out = np.empty((4, 4096, 1024), np.float32)
    for c in range(8):
        b, h = c >> 1, c & 1
        out[b, 2048 * h:2048 * h + 2048] = np.asarray(res.results[c]["out"])
    return out


# version 07
# speedup vs baseline: 1.0120x; 1.0120x over previous
import contextlib
import numpy as np

import concourse.bass as bass
import concourse.mybir as mybir
from concourse.bass_utils import run_bass_kernel_spmd

F32 = mybir.dt.float32
BF16 = mybir.dt.bfloat16
I32 = mybir.dt.int32
AF = mybir.ActivationFunctionType
ALU = mybir.AluOpType
AX = mybir.AxisListType

NTOK = 2048
NFULL = 4096
DM = 1024
DFF = 4096
NTC = NTOK // 128
KILL = -30000.0

DEV_TAPS = ()


def _rope_tables(h, dtype):
    i = np.arange(32, dtype=np.float64)
    inv = 1.0 / (10000.0 ** (2.0 * i / 64.0))
    pos = np.arange(2048 * h, 2048 * h + 2048, dtype=np.float64)
    rows = np.arange(128)
    fr = pos[None, :] * inv[(rows % 64) // 2][:, None]
    cos = np.cos(fr)
    sgn = np.where(rows % 2 == 0, -1.0, 1.0)[:, None]
    sin = np.sin(fr) * sgn
    return cos.astype(dtype), sin.astype(dtype)


_SHARED_CACHE = {}


def _prep_shared(inp):
    key = id(inp.get("L0_attn_Wq"))
    if _SHARED_CACHE.get("key") == key:
        return _SHARED_CACHE["d"]
    f32 = np.float32
    import ml_dtypes
    d = {}
    d["wq"] = np.ascontiguousarray(inp["L0_attn_Wq"], f32)
    d["wk"] = np.ascontiguousarray(inp["L0_attn_Wk"], f32)
    d["wv"] = np.ascontiguousarray(inp["L0_attn_Wv"], f32)
    d["wo"] = np.ascontiguousarray(inp["L0_attn_Wo"], f32)
    d["lamv"] = np.concatenate([inp["L0_attn_lq1"], inp["L0_attn_lk1"],
                                inp["L0_attn_lq2"], inp["L0_attn_lk2"]]).reshape(1, 256).astype(f32)
    d["lng"] = np.ascontiguousarray(inp["L0_attn_ln_g"].reshape(128, 1), f32)
    d["lnb"] = np.ascontiguousarray(inp["L0_attn_ln_b"].reshape(128, 1), f32)
    d["gw1T"] = np.ascontiguousarray(np.asarray(inp["L0_ffn_gate_w1"], f32).T)
    d["gb1T"] = np.ascontiguousarray(np.asarray(inp["L0_ffn_gate_b1"], f32).reshape(8, 128).T)
    d["gw2T"] = np.ascontiguousarray(np.asarray(inp["L0_ffn_gate_w2"], f32).T)
    d["gb2c"] = np.ascontiguousarray(np.asarray(inp["L0_ffn_gate_b2"], f32).reshape(8, 1))
    d["w1T"] = np.ascontiguousarray(np.asarray(inp["L0_ffn_W1"], f32).T)
    d["tw1T"] = np.ascontiguousarray(np.asarray(inp["L0_ffn_TW1"], f32).transpose(0, 2, 1))
    d["b1r"] = np.ascontiguousarray(np.asarray(inp["L0_ffn_b1"], f32).reshape(1, 4096))
    d["b1T"] = np.ascontiguousarray(np.asarray(inp["L0_ffn_b1"], f32).reshape(32, 128).T)
    d["tb1"] = np.ascontiguousarray(inp["L0_ffn_Tb1"], f32)
    d["w2T"] = np.ascontiguousarray(np.asarray(inp["L0_ffn_W2"], f32).T)
    d["tw2T"] = np.ascontiguousarray(np.asarray(inp["L0_ffn_TW2"], f32).transpose(0, 2, 1))
    d["b2r"] = np.ascontiguousarray(np.asarray(inp["L0_ffn_b2"], f32).reshape(1, 1024))
    d["tb2"] = np.ascontiguousarray(inp["L0_ffn_Tb2"], f32)
    d["wqkv"] = np.ascontiguousarray(inp["L1_attn_Wqkv"], f32)
    d["wout"] = np.ascontiguousarray(inp["L1_attn_Wout"], f32)
    d["boutr"] = np.ascontiguousarray(np.asarray(inp["L1_attn_bout"], f32).reshape(1, 1024))
    d["l1w1T"] = np.ascontiguousarray(np.asarray(inp["L1_ffn_w1"], f32).T)
    d["l1w2T"] = np.ascontiguousarray(np.asarray(inp["L1_ffn_w2"], f32).T)
    cw = np.asarray(inp["L1_ffn_conv_w"], f32)[:, 0, :]
    for j in range(3):
        d[f"cw{j}"] = np.ascontiguousarray(cw[:, j].reshape(32, 128).T)
    d["cb"] = np.ascontiguousarray(np.asarray(inp["L1_ffn_conv_b"], f32).reshape(32, 128).T)
    kk = np.arange(128)[:, None]
    qq = np.arange(512)[None, :]
    dmv = np.stack([(kk + 128 * i <= qq) for i in range(4)], 0).astype(f32)
    d["dmaskT"] = np.ascontiguousarray(dmv.reshape(4 * 128, 512)).astype(ml_dtypes.bfloat16)
    d["identT"] = np.eye(128, dtype=f32).astype(ml_dtypes.bfloat16)
    d["identF"] = np.eye(128, dtype=f32)
    d["hs2T"] = np.ascontiguousarray(
        np.stack([np.arange(128) < 64, np.arange(128) >= 64], 1).astype(f32)).astype(ml_dtypes.bfloat16)
    _SHARED_CACHE["key"] = key
    _SHARED_CACHE["d"] = d
    return d


def _prep_core(c, inp):
    b, h = c >> 1, c & 1
    own = slice(2048 * h, 2048 * h + 2048)
    oth = slice(2048 * (1 - h), 2048 * (1 - h) + 2048)
    f32 = np.float32
    x = np.asarray(inp["L0_attn_x"][b], f32)
    d = dict(_prep_shared(inp))
    d["x0"] = np.ascontiguousarray(np.concatenate([x[own], x[oth]], 0))
    cos, sin = _rope_tables(h, f32)
    d["ropec"] = cos
    d["ropes"] = sin
    d["parity"] = np.array([[h]], np.int32)
    d["killpb"] = np.full((128, 1), KILL * (1 - h), f32)
    d["halom"] = np.tile(np.array([[float(h), float(1 - h)]], f32), (128, 1))
    return d


class Sem:
    def __init__(self, h):
        self.h = h
        self.n = 0


class Em:
    def __init__(self):
        self.TH = {k: [] for k in "tvsgy"}

    def add(self, e, fn):
        self.TH[e].append(fn)

    def wait(self, e, sem, val=None):
        v = sem.n if val is None else val
        if v > 0:
            self.add(e, lambda E, h=sem.h, v=v: E.wait_ge(h, v))

    def run(self, e, fn, sem=None, k=1, selfwait=False):
        if sem is not None:
            if selfwait and sem.n > 0:
                v0 = sem.n
                self.add(e, lambda E, h=sem.h, v0=v0: E.wait_ge(h, v0))
            sem.n += k
            self.add(e, lambda E, fn=fn, h=sem.h, k=k: fn(E).then_inc(h, k))
        else:
            self.add(e, fn)


def build():
    nc = bass.Bass("TRN2", target_bir_lowering=False)
    es = contextlib.ExitStack()

    ap = lambda t: t.ap()
    x0 = ap(nc.dram_tensor("x0", [NFULL, DM], F32, kind="ExternalInput"))
    wq = ap(nc.dram_tensor("wq", [DM, DM], F32, kind="ExternalInput"))
    wk = ap(nc.dram_tensor("wk", [DM, DM], F32, kind="ExternalInput"))
    wv = ap(nc.dram_tensor("wv", [DM, DM], F32, kind="ExternalInput"))
    wo = ap(nc.dram_tensor("wo", [DM, DM], F32, kind="ExternalInput"))
    lamv = ap(nc.dram_tensor("lamv", [1, 256], F32, kind="ExternalInput"))
    lng = ap(nc.dram_tensor("lng", [128, 1], F32, kind="ExternalInput"))
    lnb = ap(nc.dram_tensor("lnb", [128, 1], F32, kind="ExternalInput"))
    gw1T = ap(nc.dram_tensor("gw1T", [DM, DM], F32, kind="ExternalInput"))
    gb1T = ap(nc.dram_tensor("gb1T", [128, 8], F32, kind="ExternalInput"))
    gw2T = ap(nc.dram_tensor("gw2T", [DM, 8], F32, kind="ExternalInput"))
    gb2c = ap(nc.dram_tensor("gb2c", [8, 1], F32, kind="ExternalInput"))
    w1T = ap(nc.dram_tensor("w1T", [DM, DFF], F32, kind="ExternalInput"))
    tw1T = ap(nc.dram_tensor("tw1T", [8, DM, DFF], F32, kind="ExternalInput"))
    b1r = ap(nc.dram_tensor("b1r", [1, DFF], F32, kind="ExternalInput"))
    b1T = ap(nc.dram_tensor("b1T", [128, 32], F32, kind="ExternalInput"))
    tb1 = ap(nc.dram_tensor("tb1", [8, DFF], F32, kind="ExternalInput"))
    w2T = ap(nc.dram_tensor("w2T", [DFF, DM], F32, kind="ExternalInput"))
    tw2T = ap(nc.dram_tensor("tw2T", [8, DFF, DM], F32, kind="ExternalInput"))
    b2r = ap(nc.dram_tensor("b2r", [1, DM], F32, kind="ExternalInput"))
    tb2 = ap(nc.dram_tensor("tb2", [8, DM], F32, kind="ExternalInput"))
    wqkv = ap(nc.dram_tensor("wqkv", [DM, 3 * DM], F32, kind="ExternalInput"))
    wout = ap(nc.dram_tensor("wout", [DM, DM], F32, kind="ExternalInput"))
    boutr = ap(nc.dram_tensor("boutr", [1, DM], F32, kind="ExternalInput"))
    l1w1T = ap(nc.dram_tensor("l1w1T", [DM, DFF], F32, kind="ExternalInput"))
    l1w2T = ap(nc.dram_tensor("l1w2T", [DFF, DM], F32, kind="ExternalInput"))
    cw0 = ap(nc.dram_tensor("cw0", [128, 32], F32, kind="ExternalInput"))
    cw1 = ap(nc.dram_tensor("cw1", [128, 32], F32, kind="ExternalInput"))
    cw2 = ap(nc.dram_tensor("cw2", [128, 32], F32, kind="ExternalInput"))
    cb = ap(nc.dram_tensor("cb", [128, 32], F32, kind="ExternalInput"))
    ropec = ap(nc.dram_tensor("ropec", [128, NTOK], F32, kind="ExternalInput"))
    ropes = ap(nc.dram_tensor("ropes", [128, NTOK], F32, kind="ExternalInput"))
    parity = ap(nc.dram_tensor("parity", [1, 1], I32, kind="ExternalInput"))
    killpb = ap(nc.dram_tensor("killpb", [128, 1], F32, kind="ExternalInput"))
    halom = ap(nc.dram_tensor("halom", [128, 2], F32, kind="ExternalInput"))
    dmaskT = ap(nc.dram_tensor("dmaskT", [4 * 128, 512], BF16, kind="ExternalInput"))
    identT = ap(nc.dram_tensor("identT", [128, 128], BF16, kind="ExternalInput"))
    identFT = ap(nc.dram_tensor("identF", [128, 128], F32, kind="ExternalInput"))
    hs2T = ap(nc.dram_tensor("hs2T", [128, 2], BF16, kind="ExternalInput"))
    out = ap(nc.dram_tensor("out", [NTOK, DM], F32, kind="ExternalOutput"))

    k_scr = ap(nc.dram_tensor("k_scr", [DM, NFULL], BF16))
    v_scr = ap(nc.dram_tensor("v_scr", [NFULL, DM], BF16))
    s_scr = ap(nc.dram_tensor("s_scr", [NTOK, DM], F32))
    x1_scr = ap(nc.dram_tensor("x1_scr", [NTOK, DM], F32))
    mw1_scr = ap(nc.dram_tensor("mw1_scr", [DM, DFF], BF16))
    mw2_scr = ap(nc.dram_tensor("mw2_scr", [DFF, DM], BF16))
    w1q_scr = ap(nc.dram_tensor("w1q_scr", [DM, DFF], BF16))
    w2q_scr = ap(nc.dram_tensor("w2q_scr", [DFF, DM], BF16))
    hs_scr = ap(nc.dram_tensor("hs_scr", [DFF, NTOK], BF16))
    kv_in = ap(nc.dram_tensor("kv_in", [2050, NTOK], BF16))
    kv_out = ap(nc.dram_tensor("kv_out", [4100, NTOK], BF16))
    halo_in = ap(nc.dram_tensor("halo_in", [2, DM], F32))
    halo_out = ap(nc.dram_tensor("halo_out", [4, DM], F32))
    taps = {}
    for t_ in DEV_TAPS:
        shp = {"xn0": [128, 8 * NTOK], "g8": [8, 1], "s_l0": [NTOK, DM],
               "x1": [NTOK, DM], "mw1": [DM, DFF], "s_l1": [NTOK, DM],
               "hs": [DFF, NTOK], "kq": [128, 16], "w1q": [DM, DFF]}[t_]
        dt_ = BF16 if t_ in ("mw1", "hs", "w1q") else F32
        taps[t_] = ap(nc.dram_tensor("tap_" + t_, shp, dt_, kind="ExternalOutput"))

    sb = lambda n, shp, dt: es.enter_context(nc.sbuf_tensor(n, shp, dt))
    xnF = sb("xnF", [128, 8 * NTOK], BF16)
    QF = sb("QF", [128, 8 * NTOK], BF16)
    POOL1 = sb("POOL1", [128, 32 * 512], BF16)
    POOL2 = sb("POOL2", [128, 4096], F32)
    POOL3 = sb("POOL3", [128, 4100], F32)
    wblk = sb("wblk", [128, 10 * 512], BF16)
    kxb = sb("kxb", [128, 2 * NFULL], BF16)
    eT = sb("eT", [128, 2048], BF16)
    oF = sb("oF", [128, 8 * 512], BF16)
    bcB = sb("bcB", [128, 1024], F32)
    bcC = sb("bcC", [128, 1024], F32)
    xti = sb("xti", [128, 2048], I32)
    mxT = sb("mxT", [128, NTOK], BF16)
    dmask = sb("dmask", [128, 4 * 512], BF16)
    rows = sb("rows", [4, 3104], F32)
    stat = sb("stat", [128, 64], F32)
    gwS = sb("gwS", [128, 64], BF16)
    ctap = sb("ctap", [128, 128], F32)
    ident = sb("ident", [128, 128], BF16)
    identF = sb("identFs", [128, 128], F32)
    hs2 = sb("hs2", [128, 2], BF16)
    onesb = sb("onesb", [128, 2], BF16)
    onesw = sb("onesw", [128, 128], BF16)
    biasP = sb("biasP", [128, 8], F32)
    biasPK = sb("biasPK", [128, 8], F32)
    onesf = sb("onesf", [128, 1], F32)
    onesr = sb("onesr", [1, 128], F32)
    mb1T = sb("mb1T", [128, 32], F32)
    biasH = sb("biasH", [128, 16], F32)
    biasHK = sb("biasHK", [128, 16], F32)
    q2m = sb("q2m", [2, 16], F32)
    k2m = sb("k2m", [2, 16], F32)
    lamS = sb("lamS", [128, 1], F32)
    klS = sb("klS", [128, 1], F32)
    hmS = sb("hmS", [128, 2], F32)
    sc1T = sb("sc1T", [128, 16], F32)
    c2T = sb("c2T", [128, 16], F32)
    xqh = sb("xqh", [128, 8], BF16)
    g8c = sb("g8c", [8, 1], F32)
    ps = es.enter_context(nc.psum_tensor("ps", [128, 4096], F32))

    kgrp = kxb
    vgrp = wblk
    uBlk = POOL1
    xqF = POOL1
    R_SUM = rows[0:1, 0:1024]
    R_RCP = rows[0:1, 1024:2048]
    R_MU = rows[0:1, 2048:2560]
    R_VAR = rows[0:1, 2560:3072]
    R_LAM = rows[0:1, 0:544]
    R_NEGB = rows[0:1, 0:16]
    R_G8R = rows[0:1, 16:24]
    R_B1 = rows[0:1, 24:152]
    R_MB2 = rows[0:1, 1024:2048]
    R_KN = rows[0:1, 0:80]
    R_KNOWN = rows[2:3, 1024:1040]
    R_HFIRST = rows[2:3, 0:1024]
    R_HLAST = rows[3:4, 0:1024]
    R_WM = rows[0:1, 3072:3080]
    R_HS = rows[0:1, 3080:3088]

    BK = lambda k: slice(512 * k, 512 * k + 512)

    MM = Sem(es.enter_context(nc.semaphore("MM")))
    SS = Sem(es.enter_context(nc.semaphore("SS")))
    VV = Sem(es.enter_context(nc.semaphore("VV")))
    CC = Sem(es.enter_context(nc.semaphore("CC")))
    STO = Sem(es.enter_context(nc.semaphore("STO")))
    WQA = Sem(es.enter_context(nc.semaphore("WQA")))
    WQB = Sem(es.enter_context(nc.semaphore("WQB")))
    LDW = Sem(es.enter_context(nc.semaphore("LDW")))
    LDO = Sem(es.enter_context(nc.semaphore("LDO")))
    MWA = Sem(es.enter_context(nc.semaphore("MWA")))
    MWB = Sem(es.enter_context(nc.semaphore("MWB")))
    LYC = Sem(es.enter_context(nc.semaphore("LYC")))
    LYX = Sem(es.enter_context(nc.semaphore("LYX")))
    LYW = Sem(es.enter_context(nc.semaphore("LYW")))
    LYO = Sem(es.enter_context(nc.semaphore("LYO")))
    LSWA = Sem(es.enter_context(nc.semaphore("LSWA")))
    LSWB = Sem(es.enter_context(nc.semaphore("LSWB")))
    LSKA = Sem(es.enter_context(nc.semaphore("LSKA")))
    LSKB = Sem(es.enter_context(nc.semaphore("LSKB")))
    LSS = Sem(es.enter_context(nc.semaphore("LSS")))
    LSX1 = Sem(es.enter_context(nc.semaphore("LSX1")))
    LSH = Sem(es.enter_context(nc.semaphore("LSH")))
    MSA = Sem(es.enter_context(nc.semaphore("MSA")))
    MSB = Sem(es.enter_context(nc.semaphore("MSB")))
    WVA = Sem(es.enter_context(nc.semaphore("WVA")))
    WVB = Sem(es.enter_context(nc.semaphore("WVB")))
    LWA = Sem(es.enter_context(nc.semaphore("LWA")))
    LWB = Sem(es.enter_context(nc.semaphore("LWB")))
    LW2A = Sem(es.enter_context(nc.semaphore("LW2A")))
    LW2B = Sem(es.enter_context(nc.semaphore("LW2B")))
    LOA = Sem(es.enter_context(nc.semaphore("LOA")))
    LOB = Sem(es.enter_context(nc.semaphore("LOB")))

    em = Em()
    ring_cons = {}

    def gdma(o, i, sem):
        em.run('g', lambda g, o=o, i=i: g.dma_start(out=o, in_=i), sem, 16)

    def godma(o, i, ns=False):
        em.wait('g', LDO)
        em.run('g', lambda g, o=o, i=i, ns=ns: g.dma_start(
            out=o, in_=i, allow_slow_non_contiguous=ns), LDO, 16)

    def ydma(o, i, sem):
        em.run('y', lambda y, o=o, i=i: y.dma_start(out=o, in_=i), sem, 16)

    def yodma(o, i):
        em.wait('y', LYO)
        em.run('y', lambda y, o=o, i=i: y.dma_start(out=o, in_=i), LYO, 16)

    def mm(o, l, r, st, sp, tp=None, sgc=False):
        em.run('t', lambda t, o=o, l=l, r=r, st=st, sp=sp, tp=tp, sgc=sgc:
               t.matmul(o, l, r, start=st, stop=sp, tile_position=tp,
                        skip_group_check=sgc), MM)

    def tr(o, i):
        mm(o, i, ident[:], True, True)

    def act(o, i, f, bias=0.0, scale=1.0, accum=None):
        em.run('s', lambda s, o=o, i=i, f=f, bias=bias, scale=scale, accum=accum:
               s.activation(out=o, in_=i, func=f, bias=bias, scale=scale,
                            accum_out=accum), SS, selfwait=True)

    def vop(name, *a, **kw):
        em.run('v', lambda v, name=name, a=a, kw=kw: getattr(v, name)(*a, **kw), VV,
               selfwait=True)

    def wt(e, sem, val=None):
        em.wait(e, sem, val)

    vop('memset', onesb[:], 1.0)
    vop('memset', onesf[:], 1.0)
    vop('memset', onesr[:], 1.0)
    vop('memset', q2m[:], 0.0)
    vop('memset', k2m[:], 0.0)
    vop('memset', stat[0:8, 20:22], 0.0)
    vop('memset', mxT[:], 0.0)
    vop('memset', onesw[:], 1.0)
    vop('memset', stat[:, 19:20], 1e-5)
    ydma(ident[:], identT, LYC)
    ydma(identF[:], identFT, LYC)
    ydma(hs2[:], hs2T, LYC)
    ydma(stat[:, 40:41], lng, LYC)
    ydma(stat[:, 41:42], lnb, LYC)
    ydma(klS[:], killpb, LYC)
    ydma(hmS[:], halom, LYC)
    ydma(ctap[:, 0:32], cw0, LYC)
    ydma(ctap[:, 32:64], cw1, LYC)
    ydma(ctap[:, 64:96], cw2, LYC)
    ydma(ctap[:, 96:128], cb, LYC)
    for j in range(4):
        ydma(dmask[:, 512 * j:512 * j + 512], dmaskT[128 * j:128 * j + 128, :], LYC)
    ydma(R_LAM[0:1, 0:256], lamv, LYC)
    lngS, lnbS = stat[:, 40:41], stat[:, 41:42]
    g8aP = stat[0:8, 21:22]

    lam_init = float(0.8 - 0.6 * np.exp(-0.3 * 12))
    wt('v', LYC)
    vop('tensor_mul', R_LAM[0:1, 256:320], R_LAM[0:1, 0:64], R_LAM[0:1, 64:128])
    vop('tensor_mul', R_LAM[0:1, 320:384], R_LAM[0:1, 128:192], R_LAM[0:1, 192:256])
    vop('reduce_sum', R_LAM[0:1, 384:385], R_LAM[0:1, 256:320], AX.X)
    vop('reduce_sum', R_LAM[0:1, 385:386], R_LAM[0:1, 320:384], AX.X)
    wt('s', VV)
    act(R_LAM[0:1, 386:388], R_LAM[0:1, 384:386], AF.Exp)
    wt('v', SS)
    vop('scalar_tensor_tensor', out=R_LAM[0:1, 388:389], in0=R_LAM[0:1, 387:388],
        scalar=-lam_init, in1=R_LAM[0:1, 386:387], op0=ALU.add, op1=ALU.subtract)
    wt('t', VV)
    mm(ps[:, BK(7)][:, 0:1], onesr[0:1, :], R_LAM[0:1, 388:389], True, True)
    wt('v', MM)
    vop('tensor_copy', lamS[:, 0:1], ps[:, BK(7)][:, 0:1])

    def wq_tiles(w):
        r, cdim = w.shape
        cs = min(cdim, 2048)
        return [(w[128 * i:128 * i + 128, cs * j:cs * j + cs], cs)
                for i in range(r // 128) for j in range(cdim // cs)]

    for wi, (wsrc, wdst) in enumerate([(l1w1T, w1q_scr), (l1w2T, w2q_scr)]):
        tiles = wq_tiles(wsrc)
        dtiles = wq_tiles(wdst)
        sumc = 43 + wi
        vop('memset', stat[:, sumc:sumc + 1], 0.0)
        for idx, (tin, cs) in enumerate(tiles):
            sl = idx % 2
            wqs = WQA if sl == 0 else WQB
            em.wait('g', VV, ring_cons.get(("wq", sl), 0))
            em.wait('g', (LSWA if sl == 0 else LSWB), ring_cons.get(("wqs", sl), 0))
            gdma(POOL1[:, 2048 * sl:2048 * sl + cs], tin, wqs)
            wt('v', wqs)
            vop('reduce_sum', stat[:, 47:48], POOL1[:, 2048 * sl:2048 * sl + cs], AX.X,
                apply_absolute_value=True)
            vop('tensor_add', stat[:, sumc:sumc + 1], stat[:, sumc:sumc + 1], stat[:, 47:48])
            ring_cons[("wq", sl)] = VV.n
        wt('t', VV)
        mm(ps[0:1, BK(7)][:, 0:1], onesf[:, 0:1], stat[:, sumc:sumc + 1], True, True)
        wt('v', MM)
        nel = float(wsrc.shape[0] * wsrc.shape[1])
        vop('tensor_scalar', out=R_WM[0:1, wi:wi + 1], in0=ps[0:1, BK(7)][:, 0:1],
            scalar1=1.0 / nel, scalar2=1e-5, op0=ALU.mult, op1=ALU.max)
        vop('reciprocal', R_WM[0:1, 2 + wi:3 + wi], R_WM[0:1, wi:wi + 1])
        wt('t', VV)
        mm(ps[:, BK(7)][:, 0:1], onesr[0:1, :], R_WM[0:1, 2 + wi:3 + wi], True, True)
        wt('v', MM)
        vop('tensor_copy', stat[:, 45 + wi * 10:46 + wi * 10], ps[:, BK(7)][:, 0:1])
        scol = stat[:, 45 + wi * 10:46 + wi * 10]
        for idx, (tin, cs) in enumerate(tiles):
            sl = idx % 2
            wqs = WQA if sl == 0 else WQB
            em.wait('g', VV, ring_cons.get(("wq", sl), 0))
            em.wait('g', (LSWA if sl == 0 else LSWB), ring_cons.get(("wqs", sl), 0))
            gdma(POOL1[:, 2048 * sl:2048 * sl + cs], tin, wqs)
            wt('v', wqs)
            src_ = POOL1[:, 2048 * sl:2048 * sl + cs]
            tpos = POOL1[:, 4096:4096 + cs]
            tneg = POOL1[:, 6144:6144 + cs]
            vop('tensor_scalar', out=tpos, in0=src_, scalar1=scol, scalar2=0.5,
                op0=ALU.mult, op1=ALU.is_ge)
            vop('tensor_scalar', out=tneg, in0=src_, scalar1=scol, scalar2=-0.5,
                op0=ALU.mult, op1=ALU.is_le)
            vop('tensor_sub', src_, tpos, tneg)
            ring_cons[("wq", sl)] = VV.n
            wt('y', VV)
            lsw = LSWA if sl == 0 else LSWB
            ydma(dtiles[idx][0], src_, lsw)
            ring_cons[("wqs", sl)] = lsw.n
    if "w1q" in taps:
        wt('g', LSWA)
        wt('g', LSWB)
        godma(taps["w1q"], w1q_scr)

    xt0 = POOL3[:, 0:1024]
    xt1 = POOL3[:, 1024:2048]
    xt2 = POOL3[:, 2048:3072]

    def ln_tile(src_rows, want_f32=False):
        wt('y', VV)
        ydma(xt0, src_rows, LYX)
        wt('v', LYX)
        vop('bn_stats', stat[:, 0:6], xt0[:, 0:512])
        vop('bn_stats', stat[:, 6:12], xt0[:, 512:1024])
        vop('bn_aggr', stat[:, 12:14], stat[:, 0:12])
        wt('s', VV)
        act(stat[:, 14:15], stat[:, 13:14], AF.Sqrt, bias=stat[:, 19:20])
        wt('v', SS)
        vop('reciprocal', stat[:, 15:16], stat[:, 14:15])
        vop('tensor_scalar', out=stat[:, 16:17], in0=stat[:, 12:13],
            scalar1=stat[:, 15:16], scalar2=-1.0, op0=ALU.mult, op1=ALU.mult)
        wt('v', MM)
        vop('tensor_scalar', out=eT[:, 0:1024], in0=xt0,
            scalar1=stat[:, 15:16], scalar2=stat[:, 16:17], op0=ALU.mult, op1=ALU.add)
        if want_f32:
            vop('tensor_scalar', out=xt2, in0=xt0,
                scalar1=stat[:, 15:16], scalar2=stat[:, 16:17], op0=ALU.mult, op1=ALU.add)

    def ln_transpose(dst_col_of_chunk, src=None):
        wt('t', VV)
        wt('t', SS)
        for j in range(8):
            tr(ps[:, j * 128:j * 128 + 128], (src or eT)[:, j * 128:j * 128 + 128])
        wt('s', MM)
        for j in range(8):
            act(dst_col_of_chunk(j), ps[:, j * 128:j * 128 + 128], AF.Copy)

    for tc in range(NTC):
        ln_tile(x0[128 * tc:128 * tc + 128, :])
        ln_transpose(lambda j, tc=tc: xnF[:, j * NTOK + tc * 128: j * NTOK + tc * 128 + 128])
    XNF_OWN_SS = SS.n

    def load_colblock(seg, w, c0, cw):
        em.wait('g', MM)
        for dinc in range(8):
            gdma(wblk[:, seg + dinc * cw: seg + dinc * cw + cw],
                 w[128 * dinc:128 * dinc + 128, c0:c0 + cw], LDW)

    def xsrc(tb, dinc, j0, jn):
        if tb < 4:
            return xnF[:, dinc * NTOK + tb * 512 + j0: dinc * NTOK + tb * 512 + j0 + jn]
        return kxb[:, dinc * 512 + j0: dinc * 512 + j0 + jn]

    kv_stage = [eT[:, 1024:1536], eT[:, 1536:2048]]
    sq_tile = oF[:, 512:1024]
    gb1S = stat[:, 48:56]
    yodma(gb1S, gb1T)
    for hc in range(8):
        godma(gwS[:, hc * 8:hc * 8 + 8], gw2T[128 * hc:128 * hc + 128, :])
    wt('s', LYO)
    wt('t', LDO)
    pwc = [0]
    WQ = [WQA, WQB]
    WV = [WVA, WVB]
    em.wait('g', VV)

    def pw_load(w, c0):
        sl = pwc[0] % 2
        pwc[0] += 1
        em.wait('g', MM, ring_cons.get(("pw", sl), 0))
        for dinc in range(8):
            gdma(wblk[:, 1024 * sl + dinc * 128: 1024 * sl + dinc * 128 + 128],
                 w[128 * dinc:128 * dinc + 128, c0:c0 + 128], WQ[sl])
        return sl

    for tb in range(8):
        if tb >= 4:
            for j in range(4):
                ln_tile(x0[2048 + 512 * (tb - 4) + 128 * j:
                           2048 + 512 * (tb - 4) + 128 * j + 128, :])
                ln_transpose(lambda dc, j=j: kxb[:, dc * 512 + j * 128: dc * 512 + j * 128 + 128])
        for which, wmat in (("k", wk), ("q", wq)):
            if which == "q" and tb >= 4:
                continue
            for dc in range(8):
                sl = pw_load(wmat, dc * 128)
                bank = 2 + (dc % 2)
                wt('t', WQ[sl])
                wt('t', SS)
                wt('t', VV)
                for dinc in range(8):
                    mm(ps[:, BK(bank)], wblk[:, 1024 * sl + dinc * 128: 1024 * sl + dinc * 128 + 128],
                       xsrc(tb, dinc, 0, 512), dinc == 0, dinc == 7)
                ring_cons[("pw", sl)] = MM.n
                wt('s', MM)
                wt('s', LSKA if dc % 2 == 0 else LSKB)
                stg = kv_stage[dc % 2]
                if which == "q":
                    act(QF[:, dc * NTOK + tb * 512: dc * NTOK + tb * 512 + 512],
                        ps[:, BK(bank)], AF.Copy, scale=0.125)
                    act(sq_tile, ps[:, BK(bank)], AF.Square, scale=0.125)
                else:
                    act(stg, ps[:, BK(bank)], AF.Copy)
                    act(sq_tile, ps[:, BK(bank)], AF.Square)
                    wt('y', SS)
                    ydma(k_scr[128 * dc:128 * dc + 128, 512 * tb:512 * tb + 512], stg,
                         LSKA if dc % 2 == 0 else LSKB)
                wt('t', SS)
                wt('t', VV)
                mm(ps[0:2, BK(5)], hs2[:], sq_tile, True, True)
                wt('v', MM)
                dstm = q2m if which == "q" else k2m
                vop('reduce_max', stat[0:2, 18:19], ps[0:2, BK(5)], AX.X)
                vop('tensor_max', dstm[0:2, dc:dc + 1], dstm[0:2, dc:dc + 1], stat[0:2, 18:19])
        vbanks = [0, 1, 6, 7]
        for col in range(2):
            for hc in range(8):
                wsl = hc % 2
                em.wait('g', MM, ring_cons.get(("wv", wsl), 0))
                gdma(wblk[:, 3072 + 512 * wsl: 3072 + 512 * wsl + 512],
                     wv[128 * hc:128 * hc + 128, col * 512: col * 512 + 512], WV[wsl])
                wt('t', WV[wsl])
                if hc == 0:
                    wt('t', SS)
                    wt('t', VV)
                for tc in range(4):
                    mm(ps[:, BK(vbanks[tc])], xsrc(tb, hc, tc * 128, 128),
                       wblk[:, 3072 + 512 * wsl: 3072 + 512 * wsl + 512],
                       hc == 0, hc == 7)
                ring_cons[("wv", wsl)] = MM.n
            wt('s', MM)
            for tc in range(4):
                stg = kv_stage[tc % 2]
                wt('s', LSKA if tc % 2 == 0 else LSKB)
                act(stg, ps[:, BK(vbanks[tc])], AF.Copy)
                wt('y', SS)
                ydma(v_scr[512 * tb + 128 * tc: 512 * tb + 128 * tc + 128,
                           col * 512: col * 512 + 512], stg,
                     LSKA if tc % 2 == 0 else LSKB)
        for hc in range(8):
            sl = pw_load(gw1T, hc * 128)
            bank = 2 + (hc % 2)
            wt('t', WQ[sl])
            wt('t', SS)
            for dinc in range(8):
                mm(ps[:, BK(bank)], wblk[:, 1024 * sl + dinc * 128: 1024 * sl + dinc * 128 + 128],
                   xsrc(tb, dinc, 0, 512), dinc == 0, dinc == 7)
            ring_cons[("pw", sl)] = MM.n
            wt('s', MM)
            act(oF[:, 0:512], ps[:, BK(bank)], AF.Relu, bias=gb1S[:, hc:hc + 1])
            wt('t', SS)
            mm(ps[0:8, BK(4)], gwS[:, hc * 8:hc * 8 + 8], oF[:, 0:512], hc == 0, hc == 7)
        wt('s', MM)
        act(bcC[0:8, 0:512], ps[0:8, BK(4)], AF.Copy, accum=stat[0:8, 20:21])
        wt('v', SS)
        vop('tensor_add', g8aP, g8aP, stat[0:8, 20:21])
    gb2S = stat[0:8, 57:58]
    yodma(gb2S, gb2c)
    wt('v', LYO)
    vop('tensor_scalar', out=g8c[0:8, 0:1], in0=g8aP, scalar1=1.0 / NFULL,
        scalar2=gb2S, op0=ALU.mult, op1=ALU.add)
    gBc = stat[:, 32:40]
    wt('t', VV)
    mm(ps[0:1, BK(5)][:, 0:8], g8c[0:8, 0:1], identF[0:8, 0:8], True, True)
    wt('v', MM)
    vop('tensor_copy', R_G8R, ps[0:1, BK(5)][:, 0:8])
    wt('t', VV)
    mm(ps[:, BK(6)][:, 0:8], onesr[0:1, :], R_G8R, True, True)
    wt('v', MM)
    vop('tensor_copy', gBc, ps[:, BK(6)][:, 0:8])
    vop('tensor_mul', q2m[0:2, 8:16], q2m[0:2, 0:8], k2m[0:2, 0:8])
    wt('s', VV)
    act(k2m[0:2, 8:16], q2m[0:2, 8:16], AF.Sqrt)
    wt('v', SS)
    vop('tensor_scalar', out=q2m[0:2, 8:16], in0=k2m[0:2, 8:16], scalar1=-0.5, op0=ALU.mult, scalar2=None)
    wt('g', VV)
    nb_v = R_NEGB.rearrange("o (a s) -> o a s", s=2)
    godma(nb_v[:, :, 0:1], q2m[0:1, 8:16], ns=True)
    godma(nb_v[:, :, 1:2], q2m[1:2, 8:16], ns=True)
    wt('t', LDO)
    mm(ps[:, BK(6)][:, 0:16], onesr[0:1, :], R_NEGB, True, True)
    wt('v', MM)
    vop('tensor_copy', biasH[:, 0:16], ps[:, BK(6)][:, 0:16])
    bp_v = biasH[:, 0:16].rearrange("p (a s) -> p a s", s=2)
    vop('tensor_tensor', biasP[:, 0:8], bp_v[:, :, 0:1], bp_v[:, :, 1:2], ALU.min)
    wt('t', VV)
    for c in range(32):
        em.wait('y', MM)
        ydma(R_B1, b1r[0:1, 128 * c:128 * c + 128], LYW)
        ydma(bcC[0:8, 0:128], tb1[:, 128 * c:128 * c + 128], LYW)
        wt('t', LYW)
        mm(ps[:, BK(5)][:, c:c + 1], bcC[0:8, 0:128], g8c[0:8, 0:1], True, False)
        mm(ps[:, BK(5)][:, c:c + 1], R_B1, onesf[0:1, 0:1], False, True)
    wt('v', MM)
    vop('tensor_copy', mb1T[:], ps[:, BK(5)][:, 0:32])
    yodma(R_MB2, b2r)
    for j in range(2):
        em.wait('y', MM)
        ydma(bcC[0:8, 128:640], tb2[:, 512 * j:512 * j + 512], LYW)
        wt('t', LYW)
        wt('t', VV)
        mm(ps[0:1, BK(6)], g8c[0:8, 0:1], bcC[0:8, 128:640], True, True)
        wt('v', MM)
        wt('v', LYO)
        vop('tensor_add', R_MB2[0:1, 512 * j:512 * j + 512],
            ps[0:1, BK(6)], R_MB2[0:1, 512 * j:512 * j + 512])
    for j in range(2):
        wt('t', VV)
        mm(ps[:, BK(6)], onesr[0:1, :], R_MB2[0:1, 512 * j:512 * j + 512], True, True)
        wt('v', MM)
        vop('tensor_copy', bcB[:, 512 * j:512 * j + 512], ps[:, BK(6)])
    if "g8" in taps:
        wt('g', VV)
        godma(taps["g8"], g8c[0:8, 0:1])
    if "kq" in taps:
        wt('g', VV)
        godma(taps["kq"], biasH[:, 0:16])
    if "xn0" in taps:
        wt('g', SS, XNF_OWN_SS)
        godma(taps["xn0"], xnF[:])

    mw_chunks = []
    for hc in range(8):
        for fcol in range(8):
            mw_chunks.append((w1T[128 * hc:128 * hc + 128, 512 * fcol:512 * fcol + 512],
                              [tw1T[e, 128 * hc:128 * hc + 128, 512 * fcol:512 * fcol + 512]
                               for e in range(8)],
                              mw1_scr[128 * hc:128 * hc + 128, 512 * fcol:512 * fcol + 512]))
    for fr in range(32):
        for col in range(2):
            mw_chunks.append((w2T[128 * fr:128 * fr + 128, 512 * col:512 * col + 512],
                              [tw2T[e, 128 * fr:128 * fr + 128, 512 * col:512 * col + 512]
                               for e in range(8)],
                              mw2_scr[128 * fr:128 * fr + 128, 512 * col:512 * col + 512]))
    mw_state = {"i": 0, "vfree": [0, 0], "yfree": [0, 0]}

    def emit_mw_chunk():
        i = mw_state["i"]
        if i >= len(mw_chunks):
            return
        mw_state["i"] = i + 1
        wsl, twl, dst = mw_chunks[i]
        sl = i % 2
        seg = 9 * 512 * sl
        mws = MWA if sl == 0 else MWB
        mss = MSA if sl == 0 else MSB
        em.wait('g', VV, mw_state["vfree"][sl])
        gdma(POOL1[:, seg:seg + 512], wsl, mws)
        for e in range(8):
            gdma(POOL1[:, seg + 512 * (e + 1):seg + 512 * (e + 2)], twl[e], mws)
        accb = mxT[:, 512 * sl:512 * sl + 512]
        wt('v', mws)
        em.wait('v', mss, mw_state["yfree"][sl])
        vop('scalar_tensor_tensor', out=accb, in0=POOL1[:, seg + 512:seg + 1024],
            scalar=gBc[:, 0:1], in1=POOL1[:, seg:seg + 512], op0=ALU.mult, op1=ALU.add)
        for e in range(1, 8):
            vop('scalar_tensor_tensor', out=accb,
                in0=POOL1[:, seg + 512 * (e + 1):seg + 512 * (e + 2)],
                scalar=gBc[:, e:e + 1], in1=accb, op0=ALU.mult, op1=ALU.add)
        mw_state["vfree"][sl] = VV.n
        wt('y', VV)
        ydma(dst, accb, mss)
        mw_state["yfree"][sl] = mss.n

    eseg = lambda st, i: eT[:, st * 2048 + i * 1024: st * 2048 + i * 1024 + 1024]
    of32 = POOL2[:, 0:512]
    otmp = POOL2[:, 512:1024]
    sevac = POOL2[:, 1024:1536]
    kbuf = [kxb[:, 0:4096], kxb[:, 4096:8192]]
    vbuf = [wblk[:, 0:4096], POOL1[:, 9216:13312]]
    WQ = [WQA, WQB]
    wt('s', LSKA)
    wt('s', LSKB)
    em.wait('g', VV)
    em.wait('g', LSWA)
    em.wait('g', LSWB)

    def l0_group_loads(g):
        if g >= 32:
            return
        vh = g % 8
        sl = g % 2
        em.wait('g', MM, ring_cons.get(("kv", sl), 0))
        em.wait('g', LSKA)
        em.wait('g', LSKB)
        gdma(kbuf[sl][:, 0:NFULL], k_scr[128 * vh:128 * vh + 128, :], WQ[sl])
        em.run('g', lambda g_, vh=vh, sl=sl: g_.dma_start(
            out=vbuf[sl][:, 0:NFULL].rearrange("p (c d) -> p c d", d=128),
            in_=v_scr[:, 128 * vh:128 * vh + 128].rearrange("(c p) d -> p c d", p=128)),
            WQ[sl], 16)

    l0_group_loads(0)
    for g in range(32):
        qb, vh = g // 8, g % 8
        sl = g % 2
        l0_group_loads(g + 1)
        kg, vg = kbuf[sl], vbuf[sl]

        def qrow(r0, r1, vh=vh, qb=qb):
            return QF[r0:r1, vh * NTOK + qb * 512: vh * NTOK + qb * 512 + 512]
        wt('t', WQ[sl])
        wt('t', VV)
        ss_after = {}
        pairbuf = lambda sp: eT[:, 0:2048] if sp == 0 else POOL3[:, 0:1024].bitcast(BF16)

        def scores2(p):
            if p >= 1:
                wt('t', SS, ss_after[p - 1])
            for j in range(2):
                c = 2 * p + j
                mm(ps[:, BK(2 * j)], kg[0:64, 128 * c:128 * c + 128],
                   qrow(0, 64), True, True, tp=(0, 0))
                mm(ps[:, BK(2 * j + 1)], kg[64:128, 128 * c:128 * c + 128],
                   qrow(64, 128), True, True, tp=(64, 0))
            wt('s', MM)
            act(pairbuf(p % 2), ps[:, 0:2048], AF.Exp, bias=biasP[:, vh:vh + 1])
            ss_after[p] = SS.n

        def av2(p, first, last):
            buf = pairbuf(p % 2)
            wt('t', SS, ss_after[p])
            for j in range(2):
                c = 2 * p + j
                fst = first and j == 0
                lst = last and j == 1
                e0 = buf[:, 1024 * j:1024 * j + 512]
                e1 = buf[:, 1024 * j + 512:1024 * j + 1024]
                mm(ps[0:128, BK(4)], vg[:, 128 * c:128 * c + 128], e0, fst, lst)
                mm(ps[0:128, BK(5)], vg[:, 128 * c:128 * c + 128], e1, fst, lst)
                mm(ps[:, BK(6)], onesw[:, 0:128], e0, fst, lst)
                mm(ps[:, BK(7)], onesw[:, 0:128], e1, fst, lst)

        scores2(0)
        for p in range(1, 16):
            scores2(p)
            av2(p - 1, p - 1 == 0, False)
        av2(15, False, True)
        ring_cons[("kv", sl)] = MM.n
        wt('v', MM)
        vop('reciprocal', bcC[:, 0:512], ps[:, BK(6)])
        vop('reciprocal', bcC[:, 512:1024], ps[:, BK(7)])
        vop('tensor_mul', of32, ps[:, BK(4)], bcC[:, 0:512])
        vop('scalar_tensor_tensor', out=otmp, in0=ps[:, BK(5)], scalar=lamS[:, 0:1],
            in1=bcC[:, 512:1024], op0=ALU.mult, op1=ALU.mult)
        vop('tensor_add', of32, of32, otmp)
        wt('s', VV)
        act(otmp, of32, AF.Square)
        wt('t', SS)
        mm(ps[64:65, BK(6)], onesf[:, 0:1], of32, True, True, tp=(0, 64))
        mm(ps[0:1, BK(1)], onesf[:, 0:1], otmp, True, True)
        wt('v', MM)
        vop('tensor_scalar', out=R_MU, in0=ps[64:65, BK(6)], scalar1=1.0 / 128,
            scalar2=None, op0=ALU.mult)
        vop('tensor_scalar', out=R_VAR, in0=ps[0:1, BK(1)], scalar1=1.0 / 128,
            scalar2=None, op0=ALU.mult)
        vop('tensor_mul', R_RCP[0:1, 0:512], R_MU, R_MU)
        vop('tensor_sub', R_VAR, R_VAR, R_RCP[0:1, 0:512])
        wt('s', VV)
        act(R_RCP[0:1, 512:1024], R_VAR, AF.Ln, bias=stat[0:1, 19:20])
        act(R_VAR, R_RCP[0:1, 512:1024], AF.Exp, scale=-0.5)
        wt('t', SS)
        mm(ps[:, BK(2)], onesr[0:1, :], R_MU, True, True)
        mm(ps[:, BK(3)], onesr[0:1, :], R_VAR, True, True)
        wt('v', MM)
        vop('tensor_copy', bcC[:, 0:512], ps[:, BK(2)])
        vop('tensor_copy', bcC[:, 512:1024], ps[:, BK(3)])
        vop('tensor_sub', of32, of32, bcC[:, 0:512])
        vop('tensor_mul', of32, of32, bcC[:, 512:1024])
        vop('tensor_scalar', out=oF[:, 512 * vh:512 * vh + 512], in0=of32,
            scalar1=lngS, scalar2=lnbS, op0=ALU.mult, op1=ALU.add)
        for _ in range(4):
            emit_mw_chunk()
        if vh == 7:
            for col in range(2):
                for vhh in range(8):
                    wsl = 4096 + 512 * (vhh % 2)
                    LO = LOA if vhh % 2 == 0 else LOB
                    em.wait('g', MM, ring_cons.get(("wo", vhh % 2), 0))
                    gdma(wblk[:, wsl:wsl + 512],
                         wo[128 * vhh:128 * vhh + 128, 512 * col:512 * col + 512], LO)
                    wt('t', LO)
                    if vhh == 0:
                        wt('t', VV)
                    for tc in range(4):
                        mm(ps[:, BK(tc)], oF[:, 512 * vhh + 128 * tc:512 * vhh + 128 * tc + 128],
                           wblk[:, wsl:wsl + 512], vhh == 0, vhh == 7)
                    ring_cons[("wo", vhh % 2)] = MM.n
                wt('v', MM)
                for tc in range(4):
                    wt('v', LSS)
                    vop('tensor_copy', sevac, ps[:, BK(tc)])
                    wt('y', VV)
                    ydma(s_scr[512 * qb + 128 * tc:512 * qb + 128 * tc + 128,
                               512 * col:512 * col + 512], sevac, LSS)
    if "s_l0" in taps:
        wt('g', LSS)
        godma(taps["s_l0"], s_scr)
    while mw_state["i"] < len(mw_chunks):
        emit_mw_chunk()
    if "mw1" in taps:
        wt('g', MSA)
        wt('g', MSB)
        godma(taps["mw1"], mw1_scr)

    for tb in range(4):
        for fc in range(32):
            sl = fc % 2
            LW = LWA if sl == 0 else LWB
            em.wait('y', MM, ring_cons.get(("mw1", sl), 0))
            if tb == 0 and fc == 0:
                em.wait('y', MSA)
                em.wait('y', MSB)
            for dinc in range(8):
                ydma(wblk[:, 1024 * sl + 128 * dinc:1024 * sl + 128 * dinc + 128],
                     mw1_scr[128 * dinc:128 * dinc + 128, fc * 128:fc * 128 + 128], LW)
            wt('t', LW)
            if fc < 2:
                wt('t', SS)
                wt('t', VV)
            else:
                wt('t', SS, ring_cons.get(("ge", sl), 0))
            for dinc in range(8):
                mm(ps[:, BK(sl)], wblk[:, 1024 * sl + 128 * dinc:1024 * sl + 128 * dinc + 128],
                   xnF[:, dinc * NTOK + tb * 512:dinc * NTOK + tb * 512 + 512],
                   dinc == 0, dinc == 7)
            ring_cons[("mw1", sl)] = MM.n
            wt('s', MM)
            act(uBlk[:, fc * 512:fc * 512 + 512], ps[:, BK(sl)], AF.Gelu_apprx_tanh,
                bias=mb1T[:, fc:fc + 1])
            ring_cons[("ge", sl)] = SS.n
        for fc in range(32):
            sl = fc % 2
            LW2 = LW2A if sl == 0 else LW2B
            em.wait('y', MM, ring_cons.get(("mw2", sl), 0))
            ydma(wblk[:, 2048 + 1024 * sl:2048 + 1024 * sl + 512],
                 mw2_scr[fc * 128:fc * 128 + 128, 0:512], LW2)
            ydma(wblk[:, 2048 + 1024 * sl + 512:2048 + 1024 * sl + 1024],
                 mw2_scr[fc * 128:fc * 128 + 128, 512:1024], LW2)
            wt('t', LW2)
            if fc == 0:
                wt('t', SS)
                wt('t', VV)
            for tc in range(4):
                for col in range(2):
                    mm(ps[:, BK(tc * 2 + col)],
                       uBlk[:, fc * 512 + tc * 128:fc * 512 + tc * 128 + 128],
                       wblk[:, 2048 + 1024 * sl + 512 * col:2048 + 1024 * sl + 512 * col + 512],
                       fc == 0, fc == 31)
            ring_cons[("mw2", sl)] = MM.n
        for tc in range(4):
            wt('y', VV)
            wt('y', LSS)
            ydma(xt0, x0[tb * 512 + tc * 128:tb * 512 + tc * 128 + 128, :], LYX)
            ydma(xt1, s_scr[tb * 512 + tc * 128:tb * 512 + tc * 128 + 128, :], LYX)
            for col in range(2):
                wt('v', MM)
                wt('v', LYX)
                wt('v', LSX1)
                vop('tensor_add', POOL2[:, 0:512], ps[:, BK(tc * 2 + col)],
                    bcB[:, 512 * col:512 * col + 512])
                wt('s', VV)
                act(POOL2[:, 512:1024], POOL2[:, 0:512], AF.Silu)
                wt('v', SS)
                vop('tensor_mul', POOL2[:, 1024:1536], POOL2[:, 512:1024],
                    xt1[:, 512 * col:512 * col + 512])
                vop('tensor_add', POOL2[:, 1536:2048], POOL2[:, 1024:1536],
                    xt0[:, 512 * col:512 * col + 512])
                wt('y', VV)
                ydma(x1_scr[tb * 512 + tc * 128:tb * 512 + tc * 128 + 128,
                            512 * col:512 * col + 512], POOL2[:, 1536:2048], LSX1)
    if "x1" in taps:
        wt('g', LSX1)
        godma(taps["x1"], x1_scr)

    wt('y', LSX1)
    for tc in range(NTC):
        wt('v', LDO)
        ln_tile(x1_scr[128 * tc:128 * tc + 128, :], want_f32=True)
        vop('reduce_max', stat[:, 22:23], xt2, AX.X, apply_absolute_value=True)
        vop('tensor_scalar_max', stat[:, 23:24], stat[:, 22:23], 1e-5)
        vop('reciprocal', stat[:, 24:25], stat[:, 23:24])
        vop('tensor_scalar', out=sc1T[:, tc:tc + 1], in0=stat[:, 23:24],
            scalar1=1.0 / 127.0, scalar2=None, op0=ALU.mult)
        vop('tensor_scalar', out=xt0, in0=xt2, scalar1=stat[:, 24:25], scalar2=127.0,
            op0=ALU.mult, op1=ALU.mult)
        vop('tensor_copy', xti[:, 0:1024], xt0)
        wt('v', MM)
        vop('tensor_copy', eT[:, 1024:2048], xti[:, 0:1024])
        if tc == 0:
            wt('g', VV)
            godma(R_HFIRST, xt2[0:1, :])
        if tc == NTC - 1:
            wt('g', VV)
            godma(R_HLAST, xt2[127:128, :])
        ln_transpose(lambda j, tc=tc: xnF[:, j * NTOK + tc * 128: j * NTOK + tc * 128 + 128])
        ln_transpose(lambda j, tc=tc: xqF[:, j * NTOK + tc * 128: j * NTOK + tc * 128 + 128],
                     src=eT[:, 1024:2048])
        if tc == 0:
            pass
    godma(halo_in[0:1, :], R_HFIRST)
    godma(halo_in[1:2, :], R_HLAST)
    wt('g', LDO)
    em.run('g', lambda g: g.collective_compute(
        "AllGather", ALU.bypass, replica_groups=[[0, 1], [2, 3], [4, 5], [6, 7]],
        ins=[halo_in], outs=[halo_out]), CC, 1)

    em.wait('g', MM)
    hfc = POOL3[:, 0:2050]
    hf2 = POOL3[:, 2050 + 0:2050 + 2048]
    ropeC = POOL2[:, 0:2048]
    ropeS = POOL2[:, 2048:4096]
    yodma(ropeC, ropec)
    yodma(ropeS, ropes)
    wt('v', LYO)
    vop('memset', q2m[:], 0.0)
    vop('memset', k2m[:], 0.0)
    P3A = POOL3[:, 0:2048]
    P3B = POOL3[:, 2048:4096]
    swap_mask = [i ^ 1 for i in range(32)]
    kstage = eT[:, 0:2048]
    ksq = mxT[:, 0:2048]

    def l1_qk_chunk(which, dc):
        base = 0 if which == "q" else 1024
        sl = pw_load(wqkv, base + dc * 128)
        for tb in range(4):
            bank = 2 + (tb % 2)
            wt('t', WQ[sl])
            wt('t', SS)
            wt('t', VV)
            for dinc in range(8):
                mm(ps[:, BK(bank)], wblk[:, 1024 * sl + dinc * 128:1024 * sl + dinc * 128 + 128],
                   xnF[:, dinc * NTOK + tb * 512:dinc * NTOK + tb * 512 + 512],
                   dinc == 0, dinc == 7)
            if tb == 3:
                ring_cons[("pw", sl)] = MM.n
            wt('s', MM)
            act(P3A[:, tb * 512:tb * 512 + 512], ps[:, BK(bank)], AF.Copy,
                scale=(0.125 if which == "q" else 1.0))
        wt('v', SS)
        vop('stream_shuffle', P3B, P3A, swap_mask)
        vop('tensor_mul', P3B, P3B, ropeS)
        vop('tensor_mul', P3A, P3A, ropeC)
        dst = (QF[:, dc * NTOK:dc * NTOK + NTOK] if which == "q" else kstage)
        if which == "k":
            wt('v', LSKA)
        vop('tensor_add', dst, P3A, P3B)
        wt('s', VV)
        act(ksq, dst, AF.Square)
        dstm = q2m if which == "q" else k2m
        for j in range(4):
            wt('t', SS)
            wt('t', VV)
            mm(ps[0:2, BK(5)], hs2[:], ksq[:, j * 512:j * 512 + 512], True, True)
            wt('v', MM)
            vop('reduce_max', stat[0:2, 18:19], ps[0:2, BK(5)], AX.X)
            vop('tensor_max', dstm[0:2, dc:dc + 1], dstm[0:2, dc:dc + 1], stat[0:2, 18:19])
        if which == "k":
            wt('y', VV)
            ydma(kv_in[128 * dc:128 * dc + 128, :], kstage, LSKA)

    for dc in range(8):
        l1_qk_chunk("q", dc)
        l1_qk_chunk("k", dc)
    for col in range(2):
        for tcg in range(4):
            for hc in range(8):
                wsl = hc % 2
                em.wait('g', MM, ring_cons.get(("wv", wsl), 0))
                gdma(wblk[:, 3072 + 512 * wsl:3072 + 512 * wsl + 512],
                     wqkv[128 * hc:128 * hc + 128, 2048 + col * 512:2048 + col * 512 + 512], WV[wsl])
                wt('t', WV[wsl])
                if hc == 0:
                    wt('t', SS)
                    wt('t', VV)
                for tc4 in range(4):
                    tc = tcg * 4 + tc4
                    mm(ps[:, BK([0, 1, 6, 7][tc4])],
                       xnF[:, hc * NTOK + tc * 128:hc * NTOK + tc * 128 + 128],
                       wblk[:, 3072 + 512 * wsl:3072 + 512 * wsl + 512],
                       hc == 0, hc == 7)
                ring_cons[("wv", wsl)] = MM.n
            wt('s', MM)
            for tc4 in range(4):
                tc = tcg * 4 + tc4
                t0 = tc * 128
                stg = kv_stage[tc4 % 2]
                wt('s', LSKA if tc4 % 2 == 0 else LSKB)
                act(stg, ps[:, BK([0, 1, 6, 7][tc4])], AF.Copy)
                wt('y', SS)
                ydma(kv_in[1024 + (t0 % 1024):1024 + (t0 % 1024) + 128,
                           (t0 // 1024) * 1024 + col * 512:(t0 // 1024) * 1024 + col * 512 + 512],
                     stg, LSKA if tc4 % 2 == 0 else LSKB)
    vop('memset', hf2[0:2, 0:2048], 0.0)
    wt('g', VV)
    godma(kv_in[2048:2050, :], hf2[0:2, 0:2048])
    knb = R_KNOWN.rearrange("o (a s) -> o a s", s=2)
    godma(knb[:, :, 0:1], k2m[0:1, 0:8], ns=True)
    godma(knb[:, :, 1:2], k2m[1:2, 0:8], ns=True)
    godma(kv_in[2049:2050, 0:16], R_KNOWN)
    wt('g', LSKA)
    wt('g', LSKB)
    wt('g', LDO)
    em.run('g', lambda g: g.collective_compute(
        "AllGather", ALU.bypass, replica_groups=[[0, 1], [2, 3], [4, 5], [6, 7]],
        ins=[kv_in], outs=[kv_out]), CC, 1)

    pari = sb("pari", [1, 1], I32)
    yodma(pari[:], parity)
    prow = rows[0:1, 2048:3072]
    _halo_lyo = LYO.n
    _halo_ldo = LDO.n

    def _halo_switch(g):
        g.wait_ge(CC.h, 1)
        g.wait_ge(LYO.h, _halo_lyo)
        g.wait_ge(LDO.h, _halo_ldo)
        with g.register() as r:
            g.load(r, pari[0:1, 0:1])
            idx = g.snap(r, min_val=0, max_val=1)
            for i in g.Switch(idx, 2):
                src = halo_out[2:3, :] if i == 0 else halo_out[1:2, :]
                g.dma_start(out=prow, in_=src).then_inc(LDO.h, 16)
    LDO.n += 16
    em.add('g', _halo_switch)

    vop('memset', mxT[:], 0.0)
    vop('memset', onesw[:], 1.0)
    wt('t', SS)
    wt('t', VV)
    for tc in range(16):
        mm(ps[0:1, tc * 128:tc * 128 + 128], sc1T[:, tc:tc + 1], identF[:], True, True)
    wt('v', MM)
    vop('tensor_copy', rows[0:1, 0:2048], ps[0:1, 0:2048])
    vop('tensor_scalar', out=rows[0:1, 0:2048], in0=rows[0:1, 0:2048],
        scalar1=R_WM[0:1, 0:1], scalar2=None, op0=ALU.mult)
    wt('t', VV)
    for j in range(4):
        mm(ps[:, 2048 + 512 * j:2560 + 512 * j], onesr[0:1, :],
           rows[0:1, 512 * j:512 * j + 512], True, True)
    wt('v', MM)
    cB = POOL2[:, 0:2048]
    vop('tensor_copy', cB, ps[:, 2048:4096])
    wt('v', LDO)
    vop('reduce_max', R_HS[0:1, 0:1], prow, AX.X, apply_absolute_value=True)
    vop('tensor_scalar_max', R_HS[0:1, 1:2], R_HS[0:1, 0:1], 1e-5)
    vop('reciprocal', R_HS[0:1, 2:3], R_HS[0:1, 1:2])
    wt('t', VV)
    for hc in range(8):
        mm(ps[:, BK(6)][:, hc:hc + 1], prow[0:1, 128 * hc:128 * hc + 128],
           onesf[0:1, 0:1], True, True)
    mm(ps[:, BK(7)][:, 0:1], onesr[0:1, :], R_HS[0:1, 2:3], True, True)
    mm(ps[:, BK(7)][:, 1:2], onesr[0:1, :], R_HS[0:1, 1:2], True, True)
    wt('v', MM)
    vop('tensor_copy', stat[:, 25:26], ps[:, BK(7)][:, 0:1])
    vop('tensor_copy', stat[:, 26:27], ps[:, BK(7)][:, 1:2])
    vop('tensor_scalar', out=xt0[:, 0:8], in0=ps[:, BK(6)][:, 0:8],
        scalar1=stat[:, 25:26], scalar2=127.0, op0=ALU.mult, op1=ALU.mult)
    vop('tensor_copy', xti[:, 0:8], xt0[:, 0:8])
    vop('tensor_copy', xqh[:, 0:8], xti[:, 0:8])
    wt('t', VV)
    mm(ps[:, BK(7)][:, 2:3], onesr[0:1, :], R_WM[0:1, 0:1], True, True)
    wt('v', MM)
    vop('tensor_copy', stat[:, 29:30], ps[:, BK(7)][:, 2:3])
    vop('tensor_scalar', out=stat[:, 28:29], in0=stat[:, 26:27],
        scalar1=stat[:, 29:30], scalar2=1.0 / 127.0, op0=ALU.mult, op1=ALU.mult)
    for fc in range(32):
        sl = fc % 2
        LW = LWA if sl == 0 else LWB
        em.wait('y', MM, ring_cons.get(("w1q", sl), 0))
        for dinc in range(8):
            ydma(wblk[:, 1024 * sl + 128 * dinc:1024 * sl + 128 * dinc + 128],
                 w1q_scr[128 * dinc:128 * dinc + 128, fc * 128:fc * 128 + 128], LW)
        wt('t', LW)
        wt('t', VV)
        for dinc in range(8):
            for b4 in range(4):
                mm(ps[:, BK(b4)], wblk[:, 1024 * sl + 128 * dinc:1024 * sl + 128 * dinc + 128],
                   xqF[:, dinc * NTOK + b4 * 512:dinc * NTOK + b4 * 512 + 512],
                   dinc == 0, dinc == 7)
            mm(ps[:, BK(4)][:, 0:8], wblk[:, 1024 * sl + 128 * dinc:1024 * sl + 128 * dinc + 128],
               xqh[:, 0:8], dinc == 0, dinc == 7)
        ring_cons[("w1q", sl)] = MM.n
        wt('v', MM)
        for b4 in range(4):
            vop('tensor_mul', hfc[:, 1 + 512 * b4:1 + 512 * b4 + 512], ps[:, BK(b4)],
                cB[:, 512 * b4:512 * b4 + 512])
        vop('tensor_scalar', out=xt0[:, 8:9], in0=ps[:, BK(4)][:, 0:1],
            scalar1=stat[:, 28:29], scalar2=None, op0=ALU.mult)
        vop('tensor_scalar', out=hfc[:, 0:1], in0=xt0[:, 8:9], scalar1=hmS[:, 0:1],
            scalar2=None, op0=ALU.mult)
        vop('tensor_scalar', out=hfc[:, 2049:2050], in0=xt0[:, 8:9], scalar1=hmS[:, 1:2],
            scalar2=None, op0=ALU.mult)
        vop('tensor_scalar', out=hf2, in0=hfc[:, 0:2048], scalar1=ctap[:, fc:fc + 1],
            scalar2=ctap[:, 96 + fc:97 + fc], op0=ALU.mult, op1=ALU.add)
        vop('scalar_tensor_tensor', out=hf2, in0=hfc[:, 1:2049],
            scalar=ctap[:, 32 + fc:33 + fc], in1=hf2, op0=ALU.mult, op1=ALU.add)
        vop('scalar_tensor_tensor', out=hf2, in0=hfc[:, 2:2050],
            scalar=ctap[:, 64 + fc:65 + fc], in1=hf2, op0=ALU.mult, op1=ALU.add)
        wt('s', VV)
        wt('s', LSH)
        act(kxb[:, 0:2048], hf2, AF.Silu)
        act(kxb[:, 2048:4096], kxb[:, 0:2048], AF.Abs)
        wt('v', SS)
        vop('tensor_max', mxT[:, 0:2048], mxT[:, 0:2048], kxb[:, 2048:4096])
        wt('y', VV)
        ydma(hs_scr[128 * fc:128 * fc + 128, :], kxb[:, 0:2048], LSH)
    if "hs" in taps:
        wt('g', LSH)
        godma(taps["hs"], hs_scr)

    ph9_ldo = LDO.n
    ph9_mm = MM.n

    def _kv_switch(g):
        g.wait_ge(CC.h, 2)
        g.wait_ge(LDO.h, ph9_ldo)
        g.wait_ge(MM.h, ph9_mm)
        with g.register() as r:
            g.load(r, pari[0:1, 0:1])
            idx = g.snap(r, min_val=0, max_val=1)
            for i in g.Switch(idx, 2):
                own, oth = (0, 2050) if i == 0 else (2050, 0)
                g.dma_start(out=k_scr[:, 0:2048], in_=kv_out[own:own + 1024, :]).then_inc(LDO.h, 16)
                g.dma_start(out=k_scr[:, 2048:4096], in_=kv_out[oth:oth + 1024, :]).then_inc(LDO.h, 16)
                g.dma_start(out=v_scr[0:1024, :], in_=kv_out[own + 1024:own + 2048, 0:1024]).then_inc(LDO.h, 16)
                g.dma_start(out=v_scr[1024:2048, :], in_=kv_out[own + 1024:own + 2048, 1024:2048]).then_inc(LDO.h, 16)
                g.dma_start(out=v_scr[2048:3072, :], in_=kv_out[oth + 1024:oth + 2048, 0:1024]).then_inc(LDO.h, 16)
                g.dma_start(out=v_scr[3072:4096, :], in_=kv_out[oth + 1024:oth + 2048, 1024:2048]).then_inc(LDO.h, 16)
                g.dma_start(out=R_KN[0:1, 16:32], in_=kv_out[oth + 2049:oth + 2050, 0:16]).then_inc(LDO.h, 16)
                g.dma_start(out=R_KN[0:1, 0:16], in_=kv_out[own + 2049:own + 2050, 0:16]).then_inc(LDO.h, 16)
    LDO.n += 8 * 16
    em.add('g', _kv_switch)
    KVSW_LD = LDO.n

    wt('v', LDO, KVSW_LD)
    qnb = R_KN[0:1, 32:48].rearrange("o (a s) -> o a s", s=2)
    godma(qnb[:, :, 0:1], q2m[0:1, 0:8], ns=True)
    godma(qnb[:, :, 1:2], q2m[1:2, 0:8], ns=True)
    wt('v', LDO)
    vop('tensor_max', R_KN[0:1, 48:64], R_KN[0:1, 0:16], R_KN[0:1, 16:32])
    vop('tensor_mul', R_KN[0:1, 48:64], R_KN[0:1, 48:64], R_KN[0:1, 32:48])
    wt('s', VV)
    act(R_KN[0:1, 64:80], R_KN[0:1, 48:64], AF.Sqrt)
    wt('v', SS)
    vop('tensor_scalar', out=R_KN[0:1, 64:80], in0=R_KN[0:1, 64:80], scalar1=-0.5,
        scalar2=None, op0=ALU.mult)
    wt('t', VV)
    mm(ps[:, BK(6)][:, 0:16], onesr[0:1, :], R_KN[0:1, 64:80], True, True)
    wt('v', MM)
    vop('tensor_copy', biasH[:, 0:16], ps[:, BK(6)][:, 0:16])
    vop('tensor_scalar', out=biasHK[:, 0:16], in0=biasH[:, 0:16], scalar1=klS[:, 0:1],
        scalar2=None, op0=ALU.add)
    bp1_v = biasH[:, 0:16].rearrange("p (a s) -> p a s", s=2)
    vop('tensor_tensor', biasP[:, 0:8], bp1_v[:, :, 0:1], bp1_v[:, :, 1:2], ALU.min)
    vop('tensor_scalar', out=biasPK[:, 0:8], in0=biasP[:, 0:8], scalar1=klS[:, 0:1],
        scalar2=None, op0=ALU.add)
    yodma(R_MB2, boutr)
    wt('t', LYO)
    for j in range(2):
        wt('t', VV)
        mm(ps[:, BK(6)], onesr[0:1, :], R_MB2[0:1, 512 * j:512 * j + 512], True, True)
        wt('v', MM)
        vop('tensor_copy', bcB[:, 512 * j:512 * j + 512], ps[:, BK(6)])
    kbuf = [kxb[:, 0:4096], kxb[:, 4096:8192]]
    vbuf1 = [POOL1[:, 8064:12224], POOL1[:, 12224:16384]]
    gseq = []
    for qb in range(4):
        for hg in range(8):
            gseq.append((qb, hg))
    wt('v', MM)
    vop('memset', POOL1[:, 8064:16384], 1.0)
    wt('s', LSKA)
    wt('s', LSKB)
    em.wait('g', VV)
    em.wait('g', MM)

    def l1_group_loads(g):
        if g >= 32:
            return
        qb, hg = gseq[g]
        sl = g % 2
        em.wait('g', MM, ring_cons.get(("kv1", sl), 0))
        em.wait('g', LSH)
        em.wait('g', LDO, KVSW_LD)
        gdma(kbuf[sl][:, 0:NFULL], k_scr[128 * hg:128 * hg + 128, :], WQ[sl])
        vv_ = vbuf1[sl][:, 0:4160].rearrange("p (c w) -> p c w", w=130)
        em.run('g', lambda g_, hg=hg, vv_=vv_: g_.dma_start(
            out=vv_[:, :, 0:64],
            in_=v_scr[:, 128 * hg:128 * hg + 64].rearrange("(c p) d -> p c d", p=128)),
            WQ[sl], 16)
        em.run('g', lambda g_, hg=hg, vv_=vv_: g_.dma_start(
            out=vv_[:, :, 65:129],
            in_=v_scr[:, 128 * hg + 64:128 * hg + 128].rearrange("(c p) d -> p c d", p=128)),
            WQ[sl], 16)

    s2b = stat[:, 26:27]
    wt('t', VV)
    mm(ps[:, BK(7)][:, 2:3], onesr[0:1, :], R_WM[0:1, 1:2], True, True)
    wt('v', MM)
    vop('tensor_copy', s2b, ps[:, BK(7)][:, 2:3])
    for tg in range(4):
        wt('t', VV)
        for j in range(4):
            tc = tg * 4 + j
            tr(ps[:, 2048 + 128 * j:2048 + 128 * j + 128], mxT[:, 128 * tc:128 * tc + 128])
        wt('v', MM)
        for j in range(4):
            tc = tg * 4 + j
            vop('reduce_max', stat[:, 22:23], ps[:, 2048 + 128 * j:2048 + 128 * j + 128], AX.X)
            vop('tensor_scalar_max', stat[:, 23:24], stat[:, 22:23], 1e-5)
            vop('tensor_scalar', out=c2T[:, tc:tc + 1], in0=stat[:, 23:24],
                scalar1=s2b, scalar2=1.0 / 127.0, op0=ALU.mult, op1=ALU.mult)
            vop('reciprocal', stat[:, 24:25], stat[:, 23:24])
            vop('tensor_scalar', out=sc1T[:, tc:tc + 1], in0=stat[:, 24:25],
                scalar1=127.0, scalar2=None, op0=ALU.mult)
    wt('t', VV)
    for tc in range(16):
        mm(ps[0:1, tc * 128:tc * 128 + 128], sc1T[:, tc:tc + 1], identF[:], True, True)
    wt('v', MM)
    vop('tensor_copy', rows[0:1, 0:2048], ps[0:1, 0:2048])
    wt('t', VV)
    for j in range(4):
        mm(ps[:, 2048 + 512 * j:2560 + 512 * j], onesr[0:1, :],
           rows[0:1, 512 * j:512 * j + 512], True, True)
    wt('v', MM)
    sc2Bp = POOL3[:, 0:2048]
    vop('tensor_copy', sc2Bp, ps[:, 2048:4096])
    rst1 = POOL2[:, 2048:3072].bitcast(BF16)
    rst2 = POOL2[:, 3072:4096].bitcast(BF16)

    def resc(fc):
        wt('y', VV)
        wt('y', LSH)
        ydma(rst1[:, 0:2048], hs_scr[128 * fc:128 * fc + 128, :], LYW)
        wt('v', LYW)
        vop('tensor_tensor', xti[:, 0:2048], rst1[:, 0:2048], sc2Bp, ALU.mult)
        vop('tensor_copy', rst2[:, 0:2048], xti[:, 0:2048])
        wt('y', VV)
        ydma(hs_scr[128 * fc:128 * fc + 128, :], rst2[:, 0:2048], LSH)

    l1_group_loads(0)
    for g in range(32):
        qb, hg = gseq[g]
        sl = g % 2
        seq = list(range(0, 4 * qb + 4)) + list(range(16, 32))
        nch = len(seq)
        diag0 = 4 * qb
        l1_group_loads(g + 1)
        kg, vg = kbuf[sl], vbuf1[sl]

        def qrow1(r0, r1, hg=hg, qb=qb):
            return QF[r0:r1, hg * NTOK + qb * 512: hg * NTOK + qb * 512 + 512]
        wt('t', WQ[sl])
        wt('t', VV)
        ss_after = {}
        vv_after = {}
        eA = lambda st, hh: eT[:, (2 * st + hh) * 512:(2 * st + hh) * 512 + 512]

        def scores1(cix):
            c = seq[cix]
            isB = c >= 16
            isdiag = (not isB) and (c >= diag0)
            bsrcP = biasPK if isB else biasP
            if cix >= 2:
                wt('t', SS, ss_after[cix - 2])
                if (cix - 2) in vv_after:
                    wt('t', VV, vv_after[cix - 2])
            st = cix % 2
            mm(ps[:, BK(2 * st)], kg[0:64, 128 * c:128 * c + 128],
               qrow1(0, 64), True, True, tp=(0, 0))
            mm(ps[:, BK(2 * st + 1)], kg[64:128, 128 * c:128 * c + 128],
               qrow1(64, 128), True, True, tp=(64, 0))
            wt('s', MM)
            act(eT[:, 1024 * st:1024 * st + 1024], ps[:, 1024 * st:1024 * st + 1024],
                AF.Exp, bias=bsrcP[:, hg:hg + 1])
            ss_after[cix] = SS.n
            if isdiag:
                di = c - diag0
                wt('v', SS, ss_after[cix])
                vop('tensor_mul', eA(st, 0), eA(st, 0), dmask[:, 512 * di:512 * di + 512])
                vop('tensor_mul', eA(st, 1), eA(st, 1), dmask[:, 512 * di:512 * di + 512])
                vv_after[cix] = VV.n

        def av1(cix, first, last):
            c = seq[cix]
            st = cix % 2
            wt('t', SS, ss_after[cix])
            if cix in vv_after:
                wt('t', VV, vv_after[cix])
            mm(ps[0:65, BK(4)], vg[:, 130 * c:130 * c + 65], eA(st, 0), first, last)
            mm(ps[64:128, BK(5)], vg[:, 130 * c + 65:130 * c + 129], eA(st, 1),
               first, last, tp=(0, 64), sgc=True)
            mm(ps[0:1, BK(5)], onesb[:, 1:2], eA(st, 1), False, last, tp=(0, 0), sgc=True)

        scores1(0)
        for cix in range(1, nch):
            scores1(cix)
            av1(cix - 1, cix - 1 == 0, False)
        av1(nch - 1, nch == 1, True)
        ring_cons[("kv1", sl)] = MM.n
        wt('v', MM)
        vop('tensor_copy', R_SUM[0:1, 0:512], ps[64:65, BK(4)])
        vop('tensor_copy', R_SUM[0:1, 512:1024], ps[0:1, BK(5)])
        vop('reciprocal', R_RCP, R_SUM)
        wt('t', VV)
        wt('t', SS)
        mm(ps[:, BK(6)], onesr[0:1, :], R_RCP[0:1, 0:512], True, True)
        mm(ps[:, BK(7)], onesr[0:1, :], R_RCP[0:1, 512:1024], True, True)
        wt('v', MM)
        vop('tensor_copy', bcC[:, 0:512], ps[:, BK(6)])
        vop('tensor_copy', bcC[:, 512:1024], ps[:, BK(7)])
        vop('tensor_mul', oF[0:64, 512 * hg:512 * hg + 512], ps[0:64, BK(4)],
            bcC[0:64, 0:512])
        vop('tensor_mul', oF[64:128, 512 * hg:512 * hg + 512], ps[64:128, BK(5)],
            bcC[64:128, 512:1024])
        if hg == 7:
            for col in range(2):
                for vhh in range(8):
                    wsl_ = 4096 + 512 * (vhh % 2)
                    LO = LOA if vhh % 2 == 0 else LOB
                    em.wait('g', MM, ring_cons.get(("wo", vhh % 2), 0))
                    gdma(wblk[:, wsl_:wsl_ + 512],
                         wout[128 * vhh:128 * vhh + 128, 512 * col:512 * col + 512], LO)
                    wt('t', LO)
                    if vhh == 0:
                        wt('t', VV)
                    for tc in range(4):
                        mm(ps[:, BK(tc)], oF[:, 512 * vhh + 128 * tc:512 * vhh + 128 * tc + 128],
                           wblk[:, wsl_:wsl_ + 512], vhh == 0, vhh == 7)
                    ring_cons[("wo", vhh % 2)] = MM.n
                wt('v', MM)
                for tc in range(4):
                    wt('v', LSS)
                    vop('tensor_add', POOL2[:, 1024:1536], ps[:, BK(tc)],
                        bcB[:, 512 * col:512 * col + 512])
                    wt('y', VV)
                    ydma(s_scr[512 * qb + 128 * tc:512 * qb + 128 * tc + 128,
                               512 * col:512 * col + 512], POOL2[:, 1024:1536], LSS)
        resc(g)
    if "s_l1" in taps:
        wt('g', LSS)
        godma(taps["s_l1"], s_scr)

    sc2B = POOL3[:, 0:2048]
    for tb in range(4):
        for fc in range(32):
            sl = fc % 2
            LW = LWA if sl == 0 else LWB
            LW2 = LW2A if sl == 0 else LW2B
            em.wait('y', MM, ring_cons.get(("bl2", sl), 0))
            if tb == 0 and fc == 0:
                em.wait('y', LSH)
            ydma(wblk[:, 1024 * sl:1024 * sl + 512],
                 hs_scr[128 * fc:128 * fc + 128, 512 * tb:512 * tb + 512], LW)
            ydma(wblk[:, 2048 + 1024 * sl:2048 + 1024 * sl + 512],
                 w2q_scr[128 * fc:128 * fc + 128, 0:512], LW2)
            ydma(wblk[:, 2048 + 1024 * sl + 512:2048 + 1024 * sl + 1024],
                 w2q_scr[128 * fc:128 * fc + 128, 512:1024], LW2)
            wt('t', LW)
            wt('t', LW2)
            if fc == 0:
                wt('t', SS)
                wt('t', VV)
            for tc in range(4):
                for col in range(2):
                    mm(ps[:, BK(tc * 2 + col)],
                       wblk[:, 1024 * sl + 128 * tc:1024 * sl + 128 * tc + 128],
                       wblk[:, 2048 + 1024 * sl + 512 * col:2048 + 1024 * sl + 512 * col + 512],
                       fc == 0, fc == 31)
            ring_cons[("bl2", sl)] = MM.n
        for tc in range(4):
            wt('y', VV)
            wt('y', LSS)
            wt('y', LSX1)
            ydma(xt0, x1_scr[tb * 512 + tc * 128:tb * 512 + tc * 128 + 128, :], LYX)
            ydma(xt1, s_scr[tb * 512 + tc * 128:tb * 512 + tc * 128 + 128, :], LYX)
            for col in range(2):
                wt('s', MM)
                wt('s', VV)
                act(POOL2[:, 2048:2560], ps[:, BK(tc * 2 + col)], AF.Silu,
                    scale=c2T[:, tb * 4 + tc:tb * 4 + tc + 1])
                wt('v', SS)
                wt('v', LYX)
                wt('v', STO)
                vop('tensor_mul', POOL2[:, 2560:3072], POOL2[:, 2048:2560],
                    xt1[:, 512 * col:512 * col + 512])
                vop('tensor_add', POOL2[:, 3072:3584], POOL2[:, 2560:3072],
                    xt0[:, 512 * col:512 * col + 512])
                wt('y', VV)
                em.run('y', lambda y, tb=tb, tc=tc, col=col: y.dma_start(
                    out=out[tb * 512 + tc * 128:tb * 512 + tc * 128 + 128,
                            512 * col:512 * col + 512],
                    in_=POOL2[:, 3072:3584]), STO, 16)
    em.add('g', lambda g: g.wait_ge(STO.h, STO.n))
    em.add('y', lambda y: y.wait_ge(STO.h, STO.n))

    with nc.Block() as block:
        @block.tensor
        def _(t):
            for f in em.TH['t']:
                f(t)

        @block.vector
        def _(v):
            for f in em.TH['v']:
                f(v)

        @block.scalar
        def _(s):
            for f in em.TH['s']:
                f(s)

        @block.gpsimd
        def _(g):
            for f in em.TH['g']:
                f(g)

        @block.sync
        def _(y):
            for f in em.TH['y']:
                f(y)

    es.close()
    return nc


_BUILT = {}


def kernel(**inputs):
    in_maps = [_prep_core(c, inputs) for c in range(8)]
    nc = build()
    res = run_bass_kernel_spmd(nc, in_maps, core_ids=list(range(8)))
    out = np.empty((4, 4096, 1024), np.float32)
    for c in range(8):
        b, h = c >> 1, c & 1
        out[b, 2048 * h:2048 * h + 2048] = np.asarray(res.results[c]["out"])
    return out


# version 08
# speedup vs baseline: 1.0328x; 1.0205x over previous
import contextlib
import numpy as np

import concourse.bass as bass
import concourse.mybir as mybir
from concourse.bass_utils import run_bass_kernel_spmd

F32 = mybir.dt.float32
BF16 = mybir.dt.bfloat16
I32 = mybir.dt.int32
AF = mybir.ActivationFunctionType
ALU = mybir.AluOpType
AX = mybir.AxisListType

NTOK = 2048
NFULL = 4096
DM = 1024
DFF = 4096
NTC = NTOK // 128
KILL = -30000.0

DEV_TAPS = ()


def _rope_tables(h, dtype):
    i = np.arange(32, dtype=np.float64)
    inv = 1.0 / (10000.0 ** (2.0 * i / 64.0))
    pos = np.arange(2048 * h, 2048 * h + 2048, dtype=np.float64)
    rows = np.arange(128)
    fr = pos[None, :] * inv[(rows % 64) // 2][:, None]
    cos = np.cos(fr)
    sgn = np.where(rows % 2 == 0, -1.0, 1.0)[:, None]
    sin = np.sin(fr) * sgn
    return cos.astype(dtype), sin.astype(dtype)


_SHARED_CACHE = {}


def _prep_shared(inp):
    key = id(inp.get("L0_attn_Wq"))
    if _SHARED_CACHE.get("key") == key:
        return _SHARED_CACHE["d"]
    f32 = np.float32
    import ml_dtypes
    d = {}
    d["wq"] = np.ascontiguousarray(inp["L0_attn_Wq"], f32)
    d["wk"] = np.ascontiguousarray(inp["L0_attn_Wk"], f32)
    d["wv"] = np.ascontiguousarray(inp["L0_attn_Wv"], f32)
    d["wo"] = np.ascontiguousarray(inp["L0_attn_Wo"], f32)
    d["lamv"] = np.concatenate([inp["L0_attn_lq1"], inp["L0_attn_lk1"],
                                inp["L0_attn_lq2"], inp["L0_attn_lk2"]]).reshape(1, 256).astype(f32)
    d["lng"] = np.ascontiguousarray(inp["L0_attn_ln_g"].reshape(128, 1), f32)
    d["lnb"] = np.ascontiguousarray(inp["L0_attn_ln_b"].reshape(128, 1), f32)
    d["gw1T"] = np.ascontiguousarray(np.asarray(inp["L0_ffn_gate_w1"], f32).T)
    d["gb1T"] = np.ascontiguousarray(np.asarray(inp["L0_ffn_gate_b1"], f32).reshape(8, 128).T)
    d["gw2T"] = np.ascontiguousarray(np.asarray(inp["L0_ffn_gate_w2"], f32).T)
    d["gb2c"] = np.ascontiguousarray(np.asarray(inp["L0_ffn_gate_b2"], f32).reshape(8, 1))
    d["w1T"] = np.ascontiguousarray(np.asarray(inp["L0_ffn_W1"], f32).T)
    d["tw1T"] = np.ascontiguousarray(np.asarray(inp["L0_ffn_TW1"], f32).transpose(0, 2, 1))
    d["b1r"] = np.ascontiguousarray(np.asarray(inp["L0_ffn_b1"], f32).reshape(1, 4096))
    d["b1T"] = np.ascontiguousarray(np.asarray(inp["L0_ffn_b1"], f32).reshape(32, 128).T)
    d["tb1"] = np.ascontiguousarray(inp["L0_ffn_Tb1"], f32)
    d["w2T"] = np.ascontiguousarray(np.asarray(inp["L0_ffn_W2"], f32).T)
    d["tw2T"] = np.ascontiguousarray(np.asarray(inp["L0_ffn_TW2"], f32).transpose(0, 2, 1))
    d["b2r"] = np.ascontiguousarray(np.asarray(inp["L0_ffn_b2"], f32).reshape(1, 1024))
    d["tb2"] = np.ascontiguousarray(inp["L0_ffn_Tb2"], f32)
    d["wqkv"] = np.ascontiguousarray(inp["L1_attn_Wqkv"], f32)
    d["wout"] = np.ascontiguousarray(inp["L1_attn_Wout"], f32)
    d["boutr"] = np.ascontiguousarray(np.asarray(inp["L1_attn_bout"], f32).reshape(1, 1024))
    d["l1w1T"] = np.ascontiguousarray(np.asarray(inp["L1_ffn_w1"], f32).T)
    d["l1w2T"] = np.ascontiguousarray(np.asarray(inp["L1_ffn_w2"], f32).T)
    cw = np.asarray(inp["L1_ffn_conv_w"], f32)[:, 0, :]
    for j in range(3):
        d[f"cw{j}"] = np.ascontiguousarray(cw[:, j].reshape(32, 128).T)
    d["cb"] = np.ascontiguousarray(np.asarray(inp["L1_ffn_conv_b"], f32).reshape(32, 128).T)
    kk = np.arange(128)[:, None]
    qq = np.arange(512)[None, :]
    dmv = np.stack([(kk + 128 * i <= qq) for i in range(4)], 0).astype(f32)
    d["dmaskT"] = np.ascontiguousarray(dmv.reshape(4 * 128, 512)).astype(ml_dtypes.bfloat16)
    d["identT"] = np.eye(128, dtype=f32).astype(ml_dtypes.bfloat16)
    d["identF"] = np.eye(128, dtype=f32)
    d["hs2T"] = np.ascontiguousarray(
        np.stack([np.arange(128) < 64, np.arange(128) >= 64], 1).astype(f32)).astype(ml_dtypes.bfloat16)
    _SHARED_CACHE["key"] = key
    _SHARED_CACHE["d"] = d
    return d


def _prep_core(c, inp):
    b, h = c >> 1, c & 1
    own = slice(2048 * h, 2048 * h + 2048)
    oth = slice(2048 * (1 - h), 2048 * (1 - h) + 2048)
    f32 = np.float32
    x = np.asarray(inp["L0_attn_x"][b], f32)
    d = dict(_prep_shared(inp))
    d["x0"] = np.ascontiguousarray(np.concatenate([x[own], x[oth]], 0))
    cos, sin = _rope_tables(h, f32)
    d["ropec"] = cos
    d["ropes"] = sin
    d["parity"] = np.array([[h]], np.int32)
    d["killpb"] = np.full((128, 1), KILL * (1 - h), f32)
    d["halom"] = np.tile(np.array([[float(h), float(1 - h)]], f32), (128, 1))
    return d


class Sem:
    def __init__(self, h):
        self.h = h
        self.n = 0


class Em:
    def __init__(self):
        self.TH = {k: [] for k in "tvsgy"}

    def add(self, e, fn):
        self.TH[e].append(fn)

    def wait(self, e, sem, val=None):
        v = sem.n if val is None else val
        if v > 0:
            self.add(e, lambda E, h=sem.h, v=v: E.wait_ge(h, v))

    def run(self, e, fn, sem=None, k=1, selfwait=False):
        if sem is not None:
            if selfwait and sem.n > 0:
                v0 = sem.n
                self.add(e, lambda E, h=sem.h, v0=v0: E.wait_ge(h, v0))
            sem.n += k
            self.add(e, lambda E, fn=fn, h=sem.h, k=k: fn(E).then_inc(h, k))
        else:
            self.add(e, fn)


def build():
    nc = bass.Bass("TRN2", target_bir_lowering=False)
    es = contextlib.ExitStack()

    ap = lambda t: t.ap()
    x0 = ap(nc.dram_tensor("x0", [NFULL, DM], F32, kind="ExternalInput"))
    wq = ap(nc.dram_tensor("wq", [DM, DM], F32, kind="ExternalInput"))
    wk = ap(nc.dram_tensor("wk", [DM, DM], F32, kind="ExternalInput"))
    wv = ap(nc.dram_tensor("wv", [DM, DM], F32, kind="ExternalInput"))
    wo = ap(nc.dram_tensor("wo", [DM, DM], F32, kind="ExternalInput"))
    lamv = ap(nc.dram_tensor("lamv", [1, 256], F32, kind="ExternalInput"))
    lng = ap(nc.dram_tensor("lng", [128, 1], F32, kind="ExternalInput"))
    lnb = ap(nc.dram_tensor("lnb", [128, 1], F32, kind="ExternalInput"))
    gw1T = ap(nc.dram_tensor("gw1T", [DM, DM], F32, kind="ExternalInput"))
    gb1T = ap(nc.dram_tensor("gb1T", [128, 8], F32, kind="ExternalInput"))
    gw2T = ap(nc.dram_tensor("gw2T", [DM, 8], F32, kind="ExternalInput"))
    gb2c = ap(nc.dram_tensor("gb2c", [8, 1], F32, kind="ExternalInput"))
    w1T = ap(nc.dram_tensor("w1T", [DM, DFF], F32, kind="ExternalInput"))
    tw1T = ap(nc.dram_tensor("tw1T", [8, DM, DFF], F32, kind="ExternalInput"))
    b1r = ap(nc.dram_tensor("b1r", [1, DFF], F32, kind="ExternalInput"))
    b1T = ap(nc.dram_tensor("b1T", [128, 32], F32, kind="ExternalInput"))
    tb1 = ap(nc.dram_tensor("tb1", [8, DFF], F32, kind="ExternalInput"))
    w2T = ap(nc.dram_tensor("w2T", [DFF, DM], F32, kind="ExternalInput"))
    tw2T = ap(nc.dram_tensor("tw2T", [8, DFF, DM], F32, kind="ExternalInput"))
    b2r = ap(nc.dram_tensor("b2r", [1, DM], F32, kind="ExternalInput"))
    tb2 = ap(nc.dram_tensor("tb2", [8, DM], F32, kind="ExternalInput"))
    wqkv = ap(nc.dram_tensor("wqkv", [DM, 3 * DM], F32, kind="ExternalInput"))
    wout = ap(nc.dram_tensor("wout", [DM, DM], F32, kind="ExternalInput"))
    boutr = ap(nc.dram_tensor("boutr", [1, DM], F32, kind="ExternalInput"))
    l1w1T = ap(nc.dram_tensor("l1w1T", [DM, DFF], F32, kind="ExternalInput"))
    l1w2T = ap(nc.dram_tensor("l1w2T", [DFF, DM], F32, kind="ExternalInput"))
    cw0 = ap(nc.dram_tensor("cw0", [128, 32], F32, kind="ExternalInput"))
    cw1 = ap(nc.dram_tensor("cw1", [128, 32], F32, kind="ExternalInput"))
    cw2 = ap(nc.dram_tensor("cw2", [128, 32], F32, kind="ExternalInput"))
    cb = ap(nc.dram_tensor("cb", [128, 32], F32, kind="ExternalInput"))
    ropec = ap(nc.dram_tensor("ropec", [128, NTOK], F32, kind="ExternalInput"))
    ropes = ap(nc.dram_tensor("ropes", [128, NTOK], F32, kind="ExternalInput"))
    parity = ap(nc.dram_tensor("parity", [1, 1], I32, kind="ExternalInput"))
    killpb = ap(nc.dram_tensor("killpb", [128, 1], F32, kind="ExternalInput"))
    halom = ap(nc.dram_tensor("halom", [128, 2], F32, kind="ExternalInput"))
    dmaskT = ap(nc.dram_tensor("dmaskT", [4 * 128, 512], BF16, kind="ExternalInput"))
    identT = ap(nc.dram_tensor("identT", [128, 128], BF16, kind="ExternalInput"))
    identFT = ap(nc.dram_tensor("identF", [128, 128], F32, kind="ExternalInput"))
    hs2T = ap(nc.dram_tensor("hs2T", [128, 2], BF16, kind="ExternalInput"))
    out = ap(nc.dram_tensor("out", [NTOK, DM], F32, kind="ExternalOutput"))

    k_scr = ap(nc.dram_tensor("k_scr", [DM, NFULL], BF16))
    v_scr = ap(nc.dram_tensor("v_scr", [NFULL, DM], BF16))
    s_scr = ap(nc.dram_tensor("s_scr", [NTOK, DM], F32))
    x1_scr = ap(nc.dram_tensor("x1_scr", [NTOK, DM], F32))
    mw1_scr = ap(nc.dram_tensor("mw1_scr", [DM, DFF], BF16))
    mw2_scr = ap(nc.dram_tensor("mw2_scr", [DFF, DM], BF16))
    w1q_scr = ap(nc.dram_tensor("w1q_scr", [DM, DFF], BF16))
    w2q_scr = ap(nc.dram_tensor("w2q_scr", [DFF, DM], BF16))
    hs_scr = ap(nc.dram_tensor("hs_scr", [DFF, NTOK], BF16))
    kv_in = ap(nc.dram_tensor("kv_in", [2050, NTOK], BF16))
    kv_out = ap(nc.dram_tensor("kv_out", [4100, NTOK], BF16))
    halo_in = ap(nc.dram_tensor("halo_in", [2, DM], F32))
    halo_out = ap(nc.dram_tensor("halo_out", [4, DM], F32))
    taps = {}
    for t_ in DEV_TAPS:
        shp = {"xn0": [128, 8 * NTOK], "g8": [8, 1], "s_l0": [NTOK, DM],
               "x1": [NTOK, DM], "mw1": [DM, DFF], "s_l1": [NTOK, DM],
               "hs": [DFF, NTOK], "kq": [128, 16], "w1q": [DM, DFF]}[t_]
        dt_ = BF16 if t_ in ("mw1", "hs", "w1q") else F32
        taps[t_] = ap(nc.dram_tensor("tap_" + t_, shp, dt_, kind="ExternalOutput"))

    sb = lambda n, shp, dt: es.enter_context(nc.sbuf_tensor(n, shp, dt))
    xnF = sb("xnF", [128, 8 * NTOK], BF16)
    QF = sb("QF", [128, 8 * NTOK], BF16)
    POOL1 = sb("POOL1", [128, 32 * 512], BF16)
    POOL2 = sb("POOL2", [128, 4096], F32)
    POOL3 = sb("POOL3", [128, 4100], F32)
    wblk = sb("wblk", [128, 10 * 512], BF16)
    kxb = sb("kxb", [128, 2 * NFULL], BF16)
    eT = sb("eT", [128, 2048], BF16)
    oF = sb("oF", [128, 8 * 512], BF16)
    bcB = sb("bcB", [128, 1024], F32)
    bcC = sb("bcC", [128, 1024], F32)
    xti = sb("xti", [128, 2048], I32)
    mxT = sb("mxT", [128, NTOK], BF16)
    dmask = sb("dmask", [128, 4 * 512], BF16)
    rows = sb("rows", [4, 3104], F32)
    stat = sb("stat", [128, 64], F32)
    gwS = sb("gwS", [128, 64], BF16)
    ctap = sb("ctap", [128, 128], F32)
    ident = sb("ident", [128, 128], BF16)
    identF = sb("identFs", [128, 128], F32)
    hs2 = sb("hs2", [128, 2], BF16)
    onesb = sb("onesb", [128, 2], BF16)
    onesw = sb("onesw", [128, 128], BF16)
    biasP = sb("biasP", [128, 8], F32)
    biasPK = sb("biasPK", [128, 8], F32)
    onesf = sb("onesf", [128, 1], F32)
    onesr = sb("onesr", [1, 128], F32)
    mb1T = sb("mb1T", [128, 32], F32)
    biasH = sb("biasH", [128, 16], F32)
    biasHK = sb("biasHK", [128, 16], F32)
    q2m = sb("q2m", [2, 16], F32)
    k2m = sb("k2m", [2, 16], F32)
    lamS = sb("lamS", [128, 1], F32)
    klS = sb("klS", [128, 1], F32)
    hmS = sb("hmS", [128, 2], F32)
    sc1T = sb("sc1T", [128, 16], F32)
    c2T = sb("c2T", [128, 16], F32)
    xqh = sb("xqh", [128, 8], BF16)
    g8c = sb("g8c", [8, 1], F32)
    ps = es.enter_context(nc.psum_tensor("ps", [128, 4096], F32))

    kgrp = kxb
    vgrp = wblk
    uBlk = POOL1
    xqF = POOL1
    R_SUM = rows[0:1, 0:1024]
    R_RCP = rows[0:1, 1024:2048]
    R_MU = rows[0:1, 2048:2560]
    R_VAR = rows[0:1, 2560:3072]
    R_LAM = rows[0:1, 0:544]
    R_NEGB = rows[0:1, 0:16]
    R_G8R = rows[0:1, 16:24]
    R_B1 = rows[0:1, 24:152]
    R_MB2 = rows[0:1, 1024:2048]
    R_KN = rows[0:1, 0:80]
    R_KNOWN = rows[2:3, 1024:1040]
    R_HFIRST = rows[2:3, 0:1024]
    R_HLAST = rows[3:4, 0:1024]
    R_WM = rows[0:1, 3072:3080]
    R_HS = rows[0:1, 3080:3088]

    BK = lambda k: slice(512 * k, 512 * k + 512)

    MM = Sem(es.enter_context(nc.semaphore("MM")))
    SS = Sem(es.enter_context(nc.semaphore("SS")))
    VV = Sem(es.enter_context(nc.semaphore("VV")))
    CC = Sem(es.enter_context(nc.semaphore("CC")))
    STO = Sem(es.enter_context(nc.semaphore("STO")))
    WQA = Sem(es.enter_context(nc.semaphore("WQA")))
    WQB = Sem(es.enter_context(nc.semaphore("WQB")))
    LDW = Sem(es.enter_context(nc.semaphore("LDW")))
    LDO = Sem(es.enter_context(nc.semaphore("LDO")))
    MWA = Sem(es.enter_context(nc.semaphore("MWA")))
    MWB = Sem(es.enter_context(nc.semaphore("MWB")))
    LYC = Sem(es.enter_context(nc.semaphore("LYC")))
    LYX = Sem(es.enter_context(nc.semaphore("LYX")))
    LYW = Sem(es.enter_context(nc.semaphore("LYW")))
    LYO = Sem(es.enter_context(nc.semaphore("LYO")))
    LSWA = Sem(es.enter_context(nc.semaphore("LSWA")))
    LSWB = Sem(es.enter_context(nc.semaphore("LSWB")))
    LSKA = Sem(es.enter_context(nc.semaphore("LSKA")))
    LSKB = Sem(es.enter_context(nc.semaphore("LSKB")))
    LSS = Sem(es.enter_context(nc.semaphore("LSS")))
    LSX1 = Sem(es.enter_context(nc.semaphore("LSX1")))
    LSH = Sem(es.enter_context(nc.semaphore("LSH")))
    MSA = Sem(es.enter_context(nc.semaphore("MSA")))
    MSB = Sem(es.enter_context(nc.semaphore("MSB")))
    WVA = Sem(es.enter_context(nc.semaphore("WVA")))
    WVB = Sem(es.enter_context(nc.semaphore("WVB")))
    LWA = Sem(es.enter_context(nc.semaphore("LWA")))
    LWB = Sem(es.enter_context(nc.semaphore("LWB")))
    LW2A = Sem(es.enter_context(nc.semaphore("LW2A")))
    LW2B = Sem(es.enter_context(nc.semaphore("LW2B")))
    LOA = Sem(es.enter_context(nc.semaphore("LOA")))
    LOB = Sem(es.enter_context(nc.semaphore("LOB")))

    em = Em()
    ring_cons = {}

    def gdma(o, i, sem):
        em.run('g', lambda g, o=o, i=i: g.dma_start(out=o, in_=i), sem, 16)

    def godma(o, i, ns=False):
        em.wait('g', LDO)
        em.run('g', lambda g, o=o, i=i, ns=ns: g.dma_start(
            out=o, in_=i, allow_slow_non_contiguous=ns), LDO, 16)

    def ydma(o, i, sem):
        em.run('y', lambda y, o=o, i=i: y.dma_start(out=o, in_=i), sem, 16)

    def yodma(o, i):
        em.wait('y', LYO)
        em.run('y', lambda y, o=o, i=i: y.dma_start(out=o, in_=i), LYO, 16)

    def mm(o, l, r, st, sp, tp=None, sgc=False):
        em.run('t', lambda t, o=o, l=l, r=r, st=st, sp=sp, tp=tp, sgc=sgc:
               t.matmul(o, l, r, start=st, stop=sp, tile_position=tp,
                        skip_group_check=sgc), MM)

    def tr(o, i):
        mm(o, i, ident[:], True, True)

    def act(o, i, f, bias=0.0, scale=1.0, accum=None):
        em.run('s', lambda s, o=o, i=i, f=f, bias=bias, scale=scale, accum=accum:
               s.activation(out=o, in_=i, func=f, bias=bias, scale=scale,
                            accum_out=accum), SS, selfwait=True)

    def vop(name, *a, **kw):
        em.run('v', lambda v, name=name, a=a, kw=kw: getattr(v, name)(*a, **kw), VV,
               selfwait=True)

    def wt(e, sem, val=None):
        em.wait(e, sem, val)

    vop('memset', onesb[:], 1.0)
    vop('memset', onesf[:], 1.0)
    vop('memset', onesr[:], 1.0)
    vop('memset', q2m[:], 0.0)
    vop('memset', k2m[:], 0.0)
    vop('memset', stat[0:8, 20:22], 0.0)
    vop('memset', mxT[:], 0.0)
    vop('memset', onesw[:], 1.0)
    vop('memset', stat[:, 19:20], 1e-5)
    ydma(ident[:], identT, LYC)
    ydma(identF[:], identFT, LYC)
    ydma(hs2[:], hs2T, LYC)
    ydma(stat[:, 40:41], lng, LYC)
    ydma(stat[:, 41:42], lnb, LYC)
    ydma(klS[:], killpb, LYC)
    ydma(hmS[:], halom, LYC)
    ydma(ctap[:, 0:32], cw0, LYC)
    ydma(ctap[:, 32:64], cw1, LYC)
    ydma(ctap[:, 64:96], cw2, LYC)
    ydma(ctap[:, 96:128], cb, LYC)
    for j in range(4):
        ydma(dmask[:, 512 * j:512 * j + 512], dmaskT[128 * j:128 * j + 128, :], LYC)
    ydma(R_LAM[0:1, 0:256], lamv, LYC)
    lngS, lnbS = stat[:, 40:41], stat[:, 41:42]
    g8aP = stat[0:8, 21:22]

    lam_init = float(0.8 - 0.6 * np.exp(-0.3 * 12))
    wt('v', LYC)
    vop('tensor_mul', R_LAM[0:1, 256:320], R_LAM[0:1, 0:64], R_LAM[0:1, 64:128])
    vop('tensor_mul', R_LAM[0:1, 320:384], R_LAM[0:1, 128:192], R_LAM[0:1, 192:256])
    vop('reduce_sum', R_LAM[0:1, 384:385], R_LAM[0:1, 256:320], AX.X)
    vop('reduce_sum', R_LAM[0:1, 385:386], R_LAM[0:1, 320:384], AX.X)
    wt('s', VV)
    act(R_LAM[0:1, 386:388], R_LAM[0:1, 384:386], AF.Exp)
    wt('v', SS)
    vop('scalar_tensor_tensor', out=R_LAM[0:1, 388:389], in0=R_LAM[0:1, 387:388],
        scalar=-lam_init, in1=R_LAM[0:1, 386:387], op0=ALU.add, op1=ALU.subtract)
    wt('t', VV)
    mm(ps[:, BK(7)][:, 0:1], onesr[0:1, :], R_LAM[0:1, 388:389], True, True)
    wt('v', MM)
    vop('tensor_copy', lamS[:, 0:1], ps[:, BK(7)][:, 0:1])

    def wq_tiles(w):
        r, cdim = w.shape
        cs = min(cdim, 2048)
        return [(w[128 * i:128 * i + 128, cs * j:cs * j + cs], cs)
                for i in range(r // 128) for j in range(cdim // cs)]

    for wi, (wsrc, wdst) in enumerate([(l1w1T, w1q_scr), (l1w2T, w2q_scr)]):
        tiles = wq_tiles(wsrc)
        dtiles = wq_tiles(wdst)
        sumc = 43 + wi
        vop('memset', stat[:, sumc:sumc + 1], 0.0)
        for idx, (tin, cs) in enumerate(tiles):
            sl = idx % 2
            wqs = WQA if sl == 0 else WQB
            em.wait('g', VV, ring_cons.get(("wq", sl), 0))
            em.wait('g', (LSWA if sl == 0 else LSWB), ring_cons.get(("wqs", sl), 0))
            gdma(POOL1[:, 2048 * sl:2048 * sl + cs], tin, wqs)
            wt('v', wqs)
            vop('reduce_sum', stat[:, 47:48], POOL1[:, 2048 * sl:2048 * sl + cs], AX.X,
                apply_absolute_value=True)
            vop('tensor_add', stat[:, sumc:sumc + 1], stat[:, sumc:sumc + 1], stat[:, 47:48])
            ring_cons[("wq", sl)] = VV.n
        wt('t', VV)
        mm(ps[0:1, BK(7)][:, 0:1], onesf[:, 0:1], stat[:, sumc:sumc + 1], True, True)
        wt('v', MM)
        nel = float(wsrc.shape[0] * wsrc.shape[1])
        vop('tensor_scalar', out=R_WM[0:1, wi:wi + 1], in0=ps[0:1, BK(7)][:, 0:1],
            scalar1=1.0 / nel, scalar2=1e-5, op0=ALU.mult, op1=ALU.max)
        vop('reciprocal', R_WM[0:1, 2 + wi:3 + wi], R_WM[0:1, wi:wi + 1])
        wt('t', VV)
        mm(ps[:, BK(7)][:, 0:1], onesr[0:1, :], R_WM[0:1, 2 + wi:3 + wi], True, True)
        wt('v', MM)
        vop('tensor_copy', stat[:, 45 + wi * 10:46 + wi * 10], ps[:, BK(7)][:, 0:1])
        scol = stat[:, 45 + wi * 10:46 + wi * 10]
        for idx, (tin, cs) in enumerate(tiles):
            sl = idx % 2
            wqs = WQA if sl == 0 else WQB
            em.wait('g', VV, ring_cons.get(("wq", sl), 0))
            em.wait('g', (LSWA if sl == 0 else LSWB), ring_cons.get(("wqs", sl), 0))
            gdma(POOL1[:, 2048 * sl:2048 * sl + cs], tin, wqs)
            wt('v', wqs)
            src_ = POOL1[:, 2048 * sl:2048 * sl + cs]
            tpos = POOL1[:, 4096:4096 + cs]
            tneg = POOL1[:, 6144:6144 + cs]
            vop('tensor_scalar', out=tpos, in0=src_, scalar1=scol, scalar2=0.5,
                op0=ALU.mult, op1=ALU.is_ge)
            vop('tensor_scalar', out=tneg, in0=src_, scalar1=scol, scalar2=-0.5,
                op0=ALU.mult, op1=ALU.is_le)
            vop('tensor_sub', src_, tpos, tneg)
            ring_cons[("wq", sl)] = VV.n
            wt('y', VV)
            lsw = LSWA if sl == 0 else LSWB
            ydma(dtiles[idx][0], src_, lsw)
            ring_cons[("wqs", sl)] = lsw.n
    if "w1q" in taps:
        wt('g', LSWA)
        wt('g', LSWB)
        godma(taps["w1q"], w1q_scr)

    xt0 = POOL3[:, 0:1024]
    xt1 = POOL3[:, 1024:2048]
    xt2 = POOL3[:, 2048:3072]

    def ln_tile(src_rows, want_f32=False):
        wt('y', VV)
        ydma(xt0, src_rows, LYX)
        wt('v', LYX)
        vop('bn_stats', stat[:, 0:6], xt0[:, 0:512])
        vop('bn_stats', stat[:, 6:12], xt0[:, 512:1024])
        vop('bn_aggr', stat[:, 12:14], stat[:, 0:12])
        wt('s', VV)
        act(stat[:, 14:15], stat[:, 13:14], AF.Sqrt, bias=stat[:, 19:20])
        wt('v', SS)
        vop('reciprocal', stat[:, 15:16], stat[:, 14:15])
        vop('tensor_scalar', out=stat[:, 16:17], in0=stat[:, 12:13],
            scalar1=stat[:, 15:16], scalar2=-1.0, op0=ALU.mult, op1=ALU.mult)
        wt('v', MM)
        vop('tensor_scalar', out=eT[:, 0:1024], in0=xt0,
            scalar1=stat[:, 15:16], scalar2=stat[:, 16:17], op0=ALU.mult, op1=ALU.add)
        if want_f32:
            vop('tensor_scalar', out=xt2, in0=xt0,
                scalar1=stat[:, 15:16], scalar2=stat[:, 16:17], op0=ALU.mult, op1=ALU.add)

    def ln_transpose(dst_col_of_chunk, src=None):
        wt('t', VV)
        wt('t', SS)
        for j in range(8):
            tr(ps[:, j * 128:j * 128 + 128], (src or eT)[:, j * 128:j * 128 + 128])
        wt('s', MM)
        for j in range(8):
            act(dst_col_of_chunk(j), ps[:, j * 128:j * 128 + 128], AF.Copy)

    for tc in range(NTC):
        ln_tile(x0[128 * tc:128 * tc + 128, :])
        ln_transpose(lambda j, tc=tc: xnF[:, j * NTOK + tc * 128: j * NTOK + tc * 128 + 128])
    XNF_OWN_SS = SS.n

    def load_colblock(seg, w, c0, cw):
        em.wait('g', MM)
        for dinc in range(8):
            gdma(wblk[:, seg + dinc * cw: seg + dinc * cw + cw],
                 w[128 * dinc:128 * dinc + 128, c0:c0 + cw], LDW)

    def xsrc(tb, dinc, j0, jn):
        if tb < 4:
            return xnF[:, dinc * NTOK + tb * 512 + j0: dinc * NTOK + tb * 512 + j0 + jn]
        return kxb[:, dinc * 512 + j0: dinc * 512 + j0 + jn]

    kv_stage = [eT[:, 1024:1536], eT[:, 1536:2048]]
    sq_tile = oF[:, 512:1024]
    gb1S = stat[:, 48:56]
    yodma(gb1S, gb1T)
    for hc in range(8):
        godma(gwS[:, hc * 8:hc * 8 + 8], gw2T[128 * hc:128 * hc + 128, :])
    wt('s', LYO)
    wt('t', LDO)
    pwc = [0]
    WQ = [WQA, WQB]
    WV = [WVA, WVB]
    em.wait('g', VV)

    def pw_load(w, c0):
        sl = pwc[0] % 2
        pwc[0] += 1
        em.wait('g', MM, ring_cons.get(("pw", sl), 0))
        em.run('g', lambda g, w=w, c0=c0, sl=sl: g.dma_start(
            out=wblk[:, 1024 * sl:1024 * sl + 1024].rearrange("p (d c) -> p d c", c=128),
            in_=w[:, c0:c0 + 128].rearrange("(d p) c -> p d c", p=128)), WQ[sl], 16)
        return sl

    for tb in range(8):
        if tb >= 4:
            for j in range(4):
                ln_tile(x0[2048 + 512 * (tb - 4) + 128 * j:
                           2048 + 512 * (tb - 4) + 128 * j + 128, :])
                ln_transpose(lambda dc, j=j: kxb[:, dc * 512 + j * 128: dc * 512 + j * 128 + 128])
        for which, wmat in (("k", wk), ("q", wq)):
            if which == "q" and tb >= 4:
                continue
            for dc in range(8):
                sl = pw_load(wmat, dc * 128)
                bank = 2 + (dc % 2)
                wt('t', WQ[sl])
                wt('t', SS)
                wt('t', VV)
                for dinc in range(8):
                    mm(ps[:, BK(bank)], wblk[:, 1024 * sl + dinc * 128: 1024 * sl + dinc * 128 + 128],
                       xsrc(tb, dinc, 0, 512), dinc == 0, dinc == 7)
                ring_cons[("pw", sl)] = MM.n
                wt('s', MM)
                wt('s', LSKA if dc % 2 == 0 else LSKB)
                stg = kv_stage[dc % 2]
                if which == "q":
                    act(QF[:, dc * NTOK + tb * 512: dc * NTOK + tb * 512 + 512],
                        ps[:, BK(bank)], AF.Copy, scale=0.125)
                    act(sq_tile, ps[:, BK(bank)], AF.Square, scale=0.125)
                else:
                    act(stg, ps[:, BK(bank)], AF.Copy)
                    act(sq_tile, ps[:, BK(bank)], AF.Square)
                    wt('y', SS)
                    ydma(k_scr[128 * dc:128 * dc + 128, 512 * tb:512 * tb + 512], stg,
                         LSKA if dc % 2 == 0 else LSKB)
                wt('t', SS)
                wt('t', VV)
                mm(ps[0:2, BK(5)], hs2[:], sq_tile, True, True)
                wt('v', MM)
                dstm = q2m if which == "q" else k2m
                vop('reduce_max', stat[0:2, 18:19], ps[0:2, BK(5)], AX.X)
                vop('tensor_max', dstm[0:2, dc:dc + 1], dstm[0:2, dc:dc + 1], stat[0:2, 18:19])
        vbanks = [0, 1, 6, 7]
        for col in range(2):
            for hc in range(8):
                wsl = hc % 2
                em.wait('g', MM, ring_cons.get(("wv", wsl), 0))
                gdma(wblk[:, 3072 + 512 * wsl: 3072 + 512 * wsl + 512],
                     wv[128 * hc:128 * hc + 128, col * 512: col * 512 + 512], WV[wsl])
                wt('t', WV[wsl])
                if hc == 0:
                    wt('t', SS)
                    wt('t', VV)
                for tc in range(4):
                    mm(ps[:, BK(vbanks[tc])], xsrc(tb, hc, tc * 128, 128),
                       wblk[:, 3072 + 512 * wsl: 3072 + 512 * wsl + 512],
                       hc == 0, hc == 7)
                ring_cons[("wv", wsl)] = MM.n
            wt('s', MM)
            for tc in range(4):
                stg = kv_stage[tc % 2]
                wt('s', LSKA if tc % 2 == 0 else LSKB)
                act(stg, ps[:, BK(vbanks[tc])], AF.Copy)
                wt('y', SS)
                ydma(v_scr[512 * tb + 128 * tc: 512 * tb + 128 * tc + 128,
                           col * 512: col * 512 + 512], stg,
                     LSKA if tc % 2 == 0 else LSKB)
        for hc in range(8):
            sl = pw_load(gw1T, hc * 128)
            bank = 2 + (hc % 2)
            wt('t', WQ[sl])
            wt('t', SS)
            for dinc in range(8):
                mm(ps[:, BK(bank)], wblk[:, 1024 * sl + dinc * 128: 1024 * sl + dinc * 128 + 128],
                   xsrc(tb, dinc, 0, 512), dinc == 0, dinc == 7)
            ring_cons[("pw", sl)] = MM.n
            wt('s', MM)
            act(oF[:, 0:512], ps[:, BK(bank)], AF.Relu, bias=gb1S[:, hc:hc + 1])
            wt('t', SS)
            mm(ps[0:8, BK(4)], gwS[:, hc * 8:hc * 8 + 8], oF[:, 0:512], hc == 0, hc == 7)
        wt('s', MM)
        act(bcC[0:8, 0:512], ps[0:8, BK(4)], AF.Copy, accum=stat[0:8, 20:21])
        wt('v', SS)
        vop('tensor_add', g8aP, g8aP, stat[0:8, 20:21])
    gb2S = stat[0:8, 57:58]
    yodma(gb2S, gb2c)
    wt('v', LYO)
    vop('tensor_scalar', out=g8c[0:8, 0:1], in0=g8aP, scalar1=1.0 / NFULL,
        scalar2=gb2S, op0=ALU.mult, op1=ALU.add)
    gBc = stat[:, 32:40]
    wt('t', VV)
    mm(ps[0:1, BK(5)][:, 0:8], g8c[0:8, 0:1], identF[0:8, 0:8], True, True)
    wt('v', MM)
    vop('tensor_copy', R_G8R, ps[0:1, BK(5)][:, 0:8])
    wt('t', VV)
    mm(ps[:, BK(6)][:, 0:8], onesr[0:1, :], R_G8R, True, True)
    wt('v', MM)
    vop('tensor_copy', gBc, ps[:, BK(6)][:, 0:8])
    vop('tensor_mul', q2m[0:2, 8:16], q2m[0:2, 0:8], k2m[0:2, 0:8])
    wt('s', VV)
    act(k2m[0:2, 8:16], q2m[0:2, 8:16], AF.Sqrt)
    wt('v', SS)
    vop('tensor_scalar', out=q2m[0:2, 8:16], in0=k2m[0:2, 8:16], scalar1=-0.5, op0=ALU.mult, scalar2=None)
    wt('g', VV)
    nb_v = R_NEGB.rearrange("o (a s) -> o a s", s=2)
    godma(nb_v[:, :, 0:1], q2m[0:1, 8:16], ns=True)
    godma(nb_v[:, :, 1:2], q2m[1:2, 8:16], ns=True)
    wt('t', LDO)
    mm(ps[:, BK(6)][:, 0:16], onesr[0:1, :], R_NEGB, True, True)
    wt('v', MM)
    vop('tensor_copy', biasH[:, 0:16], ps[:, BK(6)][:, 0:16])
    bp_v = biasH[:, 0:16].rearrange("p (a s) -> p a s", s=2)
    vop('tensor_tensor', biasP[:, 0:8], bp_v[:, :, 0:1], bp_v[:, :, 1:2], ALU.min)
    wt('t', VV)
    for c in range(32):
        em.wait('y', MM)
        ydma(R_B1, b1r[0:1, 128 * c:128 * c + 128], LYW)
        ydma(bcC[0:8, 0:128], tb1[:, 128 * c:128 * c + 128], LYW)
        wt('t', LYW)
        mm(ps[:, BK(5)][:, c:c + 1], bcC[0:8, 0:128], g8c[0:8, 0:1], True, False)
        mm(ps[:, BK(5)][:, c:c + 1], R_B1, onesf[0:1, 0:1], False, True)
    wt('v', MM)
    vop('tensor_copy', mb1T[:], ps[:, BK(5)][:, 0:32])
    yodma(R_MB2, b2r)
    for j in range(2):
        em.wait('y', MM)
        ydma(bcC[0:8, 128:640], tb2[:, 512 * j:512 * j + 512], LYW)
        wt('t', LYW)
        wt('t', VV)
        mm(ps[0:1, BK(6)], g8c[0:8, 0:1], bcC[0:8, 128:640], True, True)
        wt('v', MM)
        wt('v', LYO)
        vop('tensor_add', R_MB2[0:1, 512 * j:512 * j + 512],
            ps[0:1, BK(6)], R_MB2[0:1, 512 * j:512 * j + 512])
    for j in range(2):
        wt('t', VV)
        mm(ps[:, BK(6)], onesr[0:1, :], R_MB2[0:1, 512 * j:512 * j + 512], True, True)
        wt('v', MM)
        vop('tensor_copy', bcB[:, 512 * j:512 * j + 512], ps[:, BK(6)])
    if "g8" in taps:
        wt('g', VV)
        godma(taps["g8"], g8c[0:8, 0:1])
    if "kq" in taps:
        wt('g', VV)
        godma(taps["kq"], biasH[:, 0:16])
    if "xn0" in taps:
        wt('g', SS, XNF_OWN_SS)
        godma(taps["xn0"], xnF[:])

    mw_chunks = []
    for hc in range(8):
        for fcol in range(8):
            mw_chunks.append((w1T[128 * hc:128 * hc + 128, 512 * fcol:512 * fcol + 512],
                              tw1T[0:8, 128 * hc:128 * hc + 128, 512 * fcol:512 * fcol + 512],
                              mw1_scr[128 * hc:128 * hc + 128, 512 * fcol:512 * fcol + 512]))
    for fr in range(32):
        for col in range(2):
            mw_chunks.append((w2T[128 * fr:128 * fr + 128, 512 * col:512 * col + 512],
                              tw2T[0:8, 128 * fr:128 * fr + 128, 512 * col:512 * col + 512],
                              mw2_scr[128 * fr:128 * fr + 128, 512 * col:512 * col + 512]))
    mw_state = {"i": 0, "vfree": [0, 0], "yfree": [0, 0]}

    def emit_mw_chunk():
        i = mw_state["i"]
        if i >= len(mw_chunks):
            return
        mw_state["i"] = i + 1
        wsl, twl, dst = mw_chunks[i]
        sl = i % 2
        seg = 9 * 512 * sl
        mws = MWA if sl == 0 else MWB
        mss = MSA if sl == 0 else MSB
        em.wait('g', VV, mw_state["vfree"][sl])
        gdma(POOL1[:, seg:seg + 512], wsl, mws)
        em.run('g', lambda g, twl=twl, seg=seg: g.dma_start(
            out=POOL1[:, seg + 512:seg + 4608].rearrange("p (e c) -> p e c", c=512),
            in_=twl.rearrange("e p c -> p e c")), mws, 16)
        accb = mxT[:, 512 * sl:512 * sl + 512]
        wt('v', mws)
        em.wait('v', mss, mw_state["yfree"][sl])
        vop('scalar_tensor_tensor', out=accb, in0=POOL1[:, seg + 512:seg + 1024],
            scalar=gBc[:, 0:1], in1=POOL1[:, seg:seg + 512], op0=ALU.mult, op1=ALU.add)
        for e in range(1, 8):
            vop('scalar_tensor_tensor', out=accb,
                in0=POOL1[:, seg + 512 * (e + 1):seg + 512 * (e + 2)],
                scalar=gBc[:, e:e + 1], in1=accb, op0=ALU.mult, op1=ALU.add)
        mw_state["vfree"][sl] = VV.n
        wt('y', VV)
        ydma(dst, accb, mss)
        mw_state["yfree"][sl] = mss.n

    eseg = lambda st, i: eT[:, st * 2048 + i * 1024: st * 2048 + i * 1024 + 1024]
    of32 = POOL2[:, 0:512]
    otmp = POOL2[:, 512:1024]
    sevac = POOL2[:, 1024:1536]
    kbuf = [kxb[:, 0:4096], kxb[:, 4096:8192]]
    vbuf = [wblk[:, 0:4096], POOL1[:, 9216:13312]]
    WQ = [WQA, WQB]
    wt('s', LSKA)
    wt('s', LSKB)
    em.wait('g', VV)
    em.wait('g', LSWA)
    em.wait('g', LSWB)

    def l0_group_loads(g):
        if g >= 32:
            return
        vh = g % 8
        sl = g % 2
        em.wait('g', MM, ring_cons.get(("kv", sl), 0))
        em.wait('g', LSKA)
        em.wait('g', LSKB)
        gdma(kbuf[sl][:, 0:NFULL], k_scr[128 * vh:128 * vh + 128, :], WQ[sl])
        em.run('g', lambda g_, vh=vh, sl=sl: g_.dma_start(
            out=vbuf[sl][:, 0:NFULL].rearrange("p (c d) -> p c d", d=128),
            in_=v_scr[:, 128 * vh:128 * vh + 128].rearrange("(c p) d -> p c d", p=128)),
            WQ[sl], 16)

    l0_group_loads(0)
    for g in range(32):
        qb, vh = g // 8, g % 8
        sl = g % 2
        l0_group_loads(g + 1)
        kg, vg = kbuf[sl], vbuf[sl]

        def qrow(r0, r1, vh=vh, qb=qb):
            return QF[r0:r1, vh * NTOK + qb * 512: vh * NTOK + qb * 512 + 512]
        wt('t', WQ[sl])
        wt('t', VV)
        ss_after = {}
        pairbuf = lambda sp: eT[:, 0:2048] if sp == 0 else POOL3[:, 0:1024].bitcast(BF16)

        def scores2(p):
            if p >= 1:
                wt('t', SS, ss_after[p - 1])
            for j in range(2):
                c = 2 * p + j
                mm(ps[:, BK(2 * j)], kg[0:64, 128 * c:128 * c + 128],
                   qrow(0, 64), True, True, tp=(0, 0))
                mm(ps[:, BK(2 * j + 1)], kg[64:128, 128 * c:128 * c + 128],
                   qrow(64, 128), True, True, tp=(64, 0))
            wt('s', MM)
            act(pairbuf(p % 2), ps[:, 0:2048], AF.Exp, bias=biasP[:, vh:vh + 1])
            ss_after[p] = SS.n

        def av2(p, first, last):
            buf = pairbuf(p % 2)
            wt('t', SS, ss_after[p])
            for j in range(2):
                c = 2 * p + j
                fst = first and j == 0
                lst = last and j == 1
                e0 = buf[:, 1024 * j:1024 * j + 512]
                e1 = buf[:, 1024 * j + 512:1024 * j + 1024]
                mm(ps[0:128, BK(4)], vg[:, 128 * c:128 * c + 128], e0, fst, lst)
                mm(ps[0:128, BK(5)], vg[:, 128 * c:128 * c + 128], e1, fst, lst)
                mm(ps[:, BK(6)], onesw[:, 0:128], e0, fst, lst)
                mm(ps[:, BK(7)], onesw[:, 0:128], e1, fst, lst)

        scores2(0)
        for p in range(1, 16):
            scores2(p)
            av2(p - 1, p - 1 == 0, False)
        av2(15, False, True)
        ring_cons[("kv", sl)] = MM.n
        wt('v', MM)
        vop('reciprocal', bcC[:, 0:512], ps[:, BK(6)])
        vop('reciprocal', bcC[:, 512:1024], ps[:, BK(7)])
        vop('tensor_mul', of32, ps[:, BK(4)], bcC[:, 0:512])
        vop('scalar_tensor_tensor', out=otmp, in0=ps[:, BK(5)], scalar=lamS[:, 0:1],
            in1=bcC[:, 512:1024], op0=ALU.mult, op1=ALU.mult)
        vop('tensor_add', of32, of32, otmp)
        wt('s', VV)
        act(otmp, of32, AF.Square)
        wt('t', SS)
        mm(ps[64:65, BK(6)], onesf[:, 0:1], of32, True, True, tp=(0, 64))
        mm(ps[0:1, BK(1)], onesf[:, 0:1], otmp, True, True)
        wt('v', MM)
        vop('tensor_scalar', out=R_MU, in0=ps[64:65, BK(6)], scalar1=1.0 / 128,
            scalar2=None, op0=ALU.mult)
        vop('tensor_scalar', out=R_VAR, in0=ps[0:1, BK(1)], scalar1=1.0 / 128,
            scalar2=None, op0=ALU.mult)
        vop('tensor_mul', R_RCP[0:1, 0:512], R_MU, R_MU)
        vop('tensor_sub', R_VAR, R_VAR, R_RCP[0:1, 0:512])
        wt('s', VV)
        act(R_RCP[0:1, 512:1024], R_VAR, AF.Ln, bias=stat[0:1, 19:20])
        act(R_VAR, R_RCP[0:1, 512:1024], AF.Exp, scale=-0.5)
        wt('t', SS)
        mm(ps[:, BK(2)], onesr[0:1, :], R_MU, True, True)
        mm(ps[:, BK(3)], onesr[0:1, :], R_VAR, True, True)
        wt('v', MM)
        vop('tensor_copy', bcC[:, 0:512], ps[:, BK(2)])
        vop('tensor_copy', bcC[:, 512:1024], ps[:, BK(3)])
        vop('tensor_sub', of32, of32, bcC[:, 0:512])
        vop('tensor_mul', of32, of32, bcC[:, 512:1024])
        vop('tensor_scalar', out=oF[:, 512 * vh:512 * vh + 512], in0=of32,
            scalar1=lngS, scalar2=lnbS, op0=ALU.mult, op1=ALU.add)
        for _ in range(4):
            emit_mw_chunk()
        if vh == 7:
            for col in range(2):
                for vhh in range(8):
                    wsl = 4096 + 512 * (vhh % 2)
                    LO = LOA if vhh % 2 == 0 else LOB
                    em.wait('g', MM, ring_cons.get(("wo", vhh % 2), 0))
                    gdma(wblk[:, wsl:wsl + 512],
                         wo[128 * vhh:128 * vhh + 128, 512 * col:512 * col + 512], LO)
                    wt('t', LO)
                    if vhh == 0:
                        wt('t', VV)
                    for tc in range(4):
                        mm(ps[:, BK(tc)], oF[:, 512 * vhh + 128 * tc:512 * vhh + 128 * tc + 128],
                           wblk[:, wsl:wsl + 512], vhh == 0, vhh == 7)
                    ring_cons[("wo", vhh % 2)] = MM.n
                wt('v', MM)
                for tc in range(4):
                    wt('v', LSS)
                    vop('tensor_copy', sevac, ps[:, BK(tc)])
                    wt('y', VV)
                    ydma(s_scr[512 * qb + 128 * tc:512 * qb + 128 * tc + 128,
                               512 * col:512 * col + 512], sevac, LSS)
    if "s_l0" in taps:
        wt('g', LSS)
        godma(taps["s_l0"], s_scr)
    while mw_state["i"] < len(mw_chunks):
        emit_mw_chunk()
    if "mw1" in taps:
        wt('g', MSA)
        wt('g', MSB)
        godma(taps["mw1"], mw1_scr)

    for tb in range(4):
        for fc in range(32):
            sl = fc % 2
            LW = LWA if sl == 0 else LWB
            em.wait('y', MM, ring_cons.get(("mw1", sl), 0))
            if tb == 0 and fc == 0:
                em.wait('y', MSA)
                em.wait('y', MSB)
            em.run('y', lambda y, fc=fc, sl=sl: y.dma_start(
                out=wblk[:, 1024 * sl:1024 * sl + 1024].rearrange("p (d c) -> p d c", c=128),
                in_=mw1_scr[:, fc * 128:fc * 128 + 128].rearrange("(d p) c -> p d c", p=128)),
                LW, 16)
            wt('t', LW)
            if fc < 2:
                wt('t', SS)
                wt('t', VV)
            else:
                wt('t', SS, ring_cons.get(("ge", sl), 0))
            for dinc in range(8):
                mm(ps[:, BK(sl)], wblk[:, 1024 * sl + 128 * dinc:1024 * sl + 128 * dinc + 128],
                   xnF[:, dinc * NTOK + tb * 512:dinc * NTOK + tb * 512 + 512],
                   dinc == 0, dinc == 7)
            ring_cons[("mw1", sl)] = MM.n
            wt('s', MM)
            act(uBlk[:, fc * 512:fc * 512 + 512], ps[:, BK(sl)], AF.Gelu_apprx_tanh,
                bias=mb1T[:, fc:fc + 1])
            ring_cons[("ge", sl)] = SS.n
        for fc in range(32):
            sl = fc % 2
            LW2 = LW2A if sl == 0 else LW2B
            em.wait('y', MM, ring_cons.get(("mw2", sl), 0))
            ydma(wblk[:, 2048 + 1024 * sl:2048 + 1024 * sl + 512],
                 mw2_scr[fc * 128:fc * 128 + 128, 0:512], LW2)
            ydma(wblk[:, 2048 + 1024 * sl + 512:2048 + 1024 * sl + 1024],
                 mw2_scr[fc * 128:fc * 128 + 128, 512:1024], LW2)
            wt('t', LW2)
            if fc == 0:
                wt('t', SS)
                wt('t', VV)
            for tc in range(4):
                for col in range(2):
                    mm(ps[:, BK(tc * 2 + col)],
                       uBlk[:, fc * 512 + tc * 128:fc * 512 + tc * 128 + 128],
                       wblk[:, 2048 + 1024 * sl + 512 * col:2048 + 1024 * sl + 512 * col + 512],
                       fc == 0, fc == 31)
            ring_cons[("mw2", sl)] = MM.n
        for tc in range(4):
            wt('y', VV)
            wt('y', LSS)
            ydma(xt0, x0[tb * 512 + tc * 128:tb * 512 + tc * 128 + 128, :], LYX)
            ydma(xt1, s_scr[tb * 512 + tc * 128:tb * 512 + tc * 128 + 128, :], LYX)
            for col in range(2):
                wt('v', MM)
                wt('v', LYX)
                wt('v', LSX1)
                vop('tensor_add', POOL2[:, 0:512], ps[:, BK(tc * 2 + col)],
                    bcB[:, 512 * col:512 * col + 512])
                wt('s', VV)
                act(POOL2[:, 512:1024], POOL2[:, 0:512], AF.Silu)
                wt('v', SS)
                vop('tensor_mul', POOL2[:, 1024:1536], POOL2[:, 512:1024],
                    xt1[:, 512 * col:512 * col + 512])
                vop('tensor_add', POOL2[:, 1536:2048], POOL2[:, 1024:1536],
                    xt0[:, 512 * col:512 * col + 512])
                wt('y', VV)
                ydma(x1_scr[tb * 512 + tc * 128:tb * 512 + tc * 128 + 128,
                            512 * col:512 * col + 512], POOL2[:, 1536:2048], LSX1)
    if "x1" in taps:
        wt('g', LSX1)
        godma(taps["x1"], x1_scr)

    wt('y', LSX1)
    for tc in range(NTC):
        wt('v', LDO)
        ln_tile(x1_scr[128 * tc:128 * tc + 128, :], want_f32=True)
        vop('reduce_max', stat[:, 22:23], xt2, AX.X, apply_absolute_value=True)
        vop('tensor_scalar_max', stat[:, 23:24], stat[:, 22:23], 1e-5)
        vop('reciprocal', stat[:, 24:25], stat[:, 23:24])
        vop('tensor_scalar', out=sc1T[:, tc:tc + 1], in0=stat[:, 23:24],
            scalar1=1.0 / 127.0, scalar2=None, op0=ALU.mult)
        vop('tensor_scalar', out=xt0, in0=xt2, scalar1=stat[:, 24:25], scalar2=127.0,
            op0=ALU.mult, op1=ALU.mult)
        vop('tensor_copy', xti[:, 0:1024], xt0)
        wt('v', MM)
        vop('tensor_copy', eT[:, 1024:2048], xti[:, 0:1024])
        if tc == 0:
            wt('g', VV)
            godma(R_HFIRST, xt2[0:1, :])
        if tc == NTC - 1:
            wt('g', VV)
            godma(R_HLAST, xt2[127:128, :])
        ln_transpose(lambda j, tc=tc: xnF[:, j * NTOK + tc * 128: j * NTOK + tc * 128 + 128])
        ln_transpose(lambda j, tc=tc: xqF[:, j * NTOK + tc * 128: j * NTOK + tc * 128 + 128],
                     src=eT[:, 1024:2048])
        if tc == 0:
            pass
    godma(halo_in[0:1, :], R_HFIRST)
    godma(halo_in[1:2, :], R_HLAST)
    wt('g', LDO)
    em.run('g', lambda g: g.collective_compute(
        "AllGather", ALU.bypass, replica_groups=[[0, 1], [2, 3], [4, 5], [6, 7]],
        ins=[halo_in], outs=[halo_out]), CC, 1)

    em.wait('g', MM)
    hfc = POOL3[:, 0:2050]
    hf2 = POOL3[:, 2050 + 0:2050 + 2048]
    ropeC = POOL2[:, 0:2048]
    ropeS = POOL2[:, 2048:4096]
    yodma(ropeC, ropec)
    yodma(ropeS, ropes)
    wt('v', LYO)
    vop('memset', q2m[:], 0.0)
    vop('memset', k2m[:], 0.0)
    P3A = POOL3[:, 0:2048]
    P3B = POOL3[:, 2048:4096]
    swap_mask = [i ^ 1 for i in range(32)]
    kstage = eT[:, 0:2048]
    ksq = mxT[:, 0:2048]

    def l1_qk_chunk(which, dc):
        base = 0 if which == "q" else 1024
        sl = pw_load(wqkv, base + dc * 128)
        for tb in range(4):
            bank = 2 + (tb % 2)
            wt('t', WQ[sl])
            wt('t', SS)
            wt('t', VV)
            for dinc in range(8):
                mm(ps[:, BK(bank)], wblk[:, 1024 * sl + dinc * 128:1024 * sl + dinc * 128 + 128],
                   xnF[:, dinc * NTOK + tb * 512:dinc * NTOK + tb * 512 + 512],
                   dinc == 0, dinc == 7)
            if tb == 3:
                ring_cons[("pw", sl)] = MM.n
            wt('s', MM)
            act(P3A[:, tb * 512:tb * 512 + 512], ps[:, BK(bank)], AF.Copy,
                scale=(0.125 if which == "q" else 1.0))
        wt('v', SS)
        vop('stream_shuffle', P3B, P3A, swap_mask)
        vop('tensor_mul', P3B, P3B, ropeS)
        vop('tensor_mul', P3A, P3A, ropeC)
        dst = (QF[:, dc * NTOK:dc * NTOK + NTOK] if which == "q" else kstage)
        if which == "k":
            wt('v', LSKA)
        vop('tensor_add', dst, P3A, P3B)
        wt('s', VV)
        act(ksq, dst, AF.Square)
        dstm = q2m if which == "q" else k2m
        for j in range(4):
            wt('t', SS)
            wt('t', VV)
            mm(ps[0:2, BK(5)], hs2[:], ksq[:, j * 512:j * 512 + 512], True, True)
            wt('v', MM)
            vop('reduce_max', stat[0:2, 18:19], ps[0:2, BK(5)], AX.X)
            vop('tensor_max', dstm[0:2, dc:dc + 1], dstm[0:2, dc:dc + 1], stat[0:2, 18:19])
        if which == "k":
            wt('y', VV)
            ydma(kv_in[128 * dc:128 * dc + 128, :], kstage, LSKA)

    for dc in range(8):
        l1_qk_chunk("q", dc)
        l1_qk_chunk("k", dc)
    for col in range(2):
        for tcg in range(4):
            for hc in range(8):
                wsl = hc % 2
                em.wait('g', MM, ring_cons.get(("wv", wsl), 0))
                gdma(wblk[:, 3072 + 512 * wsl:3072 + 512 * wsl + 512],
                     wqkv[128 * hc:128 * hc + 128, 2048 + col * 512:2048 + col * 512 + 512], WV[wsl])
                wt('t', WV[wsl])
                if hc == 0:
                    wt('t', SS)
                    wt('t', VV)
                for tc4 in range(4):
                    tc = tcg * 4 + tc4
                    mm(ps[:, BK([0, 1, 6, 7][tc4])],
                       xnF[:, hc * NTOK + tc * 128:hc * NTOK + tc * 128 + 128],
                       wblk[:, 3072 + 512 * wsl:3072 + 512 * wsl + 512],
                       hc == 0, hc == 7)
                ring_cons[("wv", wsl)] = MM.n
            wt('s', MM)
            for tc4 in range(4):
                tc = tcg * 4 + tc4
                t0 = tc * 128
                stg = kv_stage[tc4 % 2]
                wt('s', LSKA if tc4 % 2 == 0 else LSKB)
                act(stg, ps[:, BK([0, 1, 6, 7][tc4])], AF.Copy)
                wt('y', SS)
                ydma(kv_in[1024 + (t0 % 1024):1024 + (t0 % 1024) + 128,
                           (t0 // 1024) * 1024 + col * 512:(t0 // 1024) * 1024 + col * 512 + 512],
                     stg, LSKA if tc4 % 2 == 0 else LSKB)
    vop('memset', hf2[0:2, 0:2048], 0.0)
    wt('g', VV)
    godma(kv_in[2048:2050, :], hf2[0:2, 0:2048])
    knb = R_KNOWN.rearrange("o (a s) -> o a s", s=2)
    godma(knb[:, :, 0:1], k2m[0:1, 0:8], ns=True)
    godma(knb[:, :, 1:2], k2m[1:2, 0:8], ns=True)
    godma(kv_in[2049:2050, 0:16], R_KNOWN)
    wt('g', LSKA)
    wt('g', LSKB)
    wt('g', LDO)
    em.run('g', lambda g: g.collective_compute(
        "AllGather", ALU.bypass, replica_groups=[[0, 1], [2, 3], [4, 5], [6, 7]],
        ins=[kv_in], outs=[kv_out]), CC, 1)

    pari = sb("pari", [1, 1], I32)
    yodma(pari[:], parity)
    prow = rows[0:1, 2048:3072]
    _halo_lyo = LYO.n
    _halo_ldo = LDO.n

    def _halo_switch(g):
        g.wait_ge(CC.h, 1)
        g.wait_ge(LYO.h, _halo_lyo)
        g.wait_ge(LDO.h, _halo_ldo)
        with g.register() as r:
            g.load(r, pari[0:1, 0:1])
            idx = g.snap(r, min_val=0, max_val=1)
            for i in g.Switch(idx, 2):
                src = halo_out[2:3, :] if i == 0 else halo_out[1:2, :]
                g.dma_start(out=prow, in_=src).then_inc(LDO.h, 16)
    LDO.n += 16
    em.add('g', _halo_switch)

    vop('memset', mxT[:], 0.0)
    vop('memset', onesw[:], 1.0)
    wt('t', SS)
    wt('t', VV)
    for tc in range(16):
        mm(ps[0:1, tc * 128:tc * 128 + 128], sc1T[:, tc:tc + 1], identF[:], True, True)
    wt('v', MM)
    vop('tensor_copy', rows[0:1, 0:2048], ps[0:1, 0:2048])
    vop('tensor_scalar', out=rows[0:1, 0:2048], in0=rows[0:1, 0:2048],
        scalar1=R_WM[0:1, 0:1], scalar2=None, op0=ALU.mult)
    wt('t', VV)
    for j in range(4):
        mm(ps[:, 2048 + 512 * j:2560 + 512 * j], onesr[0:1, :],
           rows[0:1, 512 * j:512 * j + 512], True, True)
    wt('v', MM)
    cB = POOL2[:, 0:2048]
    vop('tensor_copy', cB, ps[:, 2048:4096])
    wt('v', LDO)
    vop('reduce_max', R_HS[0:1, 0:1], prow, AX.X, apply_absolute_value=True)
    vop('tensor_scalar_max', R_HS[0:1, 1:2], R_HS[0:1, 0:1], 1e-5)
    vop('reciprocal', R_HS[0:1, 2:3], R_HS[0:1, 1:2])
    wt('t', VV)
    for hc in range(8):
        mm(ps[:, BK(6)][:, hc:hc + 1], prow[0:1, 128 * hc:128 * hc + 128],
           onesf[0:1, 0:1], True, True)
    mm(ps[:, BK(7)][:, 0:1], onesr[0:1, :], R_HS[0:1, 2:3], True, True)
    mm(ps[:, BK(7)][:, 1:2], onesr[0:1, :], R_HS[0:1, 1:2], True, True)
    wt('v', MM)
    vop('tensor_copy', stat[:, 25:26], ps[:, BK(7)][:, 0:1])
    vop('tensor_copy', stat[:, 26:27], ps[:, BK(7)][:, 1:2])
    vop('tensor_scalar', out=xt0[:, 0:8], in0=ps[:, BK(6)][:, 0:8],
        scalar1=stat[:, 25:26], scalar2=127.0, op0=ALU.mult, op1=ALU.mult)
    vop('tensor_copy', xti[:, 0:8], xt0[:, 0:8])
    vop('tensor_copy', xqh[:, 0:8], xti[:, 0:8])
    wt('t', VV)
    mm(ps[:, BK(7)][:, 2:3], onesr[0:1, :], R_WM[0:1, 0:1], True, True)
    wt('v', MM)
    vop('tensor_copy', stat[:, 29:30], ps[:, BK(7)][:, 2:3])
    vop('tensor_scalar', out=stat[:, 28:29], in0=stat[:, 26:27],
        scalar1=stat[:, 29:30], scalar2=1.0 / 127.0, op0=ALU.mult, op1=ALU.mult)
    for fc in range(32):
        sl = fc % 2
        LW = LWA if sl == 0 else LWB
        em.wait('y', MM, ring_cons.get(("w1q", sl), 0))
        em.run('y', lambda y, fc=fc, sl=sl: y.dma_start(
            out=wblk[:, 1024 * sl:1024 * sl + 1024].rearrange("p (d c) -> p d c", c=128),
            in_=w1q_scr[:, fc * 128:fc * 128 + 128].rearrange("(d p) c -> p d c", p=128)),
            LW, 16)
        wt('t', LW)
        wt('t', VV)
        for dinc in range(8):
            for b4 in range(4):
                mm(ps[:, BK(b4)], wblk[:, 1024 * sl + 128 * dinc:1024 * sl + 128 * dinc + 128],
                   xqF[:, dinc * NTOK + b4 * 512:dinc * NTOK + b4 * 512 + 512],
                   dinc == 0, dinc == 7)
            mm(ps[:, BK(4)][:, 0:8], wblk[:, 1024 * sl + 128 * dinc:1024 * sl + 128 * dinc + 128],
               xqh[:, 0:8], dinc == 0, dinc == 7)
        ring_cons[("w1q", sl)] = MM.n
        wt('v', MM)
        for b4 in range(4):
            vop('tensor_mul', hfc[:, 1 + 512 * b4:1 + 512 * b4 + 512], ps[:, BK(b4)],
                cB[:, 512 * b4:512 * b4 + 512])
        vop('tensor_scalar', out=xt0[:, 8:9], in0=ps[:, BK(4)][:, 0:1],
            scalar1=stat[:, 28:29], scalar2=None, op0=ALU.mult)
        vop('tensor_scalar', out=hfc[:, 0:1], in0=xt0[:, 8:9], scalar1=hmS[:, 0:1],
            scalar2=None, op0=ALU.mult)
        vop('tensor_scalar', out=hfc[:, 2049:2050], in0=xt0[:, 8:9], scalar1=hmS[:, 1:2],
            scalar2=None, op0=ALU.mult)
        vop('tensor_scalar', out=hf2, in0=hfc[:, 0:2048], scalar1=ctap[:, fc:fc + 1],
            scalar2=ctap[:, 96 + fc:97 + fc], op0=ALU.mult, op1=ALU.add)
        vop('scalar_tensor_tensor', out=hf2, in0=hfc[:, 1:2049],
            scalar=ctap[:, 32 + fc:33 + fc], in1=hf2, op0=ALU.mult, op1=ALU.add)
        vop('scalar_tensor_tensor', out=hf2, in0=hfc[:, 2:2050],
            scalar=ctap[:, 64 + fc:65 + fc], in1=hf2, op0=ALU.mult, op1=ALU.add)
        wt('s', VV)
        wt('s', LSH)
        act(kxb[:, 0:2048], hf2, AF.Silu)
        act(kxb[:, 2048:4096], kxb[:, 0:2048], AF.Abs)
        wt('v', SS)
        vop('tensor_max', mxT[:, 0:2048], mxT[:, 0:2048], kxb[:, 2048:4096])
        wt('y', VV)
        ydma(hs_scr[128 * fc:128 * fc + 128, :], kxb[:, 0:2048], LSH)
    if "hs" in taps:
        wt('g', LSH)
        godma(taps["hs"], hs_scr)

    ph9_ldo = LDO.n
    ph9_mm = MM.n

    def _kv_switch(g):
        g.wait_ge(CC.h, 2)
        g.wait_ge(LDO.h, ph9_ldo)
        g.wait_ge(MM.h, ph9_mm)
        with g.register() as r:
            g.load(r, pari[0:1, 0:1])
            idx = g.snap(r, min_val=0, max_val=1)
            for i in g.Switch(idx, 2):
                own, oth = (0, 2050) if i == 0 else (2050, 0)
                g.dma_start(out=k_scr[:, 0:2048], in_=kv_out[own:own + 1024, :]).then_inc(LDO.h, 16)
                g.dma_start(out=k_scr[:, 2048:4096], in_=kv_out[oth:oth + 1024, :]).then_inc(LDO.h, 16)
                g.dma_start(out=v_scr[0:1024, :], in_=kv_out[own + 1024:own + 2048, 0:1024]).then_inc(LDO.h, 16)
                g.dma_start(out=v_scr[1024:2048, :], in_=kv_out[own + 1024:own + 2048, 1024:2048]).then_inc(LDO.h, 16)
                g.dma_start(out=v_scr[2048:3072, :], in_=kv_out[oth + 1024:oth + 2048, 0:1024]).then_inc(LDO.h, 16)
                g.dma_start(out=v_scr[3072:4096, :], in_=kv_out[oth + 1024:oth + 2048, 1024:2048]).then_inc(LDO.h, 16)
                g.dma_start(out=R_KN[0:1, 16:32], in_=kv_out[oth + 2049:oth + 2050, 0:16]).then_inc(LDO.h, 16)
                g.dma_start(out=R_KN[0:1, 0:16], in_=kv_out[own + 2049:own + 2050, 0:16]).then_inc(LDO.h, 16)
    LDO.n += 8 * 16
    em.add('g', _kv_switch)
    KVSW_LD = LDO.n

    wt('v', LDO, KVSW_LD)
    qnb = R_KN[0:1, 32:48].rearrange("o (a s) -> o a s", s=2)
    godma(qnb[:, :, 0:1], q2m[0:1, 0:8], ns=True)
    godma(qnb[:, :, 1:2], q2m[1:2, 0:8], ns=True)
    wt('v', LDO)
    vop('tensor_max', R_KN[0:1, 48:64], R_KN[0:1, 0:16], R_KN[0:1, 16:32])
    vop('tensor_mul', R_KN[0:1, 48:64], R_KN[0:1, 48:64], R_KN[0:1, 32:48])
    wt('s', VV)
    act(R_KN[0:1, 64:80], R_KN[0:1, 48:64], AF.Sqrt)
    wt('v', SS)
    vop('tensor_scalar', out=R_KN[0:1, 64:80], in0=R_KN[0:1, 64:80], scalar1=-0.5,
        scalar2=None, op0=ALU.mult)
    wt('t', VV)
    mm(ps[:, BK(6)][:, 0:16], onesr[0:1, :], R_KN[0:1, 64:80], True, True)
    wt('v', MM)
    vop('tensor_copy', biasH[:, 0:16], ps[:, BK(6)][:, 0:16])
    vop('tensor_scalar', out=biasHK[:, 0:16], in0=biasH[:, 0:16], scalar1=klS[:, 0:1],
        scalar2=None, op0=ALU.add)
    bp1_v = biasH[:, 0:16].rearrange("p (a s) -> p a s", s=2)
    vop('tensor_tensor', biasP[:, 0:8], bp1_v[:, :, 0:1], bp1_v[:, :, 1:2], ALU.min)
    vop('tensor_scalar', out=biasPK[:, 0:8], in0=biasP[:, 0:8], scalar1=klS[:, 0:1],
        scalar2=None, op0=ALU.add)
    yodma(R_MB2, boutr)
    wt('t', LYO)
    for j in range(2):
        wt('t', VV)
        mm(ps[:, BK(6)], onesr[0:1, :], R_MB2[0:1, 512 * j:512 * j + 512], True, True)
        wt('v', MM)
        vop('tensor_copy', bcB[:, 512 * j:512 * j + 512], ps[:, BK(6)])
    kbuf = [kxb[:, 0:4096], kxb[:, 4096:8192]]
    vbuf1 = [POOL1[:, 8064:12224], POOL1[:, 12224:16384]]
    gseq = []
    for qb in range(4):
        for hg in range(8):
            gseq.append((qb, hg))
    wt('v', MM)
    vop('memset', POOL1[:, 8064:16384], 1.0)
    wt('s', LSKA)
    wt('s', LSKB)
    em.wait('g', VV)
    em.wait('g', MM)

    def l1_group_loads(g):
        if g >= 32:
            return
        qb, hg = gseq[g]
        sl = g % 2
        em.wait('g', MM, ring_cons.get(("kv1", sl), 0))
        em.wait('g', LSH)
        em.wait('g', LDO, KVSW_LD)
        gdma(kbuf[sl][:, 0:NFULL], k_scr[128 * hg:128 * hg + 128, :], WQ[sl])
        vv_ = vbuf1[sl][:, 0:4160].rearrange("p (c w) -> p c w", w=130)
        em.run('g', lambda g_, hg=hg, vv_=vv_: g_.dma_start(
            out=vv_[:, :, 0:64],
            in_=v_scr[:, 128 * hg:128 * hg + 64].rearrange("(c p) d -> p c d", p=128)),
            WQ[sl], 16)
        em.run('g', lambda g_, hg=hg, vv_=vv_: g_.dma_start(
            out=vv_[:, :, 65:129],
            in_=v_scr[:, 128 * hg + 64:128 * hg + 128].rearrange("(c p) d -> p c d", p=128)),
            WQ[sl], 16)

    l1_group_loads(0)
    for g in range(32):
        qb, hg = gseq[g]
        sl = g % 2
        seq = list(range(0, 4 * qb + 4)) + list(range(16, 32))
        nch = len(seq)
        diag0 = 4 * qb
        l1_group_loads(g + 1)
        kg, vg = kbuf[sl], vbuf1[sl]

        def qrow1(r0, r1, hg=hg, qb=qb):
            return QF[r0:r1, hg * NTOK + qb * 512: hg * NTOK + qb * 512 + 512]
        wt('t', WQ[sl])
        wt('t', VV)
        ss_after = {}
        vv_after = {}
        eA = lambda st, hh: eT[:, (2 * st + hh) * 512:(2 * st + hh) * 512 + 512]

        def scores1(cix):
            c = seq[cix]
            isB = c >= 16
            isdiag = (not isB) and (c >= diag0)
            bsrcP = biasPK if isB else biasP
            if cix >= 2:
                wt('t', SS, ss_after[cix - 2])
                if (cix - 2) in vv_after:
                    wt('t', VV, vv_after[cix - 2])
            st = cix % 2
            mm(ps[:, BK(2 * st)], kg[0:64, 128 * c:128 * c + 128],
               qrow1(0, 64), True, True, tp=(0, 0))
            mm(ps[:, BK(2 * st + 1)], kg[64:128, 128 * c:128 * c + 128],
               qrow1(64, 128), True, True, tp=(64, 0))
            wt('s', MM)
            act(eT[:, 1024 * st:1024 * st + 1024], ps[:, 1024 * st:1024 * st + 1024],
                AF.Exp, bias=bsrcP[:, hg:hg + 1])
            ss_after[cix] = SS.n
            if isdiag:
                di = c - diag0
                wt('v', SS, ss_after[cix])
                vop('tensor_mul', eA(st, 0), eA(st, 0), dmask[:, 512 * di:512 * di + 512])
                vop('tensor_mul', eA(st, 1), eA(st, 1), dmask[:, 512 * di:512 * di + 512])
                vv_after[cix] = VV.n

        def av1(cix, first, last):
            c = seq[cix]
            st = cix % 2
            wt('t', SS, ss_after[cix])
            if cix in vv_after:
                wt('t', VV, vv_after[cix])
            mm(ps[0:65, BK(4)], vg[:, 130 * c:130 * c + 65], eA(st, 0), first, last)
            mm(ps[64:128, BK(5)], vg[:, 130 * c + 65:130 * c + 129], eA(st, 1),
               first, last, tp=(0, 64), sgc=True)
            mm(ps[0:1, BK(5)], onesb[:, 1:2], eA(st, 1), False, last, tp=(0, 0), sgc=True)

        scores1(0)
        for cix in range(1, nch):
            scores1(cix)
            av1(cix - 1, cix - 1 == 0, False)
        av1(nch - 1, nch == 1, True)
        ring_cons[("kv1", sl)] = MM.n
        wt('v', MM)
        vop('tensor_copy', R_SUM[0:1, 0:512], ps[64:65, BK(4)])
        vop('tensor_copy', R_SUM[0:1, 512:1024], ps[0:1, BK(5)])
        vop('reciprocal', R_RCP, R_SUM)
        wt('t', VV)
        wt('t', SS)
        mm(ps[:, BK(6)], onesr[0:1, :], R_RCP[0:1, 0:512], True, True)
        mm(ps[:, BK(7)], onesr[0:1, :], R_RCP[0:1, 512:1024], True, True)
        wt('v', MM)
        vop('tensor_copy', bcC[:, 0:512], ps[:, BK(6)])
        vop('tensor_copy', bcC[:, 512:1024], ps[:, BK(7)])
        vop('tensor_mul', oF[0:64, 512 * hg:512 * hg + 512], ps[0:64, BK(4)],
            bcC[0:64, 0:512])
        vop('tensor_mul', oF[64:128, 512 * hg:512 * hg + 512], ps[64:128, BK(5)],
            bcC[64:128, 512:1024])
        if hg == 7:
            for col in range(2):
                for vhh in range(8):
                    wsl_ = 4096 + 512 * (vhh % 2)
                    LO = LOA if vhh % 2 == 0 else LOB
                    em.wait('g', MM, ring_cons.get(("wo", vhh % 2), 0))
                    gdma(wblk[:, wsl_:wsl_ + 512],
                         wout[128 * vhh:128 * vhh + 128, 512 * col:512 * col + 512], LO)
                    wt('t', LO)
                    if vhh == 0:
                        wt('t', VV)
                    for tc in range(4):
                        mm(ps[:, BK(tc)], oF[:, 512 * vhh + 128 * tc:512 * vhh + 128 * tc + 128],
                           wblk[:, wsl_:wsl_ + 512], vhh == 0, vhh == 7)
                    ring_cons[("wo", vhh % 2)] = MM.n
                wt('v', MM)
                for tc in range(4):
                    wt('v', LSS)
                    vop('tensor_add', POOL2[:, 1024:1536], ps[:, BK(tc)],
                        bcB[:, 512 * col:512 * col + 512])
                    wt('y', VV)
                    ydma(s_scr[512 * qb + 128 * tc:512 * qb + 128 * tc + 128,
                               512 * col:512 * col + 512], POOL2[:, 1024:1536], LSS)
    if "s_l1" in taps:
        wt('g', LSS)
        godma(taps["s_l1"], s_scr)

    s2b = stat[:, 26:27]
    wt('t', VV)
    mm(ps[:, BK(7)][:, 2:3], onesr[0:1, :], R_WM[0:1, 1:2], True, True)
    wt('v', MM)
    vop('tensor_copy', s2b, ps[:, BK(7)][:, 2:3])
    for tg in range(4):
        wt('t', VV)
        for j in range(4):
            tc = tg * 4 + j
            tr(ps[:, 2048 + 128 * j:2048 + 128 * j + 128], mxT[:, 128 * tc:128 * tc + 128])
        wt('v', MM)
        for j in range(4):
            tc = tg * 4 + j
            vop('reduce_max', stat[:, 22:23], ps[:, 2048 + 128 * j:2048 + 128 * j + 128], AX.X)
            vop('tensor_scalar_max', stat[:, 23:24], stat[:, 22:23], 1e-5)
            vop('tensor_scalar', out=c2T[:, tc:tc + 1], in0=stat[:, 23:24],
                scalar1=s2b, scalar2=1.0 / 127.0, op0=ALU.mult, op1=ALU.mult)
            vop('reciprocal', stat[:, 24:25], stat[:, 23:24])
            vop('tensor_scalar', out=sc1T[:, tc:tc + 1], in0=stat[:, 24:25],
                scalar1=127.0, scalar2=None, op0=ALU.mult)
    wt('t', VV)
    for tc in range(16):
        mm(ps[0:1, tc * 128:tc * 128 + 128], sc1T[:, tc:tc + 1], identF[:], True, True)
    wt('v', MM)
    vop('tensor_copy', rows[0:1, 0:2048], ps[0:1, 0:2048])
    wt('t', VV)
    for j in range(4):
        mm(ps[:, 2048 + 512 * j:2560 + 512 * j], onesr[0:1, :],
           rows[0:1, 512 * j:512 * j + 512], True, True)
    wt('v', MM)
    wt('v', LSS)
    sc2B = POOL2[:, 0:2048]
    vop('tensor_copy', sc2B, ps[:, 2048:4096])
    for fc in range(32):
        wt('y', VV)
        wt('y', LSH)
        ydma(kxb[:, 0:2048], hs_scr[128 * fc:128 * fc + 128, :], LYW)
        wt('v', LYW)
        vop('tensor_tensor', xti[:, 0:2048], kxb[:, 0:2048], sc2B, ALU.mult)
        vop('tensor_copy', kxb[:, 2048:4096], xti[:, 0:2048])
        wt('y', VV)
        ydma(hs_scr[128 * fc:128 * fc + 128, :], kxb[:, 2048:4096], LSH)
    for tb in range(4):
        for fc in range(32):
            sl = fc % 2
            LW = LWA if sl == 0 else LWB
            LW2 = LW2A if sl == 0 else LW2B
            em.wait('y', MM, ring_cons.get(("bl2", sl), 0))
            if tb == 0 and fc == 0:
                em.wait('y', LSH)
            ydma(wblk[:, 1024 * sl:1024 * sl + 512],
                 hs_scr[128 * fc:128 * fc + 128, 512 * tb:512 * tb + 512], LW)
            ydma(wblk[:, 2048 + 1024 * sl:2048 + 1024 * sl + 512],
                 w2q_scr[128 * fc:128 * fc + 128, 0:512], LW2)
            ydma(wblk[:, 2048 + 1024 * sl + 512:2048 + 1024 * sl + 1024],
                 w2q_scr[128 * fc:128 * fc + 128, 512:1024], LW2)
            wt('t', LW)
            wt('t', LW2)
            if fc == 0:
                wt('t', SS)
                wt('t', VV)
            for tc in range(4):
                for col in range(2):
                    mm(ps[:, BK(tc * 2 + col)],
                       wblk[:, 1024 * sl + 128 * tc:1024 * sl + 128 * tc + 128],
                       wblk[:, 2048 + 1024 * sl + 512 * col:2048 + 1024 * sl + 512 * col + 512],
                       fc == 0, fc == 31)
            ring_cons[("bl2", sl)] = MM.n
        for tc in range(4):
            wt('y', VV)
            wt('y', LSS)
            wt('y', LSX1)
            ydma(xt0, x1_scr[tb * 512 + tc * 128:tb * 512 + tc * 128 + 128, :], LYX)
            ydma(xt1, s_scr[tb * 512 + tc * 128:tb * 512 + tc * 128 + 128, :], LYX)
            for col in range(2):
                wt('s', MM)
                wt('s', VV)
                act(POOL2[:, 2048:2560], ps[:, BK(tc * 2 + col)], AF.Silu,
                    scale=c2T[:, tb * 4 + tc:tb * 4 + tc + 1])
                wt('v', SS)
                wt('v', LYX)
                wt('v', STO)
                vop('tensor_mul', POOL2[:, 2560:3072], POOL2[:, 2048:2560],
                    xt1[:, 512 * col:512 * col + 512])
                vop('tensor_add', POOL2[:, 3072:3584], POOL2[:, 2560:3072],
                    xt0[:, 512 * col:512 * col + 512])
                wt('y', VV)
                em.run('y', lambda y, tb=tb, tc=tc, col=col: y.dma_start(
                    out=out[tb * 512 + tc * 128:tb * 512 + tc * 128 + 128,
                            512 * col:512 * col + 512],
                    in_=POOL2[:, 3072:3584]), STO, 16)
    em.add('g', lambda g: g.wait_ge(STO.h, STO.n))
    em.add('y', lambda y: y.wait_ge(STO.h, STO.n))

    with nc.Block() as block:
        @block.tensor
        def _(t):
            for f in em.TH['t']:
                f(t)

        @block.vector
        def _(v):
            for f in em.TH['v']:
                f(v)

        @block.scalar
        def _(s):
            for f in em.TH['s']:
                f(s)

        @block.gpsimd
        def _(g):
            for f in em.TH['g']:
                f(g)

        @block.sync
        def _(y):
            for f in em.TH['y']:
                f(y)

    es.close()
    return nc


_BUILT = {}


def kernel(**inputs):
    in_maps = [_prep_core(c, inputs) for c in range(8)]
    nc = build()
    res = run_bass_kernel_spmd(nc, in_maps, core_ids=list(range(8)))
    out = np.empty((4, 4096, 1024), np.float32)
    for c in range(8):
        b, h = c >> 1, c & 1
        out[b, 2048 * h:2048 * h + 2048] = np.asarray(res.results[c]["out"])
    return out


# version 09
# speedup vs baseline: 1.0453x; 1.0121x over previous
import contextlib
import numpy as np

import concourse.bass as bass
import concourse.mybir as mybir
from concourse.bass_utils import run_bass_kernel_spmd

F32 = mybir.dt.float32
BF16 = mybir.dt.bfloat16
I32 = mybir.dt.int32
AF = mybir.ActivationFunctionType
ALU = mybir.AluOpType
AX = mybir.AxisListType

NTOK = 2048
NFULL = 4096
DM = 1024
DFF = 4096
NTC = NTOK // 128
KILL = -30000.0

DEV_TAPS = ()


def _rope_tables(h, dtype):
    i = np.arange(32, dtype=np.float64)
    inv = 1.0 / (10000.0 ** (2.0 * i / 64.0))
    pos = np.arange(2048 * h, 2048 * h + 2048, dtype=np.float64)
    rows = np.arange(128)
    fr = pos[None, :] * inv[(rows % 64) // 2][:, None]
    cos = np.cos(fr)
    sgn = np.where(rows % 2 == 0, -1.0, 1.0)[:, None]
    sin = np.sin(fr) * sgn
    return cos.astype(dtype), sin.astype(dtype)


_SHARED_CACHE = {}


def _prep_shared(inp):
    key = id(inp.get("L0_attn_Wq"))
    if _SHARED_CACHE.get("key") == key:
        return _SHARED_CACHE["d"]
    f32 = np.float32
    import ml_dtypes
    d = {}
    d["wq"] = np.ascontiguousarray(inp["L0_attn_Wq"], f32)
    d["wk"] = np.ascontiguousarray(inp["L0_attn_Wk"], f32)
    d["wv"] = np.ascontiguousarray(inp["L0_attn_Wv"], f32)
    d["wo"] = np.ascontiguousarray(inp["L0_attn_Wo"], f32)
    d["lamv"] = np.concatenate([inp["L0_attn_lq1"], inp["L0_attn_lk1"],
                                inp["L0_attn_lq2"], inp["L0_attn_lk2"]]).reshape(1, 256).astype(f32)
    d["lng"] = np.ascontiguousarray(inp["L0_attn_ln_g"].reshape(128, 1), f32)
    d["lnb"] = np.ascontiguousarray(inp["L0_attn_ln_b"].reshape(128, 1), f32)
    d["gw1T"] = np.ascontiguousarray(np.asarray(inp["L0_ffn_gate_w1"], f32).T)
    d["gb1T"] = np.ascontiguousarray(np.asarray(inp["L0_ffn_gate_b1"], f32).reshape(8, 128).T)
    d["gw2T"] = np.ascontiguousarray(np.asarray(inp["L0_ffn_gate_w2"], f32).T)
    d["gb2c"] = np.ascontiguousarray(np.asarray(inp["L0_ffn_gate_b2"], f32).reshape(8, 1))
    d["w1T"] = np.ascontiguousarray(np.asarray(inp["L0_ffn_W1"], f32).T)
    d["tw1T"] = np.ascontiguousarray(np.asarray(inp["L0_ffn_TW1"], f32).transpose(0, 2, 1))
    d["b1r"] = np.ascontiguousarray(np.asarray(inp["L0_ffn_b1"], f32).reshape(1, 4096))
    d["b1T"] = np.ascontiguousarray(np.asarray(inp["L0_ffn_b1"], f32).reshape(32, 128).T)
    d["tb1"] = np.ascontiguousarray(inp["L0_ffn_Tb1"], f32)
    d["w2T"] = np.ascontiguousarray(np.asarray(inp["L0_ffn_W2"], f32).T)
    d["tw2T"] = np.ascontiguousarray(np.asarray(inp["L0_ffn_TW2"], f32).transpose(0, 2, 1))
    d["b2r"] = np.ascontiguousarray(np.asarray(inp["L0_ffn_b2"], f32).reshape(1, 1024))
    d["tb2"] = np.ascontiguousarray(inp["L0_ffn_Tb2"], f32)
    d["wqkv"] = np.ascontiguousarray(inp["L1_attn_Wqkv"], f32)
    d["wout"] = np.ascontiguousarray(inp["L1_attn_Wout"], f32)
    d["boutr"] = np.ascontiguousarray(np.asarray(inp["L1_attn_bout"], f32).reshape(1, 1024))
    d["l1w1T"] = np.ascontiguousarray(np.asarray(inp["L1_ffn_w1"], f32).T)
    d["l1w2T"] = np.ascontiguousarray(np.asarray(inp["L1_ffn_w2"], f32).T)
    cw = np.asarray(inp["L1_ffn_conv_w"], f32)[:, 0, :]
    for j in range(3):
        d[f"cw{j}"] = np.ascontiguousarray(cw[:, j].reshape(32, 128).T)
    d["cb"] = np.ascontiguousarray(np.asarray(inp["L1_ffn_conv_b"], f32).reshape(32, 128).T)
    kk = np.arange(128)[:, None]
    qq = np.arange(512)[None, :]
    dmv = np.stack([(kk + 128 * i <= qq) for i in range(4)], 0).astype(f32)
    d["dmaskT"] = np.ascontiguousarray(dmv.reshape(4 * 128, 512)).astype(ml_dtypes.bfloat16)
    d["identT"] = np.eye(128, dtype=f32).astype(ml_dtypes.bfloat16)
    d["identF"] = np.eye(128, dtype=f32)
    d["hs2T"] = np.ascontiguousarray(
        np.stack([np.arange(128) < 64, np.arange(128) >= 64], 1).astype(f32)).astype(ml_dtypes.bfloat16)
    _SHARED_CACHE["key"] = key
    _SHARED_CACHE["d"] = d
    return d


def _prep_core(c, inp):
    b, h = c >> 1, c & 1
    own = slice(2048 * h, 2048 * h + 2048)
    oth = slice(2048 * (1 - h), 2048 * (1 - h) + 2048)
    f32 = np.float32
    x = np.asarray(inp["L0_attn_x"][b], f32)
    d = dict(_prep_shared(inp))
    d["x0"] = np.ascontiguousarray(np.concatenate([x[own], x[oth]], 0))
    cos, sin = _rope_tables(h, f32)
    d["ropec"] = cos
    d["ropes"] = sin
    d["parity"] = np.array([[h]], np.int32)
    d["killpb"] = np.full((128, 1), KILL * (1 - h), f32)
    d["halom"] = np.tile(np.array([[float(h), float(1 - h)]], f32), (128, 1))
    return d


class Sem:
    def __init__(self, h):
        self.h = h
        self.n = 0


class Em:
    def __init__(self):
        self.TH = {k: [] for k in "tvsgy"}

    def add(self, e, fn):
        self.TH[e].append(fn)

    def wait(self, e, sem, val=None):
        v = sem.n if val is None else val
        if v > 0:
            self.add(e, lambda E, h=sem.h, v=v: E.wait_ge(h, v))

    def run(self, e, fn, sem=None, k=1, selfwait=False):
        if sem is not None:
            if selfwait and sem.n > 0:
                v0 = sem.n
                self.add(e, lambda E, h=sem.h, v0=v0: E.wait_ge(h, v0))
            sem.n += k
            self.add(e, lambda E, fn=fn, h=sem.h, k=k: fn(E).then_inc(h, k))
        else:
            self.add(e, fn)


def build():
    nc = bass.Bass("TRN2", target_bir_lowering=False)
    es = contextlib.ExitStack()

    ap = lambda t: t.ap()
    x0 = ap(nc.dram_tensor("x0", [NFULL, DM], F32, kind="ExternalInput"))
    wq = ap(nc.dram_tensor("wq", [DM, DM], F32, kind="ExternalInput"))
    wk = ap(nc.dram_tensor("wk", [DM, DM], F32, kind="ExternalInput"))
    wv = ap(nc.dram_tensor("wv", [DM, DM], F32, kind="ExternalInput"))
    wo = ap(nc.dram_tensor("wo", [DM, DM], F32, kind="ExternalInput"))
    lamv = ap(nc.dram_tensor("lamv", [1, 256], F32, kind="ExternalInput"))
    lng = ap(nc.dram_tensor("lng", [128, 1], F32, kind="ExternalInput"))
    lnb = ap(nc.dram_tensor("lnb", [128, 1], F32, kind="ExternalInput"))
    gw1T = ap(nc.dram_tensor("gw1T", [DM, DM], F32, kind="ExternalInput"))
    gb1T = ap(nc.dram_tensor("gb1T", [128, 8], F32, kind="ExternalInput"))
    gw2T = ap(nc.dram_tensor("gw2T", [DM, 8], F32, kind="ExternalInput"))
    gb2c = ap(nc.dram_tensor("gb2c", [8, 1], F32, kind="ExternalInput"))
    w1T = ap(nc.dram_tensor("w1T", [DM, DFF], F32, kind="ExternalInput"))
    tw1T = ap(nc.dram_tensor("tw1T", [8, DM, DFF], F32, kind="ExternalInput"))
    b1r = ap(nc.dram_tensor("b1r", [1, DFF], F32, kind="ExternalInput"))
    b1T = ap(nc.dram_tensor("b1T", [128, 32], F32, kind="ExternalInput"))
    tb1 = ap(nc.dram_tensor("tb1", [8, DFF], F32, kind="ExternalInput"))
    w2T = ap(nc.dram_tensor("w2T", [DFF, DM], F32, kind="ExternalInput"))
    tw2T = ap(nc.dram_tensor("tw2T", [8, DFF, DM], F32, kind="ExternalInput"))
    b2r = ap(nc.dram_tensor("b2r", [1, DM], F32, kind="ExternalInput"))
    tb2 = ap(nc.dram_tensor("tb2", [8, DM], F32, kind="ExternalInput"))
    wqkv = ap(nc.dram_tensor("wqkv", [DM, 3 * DM], F32, kind="ExternalInput"))
    wout = ap(nc.dram_tensor("wout", [DM, DM], F32, kind="ExternalInput"))
    boutr = ap(nc.dram_tensor("boutr", [1, DM], F32, kind="ExternalInput"))
    l1w1T = ap(nc.dram_tensor("l1w1T", [DM, DFF], F32, kind="ExternalInput"))
    l1w2T = ap(nc.dram_tensor("l1w2T", [DFF, DM], F32, kind="ExternalInput"))
    cw0 = ap(nc.dram_tensor("cw0", [128, 32], F32, kind="ExternalInput"))
    cw1 = ap(nc.dram_tensor("cw1", [128, 32], F32, kind="ExternalInput"))
    cw2 = ap(nc.dram_tensor("cw2", [128, 32], F32, kind="ExternalInput"))
    cb = ap(nc.dram_tensor("cb", [128, 32], F32, kind="ExternalInput"))
    ropec = ap(nc.dram_tensor("ropec", [128, NTOK], F32, kind="ExternalInput"))
    ropes = ap(nc.dram_tensor("ropes", [128, NTOK], F32, kind="ExternalInput"))
    parity = ap(nc.dram_tensor("parity", [1, 1], I32, kind="ExternalInput"))
    killpb = ap(nc.dram_tensor("killpb", [128, 1], F32, kind="ExternalInput"))
    halom = ap(nc.dram_tensor("halom", [128, 2], F32, kind="ExternalInput"))
    dmaskT = ap(nc.dram_tensor("dmaskT", [4 * 128, 512], BF16, kind="ExternalInput"))
    identT = ap(nc.dram_tensor("identT", [128, 128], BF16, kind="ExternalInput"))
    identFT = ap(nc.dram_tensor("identF", [128, 128], F32, kind="ExternalInput"))
    hs2T = ap(nc.dram_tensor("hs2T", [128, 2], BF16, kind="ExternalInput"))
    out = ap(nc.dram_tensor("out", [NTOK, DM], F32, kind="ExternalOutput"))

    k_scr = ap(nc.dram_tensor("k_scr", [DM, NFULL], BF16))
    v_scr = ap(nc.dram_tensor("v_scr", [NFULL, DM], BF16))
    s_scr = ap(nc.dram_tensor("s_scr", [NTOK, DM], F32))
    x1_scr = ap(nc.dram_tensor("x1_scr", [NTOK, DM], F32))
    mw1_scr = ap(nc.dram_tensor("mw1_scr", [DM, DFF], BF16))
    mw2_scr = ap(nc.dram_tensor("mw2_scr", [DFF, DM], BF16))
    w1q_scr = ap(nc.dram_tensor("w1q_scr", [DM, DFF], BF16))
    w2q_scr = ap(nc.dram_tensor("w2q_scr", [DFF, DM], BF16))
    hs_scr = ap(nc.dram_tensor("hs_scr", [DFF, NTOK], BF16))
    kv_in = ap(nc.dram_tensor("kv_in", [2050, NTOK], BF16))
    kv_out = ap(nc.dram_tensor("kv_out", [4100, NTOK], BF16))
    halo_in = ap(nc.dram_tensor("halo_in", [2, DM], F32))
    halo_out = ap(nc.dram_tensor("halo_out", [4, DM], F32))
    taps = {}
    for t_ in DEV_TAPS:
        shp = {"xn0": [128, 8 * NTOK], "g8": [8, 1], "s_l0": [NTOK, DM],
               "x1": [NTOK, DM], "mw1": [DM, DFF], "s_l1": [NTOK, DM],
               "hs": [DFF, NTOK], "kq": [128, 16], "w1q": [DM, DFF]}[t_]
        dt_ = BF16 if t_ in ("mw1", "hs", "w1q") else F32
        taps[t_] = ap(nc.dram_tensor("tap_" + t_, shp, dt_, kind="ExternalOutput"))

    sb = lambda n, shp, dt: es.enter_context(nc.sbuf_tensor(n, shp, dt))
    xnF = sb("xnF", [128, 8 * NTOK], BF16)
    QF = sb("QF", [128, 8 * NTOK], BF16)
    POOL1 = sb("POOL1", [128, 32 * 512], BF16)
    POOL2 = sb("POOL2", [128, 4096], F32)
    POOL3 = sb("POOL3", [128, 4100], F32)
    wblk = sb("wblk", [128, 10 * 512], BF16)
    kxb = sb("kxb", [128, 2 * NFULL], BF16)
    eT = sb("eT", [128, 2048], BF16)
    oF = sb("oF", [128, 8 * 512], BF16)
    bcB = sb("bcB", [128, 1024], F32)
    bcC = sb("bcC", [128, 1024], F32)
    xti = sb("xti", [128, 2048], I32)
    mxT = sb("mxT", [128, NTOK], BF16)
    dmask = sb("dmask", [128, 4 * 512], BF16)
    rows = sb("rows", [4, 3104], F32)
    stat = sb("stat", [128, 64], F32)
    gwS = sb("gwS", [128, 64], BF16)
    ctap = sb("ctap", [128, 128], F32)
    ident = sb("ident", [128, 128], BF16)
    identF = sb("identFs", [128, 128], F32)
    hs2 = sb("hs2", [128, 2], BF16)
    onesb = sb("onesb", [128, 2], BF16)
    onesw = sb("onesw", [128, 128], BF16)
    biasP = sb("biasP", [128, 8], F32)
    biasPK = sb("biasPK", [128, 8], F32)
    onesf = sb("onesf", [128, 1], F32)
    onesr = sb("onesr", [1, 128], F32)
    mb1T = sb("mb1T", [128, 32], F32)
    biasH = sb("biasH", [128, 16], F32)
    biasHK = sb("biasHK", [128, 16], F32)
    q2m = sb("q2m", [2, 16], F32)
    k2m = sb("k2m", [2, 16], F32)
    lamS = sb("lamS", [128, 1], F32)
    klS = sb("klS", [128, 1], F32)
    hmS = sb("hmS", [128, 2], F32)
    sc1T = sb("sc1T", [128, 16], F32)
    c2T = sb("c2T", [128, 16], F32)
    xqh = sb("xqh", [128, 8], BF16)
    g8c = sb("g8c", [8, 1], F32)
    ps = es.enter_context(nc.psum_tensor("ps", [128, 4096], F32))

    kgrp = kxb
    vgrp = wblk
    uBlk = POOL1
    xqF = POOL1
    R_SUM = rows[0:1, 0:1024]
    R_RCP = rows[0:1, 1024:2048]
    R_MU = rows[0:1, 2048:2560]
    R_VAR = rows[0:1, 2560:3072]
    R_LAM = rows[0:1, 0:544]
    R_NEGB = rows[0:1, 0:16]
    R_G8R = rows[0:1, 16:24]
    R_B1 = rows[0:1, 24:152]
    R_MB2 = rows[0:1, 1024:2048]
    R_KN = rows[0:1, 0:80]
    R_KNOWN = rows[2:3, 1024:1040]
    R_HFIRST = rows[2:3, 0:1024]
    R_HLAST = rows[3:4, 0:1024]
    R_WM = rows[0:1, 3072:3080]
    R_HS = rows[0:1, 3080:3088]

    BK = lambda k: slice(512 * k, 512 * k + 512)

    MM = Sem(es.enter_context(nc.semaphore("MM")))
    SS = Sem(es.enter_context(nc.semaphore("SS")))
    VV = Sem(es.enter_context(nc.semaphore("VV")))
    CC = Sem(es.enter_context(nc.semaphore("CC")))
    STO = Sem(es.enter_context(nc.semaphore("STO")))
    WQA = Sem(es.enter_context(nc.semaphore("WQA")))
    WQB = Sem(es.enter_context(nc.semaphore("WQB")))
    LDW = Sem(es.enter_context(nc.semaphore("LDW")))
    LDO = Sem(es.enter_context(nc.semaphore("LDO")))
    MWA = Sem(es.enter_context(nc.semaphore("MWA")))
    MWB = Sem(es.enter_context(nc.semaphore("MWB")))
    LYC = Sem(es.enter_context(nc.semaphore("LYC")))
    LYX = Sem(es.enter_context(nc.semaphore("LYX")))
    LYW = Sem(es.enter_context(nc.semaphore("LYW")))
    LYO = Sem(es.enter_context(nc.semaphore("LYO")))
    LSWA = Sem(es.enter_context(nc.semaphore("LSWA")))
    LSWB = Sem(es.enter_context(nc.semaphore("LSWB")))
    LSKA = Sem(es.enter_context(nc.semaphore("LSKA")))
    LSKB = Sem(es.enter_context(nc.semaphore("LSKB")))
    LSS = Sem(es.enter_context(nc.semaphore("LSS")))
    LSX1 = Sem(es.enter_context(nc.semaphore("LSX1")))
    LSH = Sem(es.enter_context(nc.semaphore("LSH")))
    MSA = Sem(es.enter_context(nc.semaphore("MSA")))
    MSB = Sem(es.enter_context(nc.semaphore("MSB")))
    WVA = Sem(es.enter_context(nc.semaphore("WVA")))
    WVB = Sem(es.enter_context(nc.semaphore("WVB")))
    LWA = Sem(es.enter_context(nc.semaphore("LWA")))
    LWB = Sem(es.enter_context(nc.semaphore("LWB")))
    LW2A = Sem(es.enter_context(nc.semaphore("LW2A")))
    LW2B = Sem(es.enter_context(nc.semaphore("LW2B")))
    LOA = Sem(es.enter_context(nc.semaphore("LOA")))
    LOB = Sem(es.enter_context(nc.semaphore("LOB")))

    em = Em()
    ring_cons = {}

    def gdma(o, i, sem):
        em.run('g', lambda g, o=o, i=i: g.dma_start(out=o, in_=i), sem, 16)

    def godma(o, i, ns=False):
        em.wait('g', LDO)
        em.run('g', lambda g, o=o, i=i, ns=ns: g.dma_start(
            out=o, in_=i, allow_slow_non_contiguous=ns), LDO, 16)

    def ydma(o, i, sem):
        em.run('y', lambda y, o=o, i=i: y.dma_start(out=o, in_=i), sem, 16)

    def yodma(o, i):
        em.wait('y', LYO)
        em.run('y', lambda y, o=o, i=i: y.dma_start(out=o, in_=i), LYO, 16)

    def mm(o, l, r, st, sp, tp=None, sgc=False):
        em.run('t', lambda t, o=o, l=l, r=r, st=st, sp=sp, tp=tp, sgc=sgc:
               t.matmul(o, l, r, start=st, stop=sp, tile_position=tp,
                        skip_group_check=sgc), MM)

    def tr(o, i):
        mm(o, i, ident[:], True, True)

    def act(o, i, f, bias=0.0, scale=1.0, accum=None):
        em.run('s', lambda s, o=o, i=i, f=f, bias=bias, scale=scale, accum=accum:
               s.activation(out=o, in_=i, func=f, bias=bias, scale=scale,
                            accum_out=accum), SS, selfwait=True)

    def vop(name, *a, **kw):
        em.run('v', lambda v, name=name, a=a, kw=kw: getattr(v, name)(*a, **kw), VV,
               selfwait=True)

    def wt(e, sem, val=None):
        em.wait(e, sem, val)

    vop('memset', onesb[:], 1.0)
    vop('memset', onesf[:], 1.0)
    vop('memset', onesr[:], 1.0)
    vop('memset', q2m[:], 0.0)
    vop('memset', k2m[:], 0.0)
    vop('memset', stat[0:8, 20:22], 0.0)
    vop('memset', mxT[:], 0.0)
    vop('memset', onesw[:], 1.0)
    vop('memset', stat[:, 19:20], 1e-5)
    ydma(ident[:], identT, LYC)
    ydma(identF[:], identFT, LYC)
    ydma(hs2[:], hs2T, LYC)
    ydma(stat[:, 40:41], lng, LYC)
    ydma(stat[:, 41:42], lnb, LYC)
    ydma(klS[:], killpb, LYC)
    ydma(hmS[:], halom, LYC)
    ydma(ctap[:, 0:32], cw0, LYC)
    ydma(ctap[:, 32:64], cw1, LYC)
    ydma(ctap[:, 64:96], cw2, LYC)
    ydma(ctap[:, 96:128], cb, LYC)
    for j in range(4):
        ydma(dmask[:, 512 * j:512 * j + 512], dmaskT[128 * j:128 * j + 128, :], LYC)
    ydma(R_LAM[0:1, 0:256], lamv, LYC)
    lngS, lnbS = stat[:, 40:41], stat[:, 41:42]
    g8aP = stat[0:8, 21:22]

    lam_init = float(0.8 - 0.6 * np.exp(-0.3 * 12))
    wt('v', LYC)
    vop('tensor_mul', R_LAM[0:1, 256:320], R_LAM[0:1, 0:64], R_LAM[0:1, 64:128])
    vop('tensor_mul', R_LAM[0:1, 320:384], R_LAM[0:1, 128:192], R_LAM[0:1, 192:256])
    vop('reduce_sum', R_LAM[0:1, 384:385], R_LAM[0:1, 256:320], AX.X)
    vop('reduce_sum', R_LAM[0:1, 385:386], R_LAM[0:1, 320:384], AX.X)
    wt('s', VV)
    act(R_LAM[0:1, 386:388], R_LAM[0:1, 384:386], AF.Exp)
    wt('v', SS)
    vop('scalar_tensor_tensor', out=R_LAM[0:1, 388:389], in0=R_LAM[0:1, 387:388],
        scalar=-lam_init, in1=R_LAM[0:1, 386:387], op0=ALU.add, op1=ALU.subtract)
    wt('t', VV)
    mm(ps[:, BK(7)][:, 0:1], onesr[0:1, :], R_LAM[0:1, 388:389], True, True)
    wt('v', MM)
    vop('tensor_copy', lamS[:, 0:1], ps[:, BK(7)][:, 0:1])

    def wq_tiles(w):
        r, cdim = w.shape
        cs = min(cdim, 2048)
        return [(w[128 * i:128 * i + 128, cs * j:cs * j + cs], cs)
                for i in range(r // 128) for j in range(cdim // cs)]

    for wi, (wsrc, wdst) in enumerate([(l1w1T, w1q_scr), (l1w2T, w2q_scr)]):
        tiles = wq_tiles(wsrc)
        dtiles = wq_tiles(wdst)
        sumc = 43 + wi
        vop('memset', stat[:, sumc:sumc + 1], 0.0)
        for idx, (tin, cs) in enumerate(tiles):
            sl = idx % 2
            wqs = WQA if sl == 0 else WQB
            em.wait('g', VV, ring_cons.get(("wq", sl), 0))
            em.wait('g', (LSWA if sl == 0 else LSWB), ring_cons.get(("wqs", sl), 0))
            gdma(POOL1[:, 2048 * sl:2048 * sl + cs], tin, wqs)
            wt('v', wqs)
            vop('reduce_sum', stat[:, 47:48], POOL1[:, 2048 * sl:2048 * sl + cs], AX.X,
                apply_absolute_value=True)
            vop('tensor_add', stat[:, sumc:sumc + 1], stat[:, sumc:sumc + 1], stat[:, 47:48])
            ring_cons[("wq", sl)] = VV.n
        wt('t', VV)
        mm(ps[0:1, BK(7)][:, 0:1], onesf[:, 0:1], stat[:, sumc:sumc + 1], True, True)
        wt('v', MM)
        nel = float(wsrc.shape[0] * wsrc.shape[1])
        vop('tensor_scalar', out=R_WM[0:1, wi:wi + 1], in0=ps[0:1, BK(7)][:, 0:1],
            scalar1=1.0 / nel, scalar2=1e-5, op0=ALU.mult, op1=ALU.max)
        vop('reciprocal', R_WM[0:1, 2 + wi:3 + wi], R_WM[0:1, wi:wi + 1])
        wt('t', VV)
        mm(ps[:, BK(7)][:, 0:1], onesr[0:1, :], R_WM[0:1, 2 + wi:3 + wi], True, True)
        wt('v', MM)
        vop('tensor_copy', stat[:, 45 + wi * 10:46 + wi * 10], ps[:, BK(7)][:, 0:1])
        scol = stat[:, 45 + wi * 10:46 + wi * 10]
        for idx, (tin, cs) in enumerate(tiles):
            sl = idx % 2
            wqs = WQA if sl == 0 else WQB
            em.wait('g', VV, ring_cons.get(("wq", sl), 0))
            em.wait('g', (LSWA if sl == 0 else LSWB), ring_cons.get(("wqs", sl), 0))
            gdma(POOL1[:, 2048 * sl:2048 * sl + cs], tin, wqs)
            wt('v', wqs)
            src_ = POOL1[:, 2048 * sl:2048 * sl + cs]
            tpos = POOL1[:, 4096:4096 + cs]
            tneg = POOL1[:, 6144:6144 + cs]
            vop('tensor_scalar', out=tpos, in0=src_, scalar1=scol, scalar2=0.5,
                op0=ALU.mult, op1=ALU.is_ge)
            vop('tensor_scalar', out=tneg, in0=src_, scalar1=scol, scalar2=-0.5,
                op0=ALU.mult, op1=ALU.is_le)
            vop('tensor_sub', src_, tpos, tneg)
            ring_cons[("wq", sl)] = VV.n
            wt('y', VV)
            lsw = LSWA if sl == 0 else LSWB
            ydma(dtiles[idx][0], src_, lsw)
            ring_cons[("wqs", sl)] = lsw.n
    if "w1q" in taps:
        wt('g', LSWA)
        wt('g', LSWB)
        godma(taps["w1q"], w1q_scr)

    xt0 = POOL3[:, 0:1024]
    xt1 = POOL3[:, 1024:2048]
    xt2 = POOL3[:, 2048:3072]

    def ln_tile(src_rows, want_f32=False):
        wt('y', VV)
        ydma(xt0, src_rows, LYX)
        wt('v', LYX)
        vop('bn_stats', stat[:, 0:6], xt0[:, 0:512])
        vop('bn_stats', stat[:, 6:12], xt0[:, 512:1024])
        vop('bn_aggr', stat[:, 12:14], stat[:, 0:12])
        wt('s', VV)
        act(stat[:, 14:15], stat[:, 13:14], AF.Sqrt, bias=stat[:, 19:20])
        wt('v', SS)
        vop('reciprocal', stat[:, 15:16], stat[:, 14:15])
        vop('tensor_scalar', out=stat[:, 16:17], in0=stat[:, 12:13],
            scalar1=stat[:, 15:16], scalar2=-1.0, op0=ALU.mult, op1=ALU.mult)
        wt('v', MM)
        vop('tensor_scalar', out=eT[:, 0:1024], in0=xt0,
            scalar1=stat[:, 15:16], scalar2=stat[:, 16:17], op0=ALU.mult, op1=ALU.add)
        if want_f32:
            vop('tensor_scalar', out=xt2, in0=xt0,
                scalar1=stat[:, 15:16], scalar2=stat[:, 16:17], op0=ALU.mult, op1=ALU.add)

    def ln_transpose(dst_col_of_chunk, src=None):
        wt('t', VV)
        wt('t', SS)
        for j in range(8):
            tr(ps[:, j * 128:j * 128 + 128], (src or eT)[:, j * 128:j * 128 + 128])
        wt('s', MM)
        for j in range(8):
            act(dst_col_of_chunk(j), ps[:, j * 128:j * 128 + 128], AF.Copy)

    for tc in range(NTC):
        ln_tile(x0[128 * tc:128 * tc + 128, :])
        ln_transpose(lambda j, tc=tc: xnF[:, j * NTOK + tc * 128: j * NTOK + tc * 128 + 128])
    XNF_OWN_SS = SS.n

    def load_colblock(seg, w, c0, cw):
        em.wait('g', MM)
        for dinc in range(8):
            gdma(wblk[:, seg + dinc * cw: seg + dinc * cw + cw],
                 w[128 * dinc:128 * dinc + 128, c0:c0 + cw], LDW)

    def xsrc(tb, dinc, j0, jn):
        if tb < 4:
            return xnF[:, dinc * NTOK + tb * 512 + j0: dinc * NTOK + tb * 512 + j0 + jn]
        return kxb[:, dinc * 512 + j0: dinc * 512 + j0 + jn]

    kv_stage = [eT[:, 1024:1536], eT[:, 1536:2048]]
    sq_tile = oF[:, 512:1024]
    gb1S = stat[:, 48:56]
    yodma(gb1S, gb1T)
    for hc in range(8):
        godma(gwS[:, hc * 8:hc * 8 + 8], gw2T[128 * hc:128 * hc + 128, :])
    wt('s', LYO)
    wt('t', LDO)
    pwc = [0]
    WQ = [WQA, WQB]
    WV = [WVA, WVB]
    em.wait('g', VV)

    def pw_load(w, c0):
        sl = pwc[0] % 2
        pwc[0] += 1
        em.wait('g', MM, ring_cons.get(("pw", sl), 0))
        em.run('g', lambda g, w=w, c0=c0, sl=sl: g.dma_start(
            out=wblk[:, 1024 * sl:1024 * sl + 1024].rearrange("p (d c) -> p d c", c=128),
            in_=w[:, c0:c0 + 128].rearrange("(d p) c -> p d c", p=128)), WQ[sl], 16)
        return sl

    for tb in range(8):
        if tb >= 4:
            for j in range(4):
                ln_tile(x0[2048 + 512 * (tb - 4) + 128 * j:
                           2048 + 512 * (tb - 4) + 128 * j + 128, :])
                ln_transpose(lambda dc, j=j: kxb[:, dc * 512 + j * 128: dc * 512 + j * 128 + 128])
        for which, wmat in (("k", wk), ("q", wq)):
            if which == "q" and tb >= 4:
                continue
            for dc in range(8):
                sl = pw_load(wmat, dc * 128)
                bank = 2 + (dc % 2)
                wt('t', WQ[sl])
                wt('t', SS)
                wt('t', VV)
                for dinc in range(8):
                    mm(ps[:, BK(bank)], wblk[:, 1024 * sl + dinc * 128: 1024 * sl + dinc * 128 + 128],
                       xsrc(tb, dinc, 0, 512), dinc == 0, dinc == 7)
                ring_cons[("pw", sl)] = MM.n
                wt('s', MM)
                wt('s', LSKA if dc % 2 == 0 else LSKB)
                stg = kv_stage[dc % 2]
                if which == "q":
                    act(QF[:, dc * NTOK + tb * 512: dc * NTOK + tb * 512 + 512],
                        ps[:, BK(bank)], AF.Copy, scale=0.125)
                    act(sq_tile, ps[:, BK(bank)], AF.Square, scale=0.125)
                else:
                    act(stg, ps[:, BK(bank)], AF.Copy)
                    act(sq_tile, ps[:, BK(bank)], AF.Square)
                    wt('y', SS)
                    ydma(k_scr[128 * dc:128 * dc + 128, 512 * tb:512 * tb + 512], stg,
                         LSKA if dc % 2 == 0 else LSKB)
                wt('t', SS)
                wt('t', VV)
                mm(ps[0:2, BK(5)], hs2[:], sq_tile, True, True)
                wt('v', MM)
                dstm = q2m if which == "q" else k2m
                vop('reduce_max', stat[0:2, 18:19], ps[0:2, BK(5)], AX.X)
                vop('tensor_max', dstm[0:2, dc:dc + 1], dstm[0:2, dc:dc + 1], stat[0:2, 18:19])
        vbanks = [0, 1, 6, 7]
        for col in range(2):
            for hc in range(8):
                wsl = hc % 2
                em.wait('g', MM, ring_cons.get(("wv", wsl), 0))
                gdma(wblk[:, 3072 + 512 * wsl: 3072 + 512 * wsl + 512],
                     wv[128 * hc:128 * hc + 128, col * 512: col * 512 + 512], WV[wsl])
                wt('t', WV[wsl])
                if hc == 0:
                    wt('t', SS)
                    wt('t', VV)
                for tc in range(4):
                    mm(ps[:, BK(vbanks[tc])], xsrc(tb, hc, tc * 128, 128),
                       wblk[:, 3072 + 512 * wsl: 3072 + 512 * wsl + 512],
                       hc == 0, hc == 7)
                ring_cons[("wv", wsl)] = MM.n
            wt('s', MM)
            for tc in range(4):
                stg = kv_stage[tc % 2]
                wt('s', LSKA if tc % 2 == 0 else LSKB)
                act(stg, ps[:, BK(vbanks[tc])], AF.Copy)
                wt('y', SS)
                ydma(v_scr[512 * tb + 128 * tc: 512 * tb + 128 * tc + 128,
                           col * 512: col * 512 + 512], stg,
                     LSKA if tc % 2 == 0 else LSKB)
        for hc in range(8):
            sl = pw_load(gw1T, hc * 128)
            bank = 2 + (hc % 2)
            wt('t', WQ[sl])
            wt('t', SS)
            for dinc in range(8):
                mm(ps[:, BK(bank)], wblk[:, 1024 * sl + dinc * 128: 1024 * sl + dinc * 128 + 128],
                   xsrc(tb, dinc, 0, 512), dinc == 0, dinc == 7)
            ring_cons[("pw", sl)] = MM.n
            wt('s', MM)
            act(oF[:, 0:512], ps[:, BK(bank)], AF.Relu, bias=gb1S[:, hc:hc + 1])
            wt('t', SS)
            mm(ps[0:8, BK(4)], gwS[:, hc * 8:hc * 8 + 8], oF[:, 0:512], hc == 0, hc == 7)
        wt('s', MM)
        act(bcC[0:8, 0:512], ps[0:8, BK(4)], AF.Copy, accum=stat[0:8, 20:21])
        wt('v', SS)
        vop('tensor_add', g8aP, g8aP, stat[0:8, 20:21])
    gb2S = stat[0:8, 57:58]
    yodma(gb2S, gb2c)
    wt('v', LYO)
    vop('tensor_scalar', out=g8c[0:8, 0:1], in0=g8aP, scalar1=1.0 / NFULL,
        scalar2=gb2S, op0=ALU.mult, op1=ALU.add)
    gBc = stat[:, 32:40]
    wt('t', VV)
    mm(ps[0:1, BK(5)][:, 0:8], g8c[0:8, 0:1], identF[0:8, 0:8], True, True)
    wt('v', MM)
    vop('tensor_copy', R_G8R, ps[0:1, BK(5)][:, 0:8])
    wt('t', VV)
    mm(ps[:, BK(6)][:, 0:8], onesr[0:1, :], R_G8R, True, True)
    wt('v', MM)
    vop('tensor_copy', gBc, ps[:, BK(6)][:, 0:8])
    vop('tensor_mul', q2m[0:2, 8:16], q2m[0:2, 0:8], k2m[0:2, 0:8])
    wt('s', VV)
    act(k2m[0:2, 8:16], q2m[0:2, 8:16], AF.Sqrt)
    wt('v', SS)
    vop('tensor_scalar', out=q2m[0:2, 8:16], in0=k2m[0:2, 8:16], scalar1=-0.5, op0=ALU.mult, scalar2=None)
    wt('g', VV)
    nb_v = R_NEGB.rearrange("o (a s) -> o a s", s=2)
    godma(nb_v[:, :, 0:1], q2m[0:1, 8:16], ns=True)
    godma(nb_v[:, :, 1:2], q2m[1:2, 8:16], ns=True)
    wt('t', LDO)
    mm(ps[:, BK(6)][:, 0:16], onesr[0:1, :], R_NEGB, True, True)
    wt('v', MM)
    vop('tensor_copy', biasH[:, 0:16], ps[:, BK(6)][:, 0:16])
    bp_v = biasH[:, 0:16].rearrange("p (a s) -> p a s", s=2)
    vop('tensor_tensor', biasP[:, 0:8], bp_v[:, :, 0:1], bp_v[:, :, 1:2], ALU.min)
    wt('t', VV)
    for c in range(32):
        em.wait('y', MM)
        ydma(R_B1, b1r[0:1, 128 * c:128 * c + 128], LYW)
        ydma(bcC[0:8, 0:128], tb1[:, 128 * c:128 * c + 128], LYW)
        wt('t', LYW)
        mm(ps[:, BK(5)][:, c:c + 1], bcC[0:8, 0:128], g8c[0:8, 0:1], True, False)
        mm(ps[:, BK(5)][:, c:c + 1], R_B1, onesf[0:1, 0:1], False, True)
    wt('v', MM)
    vop('tensor_copy', mb1T[:], ps[:, BK(5)][:, 0:32])
    yodma(R_MB2, b2r)
    for j in range(2):
        em.wait('y', MM)
        ydma(bcC[0:8, 128:640], tb2[:, 512 * j:512 * j + 512], LYW)
        wt('t', LYW)
        wt('t', VV)
        mm(ps[0:1, BK(6)], g8c[0:8, 0:1], bcC[0:8, 128:640], True, True)
        wt('v', MM)
        wt('v', LYO)
        vop('tensor_add', R_MB2[0:1, 512 * j:512 * j + 512],
            ps[0:1, BK(6)], R_MB2[0:1, 512 * j:512 * j + 512])
    for j in range(2):
        wt('t', VV)
        mm(ps[:, BK(6)], onesr[0:1, :], R_MB2[0:1, 512 * j:512 * j + 512], True, True)
        wt('v', MM)
        vop('tensor_copy', bcB[:, 512 * j:512 * j + 512], ps[:, BK(6)])
    if "g8" in taps:
        wt('g', VV)
        godma(taps["g8"], g8c[0:8, 0:1])
    if "kq" in taps:
        wt('g', VV)
        godma(taps["kq"], biasH[:, 0:16])
    if "xn0" in taps:
        wt('g', SS, XNF_OWN_SS)
        godma(taps["xn0"], xnF[:])

    mw_chunks = []
    for hc in range(8):
        for fcol in range(8):
            mw_chunks.append((w1T[128 * hc:128 * hc + 128, 512 * fcol:512 * fcol + 512],
                              tw1T[0:8, 128 * hc:128 * hc + 128, 512 * fcol:512 * fcol + 512],
                              mw1_scr[128 * hc:128 * hc + 128, 512 * fcol:512 * fcol + 512]))
    for fr in range(32):
        for col in range(2):
            mw_chunks.append((w2T[128 * fr:128 * fr + 128, 512 * col:512 * col + 512],
                              tw2T[0:8, 128 * fr:128 * fr + 128, 512 * col:512 * col + 512],
                              mw2_scr[128 * fr:128 * fr + 128, 512 * col:512 * col + 512]))
    mw_state = {"i": 0, "vfree": [0, 0], "yfree": [0, 0]}

    def emit_mw_chunk():
        i = mw_state["i"]
        if i >= len(mw_chunks):
            return
        mw_state["i"] = i + 1
        wsl, twl, dst = mw_chunks[i]
        sl = i % 2
        seg = 9 * 512 * sl
        mws = MWA if sl == 0 else MWB
        mss = MSA if sl == 0 else MSB
        em.wait('g', VV, mw_state["vfree"][sl])
        gdma(POOL1[:, seg:seg + 512], wsl, mws)
        em.run('g', lambda g, twl=twl, seg=seg: g.dma_start(
            out=POOL1[:, seg + 512:seg + 4608].rearrange("p (e c) -> p e c", c=512),
            in_=twl.rearrange("e p c -> p e c")), mws, 16)
        accb = mxT[:, 512 * sl:512 * sl + 512]
        wt('v', mws)
        em.wait('v', mss, mw_state["yfree"][sl])
        vop('scalar_tensor_tensor', out=accb, in0=POOL1[:, seg + 512:seg + 1024],
            scalar=gBc[:, 0:1], in1=POOL1[:, seg:seg + 512], op0=ALU.mult, op1=ALU.add)
        for e in range(1, 8):
            vop('scalar_tensor_tensor', out=accb,
                in0=POOL1[:, seg + 512 * (e + 1):seg + 512 * (e + 2)],
                scalar=gBc[:, e:e + 1], in1=accb, op0=ALU.mult, op1=ALU.add)
        mw_state["vfree"][sl] = VV.n
        wt('y', VV)
        ydma(dst, accb, mss)
        mw_state["yfree"][sl] = mss.n

    eseg = lambda st, i: eT[:, st * 2048 + i * 1024: st * 2048 + i * 1024 + 1024]
    of32 = POOL2[:, 0:512]
    otmp = POOL2[:, 512:1024]
    sevac = POOL2[:, 1024:1536]
    kbuf = [kxb[:, 0:4096], kxb[:, 4096:8192]]
    vbuf = [wblk[:, 0:4096], POOL1[:, 9216:13312]]
    WQ = [WQA, WQB]
    wt('s', LSKA)
    wt('s', LSKB)
    em.wait('g', VV)
    em.wait('g', LSWA)
    em.wait('g', LSWB)

    def l0_group_loads(g):
        if g >= 32:
            return
        vh = g % 8
        sl = g % 2
        em.wait('g', MM, ring_cons.get(("kv", sl), 0))
        em.wait('g', LSKA)
        em.wait('g', LSKB)
        gdma(kbuf[sl][:, 0:NFULL], k_scr[128 * vh:128 * vh + 128, :], WQ[sl])
        em.run('g', lambda g_, vh=vh, sl=sl: g_.dma_start(
            out=vbuf[sl][:, 0:NFULL].rearrange("p (c d) -> p c d", d=128),
            in_=v_scr[:, 128 * vh:128 * vh + 128].rearrange("(c p) d -> p c d", p=128)),
            WQ[sl], 16)

    l0_group_loads(0)
    for g in range(32):
        qb, vh = g // 8, g % 8
        sl = g % 2
        l0_group_loads(g + 1)
        kg, vg = kbuf[sl], vbuf[sl]

        def qrow(r0, r1, vh=vh, qb=qb):
            return QF[r0:r1, vh * NTOK + qb * 512: vh * NTOK + qb * 512 + 512]
        wt('t', WQ[sl])
        wt('t', VV)
        ss_after = {}
        pairbuf = lambda sp: eT[:, 0:2048] if sp == 0 else POOL3[:, 0:1024].bitcast(BF16)

        def scores2(p):
            if p >= 1:
                wt('t', SS, ss_after[p - 1])
            for j in range(2):
                c = 2 * p + j
                mm(ps[:, BK(2 * j)], kg[0:64, 128 * c:128 * c + 128],
                   qrow(0, 64), True, True, tp=(0, 0))
                mm(ps[:, BK(2 * j + 1)], kg[64:128, 128 * c:128 * c + 128],
                   qrow(64, 128), True, True, tp=(64, 0))
            wt('s', MM)
            act(pairbuf(p % 2), ps[:, 0:2048], AF.Exp, bias=biasP[:, vh:vh + 1])
            ss_after[p] = SS.n

        def av2(p, first, last):
            buf = pairbuf(p % 2)
            wt('t', SS, ss_after[p])
            for j in range(2):
                c = 2 * p + j
                fst = first and j == 0
                lst = last and j == 1
                e0 = buf[:, 1024 * j:1024 * j + 512]
                e1 = buf[:, 1024 * j + 512:1024 * j + 1024]
                mm(ps[0:128, BK(4)], vg[:, 128 * c:128 * c + 128], e0, fst, lst)
                mm(ps[0:128, BK(5)], vg[:, 128 * c:128 * c + 128], e1, fst, lst)
                mm(ps[:, BK(6)], onesw[:, 0:128], e0, fst, lst)
                mm(ps[:, BK(7)], onesw[:, 0:128], e1, fst, lst)

        scores2(0)
        for p in range(1, 16):
            scores2(p)
            av2(p - 1, p - 1 == 0, False)
        av2(15, False, True)
        ring_cons[("kv", sl)] = MM.n
        wt('v', MM)
        vop('reciprocal', bcC[:, 0:512], ps[:, BK(6)])
        vop('reciprocal', bcC[:, 512:1024], ps[:, BK(7)])
        vop('tensor_mul', of32, ps[:, BK(4)], bcC[:, 0:512])
        vop('scalar_tensor_tensor', out=otmp, in0=ps[:, BK(5)], scalar=lamS[:, 0:1],
            in1=bcC[:, 512:1024], op0=ALU.mult, op1=ALU.mult)
        vop('tensor_add', of32, of32, otmp)
        wt('s', VV)
        act(otmp, of32, AF.Square)
        wt('t', SS)
        mm(ps[64:65, BK(6)], onesf[:, 0:1], of32, True, True, tp=(0, 64))
        mm(ps[0:1, BK(1)], onesf[:, 0:1], otmp, True, True)
        wt('v', MM)
        vop('tensor_scalar', out=R_MU, in0=ps[64:65, BK(6)], scalar1=1.0 / 128,
            scalar2=None, op0=ALU.mult)
        vop('tensor_scalar', out=R_VAR, in0=ps[0:1, BK(1)], scalar1=1.0 / 128,
            scalar2=None, op0=ALU.mult)
        vop('tensor_mul', R_RCP[0:1, 0:512], R_MU, R_MU)
        vop('tensor_sub', R_VAR, R_VAR, R_RCP[0:1, 0:512])
        wt('s', VV)
        act(R_RCP[0:1, 512:1024], R_VAR, AF.Ln, bias=stat[0:1, 19:20])
        act(R_VAR, R_RCP[0:1, 512:1024], AF.Exp, scale=-0.5)
        wt('t', SS)
        mm(ps[:, BK(2)], onesr[0:1, :], R_MU, True, True)
        mm(ps[:, BK(3)], onesr[0:1, :], R_VAR, True, True)
        wt('v', MM)
        vop('tensor_copy', bcC[:, 0:512], ps[:, BK(2)])
        vop('tensor_copy', bcC[:, 512:1024], ps[:, BK(3)])
        vop('tensor_sub', of32, of32, bcC[:, 0:512])
        vop('tensor_mul', of32, of32, bcC[:, 512:1024])
        vop('tensor_scalar', out=oF[:, 512 * vh:512 * vh + 512], in0=of32,
            scalar1=lngS, scalar2=lnbS, op0=ALU.mult, op1=ALU.add)
        for _ in range(4):
            emit_mw_chunk()
        if vh == 7:
            for col in range(2):
                for vhh in range(8):
                    wsl = 4096 + 512 * (vhh % 2)
                    LO = LOA if vhh % 2 == 0 else LOB
                    em.wait('g', MM, ring_cons.get(("wo", vhh % 2), 0))
                    gdma(wblk[:, wsl:wsl + 512],
                         wo[128 * vhh:128 * vhh + 128, 512 * col:512 * col + 512], LO)
                    wt('t', LO)
                    if vhh == 0:
                        wt('t', VV)
                    for tc in range(4):
                        mm(ps[:, BK(tc)], oF[:, 512 * vhh + 128 * tc:512 * vhh + 128 * tc + 128],
                           wblk[:, wsl:wsl + 512], vhh == 0, vhh == 7)
                    ring_cons[("wo", vhh % 2)] = MM.n
                wt('v', MM)
                for tc in range(4):
                    wt('v', LSS)
                    vop('tensor_copy', sevac, ps[:, BK(tc)])
                    wt('y', VV)
                    ydma(s_scr[512 * qb + 128 * tc:512 * qb + 128 * tc + 128,
                               512 * col:512 * col + 512], sevac, LSS)
    if "s_l0" in taps:
        wt('g', LSS)
        godma(taps["s_l0"], s_scr)
    while mw_state["i"] < len(mw_chunks):
        emit_mw_chunk()
    if "mw1" in taps:
        wt('g', MSA)
        wt('g', MSB)
        godma(taps["mw1"], mw1_scr)

    for tb in range(4):
        for fc in range(32):
            sl = fc % 2
            LW = LWA if sl == 0 else LWB
            em.wait('y', MM, ring_cons.get(("mw1", sl), 0))
            if tb == 0 and fc == 0:
                em.wait('y', MSA)
                em.wait('y', MSB)
            em.run('y', lambda y, fc=fc, sl=sl: y.dma_start(
                out=wblk[:, 1024 * sl:1024 * sl + 1024].rearrange("p (d c) -> p d c", c=128),
                in_=mw1_scr[:, fc * 128:fc * 128 + 128].rearrange("(d p) c -> p d c", p=128)),
                LW, 16)
            wt('t', LW)
            if fc < 2:
                wt('t', SS)
                wt('t', VV)
            else:
                wt('t', SS, ring_cons.get(("ge", sl), 0))
            for dinc in range(8):
                mm(ps[:, BK(sl)], wblk[:, 1024 * sl + 128 * dinc:1024 * sl + 128 * dinc + 128],
                   xnF[:, dinc * NTOK + tb * 512:dinc * NTOK + tb * 512 + 512],
                   dinc == 0, dinc == 7)
            ring_cons[("mw1", sl)] = MM.n
            wt('s', MM)
            act(uBlk[:, fc * 512:fc * 512 + 512], ps[:, BK(sl)], AF.Gelu_apprx_tanh,
                bias=mb1T[:, fc:fc + 1])
            ring_cons[("ge", sl)] = SS.n
        for fc in range(32):
            sl = fc % 2
            LW2 = LW2A if sl == 0 else LW2B
            em.wait('y', MM, ring_cons.get(("mw2", sl), 0))
            ydma(wblk[:, 2048 + 1024 * sl:2048 + 1024 * sl + 1024],
                 mw2_scr[fc * 128:fc * 128 + 128, 0:1024], LW2)
            wt('t', LW2)
            if fc == 0:
                wt('t', SS)
                wt('t', VV)
            for tc in range(4):
                for col in range(2):
                    mm(ps[:, BK(tc * 2 + col)],
                       uBlk[:, fc * 512 + tc * 128:fc * 512 + tc * 128 + 128],
                       wblk[:, 2048 + 1024 * sl + 512 * col:2048 + 1024 * sl + 512 * col + 512],
                       fc == 0, fc == 31)
            ring_cons[("mw2", sl)] = MM.n
        for tc in range(4):
            wt('y', VV)
            wt('y', LSS)
            ydma(xt0, x0[tb * 512 + tc * 128:tb * 512 + tc * 128 + 128, :], LYX)
            ydma(xt1, s_scr[tb * 512 + tc * 128:tb * 512 + tc * 128 + 128, :], LYX)
            for col in range(2):
                wt('v', MM)
                wt('v', LYX)
                wt('v', LSX1)
                vop('tensor_add', POOL2[:, 0:512], ps[:, BK(tc * 2 + col)],
                    bcB[:, 512 * col:512 * col + 512])
                wt('s', VV)
                act(POOL2[:, 512:1024], POOL2[:, 0:512], AF.Silu)
                wt('v', SS)
                vop('tensor_mul', POOL2[:, 1024:1536], POOL2[:, 512:1024],
                    xt1[:, 512 * col:512 * col + 512])
                vop('tensor_add', POOL2[:, 1536:2048], POOL2[:, 1024:1536],
                    xt0[:, 512 * col:512 * col + 512])
                wt('y', VV)
                ydma(x1_scr[tb * 512 + tc * 128:tb * 512 + tc * 128 + 128,
                            512 * col:512 * col + 512], POOL2[:, 1536:2048], LSX1)
    if "x1" in taps:
        wt('g', LSX1)
        godma(taps["x1"], x1_scr)

    wt('y', LSX1)
    for tc in range(NTC):
        wt('v', LDO)
        ln_tile(x1_scr[128 * tc:128 * tc + 128, :], want_f32=True)
        vop('reduce_max', stat[:, 22:23], xt2, AX.X, apply_absolute_value=True)
        vop('tensor_scalar_max', stat[:, 23:24], stat[:, 22:23], 1e-5)
        vop('reciprocal', stat[:, 24:25], stat[:, 23:24])
        vop('tensor_scalar', out=sc1T[:, tc:tc + 1], in0=stat[:, 23:24],
            scalar1=1.0 / 127.0, scalar2=None, op0=ALU.mult)
        vop('tensor_scalar', out=xt0, in0=xt2, scalar1=stat[:, 24:25], scalar2=127.0,
            op0=ALU.mult, op1=ALU.mult)
        vop('tensor_copy', xti[:, 0:1024], xt0)
        wt('v', MM)
        vop('tensor_copy', eT[:, 1024:2048], xti[:, 0:1024])
        if tc == 0:
            wt('g', VV)
            godma(R_HFIRST, xt2[0:1, :])
        if tc == NTC - 1:
            wt('g', VV)
            godma(R_HLAST, xt2[127:128, :])
        ln_transpose(lambda j, tc=tc: xnF[:, j * NTOK + tc * 128: j * NTOK + tc * 128 + 128])
        ln_transpose(lambda j, tc=tc: xqF[:, j * NTOK + tc * 128: j * NTOK + tc * 128 + 128],
                     src=eT[:, 1024:2048])
        if tc == 0:
            pass
    godma(halo_in[0:1, :], R_HFIRST)
    godma(halo_in[1:2, :], R_HLAST)
    wt('g', LDO)
    em.run('g', lambda g: g.collective_compute(
        "AllGather", ALU.bypass, replica_groups=[[0, 1], [2, 3], [4, 5], [6, 7]],
        ins=[halo_in], outs=[halo_out]), CC, 1)

    em.wait('g', MM)
    hfc = POOL3[:, 0:2050]
    hf2 = POOL3[:, 2050 + 0:2050 + 2048]
    ropeC = POOL2[:, 0:2048]
    ropeS = POOL2[:, 2048:4096]
    yodma(ropeC, ropec)
    yodma(ropeS, ropes)
    wt('v', LYO)
    vop('memset', q2m[:], 0.0)
    vop('memset', k2m[:], 0.0)
    P3A = POOL3[:, 0:2048]
    P3B = POOL3[:, 2048:4096]
    swap_mask = [i ^ 1 for i in range(32)]
    kstage = eT[:, 0:2048]
    ksq = mxT[:, 0:2048]

    def l1_qk_chunk(which, dc):
        base = 0 if which == "q" else 1024
        sl = pw_load(wqkv, base + dc * 128)
        for tb in range(4):
            bank = 2 + (tb % 2)
            wt('t', WQ[sl])
            wt('t', SS)
            wt('t', VV)
            for dinc in range(8):
                mm(ps[:, BK(bank)], wblk[:, 1024 * sl + dinc * 128:1024 * sl + dinc * 128 + 128],
                   xnF[:, dinc * NTOK + tb * 512:dinc * NTOK + tb * 512 + 512],
                   dinc == 0, dinc == 7)
            if tb == 3:
                ring_cons[("pw", sl)] = MM.n
            wt('s', MM)
            act(P3A[:, tb * 512:tb * 512 + 512], ps[:, BK(bank)], AF.Copy,
                scale=(0.125 if which == "q" else 1.0))
        wt('v', SS)
        vop('stream_shuffle', P3B, P3A, swap_mask)
        vop('tensor_mul', P3B, P3B, ropeS)
        vop('tensor_mul', P3A, P3A, ropeC)
        dst = (QF[:, dc * NTOK:dc * NTOK + NTOK] if which == "q" else kstage)
        if which == "k":
            wt('v', LSKA)
        vop('tensor_add', dst, P3A, P3B)
        wt('s', VV)
        act(ksq, dst, AF.Square)
        dstm = q2m if which == "q" else k2m
        for j in range(4):
            wt('t', SS)
            wt('t', VV)
            mm(ps[0:2, BK(5)], hs2[:], ksq[:, j * 512:j * 512 + 512], True, True)
            wt('v', MM)
            vop('reduce_max', stat[0:2, 18:19], ps[0:2, BK(5)], AX.X)
            vop('tensor_max', dstm[0:2, dc:dc + 1], dstm[0:2, dc:dc + 1], stat[0:2, 18:19])
        if which == "k":
            wt('y', VV)
            ydma(kv_in[128 * dc:128 * dc + 128, :], kstage, LSKA)

    for dc in range(8):
        l1_qk_chunk("q", dc)
        l1_qk_chunk("k", dc)
    for col in range(2):
        for tcg in range(4):
            for hc in range(8):
                wsl = hc % 2
                em.wait('g', MM, ring_cons.get(("wv", wsl), 0))
                gdma(wblk[:, 3072 + 512 * wsl:3072 + 512 * wsl + 512],
                     wqkv[128 * hc:128 * hc + 128, 2048 + col * 512:2048 + col * 512 + 512], WV[wsl])
                wt('t', WV[wsl])
                if hc == 0:
                    wt('t', SS)
                    wt('t', VV)
                for tc4 in range(4):
                    tc = tcg * 4 + tc4
                    mm(ps[:, BK([0, 1, 6, 7][tc4])],
                       xnF[:, hc * NTOK + tc * 128:hc * NTOK + tc * 128 + 128],
                       wblk[:, 3072 + 512 * wsl:3072 + 512 * wsl + 512],
                       hc == 0, hc == 7)
                ring_cons[("wv", wsl)] = MM.n
            wt('s', MM)
            for tc4 in range(4):
                tc = tcg * 4 + tc4
                t0 = tc * 128
                stg = kv_stage[tc4 % 2]
                wt('s', LSKA if tc4 % 2 == 0 else LSKB)
                act(stg, ps[:, BK([0, 1, 6, 7][tc4])], AF.Copy)
                wt('y', SS)
                ydma(kv_in[1024 + (t0 % 1024):1024 + (t0 % 1024) + 128,
                           (t0 // 1024) * 1024 + col * 512:(t0 // 1024) * 1024 + col * 512 + 512],
                     stg, LSKA if tc4 % 2 == 0 else LSKB)
    vop('memset', hf2[0:2, 0:2048], 0.0)
    wt('g', VV)
    godma(kv_in[2048:2050, :], hf2[0:2, 0:2048])
    knb = R_KNOWN.rearrange("o (a s) -> o a s", s=2)
    godma(knb[:, :, 0:1], k2m[0:1, 0:8], ns=True)
    godma(knb[:, :, 1:2], k2m[1:2, 0:8], ns=True)
    godma(kv_in[2049:2050, 0:16], R_KNOWN)
    wt('g', LSKA)
    wt('g', LSKB)
    wt('g', LDO)
    em.run('g', lambda g: g.collective_compute(
        "AllGather", ALU.bypass, replica_groups=[[0, 1], [2, 3], [4, 5], [6, 7]],
        ins=[kv_in], outs=[kv_out]), CC, 1)

    pari = sb("pari", [1, 1], I32)
    yodma(pari[:], parity)
    prow = rows[0:1, 2048:3072]
    _halo_lyo = LYO.n
    _halo_ldo = LDO.n

    def _halo_switch(g):
        g.wait_ge(CC.h, 1)
        g.wait_ge(LYO.h, _halo_lyo)
        g.wait_ge(LDO.h, _halo_ldo)
        with g.register() as r:
            g.load(r, pari[0:1, 0:1])
            idx = g.snap(r, min_val=0, max_val=1)
            for i in g.Switch(idx, 2):
                src = halo_out[2:3, :] if i == 0 else halo_out[1:2, :]
                g.dma_start(out=prow, in_=src).then_inc(LDO.h, 16)
    LDO.n += 16
    em.add('g', _halo_switch)

    vop('memset', mxT[:], 0.0)
    vop('memset', onesw[:], 1.0)
    wt('t', SS)
    wt('t', VV)
    for tc in range(16):
        mm(ps[0:1, tc * 128:tc * 128 + 128], sc1T[:, tc:tc + 1], identF[:], True, True)
    wt('v', MM)
    vop('tensor_copy', rows[0:1, 0:2048], ps[0:1, 0:2048])
    vop('tensor_scalar', out=rows[0:1, 0:2048], in0=rows[0:1, 0:2048],
        scalar1=R_WM[0:1, 0:1], scalar2=None, op0=ALU.mult)
    wt('t', VV)
    for j in range(4):
        mm(ps[:, 2048 + 512 * j:2560 + 512 * j], onesr[0:1, :],
           rows[0:1, 512 * j:512 * j + 512], True, True)
    wt('v', MM)
    cB = POOL2[:, 0:2048]
    vop('tensor_copy', cB, ps[:, 2048:4096])
    wt('v', LDO)
    vop('reduce_max', R_HS[0:1, 0:1], prow, AX.X, apply_absolute_value=True)
    vop('tensor_scalar_max', R_HS[0:1, 1:2], R_HS[0:1, 0:1], 1e-5)
    vop('reciprocal', R_HS[0:1, 2:3], R_HS[0:1, 1:2])
    wt('t', VV)
    for hc in range(8):
        mm(ps[:, BK(6)][:, hc:hc + 1], prow[0:1, 128 * hc:128 * hc + 128],
           onesf[0:1, 0:1], True, True)
    mm(ps[:, BK(7)][:, 0:1], onesr[0:1, :], R_HS[0:1, 2:3], True, True)
    mm(ps[:, BK(7)][:, 1:2], onesr[0:1, :], R_HS[0:1, 1:2], True, True)
    wt('v', MM)
    vop('tensor_copy', stat[:, 25:26], ps[:, BK(7)][:, 0:1])
    vop('tensor_copy', stat[:, 26:27], ps[:, BK(7)][:, 1:2])
    vop('tensor_scalar', out=xt0[:, 0:8], in0=ps[:, BK(6)][:, 0:8],
        scalar1=stat[:, 25:26], scalar2=127.0, op0=ALU.mult, op1=ALU.mult)
    vop('tensor_copy', xti[:, 0:8], xt0[:, 0:8])
    vop('tensor_copy', xqh[:, 0:8], xti[:, 0:8])
    wt('t', VV)
    mm(ps[:, BK(7)][:, 2:3], onesr[0:1, :], R_WM[0:1, 0:1], True, True)
    wt('v', MM)
    vop('tensor_copy', stat[:, 29:30], ps[:, BK(7)][:, 2:3])
    vop('tensor_scalar', out=stat[:, 28:29], in0=stat[:, 26:27],
        scalar1=stat[:, 29:30], scalar2=1.0 / 127.0, op0=ALU.mult, op1=ALU.mult)
    for fc in range(32):
        sl = fc % 2
        LW = LWA if sl == 0 else LWB
        em.wait('y', MM, ring_cons.get(("w1q", sl), 0))
        em.run('y', lambda y, fc=fc, sl=sl: y.dma_start(
            out=wblk[:, 1024 * sl:1024 * sl + 1024].rearrange("p (d c) -> p d c", c=128),
            in_=w1q_scr[:, fc * 128:fc * 128 + 128].rearrange("(d p) c -> p d c", p=128)),
            LW, 16)
        wt('t', LW)
        wt('t', VV)
        for dinc in range(8):
            for b4 in range(4):
                mm(ps[:, BK(b4)], wblk[:, 1024 * sl + 128 * dinc:1024 * sl + 128 * dinc + 128],
                   xqF[:, dinc * NTOK + b4 * 512:dinc * NTOK + b4 * 512 + 512],
                   dinc == 0, dinc == 7)
            mm(ps[:, BK(4)][:, 0:8], wblk[:, 1024 * sl + 128 * dinc:1024 * sl + 128 * dinc + 128],
               xqh[:, 0:8], dinc == 0, dinc == 7)
        ring_cons[("w1q", sl)] = MM.n
        wt('v', MM)
        for b4 in range(4):
            vop('tensor_mul', hfc[:, 1 + 512 * b4:1 + 512 * b4 + 512], ps[:, BK(b4)],
                cB[:, 512 * b4:512 * b4 + 512])
        vop('tensor_scalar', out=xt0[:, 8:9], in0=ps[:, BK(4)][:, 0:1],
            scalar1=stat[:, 28:29], scalar2=None, op0=ALU.mult)
        vop('tensor_scalar', out=hfc[:, 0:1], in0=xt0[:, 8:9], scalar1=hmS[:, 0:1],
            scalar2=None, op0=ALU.mult)
        vop('tensor_scalar', out=hfc[:, 2049:2050], in0=xt0[:, 8:9], scalar1=hmS[:, 1:2],
            scalar2=None, op0=ALU.mult)
        vop('tensor_scalar', out=hf2, in0=hfc[:, 0:2048], scalar1=ctap[:, fc:fc + 1],
            scalar2=ctap[:, 96 + fc:97 + fc], op0=ALU.mult, op1=ALU.add)
        vop('scalar_tensor_tensor', out=hf2, in0=hfc[:, 1:2049],
            scalar=ctap[:, 32 + fc:33 + fc], in1=hf2, op0=ALU.mult, op1=ALU.add)
        vop('scalar_tensor_tensor', out=hf2, in0=hfc[:, 2:2050],
            scalar=ctap[:, 64 + fc:65 + fc], in1=hf2, op0=ALU.mult, op1=ALU.add)
        wt('s', VV)
        wt('s', LSH)
        act(kxb[:, 0:2048], hf2, AF.Silu)
        act(kxb[:, 2048:4096], kxb[:, 0:2048], AF.Abs)
        wt('v', SS)
        vop('tensor_max', mxT[:, 0:2048], mxT[:, 0:2048], kxb[:, 2048:4096])
        wt('y', VV)
        ydma(hs_scr[128 * fc:128 * fc + 128, :], kxb[:, 0:2048], LSH)
    if "hs" in taps:
        wt('g', LSH)
        godma(taps["hs"], hs_scr)

    ph9_ldo = LDO.n
    ph9_mm = MM.n

    def _kv_switch(g):
        g.wait_ge(CC.h, 2)
        g.wait_ge(LDO.h, ph9_ldo)
        g.wait_ge(MM.h, ph9_mm)
        with g.register() as r:
            g.load(r, pari[0:1, 0:1])
            idx = g.snap(r, min_val=0, max_val=1)
            for i in g.Switch(idx, 2):
                own, oth = (0, 2050) if i == 0 else (2050, 0)
                g.dma_start(out=k_scr[:, 0:2048], in_=kv_out[own:own + 1024, :]).then_inc(LDO.h, 16)
                g.dma_start(out=k_scr[:, 2048:4096], in_=kv_out[oth:oth + 1024, :]).then_inc(LDO.h, 16)
                g.dma_start(out=v_scr[0:1024, :], in_=kv_out[own + 1024:own + 2048, 0:1024]).then_inc(LDO.h, 16)
                g.dma_start(out=v_scr[1024:2048, :], in_=kv_out[own + 1024:own + 2048, 1024:2048]).then_inc(LDO.h, 16)
                g.dma_start(out=v_scr[2048:3072, :], in_=kv_out[oth + 1024:oth + 2048, 0:1024]).then_inc(LDO.h, 16)
                g.dma_start(out=v_scr[3072:4096, :], in_=kv_out[oth + 1024:oth + 2048, 1024:2048]).then_inc(LDO.h, 16)
                g.dma_start(out=R_KN[0:1, 16:32], in_=kv_out[oth + 2049:oth + 2050, 0:16]).then_inc(LDO.h, 16)
                g.dma_start(out=R_KN[0:1, 0:16], in_=kv_out[own + 2049:own + 2050, 0:16]).then_inc(LDO.h, 16)
    LDO.n += 8 * 16
    em.add('g', _kv_switch)
    KVSW_LD = LDO.n

    wt('v', LDO, KVSW_LD)
    qnb = R_KN[0:1, 32:48].rearrange("o (a s) -> o a s", s=2)
    godma(qnb[:, :, 0:1], q2m[0:1, 0:8], ns=True)
    godma(qnb[:, :, 1:2], q2m[1:2, 0:8], ns=True)
    wt('v', LDO)
    vop('tensor_max', R_KN[0:1, 48:64], R_KN[0:1, 0:16], R_KN[0:1, 16:32])
    vop('tensor_mul', R_KN[0:1, 48:64], R_KN[0:1, 48:64], R_KN[0:1, 32:48])
    wt('s', VV)
    act(R_KN[0:1, 64:80], R_KN[0:1, 48:64], AF.Sqrt)
    wt('v', SS)
    vop('tensor_scalar', out=R_KN[0:1, 64:80], in0=R_KN[0:1, 64:80], scalar1=-0.5,
        scalar2=None, op0=ALU.mult)
    wt('t', VV)
    mm(ps[:, BK(6)][:, 0:16], onesr[0:1, :], R_KN[0:1, 64:80], True, True)
    wt('v', MM)
    vop('tensor_copy', biasH[:, 0:16], ps[:, BK(6)][:, 0:16])
    vop('tensor_scalar', out=biasHK[:, 0:16], in0=biasH[:, 0:16], scalar1=klS[:, 0:1],
        scalar2=None, op0=ALU.add)
    bp1_v = biasH[:, 0:16].rearrange("p (a s) -> p a s", s=2)
    vop('tensor_tensor', biasP[:, 0:8], bp1_v[:, :, 0:1], bp1_v[:, :, 1:2], ALU.min)
    vop('tensor_scalar', out=biasPK[:, 0:8], in0=biasP[:, 0:8], scalar1=klS[:, 0:1],
        scalar2=None, op0=ALU.add)
    yodma(R_MB2, boutr)
    wt('t', LYO)
    for j in range(2):
        wt('t', VV)
        mm(ps[:, BK(6)], onesr[0:1, :], R_MB2[0:1, 512 * j:512 * j + 512], True, True)
        wt('v', MM)
        vop('tensor_copy', bcB[:, 512 * j:512 * j + 512], ps[:, BK(6)])
    kbuf = [kxb[:, 0:4096], kxb[:, 4096:8192]]
    vbuf1 = [POOL1[:, 8064:12224], POOL1[:, 12224:16384]]
    gseq = []
    for qb in range(4):
        for hg in range(8):
            gseq.append((qb, hg))
    wt('v', MM)
    vop('memset', POOL1[:, 8064:16384], 1.0)
    wt('s', LSKA)
    wt('s', LSKB)
    em.wait('g', VV)
    em.wait('g', MM)

    def l1_group_loads(g):
        if g >= 32:
            return
        qb, hg = gseq[g]
        sl = g % 2
        em.wait('g', MM, ring_cons.get(("kv1", sl), 0))
        em.wait('g', LSH)
        em.wait('g', LDO, KVSW_LD)
        gdma(kbuf[sl][:, 0:NFULL], k_scr[128 * hg:128 * hg + 128, :], WQ[sl])
        vv_ = vbuf1[sl][:, 0:4160].rearrange("p (c w) -> p c w", w=130)
        em.run('g', lambda g_, hg=hg, vv_=vv_: g_.dma_start(
            out=vv_[:, :, 0:64],
            in_=v_scr[:, 128 * hg:128 * hg + 64].rearrange("(c p) d -> p c d", p=128)),
            WQ[sl], 16)
        em.run('g', lambda g_, hg=hg, vv_=vv_: g_.dma_start(
            out=vv_[:, :, 65:129],
            in_=v_scr[:, 128 * hg + 64:128 * hg + 128].rearrange("(c p) d -> p c d", p=128)),
            WQ[sl], 16)

    l1_group_loads(0)
    for g in range(32):
        qb, hg = gseq[g]
        sl = g % 2
        seq = list(range(0, 4 * qb + 4)) + list(range(16, 32))
        nch = len(seq)
        diag0 = 4 * qb
        l1_group_loads(g + 1)
        kg, vg = kbuf[sl], vbuf1[sl]

        def qrow1(r0, r1, hg=hg, qb=qb):
            return QF[r0:r1, hg * NTOK + qb * 512: hg * NTOK + qb * 512 + 512]
        wt('t', WQ[sl])
        wt('t', VV)
        ss_after = {}
        vv_after = {}
        eA = lambda st, hh: eT[:, (2 * st + hh) * 512:(2 * st + hh) * 512 + 512]

        def scores1(cix):
            c = seq[cix]
            isB = c >= 16
            isdiag = (not isB) and (c >= diag0)
            bsrcP = biasPK if isB else biasP
            if cix >= 2:
                wt('t', SS, ss_after[cix - 2])
                if (cix - 2) in vv_after:
                    wt('t', VV, vv_after[cix - 2])
            st = cix % 2
            mm(ps[:, BK(2 * st)], kg[0:64, 128 * c:128 * c + 128],
               qrow1(0, 64), True, True, tp=(0, 0))
            mm(ps[:, BK(2 * st + 1)], kg[64:128, 128 * c:128 * c + 128],
               qrow1(64, 128), True, True, tp=(64, 0))
            wt('s', MM)
            act(eT[:, 1024 * st:1024 * st + 1024], ps[:, 1024 * st:1024 * st + 1024],
                AF.Exp, bias=bsrcP[:, hg:hg + 1])
            ss_after[cix] = SS.n
            if isdiag:
                di = c - diag0
                wt('v', SS, ss_after[cix])
                vop('tensor_mul', eA(st, 0), eA(st, 0), dmask[:, 512 * di:512 * di + 512])
                vop('tensor_mul', eA(st, 1), eA(st, 1), dmask[:, 512 * di:512 * di + 512])
                vv_after[cix] = VV.n

        def av1(cix, first, last):
            c = seq[cix]
            st = cix % 2
            wt('t', SS, ss_after[cix])
            if cix in vv_after:
                wt('t', VV, vv_after[cix])
            mm(ps[0:65, BK(4)], vg[:, 130 * c:130 * c + 65], eA(st, 0), first, last)
            mm(ps[64:128, BK(5)], vg[:, 130 * c + 65:130 * c + 129], eA(st, 1),
               first, last, tp=(0, 64), sgc=True)
            mm(ps[0:1, BK(5)], onesb[:, 1:2], eA(st, 1), False, last, tp=(0, 0), sgc=True)

        scores1(0)
        for cix in range(1, nch):
            scores1(cix)
            av1(cix - 1, cix - 1 == 0, False)
        av1(nch - 1, nch == 1, True)
        ring_cons[("kv1", sl)] = MM.n
        wt('v', MM)
        vop('tensor_copy', R_SUM[0:1, 0:512], ps[64:65, BK(4)])
        vop('tensor_copy', R_SUM[0:1, 512:1024], ps[0:1, BK(5)])
        vop('reciprocal', R_RCP, R_SUM)
        wt('t', VV)
        wt('t', SS)
        mm(ps[:, BK(6)], onesr[0:1, :], R_RCP[0:1, 0:512], True, True)
        mm(ps[:, BK(7)], onesr[0:1, :], R_RCP[0:1, 512:1024], True, True)
        wt('v', MM)
        vop('tensor_copy', bcC[:, 0:512], ps[:, BK(6)])
        vop('tensor_copy', bcC[:, 512:1024], ps[:, BK(7)])
        vop('tensor_mul', oF[0:64, 512 * hg:512 * hg + 512], ps[0:64, BK(4)],
            bcC[0:64, 0:512])
        vop('tensor_mul', oF[64:128, 512 * hg:512 * hg + 512], ps[64:128, BK(5)],
            bcC[64:128, 512:1024])
        if hg == 7:
            for col in range(2):
                for vhh in range(8):
                    wsl_ = 4096 + 512 * (vhh % 2)
                    LO = LOA if vhh % 2 == 0 else LOB
                    em.wait('g', MM, ring_cons.get(("wo", vhh % 2), 0))
                    gdma(wblk[:, wsl_:wsl_ + 512],
                         wout[128 * vhh:128 * vhh + 128, 512 * col:512 * col + 512], LO)
                    wt('t', LO)
                    if vhh == 0:
                        wt('t', VV)
                    for tc in range(4):
                        mm(ps[:, BK(tc)], oF[:, 512 * vhh + 128 * tc:512 * vhh + 128 * tc + 128],
                           wblk[:, wsl_:wsl_ + 512], vhh == 0, vhh == 7)
                    ring_cons[("wo", vhh % 2)] = MM.n
                wt('v', MM)
                for tc in range(4):
                    wt('v', LSS)
                    vop('tensor_add', POOL2[:, 1024:1536], ps[:, BK(tc)],
                        bcB[:, 512 * col:512 * col + 512])
                    wt('y', VV)
                    ydma(s_scr[512 * qb + 128 * tc:512 * qb + 128 * tc + 128,
                               512 * col:512 * col + 512], POOL2[:, 1024:1536], LSS)
    if "s_l1" in taps:
        wt('g', LSS)
        godma(taps["s_l1"], s_scr)

    s2b = stat[:, 26:27]
    wt('t', VV)
    mm(ps[:, BK(7)][:, 2:3], onesr[0:1, :], R_WM[0:1, 1:2], True, True)
    wt('v', MM)
    vop('tensor_copy', s2b, ps[:, BK(7)][:, 2:3])
    for tg in range(4):
        wt('t', VV)
        for j in range(4):
            tc = tg * 4 + j
            tr(ps[:, 2048 + 128 * j:2048 + 128 * j + 128], mxT[:, 128 * tc:128 * tc + 128])
        wt('v', MM)
        for j in range(4):
            tc = tg * 4 + j
            vop('reduce_max', stat[:, 22:23], ps[:, 2048 + 128 * j:2048 + 128 * j + 128], AX.X)
            vop('tensor_scalar_max', stat[:, 23:24], stat[:, 22:23], 1e-5)
            vop('tensor_scalar', out=c2T[:, tc:tc + 1], in0=stat[:, 23:24],
                scalar1=s2b, scalar2=1.0 / 127.0, op0=ALU.mult, op1=ALU.mult)
            vop('reciprocal', stat[:, 24:25], stat[:, 23:24])
            vop('tensor_scalar', out=sc1T[:, tc:tc + 1], in0=stat[:, 24:25],
                scalar1=127.0, scalar2=None, op0=ALU.mult)
    wt('t', VV)
    for tc in range(16):
        mm(ps[0:1, tc * 128:tc * 128 + 128], sc1T[:, tc:tc + 1], identF[:], True, True)
    wt('v', MM)
    vop('tensor_copy', rows[0:1, 0:2048], ps[0:1, 0:2048])
    wt('t', VV)
    for j in range(4):
        mm(ps[:, 2048 + 512 * j:2560 + 512 * j], onesr[0:1, :],
           rows[0:1, 512 * j:512 * j + 512], True, True)
    wt('v', MM)
    wt('v', LSS)
    sc2B = POOL2[:, 0:2048]
    vop('tensor_copy', sc2B, ps[:, 2048:4096])
    for fc in range(32):
        wt('y', VV)
        wt('y', LSH)
        ydma(kxb[:, 0:2048], hs_scr[128 * fc:128 * fc + 128, :], LYW)
        wt('v', LYW)
        vop('tensor_tensor', xti[:, 0:2048], kxb[:, 0:2048], sc2B, ALU.mult)
        vop('tensor_copy', kxb[:, 2048:4096], xti[:, 0:2048])
        wt('y', VV)
        ydma(hs_scr[128 * fc:128 * fc + 128, :], kxb[:, 2048:4096], LSH)
    for tb in range(4):
        for fc in range(32):
            sl = fc % 2
            LW = LWA if sl == 0 else LWB
            LW2 = LW2A if sl == 0 else LW2B
            em.wait('y', MM, ring_cons.get(("bl2", sl), 0))
            if tb == 0 and fc == 0:
                em.wait('y', LSH)
            ydma(wblk[:, 1024 * sl:1024 * sl + 512],
                 hs_scr[128 * fc:128 * fc + 128, 512 * tb:512 * tb + 512], LW)
            ydma(wblk[:, 2048 + 1024 * sl:2048 + 1024 * sl + 1024],
                 w2q_scr[128 * fc:128 * fc + 128, 0:1024], LW2)
            wt('t', LW)
            wt('t', LW2)
            if fc == 0:
                wt('t', SS)
                wt('t', VV)
            for tc in range(4):
                for col in range(2):
                    mm(ps[:, BK(tc * 2 + col)],
                       wblk[:, 1024 * sl + 128 * tc:1024 * sl + 128 * tc + 128],
                       wblk[:, 2048 + 1024 * sl + 512 * col:2048 + 1024 * sl + 512 * col + 512],
                       fc == 0, fc == 31)
            ring_cons[("bl2", sl)] = MM.n
        for tc in range(4):
            wt('y', VV)
            wt('y', LSS)
            wt('y', LSX1)
            ydma(xt0, x1_scr[tb * 512 + tc * 128:tb * 512 + tc * 128 + 128, :], LYX)
            ydma(xt1, s_scr[tb * 512 + tc * 128:tb * 512 + tc * 128 + 128, :], LYX)
            for col in range(2):
                wt('s', MM)
                wt('s', VV)
                act(POOL2[:, 2048:2560], ps[:, BK(tc * 2 + col)], AF.Silu,
                    scale=c2T[:, tb * 4 + tc:tb * 4 + tc + 1])
                wt('v', SS)
                wt('v', LYX)
                wt('v', STO)
                vop('tensor_mul', POOL2[:, 2560:3072], POOL2[:, 2048:2560],
                    xt1[:, 512 * col:512 * col + 512])
                vop('tensor_add', POOL2[:, 3072:3584], POOL2[:, 2560:3072],
                    xt0[:, 512 * col:512 * col + 512])
                wt('y', VV)
                em.run('y', lambda y, tb=tb, tc=tc, col=col: y.dma_start(
                    out=out[tb * 512 + tc * 128:tb * 512 + tc * 128 + 128,
                            512 * col:512 * col + 512],
                    in_=POOL2[:, 3072:3584]), STO, 16)
    em.add('g', lambda g: g.wait_ge(STO.h, STO.n))
    em.add('y', lambda y: y.wait_ge(STO.h, STO.n))

    with nc.Block() as block:
        @block.tensor
        def _(t):
            for f in em.TH['t']:
                f(t)

        @block.vector
        def _(v):
            for f in em.TH['v']:
                f(v)

        @block.scalar
        def _(s):
            for f in em.TH['s']:
                f(s)

        @block.gpsimd
        def _(g):
            for f in em.TH['g']:
                f(g)

        @block.sync
        def _(y):
            for f in em.TH['y']:
                f(y)

    es.close()
    return nc


_BUILT = {}


def kernel(**inputs):
    in_maps = [_prep_core(c, inputs) for c in range(8)]
    nc = build()
    res = run_bass_kernel_spmd(nc, in_maps, core_ids=list(range(8)))
    out = np.empty((4, 4096, 1024), np.float32)
    for c in range(8):
        b, h = c >> 1, c & 1
        out[b, 2048 * h:2048 * h + 2048] = np.asarray(res.results[c]["out"])
    return out


# version 11 (final)
# speedup vs baseline: 1.0498x; 1.0043x over previous
import contextlib
import numpy as np

import concourse.bass as bass
import concourse.mybir as mybir
from concourse.bass_utils import run_bass_kernel_spmd

F32 = mybir.dt.float32
BF16 = mybir.dt.bfloat16
I32 = mybir.dt.int32
AF = mybir.ActivationFunctionType
ALU = mybir.AluOpType
AX = mybir.AxisListType

NTOK = 2048
NFULL = 4096
DM = 1024
DFF = 4096
NTC = NTOK // 128
KILL = -30000.0

DEV_TAPS = ()


def _rope_tables(h, dtype):
    i = np.arange(32, dtype=np.float64)
    inv = 1.0 / (10000.0 ** (2.0 * i / 64.0))
    pos = np.arange(2048 * h, 2048 * h + 2048, dtype=np.float64)
    rows = np.arange(128)
    fr = pos[None, :] * inv[(rows % 64) // 2][:, None]
    cos = np.cos(fr)
    sgn = np.where(rows % 2 == 0, -1.0, 1.0)[:, None]
    sin = np.sin(fr) * sgn
    return cos.astype(dtype), sin.astype(dtype)


_SHARED_CACHE = {}


def _prep_shared(inp):
    key = id(inp.get("L0_attn_Wq"))
    if _SHARED_CACHE.get("key") == key:
        return _SHARED_CACHE["d"]
    f32 = np.float32
    import ml_dtypes
    d = {}
    d["wq"] = np.ascontiguousarray(inp["L0_attn_Wq"], f32)
    d["wk"] = np.ascontiguousarray(inp["L0_attn_Wk"], f32)
    d["wv"] = np.ascontiguousarray(inp["L0_attn_Wv"], f32)
    d["wo"] = np.ascontiguousarray(inp["L0_attn_Wo"], f32)
    d["lamv"] = np.concatenate([inp["L0_attn_lq1"], inp["L0_attn_lk1"],
                                inp["L0_attn_lq2"], inp["L0_attn_lk2"]]).reshape(1, 256).astype(f32)
    d["lng"] = np.ascontiguousarray(inp["L0_attn_ln_g"].reshape(128, 1), f32)
    d["lnb"] = np.ascontiguousarray(inp["L0_attn_ln_b"].reshape(128, 1), f32)
    d["gw1T"] = np.ascontiguousarray(np.asarray(inp["L0_ffn_gate_w1"], f32).T)
    d["gb1T"] = np.ascontiguousarray(np.asarray(inp["L0_ffn_gate_b1"], f32).reshape(8, 128).T)
    d["gw2T"] = np.ascontiguousarray(np.asarray(inp["L0_ffn_gate_w2"], f32).T)
    d["gb2c"] = np.ascontiguousarray(np.asarray(inp["L0_ffn_gate_b2"], f32).reshape(8, 1))
    d["w1T"] = np.ascontiguousarray(np.asarray(inp["L0_ffn_W1"], f32).T)
    d["tw1T"] = np.ascontiguousarray(np.asarray(inp["L0_ffn_TW1"], f32).transpose(0, 2, 1))
    d["b1r"] = np.ascontiguousarray(np.asarray(inp["L0_ffn_b1"], f32).reshape(1, 4096))
    d["b1T"] = np.ascontiguousarray(np.asarray(inp["L0_ffn_b1"], f32).reshape(32, 128).T)
    d["tb1"] = np.ascontiguousarray(inp["L0_ffn_Tb1"], f32)
    d["w2T"] = np.ascontiguousarray(np.asarray(inp["L0_ffn_W2"], f32).T)
    d["tw2T"] = np.ascontiguousarray(np.asarray(inp["L0_ffn_TW2"], f32).transpose(0, 2, 1))
    d["b2r"] = np.ascontiguousarray(np.asarray(inp["L0_ffn_b2"], f32).reshape(1, 1024))
    d["tb2"] = np.ascontiguousarray(inp["L0_ffn_Tb2"], f32)
    d["wqkv"] = np.ascontiguousarray(inp["L1_attn_Wqkv"], f32)
    d["wout"] = np.ascontiguousarray(inp["L1_attn_Wout"], f32)
    d["boutr"] = np.ascontiguousarray(np.asarray(inp["L1_attn_bout"], f32).reshape(1, 1024))
    d["l1w1T"] = np.ascontiguousarray(np.asarray(inp["L1_ffn_w1"], f32).T)
    d["l1w2T"] = np.ascontiguousarray(np.asarray(inp["L1_ffn_w2"], f32).T)
    cw = np.asarray(inp["L1_ffn_conv_w"], f32)[:, 0, :]
    for j in range(3):
        d[f"cw{j}"] = np.ascontiguousarray(cw[:, j].reshape(32, 128).T)
    d["cb"] = np.ascontiguousarray(np.asarray(inp["L1_ffn_conv_b"], f32).reshape(32, 128).T)
    kk = np.arange(128)[:, None]
    qq = np.arange(512)[None, :]
    dmv = np.stack([(kk + 128 * i <= qq) for i in range(4)], 0).astype(f32)
    d["dmaskT"] = np.ascontiguousarray(dmv.reshape(4 * 128, 512)).astype(ml_dtypes.bfloat16)
    d["identT"] = np.eye(128, dtype=f32).astype(ml_dtypes.bfloat16)
    d["identF"] = np.eye(128, dtype=f32)
    d["hs2T"] = np.ascontiguousarray(
        np.stack([np.arange(128) < 64, np.arange(128) >= 64], 1).astype(f32)).astype(ml_dtypes.bfloat16)
    _SHARED_CACHE["key"] = key
    _SHARED_CACHE["d"] = d
    return d


def _prep_core(c, inp):
    b, h = c >> 1, c & 1
    own = slice(2048 * h, 2048 * h + 2048)
    oth = slice(2048 * (1 - h), 2048 * (1 - h) + 2048)
    f32 = np.float32
    x = np.asarray(inp["L0_attn_x"][b], f32)
    d = dict(_prep_shared(inp))
    d["x0"] = np.ascontiguousarray(np.concatenate([x[own], x[oth]], 0))
    cos, sin = _rope_tables(h, f32)
    d["ropec"] = cos
    d["ropes"] = sin
    d["parity"] = np.array([[h]], np.int32)
    d["killpb"] = np.full((128, 1), KILL * (1 - h), f32)
    d["halom"] = np.tile(np.array([[float(h), float(1 - h)]], f32), (128, 1))
    return d


class Sem:
    def __init__(self, h):
        self.h = h
        self.n = 0


class Em:
    def __init__(self):
        self.TH = {k: [] for k in "tvsgy"}

    def add(self, e, fn):
        self.TH[e].append(fn)

    def wait(self, e, sem, val=None):
        v = sem.n if val is None else val
        if v > 0:
            self.add(e, lambda E, h=sem.h, v=v: E.wait_ge(h, v))

    def run(self, e, fn, sem=None, k=1, selfwait=False):
        if sem is not None:
            if selfwait and sem.n > 0:
                v0 = sem.n
                self.add(e, lambda E, h=sem.h, v0=v0: E.wait_ge(h, v0))
            sem.n += k
            self.add(e, lambda E, fn=fn, h=sem.h, k=k: fn(E).then_inc(h, k))
        else:
            self.add(e, fn)


def build():
    nc = bass.Bass("TRN2", target_bir_lowering=False)
    es = contextlib.ExitStack()

    ap = lambda t: t.ap()
    x0 = ap(nc.dram_tensor("x0", [NFULL, DM], F32, kind="ExternalInput"))
    wq = ap(nc.dram_tensor("wq", [DM, DM], F32, kind="ExternalInput"))
    wk = ap(nc.dram_tensor("wk", [DM, DM], F32, kind="ExternalInput"))
    wv = ap(nc.dram_tensor("wv", [DM, DM], F32, kind="ExternalInput"))
    wo = ap(nc.dram_tensor("wo", [DM, DM], F32, kind="ExternalInput"))
    lamv = ap(nc.dram_tensor("lamv", [1, 256], F32, kind="ExternalInput"))
    lng = ap(nc.dram_tensor("lng", [128, 1], F32, kind="ExternalInput"))
    lnb = ap(nc.dram_tensor("lnb", [128, 1], F32, kind="ExternalInput"))
    gw1T = ap(nc.dram_tensor("gw1T", [DM, DM], F32, kind="ExternalInput"))
    gb1T = ap(nc.dram_tensor("gb1T", [128, 8], F32, kind="ExternalInput"))
    gw2T = ap(nc.dram_tensor("gw2T", [DM, 8], F32, kind="ExternalInput"))
    gb2c = ap(nc.dram_tensor("gb2c", [8, 1], F32, kind="ExternalInput"))
    w1T = ap(nc.dram_tensor("w1T", [DM, DFF], F32, kind="ExternalInput"))
    tw1T = ap(nc.dram_tensor("tw1T", [8, DM, DFF], F32, kind="ExternalInput"))
    b1r = ap(nc.dram_tensor("b1r", [1, DFF], F32, kind="ExternalInput"))
    b1T = ap(nc.dram_tensor("b1T", [128, 32], F32, kind="ExternalInput"))
    tb1 = ap(nc.dram_tensor("tb1", [8, DFF], F32, kind="ExternalInput"))
    w2T = ap(nc.dram_tensor("w2T", [DFF, DM], F32, kind="ExternalInput"))
    tw2T = ap(nc.dram_tensor("tw2T", [8, DFF, DM], F32, kind="ExternalInput"))
    b2r = ap(nc.dram_tensor("b2r", [1, DM], F32, kind="ExternalInput"))
    tb2 = ap(nc.dram_tensor("tb2", [8, DM], F32, kind="ExternalInput"))
    wqkv = ap(nc.dram_tensor("wqkv", [DM, 3 * DM], F32, kind="ExternalInput"))
    wout = ap(nc.dram_tensor("wout", [DM, DM], F32, kind="ExternalInput"))
    boutr = ap(nc.dram_tensor("boutr", [1, DM], F32, kind="ExternalInput"))
    l1w1T = ap(nc.dram_tensor("l1w1T", [DM, DFF], F32, kind="ExternalInput"))
    l1w2T = ap(nc.dram_tensor("l1w2T", [DFF, DM], F32, kind="ExternalInput"))
    cw0 = ap(nc.dram_tensor("cw0", [128, 32], F32, kind="ExternalInput"))
    cw1 = ap(nc.dram_tensor("cw1", [128, 32], F32, kind="ExternalInput"))
    cw2 = ap(nc.dram_tensor("cw2", [128, 32], F32, kind="ExternalInput"))
    cb = ap(nc.dram_tensor("cb", [128, 32], F32, kind="ExternalInput"))
    ropec = ap(nc.dram_tensor("ropec", [128, NTOK], F32, kind="ExternalInput"))
    ropes = ap(nc.dram_tensor("ropes", [128, NTOK], F32, kind="ExternalInput"))
    parity = ap(nc.dram_tensor("parity", [1, 1], I32, kind="ExternalInput"))
    killpb = ap(nc.dram_tensor("killpb", [128, 1], F32, kind="ExternalInput"))
    halom = ap(nc.dram_tensor("halom", [128, 2], F32, kind="ExternalInput"))
    dmaskT = ap(nc.dram_tensor("dmaskT", [4 * 128, 512], BF16, kind="ExternalInput"))
    identT = ap(nc.dram_tensor("identT", [128, 128], BF16, kind="ExternalInput"))
    identFT = ap(nc.dram_tensor("identF", [128, 128], F32, kind="ExternalInput"))
    hs2T = ap(nc.dram_tensor("hs2T", [128, 2], BF16, kind="ExternalInput"))
    out = ap(nc.dram_tensor("out", [NTOK, DM], F32, kind="ExternalOutput"))

    k_scr = ap(nc.dram_tensor("k_scr", [DM, NFULL], BF16))
    v_scr = ap(nc.dram_tensor("v_scr", [NFULL, DM], BF16))
    s_scr = ap(nc.dram_tensor("s_scr", [NTOK, DM], F32))
    x1_scr = ap(nc.dram_tensor("x1_scr", [NTOK, DM], F32))
    mw1_scr = ap(nc.dram_tensor("mw1_scr", [DM, DFF], BF16))
    mw2_scr = ap(nc.dram_tensor("mw2_scr", [DFF, DM], BF16))
    w1q_scr = ap(nc.dram_tensor("w1q_scr", [DM, DFF], BF16))
    w2q_scr = ap(nc.dram_tensor("w2q_scr", [DFF, DM], BF16))
    hs_scr = ap(nc.dram_tensor("hs_scr", [DFF, NTOK], BF16))
    kv_in = ap(nc.dram_tensor("kv_in", [2050, NTOK], BF16))
    kv_out = ap(nc.dram_tensor("kv_out", [4100, NTOK], BF16))
    halo_in = ap(nc.dram_tensor("halo_in", [2, DM], F32))
    halo_out = ap(nc.dram_tensor("halo_out", [4, DM], F32))
    taps = {}
    for t_ in DEV_TAPS:
        shp = {"xn0": [128, 8 * NTOK], "g8": [8, 1], "s_l0": [NTOK, DM],
               "x1": [NTOK, DM], "mw1": [DM, DFF], "s_l1": [NTOK, DM],
               "hs": [DFF, NTOK], "kq": [128, 16], "w1q": [DM, DFF]}[t_]
        dt_ = BF16 if t_ in ("mw1", "hs", "w1q") else F32
        taps[t_] = ap(nc.dram_tensor("tap_" + t_, shp, dt_, kind="ExternalOutput"))

    sb = lambda n, shp, dt: es.enter_context(nc.sbuf_tensor(n, shp, dt))
    xnF = sb("xnF", [128, 8 * NTOK], BF16)
    QF = sb("QF", [128, 8 * NTOK], BF16)
    POOL1 = sb("POOL1", [128, 32 * 512], BF16)
    POOL2 = sb("POOL2", [128, 4096], F32)
    POOL3 = sb("POOL3", [128, 4100], F32)
    wblk = sb("wblk", [128, 10 * 512], BF16)
    kxb = sb("kxb", [128, 2 * NFULL], BF16)
    eT = sb("eT", [128, 2048], BF16)
    oF = sb("oF", [128, 8 * 512], BF16)
    bcB = sb("bcB", [128, 1024], F32)
    bcC = sb("bcC", [128, 1024], F32)
    xti = sb("xti", [128, 2048], I32)
    mxT = sb("mxT", [128, NTOK], BF16)
    dmask = sb("dmask", [128, 4 * 512], BF16)
    rows = sb("rows", [4, 3104], F32)
    stat = sb("stat", [128, 64], F32)
    gwS = sb("gwS", [128, 64], BF16)
    ctap = sb("ctap", [128, 128], F32)
    ident = sb("ident", [128, 128], BF16)
    identF = sb("identFs", [128, 128], F32)
    hs2 = sb("hs2", [128, 2], BF16)
    onesb = sb("onesb", [128, 2], BF16)
    onesw = sb("onesw", [128, 128], BF16)
    biasP = sb("biasP", [128, 8], F32)
    biasPK = sb("biasPK", [128, 8], F32)
    onesf = sb("onesf", [128, 1], F32)
    onesr = sb("onesr", [1, 128], F32)
    mb1T = sb("mb1T", [128, 32], F32)
    biasH = sb("biasH", [128, 16], F32)
    biasHK = sb("biasHK", [128, 16], F32)
    q2m = sb("q2m", [2, 16], F32)
    k2m = sb("k2m", [2, 16], F32)
    lamS = sb("lamS", [128, 1], F32)
    klS = sb("klS", [128, 1], F32)
    hmS = sb("hmS", [128, 2], F32)
    sc1T = sb("sc1T", [128, 16], F32)
    c2T = sb("c2T", [128, 16], F32)
    xqh = sb("xqh", [128, 8], BF16)
    g8c = sb("g8c", [8, 1], F32)
    ps = es.enter_context(nc.psum_tensor("ps", [128, 4096], F32))

    kgrp = kxb
    vgrp = wblk
    uBlk = POOL1
    xqF = POOL1
    R_SUM = rows[0:1, 0:1024]
    R_RCP = rows[0:1, 1024:2048]
    R_MU = rows[0:1, 2048:2560]
    R_VAR = rows[0:1, 2560:3072]
    R_LAM = rows[0:1, 0:544]
    R_NEGB = rows[0:1, 0:16]
    R_G8R = rows[0:1, 16:24]
    R_B1 = rows[0:1, 24:152]
    R_MB2 = rows[0:1, 1024:2048]
    R_KN = rows[0:1, 0:80]
    R_KNOWN = rows[2:3, 1024:1040]
    R_HFIRST = rows[2:3, 0:1024]
    R_HLAST = rows[3:4, 0:1024]
    R_WM = rows[0:1, 3072:3080]
    R_HS = rows[0:1, 3080:3088]

    BK = lambda k: slice(512 * k, 512 * k + 512)

    MM = Sem(es.enter_context(nc.semaphore("MM")))
    SS = Sem(es.enter_context(nc.semaphore("SS")))
    VV = Sem(es.enter_context(nc.semaphore("VV")))
    CC = Sem(es.enter_context(nc.semaphore("CC")))
    STO = Sem(es.enter_context(nc.semaphore("STO")))
    WQA = Sem(es.enter_context(nc.semaphore("WQA")))
    WQB = Sem(es.enter_context(nc.semaphore("WQB")))
    LDW = Sem(es.enter_context(nc.semaphore("LDW")))
    LDO = Sem(es.enter_context(nc.semaphore("LDO")))
    MWA = Sem(es.enter_context(nc.semaphore("MWA")))
    MWB = Sem(es.enter_context(nc.semaphore("MWB")))
    LYC = Sem(es.enter_context(nc.semaphore("LYC")))
    LYX = Sem(es.enter_context(nc.semaphore("LYX")))
    LYW = Sem(es.enter_context(nc.semaphore("LYW")))
    LYO = Sem(es.enter_context(nc.semaphore("LYO")))
    LSWA = Sem(es.enter_context(nc.semaphore("LSWA")))
    LSWB = Sem(es.enter_context(nc.semaphore("LSWB")))
    LSKA = Sem(es.enter_context(nc.semaphore("LSKA")))
    LSKB = Sem(es.enter_context(nc.semaphore("LSKB")))
    LSS = Sem(es.enter_context(nc.semaphore("LSS")))
    LSX1 = Sem(es.enter_context(nc.semaphore("LSX1")))
    LSH = Sem(es.enter_context(nc.semaphore("LSH")))
    MSA = Sem(es.enter_context(nc.semaphore("MSA")))
    MSB = Sem(es.enter_context(nc.semaphore("MSB")))
    WVA = Sem(es.enter_context(nc.semaphore("WVA")))
    WVB = Sem(es.enter_context(nc.semaphore("WVB")))
    LWA = Sem(es.enter_context(nc.semaphore("LWA")))
    LWB = Sem(es.enter_context(nc.semaphore("LWB")))
    LW2A = Sem(es.enter_context(nc.semaphore("LW2A")))
    LW2B = Sem(es.enter_context(nc.semaphore("LW2B")))
    LOA = Sem(es.enter_context(nc.semaphore("LOA")))
    LOB = Sem(es.enter_context(nc.semaphore("LOB")))

    em = Em()
    ring_cons = {}

    def gdma(o, i, sem):
        em.run('g', lambda g, o=o, i=i: g.dma_start(out=o, in_=i), sem, 16)

    def godma(o, i, ns=False):
        em.wait('g', LDO)
        em.run('g', lambda g, o=o, i=i, ns=ns: g.dma_start(
            out=o, in_=i, allow_slow_non_contiguous=ns), LDO, 16)

    def ydma(o, i, sem):
        em.run('y', lambda y, o=o, i=i: y.dma_start(out=o, in_=i), sem, 16)

    def yodma(o, i):
        em.wait('y', LYO)
        em.run('y', lambda y, o=o, i=i: y.dma_start(out=o, in_=i), LYO, 16)

    def mm(o, l, r, st, sp, tp=None, sgc=False):
        em.run('t', lambda t, o=o, l=l, r=r, st=st, sp=sp, tp=tp, sgc=sgc:
               t.matmul(o, l, r, start=st, stop=sp, tile_position=tp,
                        skip_group_check=sgc), MM)

    def tr(o, i):
        mm(o, i, ident[:], True, True)

    def act(o, i, f, bias=0.0, scale=1.0, accum=None):
        em.run('s', lambda s, o=o, i=i, f=f, bias=bias, scale=scale, accum=accum:
               s.activation(out=o, in_=i, func=f, bias=bias, scale=scale,
                            accum_out=accum), SS, selfwait=True)

    def vop(name, *a, **kw):
        em.run('v', lambda v, name=name, a=a, kw=kw: getattr(v, name)(*a, **kw), VV,
               selfwait=True)

    def wt(e, sem, val=None):
        em.wait(e, sem, val)

    vop('memset', onesb[:], 1.0)
    vop('memset', onesf[:], 1.0)
    vop('memset', onesr[:], 1.0)
    vop('memset', q2m[:], 0.0)
    vop('memset', k2m[:], 0.0)
    vop('memset', stat[0:8, 20:22], 0.0)
    vop('memset', mxT[:], 0.0)
    vop('memset', onesw[:], 1.0)
    vop('memset', stat[:, 19:20], 1e-5)
    ydma(ident[:], identT, LYC)
    ydma(identF[:], identFT, LYC)
    ydma(hs2[:], hs2T, LYC)
    ydma(stat[:, 40:41], lng, LYC)
    ydma(stat[:, 41:42], lnb, LYC)
    ydma(klS[:], killpb, LYC)
    ydma(hmS[:], halom, LYC)
    ydma(ctap[:, 0:32], cw0, LYC)
    ydma(ctap[:, 32:64], cw1, LYC)
    ydma(ctap[:, 64:96], cw2, LYC)
    ydma(ctap[:, 96:128], cb, LYC)
    for j in range(4):
        ydma(dmask[:, 512 * j:512 * j + 512], dmaskT[128 * j:128 * j + 128, :], LYC)
    ydma(R_LAM[0:1, 0:256], lamv, LYC)
    lngS, lnbS = stat[:, 40:41], stat[:, 41:42]
    g8aP = stat[0:8, 21:22]

    lam_init = float(0.8 - 0.6 * np.exp(-0.3 * 12))
    wt('v', LYC)
    vop('tensor_mul', R_LAM[0:1, 256:320], R_LAM[0:1, 0:64], R_LAM[0:1, 64:128])
    vop('tensor_mul', R_LAM[0:1, 320:384], R_LAM[0:1, 128:192], R_LAM[0:1, 192:256])
    vop('reduce_sum', R_LAM[0:1, 384:385], R_LAM[0:1, 256:320], AX.X)
    vop('reduce_sum', R_LAM[0:1, 385:386], R_LAM[0:1, 320:384], AX.X)
    wt('s', VV)
    act(R_LAM[0:1, 386:388], R_LAM[0:1, 384:386], AF.Exp)
    wt('v', SS)
    vop('scalar_tensor_tensor', out=R_LAM[0:1, 388:389], in0=R_LAM[0:1, 387:388],
        scalar=-lam_init, in1=R_LAM[0:1, 386:387], op0=ALU.add, op1=ALU.subtract)
    wt('t', VV)
    mm(ps[:, BK(7)][:, 0:1], onesr[0:1, :], R_LAM[0:1, 388:389], True, True)
    wt('v', MM)
    vop('tensor_copy', lamS[:, 0:1], ps[:, BK(7)][:, 0:1])

    def wq_tiles(w):
        r, cdim = w.shape
        cs = min(cdim, 2048)
        return [(w[128 * i:128 * i + 128, cs * j:cs * j + cs], cs)
                for i in range(r // 128) for j in range(cdim // cs)]

    for wi, (wsrc, wdst) in enumerate([(l1w1T, w1q_scr), (l1w2T, w2q_scr)]):
        tiles = wq_tiles(wsrc)
        dtiles = wq_tiles(wdst)
        sumc = 43 + wi
        vop('memset', stat[:, sumc:sumc + 1], 0.0)
        for idx, (tin, cs) in enumerate(tiles):
            sl = idx % 2
            wqs = WQA if sl == 0 else WQB
            em.wait('g', VV, ring_cons.get(("wq", sl), 0))
            em.wait('g', (LSWA if sl == 0 else LSWB), ring_cons.get(("wqs", sl), 0))
            gdma(POOL1[:, 2048 * sl:2048 * sl + cs], tin, wqs)
            wt('v', wqs)
            vop('reduce_sum', stat[:, 47:48], POOL1[:, 2048 * sl:2048 * sl + cs], AX.X,
                apply_absolute_value=True)
            vop('tensor_add', stat[:, sumc:sumc + 1], stat[:, sumc:sumc + 1], stat[:, 47:48])
            ring_cons[("wq", sl)] = VV.n
        wt('t', VV)
        mm(ps[0:1, BK(7)][:, 0:1], onesf[:, 0:1], stat[:, sumc:sumc + 1], True, True)
        wt('v', MM)
        nel = float(wsrc.shape[0] * wsrc.shape[1])
        vop('tensor_scalar', out=R_WM[0:1, wi:wi + 1], in0=ps[0:1, BK(7)][:, 0:1],
            scalar1=1.0 / nel, scalar2=1e-5, op0=ALU.mult, op1=ALU.max)
        vop('reciprocal', R_WM[0:1, 2 + wi:3 + wi], R_WM[0:1, wi:wi + 1])
        wt('t', VV)
        mm(ps[:, BK(7)][:, 0:1], onesr[0:1, :], R_WM[0:1, 2 + wi:3 + wi], True, True)
        wt('v', MM)
        vop('tensor_copy', stat[:, 45 + wi * 10:46 + wi * 10], ps[:, BK(7)][:, 0:1])
        scol = stat[:, 45 + wi * 10:46 + wi * 10]
        for idx, (tin, cs) in enumerate(tiles):
            sl = idx % 2
            wqs = WQA if sl == 0 else WQB
            em.wait('g', VV, ring_cons.get(("wq", sl), 0))
            em.wait('g', (LSWA if sl == 0 else LSWB), ring_cons.get(("wqs", sl), 0))
            gdma(POOL1[:, 2048 * sl:2048 * sl + cs], tin, wqs)
            wt('v', wqs)
            src_ = POOL1[:, 2048 * sl:2048 * sl + cs]
            tpos = POOL1[:, 4096:4096 + cs]
            tneg = POOL1[:, 6144:6144 + cs]
            vop('tensor_scalar', out=tpos, in0=src_, scalar1=scol, scalar2=0.5,
                op0=ALU.mult, op1=ALU.is_ge)
            vop('tensor_scalar', out=tneg, in0=src_, scalar1=scol, scalar2=-0.5,
                op0=ALU.mult, op1=ALU.is_le)
            vop('tensor_sub', src_, tpos, tneg)
            ring_cons[("wq", sl)] = VV.n
            wt('y', VV)
            lsw = LSWA if sl == 0 else LSWB
            ydma(dtiles[idx][0], src_, lsw)
            ring_cons[("wqs", sl)] = lsw.n
    if "w1q" in taps:
        wt('g', LSWA)
        wt('g', LSWB)
        godma(taps["w1q"], w1q_scr)

    xt0 = POOL3[:, 0:1024]
    xt1 = POOL3[:, 1024:2048]
    xt2 = POOL3[:, 2048:3072]

    def ln_tile(src_rows, want_f32=False):
        wt('y', VV)
        ydma(xt0, src_rows, LYX)
        wt('v', LYX)
        vop('bn_stats', stat[:, 0:6], xt0[:, 0:512])
        vop('bn_stats', stat[:, 6:12], xt0[:, 512:1024])
        vop('bn_aggr', stat[:, 12:14], stat[:, 0:12])
        wt('s', VV)
        act(stat[:, 14:15], stat[:, 13:14], AF.Sqrt, bias=stat[:, 19:20])
        wt('v', SS)
        vop('reciprocal', stat[:, 15:16], stat[:, 14:15])
        vop('tensor_scalar', out=stat[:, 16:17], in0=stat[:, 12:13],
            scalar1=stat[:, 15:16], scalar2=-1.0, op0=ALU.mult, op1=ALU.mult)
        wt('v', MM)
        vop('tensor_scalar', out=eT[:, 0:1024], in0=xt0,
            scalar1=stat[:, 15:16], scalar2=stat[:, 16:17], op0=ALU.mult, op1=ALU.add)
        if want_f32:
            vop('tensor_scalar', out=xt2, in0=xt0,
                scalar1=stat[:, 15:16], scalar2=stat[:, 16:17], op0=ALU.mult, op1=ALU.add)

    def ln_transpose(dst_col_of_chunk, src=None):
        wt('t', VV)
        wt('t', SS)
        for j in range(8):
            tr(ps[:, j * 128:j * 128 + 128], (src or eT)[:, j * 128:j * 128 + 128])
        wt('s', MM)
        for j in range(8):
            act(dst_col_of_chunk(j), ps[:, j * 128:j * 128 + 128], AF.Copy)

    for tc in range(NTC):
        ln_tile(x0[128 * tc:128 * tc + 128, :])
        ln_transpose(lambda j, tc=tc: xnF[:, j * NTOK + tc * 128: j * NTOK + tc * 128 + 128])
    XNF_OWN_SS = SS.n

    def load_colblock(seg, w, c0, cw):
        em.wait('g', MM)
        for dinc in range(8):
            gdma(wblk[:, seg + dinc * cw: seg + dinc * cw + cw],
                 w[128 * dinc:128 * dinc + 128, c0:c0 + cw], LDW)

    def xsrc(tb, dinc, j0, jn):
        if tb < 4:
            return xnF[:, dinc * NTOK + tb * 512 + j0: dinc * NTOK + tb * 512 + j0 + jn]
        return kxb[:, dinc * 512 + j0: dinc * 512 + j0 + jn]

    kv_stage = [eT[:, 1024:1536], eT[:, 1536:2048]]
    sq_tile = oF[:, 512:1024]
    gb1S = stat[:, 48:56]
    yodma(gb1S, gb1T)
    for hc in range(8):
        godma(gwS[:, hc * 8:hc * 8 + 8], gw2T[128 * hc:128 * hc + 128, :])
    wt('s', LYO)
    wt('t', LDO)
    pwc = [0]
    WQ = [WQA, WQB]
    WV = [WVA, WVB]
    em.wait('g', VV)

    def pw_load(w, c0):
        sl = pwc[0] % 2
        pwc[0] += 1
        em.wait('g', MM, ring_cons.get(("pw", sl), 0))
        em.run('g', lambda g, w=w, c0=c0, sl=sl: g.dma_start(
            out=wblk[:, 1024 * sl:1024 * sl + 1024].rearrange("p (d c) -> p d c", c=128),
            in_=w[:, c0:c0 + 128].rearrange("(d p) c -> p d c", p=128)), WQ[sl], 16)
        return sl

    for tb in range(8):
        if tb >= 4:
            for j in range(4):
                ln_tile(x0[2048 + 512 * (tb - 4) + 128 * j:
                           2048 + 512 * (tb - 4) + 128 * j + 128, :])
                ln_transpose(lambda dc, j=j: kxb[:, dc * 512 + j * 128: dc * 512 + j * 128 + 128])
        for which, wmat in (("k", wk), ("q", wq)):
            if which == "q" and tb >= 4:
                continue
            for dc in range(8):
                sl = pw_load(wmat, dc * 128)
                bank = 2 + (dc % 2)
                wt('t', WQ[sl])
                wt('t', SS)
                wt('t', VV)
                for dinc in range(8):
                    mm(ps[:, BK(bank)], wblk[:, 1024 * sl + dinc * 128: 1024 * sl + dinc * 128 + 128],
                       xsrc(tb, dinc, 0, 512), dinc == 0, dinc == 7)
                ring_cons[("pw", sl)] = MM.n
                wt('s', MM)
                wt('s', LSKA if dc % 2 == 0 else LSKB)
                stg = kv_stage[dc % 2]
                if which == "q":
                    act(QF[:, dc * NTOK + tb * 512: dc * NTOK + tb * 512 + 512],
                        ps[:, BK(bank)], AF.Copy, scale=0.125)
                    act(sq_tile, ps[:, BK(bank)], AF.Square, scale=0.125)
                else:
                    act(stg, ps[:, BK(bank)], AF.Copy)
                    act(sq_tile, ps[:, BK(bank)], AF.Square)
                    wt('y', SS)
                    ydma(k_scr[128 * dc:128 * dc + 128, 512 * tb:512 * tb + 512], stg,
                         LSKA if dc % 2 == 0 else LSKB)
                wt('t', SS)
                wt('t', VV)
                mm(ps[0:2, BK(5)], hs2[:], sq_tile, True, True)
                wt('v', MM)
                dstm = q2m if which == "q" else k2m
                vop('reduce_max', stat[0:2, 18:19], ps[0:2, BK(5)], AX.X)
                vop('tensor_max', dstm[0:2, dc:dc + 1], dstm[0:2, dc:dc + 1], stat[0:2, 18:19])
        vbanks = [0, 1, 6, 7]
        for col in range(2):
            for hc in range(8):
                wsl = hc % 2
                em.wait('g', MM, ring_cons.get(("wv", wsl), 0))
                gdma(wblk[:, 3072 + 512 * wsl: 3072 + 512 * wsl + 512],
                     wv[128 * hc:128 * hc + 128, col * 512: col * 512 + 512], WV[wsl])
                wt('t', WV[wsl])
                if hc == 0:
                    wt('t', SS)
                    wt('t', VV)
                for tc in range(4):
                    mm(ps[:, BK(vbanks[tc])], xsrc(tb, hc, tc * 128, 128),
                       wblk[:, 3072 + 512 * wsl: 3072 + 512 * wsl + 512],
                       hc == 0, hc == 7)
                ring_cons[("wv", wsl)] = MM.n
            wt('s', MM)
            for tc in range(4):
                stg = kv_stage[tc % 2]
                wt('s', LSKA if tc % 2 == 0 else LSKB)
                act(stg, ps[:, BK(vbanks[tc])], AF.Copy)
                wt('y', SS)
                ydma(v_scr[512 * tb + 128 * tc: 512 * tb + 128 * tc + 128,
                           col * 512: col * 512 + 512], stg,
                     LSKA if tc % 2 == 0 else LSKB)
        for hc in range(8):
            sl = pw_load(gw1T, hc * 128)
            bank = 2 + (hc % 2)
            wt('t', WQ[sl])
            wt('t', SS)
            for dinc in range(8):
                mm(ps[:, BK(bank)], wblk[:, 1024 * sl + dinc * 128: 1024 * sl + dinc * 128 + 128],
                   xsrc(tb, dinc, 0, 512), dinc == 0, dinc == 7)
            ring_cons[("pw", sl)] = MM.n
            wt('s', MM)
            act(oF[:, 0:512], ps[:, BK(bank)], AF.Relu, bias=gb1S[:, hc:hc + 1])
            wt('t', SS)
            mm(ps[0:8, BK(4)], gwS[:, hc * 8:hc * 8 + 8], oF[:, 0:512], hc == 0, hc == 7)
        wt('s', MM)
        act(bcC[0:8, 0:512], ps[0:8, BK(4)], AF.Copy, accum=stat[0:8, 20:21])
        wt('v', SS)
        vop('tensor_add', g8aP, g8aP, stat[0:8, 20:21])
    gb2S = stat[0:8, 57:58]
    yodma(gb2S, gb2c)
    wt('v', LYO)
    vop('tensor_scalar', out=g8c[0:8, 0:1], in0=g8aP, scalar1=1.0 / NFULL,
        scalar2=gb2S, op0=ALU.mult, op1=ALU.add)
    gBc = stat[:, 32:40]
    wt('t', VV)
    mm(ps[0:1, BK(5)][:, 0:8], g8c[0:8, 0:1], identF[0:8, 0:8], True, True)
    wt('v', MM)
    vop('tensor_copy', R_G8R, ps[0:1, BK(5)][:, 0:8])
    wt('t', VV)
    mm(ps[:, BK(6)][:, 0:8], onesr[0:1, :], R_G8R, True, True)
    wt('v', MM)
    vop('tensor_copy', gBc, ps[:, BK(6)][:, 0:8])
    vop('tensor_mul', q2m[0:2, 8:16], q2m[0:2, 0:8], k2m[0:2, 0:8])
    wt('s', VV)
    act(k2m[0:2, 8:16], q2m[0:2, 8:16], AF.Sqrt)
    wt('v', SS)
    vop('tensor_scalar', out=q2m[0:2, 8:16], in0=k2m[0:2, 8:16], scalar1=-0.5, op0=ALU.mult, scalar2=None)
    wt('g', VV)
    nb_v = R_NEGB.rearrange("o (a s) -> o a s", s=2)
    godma(nb_v[:, :, 0:1], q2m[0:1, 8:16], ns=True)
    godma(nb_v[:, :, 1:2], q2m[1:2, 8:16], ns=True)
    wt('t', LDO)
    mm(ps[:, BK(6)][:, 0:16], onesr[0:1, :], R_NEGB, True, True)
    wt('v', MM)
    vop('tensor_copy', biasH[:, 0:16], ps[:, BK(6)][:, 0:16])
    bp_v = biasH[:, 0:16].rearrange("p (a s) -> p a s", s=2)
    vop('tensor_tensor', biasP[:, 0:8], bp_v[:, :, 0:1], bp_v[:, :, 1:2], ALU.min)
    wt('t', VV)
    for c in range(32):
        em.wait('y', MM)
        ydma(R_B1, b1r[0:1, 128 * c:128 * c + 128], LYW)
        ydma(bcC[0:8, 0:128], tb1[:, 128 * c:128 * c + 128], LYW)
        wt('t', LYW)
        mm(ps[:, BK(5)][:, c:c + 1], bcC[0:8, 0:128], g8c[0:8, 0:1], True, False)
        mm(ps[:, BK(5)][:, c:c + 1], R_B1, onesf[0:1, 0:1], False, True)
    wt('v', MM)
    vop('tensor_copy', mb1T[:], ps[:, BK(5)][:, 0:32])
    yodma(R_MB2, b2r)
    for j in range(2):
        em.wait('y', MM)
        ydma(bcC[0:8, 128:640], tb2[:, 512 * j:512 * j + 512], LYW)
        wt('t', LYW)
        wt('t', VV)
        mm(ps[0:1, BK(6)], g8c[0:8, 0:1], bcC[0:8, 128:640], True, True)
        wt('v', MM)
        wt('v', LYO)
        vop('tensor_add', R_MB2[0:1, 512 * j:512 * j + 512],
            ps[0:1, BK(6)], R_MB2[0:1, 512 * j:512 * j + 512])
    for j in range(2):
        wt('t', VV)
        mm(ps[:, BK(6)], onesr[0:1, :], R_MB2[0:1, 512 * j:512 * j + 512], True, True)
        wt('v', MM)
        vop('tensor_copy', bcB[:, 512 * j:512 * j + 512], ps[:, BK(6)])
    if "g8" in taps:
        wt('g', VV)
        godma(taps["g8"], g8c[0:8, 0:1])
    if "kq" in taps:
        wt('g', VV)
        godma(taps["kq"], biasH[:, 0:16])
    if "xn0" in taps:
        wt('g', SS, XNF_OWN_SS)
        godma(taps["xn0"], xnF[:])

    mw_chunks = []
    for hc in range(8):
        for fcol in range(8):
            mw_chunks.append((w1T[128 * hc:128 * hc + 128, 512 * fcol:512 * fcol + 512],
                              tw1T[0:8, 128 * hc:128 * hc + 128, 512 * fcol:512 * fcol + 512],
                              mw1_scr[128 * hc:128 * hc + 128, 512 * fcol:512 * fcol + 512]))
    for fr in range(32):
        for col in range(2):
            mw_chunks.append((w2T[128 * fr:128 * fr + 128, 512 * col:512 * col + 512],
                              tw2T[0:8, 128 * fr:128 * fr + 128, 512 * col:512 * col + 512],
                              mw2_scr[128 * fr:128 * fr + 128, 512 * col:512 * col + 512]))
    mw_state = {"i": 0, "vfree": [0, 0], "yfree": [0, 0]}

    def emit_mw_chunk():
        i = mw_state["i"]
        if i >= len(mw_chunks):
            return
        mw_state["i"] = i + 1
        wsl, twl, dst = mw_chunks[i]
        sl = i % 2
        seg = 9 * 512 * sl
        mws = MWA if sl == 0 else MWB
        mss = MSA if sl == 0 else MSB
        em.wait('g', VV, mw_state["vfree"][sl])
        gdma(POOL1[:, seg:seg + 512], wsl, mws)
        em.run('g', lambda g, twl=twl, seg=seg: g.dma_start(
            out=POOL1[:, seg + 512:seg + 4608].rearrange("p (e c) -> p e c", c=512),
            in_=twl.rearrange("e p c -> p e c")), mws, 16)
        accb = mxT[:, 512 * sl:512 * sl + 512]
        wt('v', mws)
        em.wait('v', mss, mw_state["yfree"][sl])
        vop('scalar_tensor_tensor', out=accb, in0=POOL1[:, seg + 512:seg + 1024],
            scalar=gBc[:, 0:1], in1=POOL1[:, seg:seg + 512], op0=ALU.mult, op1=ALU.add)
        for e in range(1, 8):
            vop('scalar_tensor_tensor', out=accb,
                in0=POOL1[:, seg + 512 * (e + 1):seg + 512 * (e + 2)],
                scalar=gBc[:, e:e + 1], in1=accb, op0=ALU.mult, op1=ALU.add)
        mw_state["vfree"][sl] = VV.n
        wt('y', VV)
        ydma(dst, accb, mss)
        mw_state["yfree"][sl] = mss.n

    eseg = lambda st, i: eT[:, st * 2048 + i * 1024: st * 2048 + i * 1024 + 1024]
    of32 = POOL2[:, 0:512]
    otmp = POOL2[:, 512:1024]
    sevac = POOL2[:, 1024:1536]
    kbuf = [kxb[:, 0:4096], kxb[:, 4096:8192]]
    vbuf = [wblk[:, 0:4096], POOL1[:, 9216:13312]]
    WQ = [WQA, WQB]
    wt('s', LSKA)
    wt('s', LSKB)
    em.wait('g', VV)
    em.wait('g', LSWA)
    em.wait('g', LSWB)

    def l0_group_loads(g):
        if g >= 32:
            return
        vh = g % 8
        sl = g % 2
        em.wait('g', MM, ring_cons.get(("kv", sl), 0))
        em.wait('g', LSKA)
        em.wait('g', LSKB)
        gdma(kbuf[sl][:, 0:NFULL], k_scr[128 * vh:128 * vh + 128, :], WQ[sl])
        em.run('g', lambda g_, vh=vh, sl=sl: g_.dma_start(
            out=vbuf[sl][:, 0:NFULL].rearrange("p (c d) -> p c d", d=128),
            in_=v_scr[:, 128 * vh:128 * vh + 128].rearrange("(c p) d -> p c d", p=128)),
            WQ[sl], 16)

    l0_group_loads(0)
    for g in range(32):
        qb, vh = g // 8, g % 8
        sl = g % 2
        l0_group_loads(g + 1)
        kg, vg = kbuf[sl], vbuf[sl]

        def qrow(r0, r1, vh=vh, qb=qb):
            return QF[r0:r1, vh * NTOK + qb * 512: vh * NTOK + qb * 512 + 512]
        wt('t', WQ[sl])
        wt('t', VV)
        ss_after = {}
        pairbuf = lambda sp: eT[:, 0:2048] if sp == 0 else POOL3[:, 0:1024].bitcast(BF16)

        def scores2(p):
            if p >= 1:
                wt('t', SS, ss_after[p - 1])
            for j in range(2):
                c = 2 * p + j
                mm(ps[:, BK(2 * j)], kg[0:64, 128 * c:128 * c + 128],
                   qrow(0, 64), True, True, tp=(0, 0))
                mm(ps[:, BK(2 * j + 1)], kg[64:128, 128 * c:128 * c + 128],
                   qrow(64, 128), True, True, tp=(64, 0))
            wt('s', MM)
            act(pairbuf(p % 2), ps[:, 0:2048], AF.Exp, bias=biasP[:, vh:vh + 1])
            ss_after[p] = SS.n

        def av2(p, first, last):
            buf = pairbuf(p % 2)
            wt('t', SS, ss_after[p])
            for j in range(2):
                c = 2 * p + j
                fst = first and j == 0
                lst = last and j == 1
                e0 = buf[:, 1024 * j:1024 * j + 512]
                e1 = buf[:, 1024 * j + 512:1024 * j + 1024]
                mm(ps[0:128, BK(4)], vg[:, 128 * c:128 * c + 128], e0, fst, lst)
                mm(ps[0:128, BK(5)], vg[:, 128 * c:128 * c + 128], e1, fst, lst)
                mm(ps[:, BK(6)], onesw[:, 0:128], e0, fst, lst)
                mm(ps[:, BK(7)], onesw[:, 0:128], e1, fst, lst)

        scores2(0)
        for p in range(1, 16):
            scores2(p)
            av2(p - 1, p - 1 == 0, False)
        av2(15, False, True)
        ring_cons[("kv", sl)] = MM.n
        wt('v', MM)
        vop('reciprocal', bcC[:, 0:512], ps[:, BK(6)])
        vop('reciprocal', bcC[:, 512:1024], ps[:, BK(7)])
        vop('tensor_mul', of32, ps[:, BK(4)], bcC[:, 0:512])
        vop('scalar_tensor_tensor', out=otmp, in0=ps[:, BK(5)], scalar=lamS[:, 0:1],
            in1=bcC[:, 512:1024], op0=ALU.mult, op1=ALU.mult)
        vop('tensor_add', of32, of32, otmp)
        wt('s', VV)
        act(otmp, of32, AF.Square)
        wt('t', SS)
        mm(ps[64:65, BK(6)], onesf[:, 0:1], of32, True, True, tp=(0, 64))
        mm(ps[0:1, BK(1)], onesf[:, 0:1], otmp, True, True)
        wt('v', MM)
        vop('tensor_scalar', out=R_MU, in0=ps[64:65, BK(6)], scalar1=1.0 / 128,
            scalar2=None, op0=ALU.mult)
        vop('tensor_scalar', out=R_VAR, in0=ps[0:1, BK(1)], scalar1=1.0 / 128,
            scalar2=None, op0=ALU.mult)
        vop('tensor_mul', R_RCP[0:1, 0:512], R_MU, R_MU)
        vop('tensor_sub', R_VAR, R_VAR, R_RCP[0:1, 0:512])
        wt('s', VV)
        act(R_RCP[0:1, 512:1024], R_VAR, AF.Ln, bias=stat[0:1, 19:20])
        act(R_VAR, R_RCP[0:1, 512:1024], AF.Exp, scale=-0.5)
        wt('t', SS)
        mm(ps[:, BK(2)], onesr[0:1, :], R_MU, True, True)
        mm(ps[:, BK(3)], onesr[0:1, :], R_VAR, True, True)
        wt('v', MM)
        vop('tensor_copy', bcC[:, 0:512], ps[:, BK(2)])
        vop('tensor_copy', bcC[:, 512:1024], ps[:, BK(3)])
        vop('tensor_sub', of32, of32, bcC[:, 0:512])
        vop('tensor_mul', of32, of32, bcC[:, 512:1024])
        vop('tensor_scalar', out=oF[:, 512 * vh:512 * vh + 512], in0=of32,
            scalar1=lngS, scalar2=lnbS, op0=ALU.mult, op1=ALU.add)
        for _ in range(4):
            emit_mw_chunk()
        if vh == 7:
            for col in range(2):
                for vhh in range(8):
                    wsl = 4096 + 512 * (vhh % 2)
                    LO = LOA if vhh % 2 == 0 else LOB
                    em.wait('g', MM, ring_cons.get(("wo", vhh % 2), 0))
                    gdma(wblk[:, wsl:wsl + 512],
                         wo[128 * vhh:128 * vhh + 128, 512 * col:512 * col + 512], LO)
                    wt('t', LO)
                    if vhh == 0:
                        wt('t', VV)
                    for tc in range(4):
                        mm(ps[:, BK(tc)], oF[:, 512 * vhh + 128 * tc:512 * vhh + 128 * tc + 128],
                           wblk[:, wsl:wsl + 512], vhh == 0, vhh == 7)
                    ring_cons[("wo", vhh % 2)] = MM.n
                wt('v', MM)
                for tc in range(4):
                    wt('v', LSS)
                    vop('tensor_copy', sevac, ps[:, BK(tc)])
                    wt('y', VV)
                    ydma(s_scr[512 * qb + 128 * tc:512 * qb + 128 * tc + 128,
                               512 * col:512 * col + 512], sevac, LSS)
    if "s_l0" in taps:
        wt('g', LSS)
        godma(taps["s_l0"], s_scr)
    while mw_state["i"] < len(mw_chunks):
        emit_mw_chunk()
    if "mw1" in taps:
        wt('g', MSA)
        wt('g', MSB)
        godma(taps["mw1"], mw1_scr)

    for tb in range(4):
        for fc in range(32):
            sl = fc % 2
            LW = LWA if sl == 0 else LWB
            em.wait('y', MM, ring_cons.get(("mw1", sl), 0))
            if tb == 0 and fc == 0:
                em.wait('y', MSA)
                em.wait('y', MSB)
            em.run('y', lambda y, fc=fc, sl=sl: y.dma_start(
                out=wblk[:, 1024 * sl:1024 * sl + 1024].rearrange("p (d c) -> p d c", c=128),
                in_=mw1_scr[:, fc * 128:fc * 128 + 128].rearrange("(d p) c -> p d c", p=128)),
                LW, 16)
            wt('t', LW)
            if fc < 2:
                wt('t', SS)
                wt('t', VV)
            else:
                wt('t', SS, ring_cons.get(("ge", sl), 0))
            for dinc in range(8):
                mm(ps[:, BK(sl)], wblk[:, 1024 * sl + 128 * dinc:1024 * sl + 128 * dinc + 128],
                   xnF[:, dinc * NTOK + tb * 512:dinc * NTOK + tb * 512 + 512],
                   dinc == 0, dinc == 7)
            ring_cons[("mw1", sl)] = MM.n
            wt('s', MM)
            act(uBlk[:, fc * 512:fc * 512 + 512], ps[:, BK(sl)], AF.Gelu_apprx_tanh,
                bias=mb1T[:, fc:fc + 1])
            ring_cons[("ge", sl)] = SS.n
        for fc in range(32):
            sl = fc % 2
            LW2 = LW2A if sl == 0 else LW2B
            em.wait('y', MM, ring_cons.get(("mw2", sl), 0))
            ydma(wblk[:, 2048 + 1024 * sl:2048 + 1024 * sl + 1024],
                 mw2_scr[fc * 128:fc * 128 + 128, 0:1024], LW2)
            wt('t', LW2)
            if fc == 0:
                wt('t', SS)
                wt('t', VV)
            for tc in range(4):
                for col in range(2):
                    mm(ps[:, BK(tc * 2 + col)],
                       uBlk[:, fc * 512 + tc * 128:fc * 512 + tc * 128 + 128],
                       wblk[:, 2048 + 1024 * sl + 512 * col:2048 + 1024 * sl + 512 * col + 512],
                       fc == 0, fc == 31)
            ring_cons[("mw2", sl)] = MM.n
        for tc in range(4):
            wt('y', VV)
            wt('y', LSS)
            ydma(xt0, x0[tb * 512 + tc * 128:tb * 512 + tc * 128 + 128, :], LYX)
            ydma(xt1, s_scr[tb * 512 + tc * 128:tb * 512 + tc * 128 + 128, :], LYX)
            for col in range(2):
                wt('v', MM)
                wt('v', LYX)
                wt('v', LSX1)
                vop('tensor_add', POOL2[:, 0:512], ps[:, BK(tc * 2 + col)],
                    bcB[:, 512 * col:512 * col + 512])
                wt('s', VV)
                act(POOL2[:, 512:1024], POOL2[:, 0:512], AF.Silu)
                wt('v', SS)
                vop('tensor_mul', POOL2[:, 1024:1536], POOL2[:, 512:1024],
                    xt1[:, 512 * col:512 * col + 512])
                vop('tensor_add', POOL2[:, 1536:2048], POOL2[:, 1024:1536],
                    xt0[:, 512 * col:512 * col + 512])
                wt('y', VV)
                ydma(x1_scr[tb * 512 + tc * 128:tb * 512 + tc * 128 + 128,
                            512 * col:512 * col + 512], POOL2[:, 1536:2048], LSX1)
    if "x1" in taps:
        wt('g', LSX1)
        godma(taps["x1"], x1_scr)

    wt('y', LSX1)
    for tc in range(NTC):
        wt('v', LDO)
        ln_tile(x1_scr[128 * tc:128 * tc + 128, :], want_f32=True)
        vop('reduce_max', stat[:, 22:23], xt2, AX.X, apply_absolute_value=True)
        vop('tensor_scalar_max', stat[:, 23:24], stat[:, 22:23], 1e-5)
        vop('reciprocal', stat[:, 24:25], stat[:, 23:24])
        vop('tensor_scalar', out=sc1T[:, tc:tc + 1], in0=stat[:, 23:24],
            scalar1=1.0 / 127.0, scalar2=None, op0=ALU.mult)
        vop('tensor_scalar', out=xt0, in0=xt2, scalar1=stat[:, 24:25], scalar2=127.0,
            op0=ALU.mult, op1=ALU.mult)
        vop('tensor_copy', xti[:, 0:1024], xt0)
        wt('v', MM)
        vop('tensor_copy', eT[:, 1024:2048], xti[:, 0:1024])
        if tc == 0:
            wt('g', VV)
            godma(R_HFIRST, xt2[0:1, :])
        if tc == NTC - 1:
            wt('g', VV)
            godma(R_HLAST, xt2[127:128, :])
        ln_transpose(lambda j, tc=tc: xnF[:, j * NTOK + tc * 128: j * NTOK + tc * 128 + 128])
        ln_transpose(lambda j, tc=tc: xqF[:, j * NTOK + tc * 128: j * NTOK + tc * 128 + 128],
                     src=eT[:, 1024:2048])
        if tc == 0:
            pass
    godma(halo_in[0:1, :], R_HFIRST)
    godma(halo_in[1:2, :], R_HLAST)
    wt('g', LDO)
    em.run('g', lambda g: g.collective_compute(
        "AllGather", ALU.bypass, replica_groups=[[0, 1], [2, 3], [4, 5], [6, 7]],
        ins=[halo_in], outs=[halo_out]), CC, 1)

    em.wait('g', MM)
    hfc = POOL3[:, 0:2050]
    hf2 = POOL3[:, 2050 + 0:2050 + 2048]
    ropeC = POOL2[:, 0:2048]
    ropeS = POOL2[:, 2048:4096]
    yodma(ropeC, ropec)
    yodma(ropeS, ropes)
    wt('v', LYO)
    vop('memset', q2m[:], 0.0)
    vop('memset', k2m[:], 0.0)
    P3A = POOL3[:, 0:2048]
    P3B = POOL3[:, 2048:4096]
    swap_mask = [i ^ 1 for i in range(32)]
    kstage = eT[:, 0:2048]
    ksq = mxT[:, 0:2048]

    def l1_qk_chunk(which, dc):
        base = 0 if which == "q" else 1024
        sl = pw_load(wqkv, base + dc * 128)
        for tb in range(4):
            bank = 2 + (tb % 2)
            wt('t', WQ[sl])
            wt('t', SS)
            wt('t', VV)
            for dinc in range(8):
                mm(ps[:, BK(bank)], wblk[:, 1024 * sl + dinc * 128:1024 * sl + dinc * 128 + 128],
                   xnF[:, dinc * NTOK + tb * 512:dinc * NTOK + tb * 512 + 512],
                   dinc == 0, dinc == 7)
            if tb == 3:
                ring_cons[("pw", sl)] = MM.n
            wt('s', MM)
            act(P3A[:, tb * 512:tb * 512 + 512], ps[:, BK(bank)], AF.Copy,
                scale=(0.125 if which == "q" else 1.0))
        wt('v', SS)
        vop('stream_shuffle', P3B, P3A, swap_mask)
        vop('tensor_mul', P3B, P3B, ropeS)
        vop('tensor_mul', P3A, P3A, ropeC)
        dst = (QF[:, dc * NTOK:dc * NTOK + NTOK] if which == "q" else kstage)
        if which == "k":
            wt('v', LSKA)
        vop('tensor_add', dst, P3A, P3B)
        wt('s', VV)
        act(ksq, dst, AF.Square)
        dstm = q2m if which == "q" else k2m
        for j in range(4):
            wt('t', SS)
            wt('t', VV)
            mm(ps[0:2, BK(5)], hs2[:], ksq[:, j * 512:j * 512 + 512], True, True)
            wt('v', MM)
            vop('reduce_max', stat[0:2, 18:19], ps[0:2, BK(5)], AX.X)
            vop('tensor_max', dstm[0:2, dc:dc + 1], dstm[0:2, dc:dc + 1], stat[0:2, 18:19])
        if which == "k":
            wt('y', VV)
            ydma(kv_in[128 * dc:128 * dc + 128, :], kstage, LSKA)

    for dc in range(8):
        l1_qk_chunk("q", dc)
        l1_qk_chunk("k", dc)
    for col in range(2):
        for tcg in range(4):
            for hc in range(8):
                wsl = hc % 2
                em.wait('g', MM, ring_cons.get(("wv", wsl), 0))
                gdma(wblk[:, 3072 + 512 * wsl:3072 + 512 * wsl + 512],
                     wqkv[128 * hc:128 * hc + 128, 2048 + col * 512:2048 + col * 512 + 512], WV[wsl])
                wt('t', WV[wsl])
                if hc == 0:
                    wt('t', SS)
                    wt('t', VV)
                for tc4 in range(4):
                    tc = tcg * 4 + tc4
                    mm(ps[:, BK([0, 1, 6, 7][tc4])],
                       xnF[:, hc * NTOK + tc * 128:hc * NTOK + tc * 128 + 128],
                       wblk[:, 3072 + 512 * wsl:3072 + 512 * wsl + 512],
                       hc == 0, hc == 7)
                ring_cons[("wv", wsl)] = MM.n
            wt('s', MM)
            for tc4 in range(4):
                tc = tcg * 4 + tc4
                t0 = tc * 128
                stg = kv_stage[tc4 % 2]
                wt('s', LSKA if tc4 % 2 == 0 else LSKB)
                act(stg, ps[:, BK([0, 1, 6, 7][tc4])], AF.Copy)
                wt('y', SS)
                ydma(kv_in[1024 + (t0 % 1024):1024 + (t0 % 1024) + 128,
                           (t0 // 1024) * 1024 + col * 512:(t0 // 1024) * 1024 + col * 512 + 512],
                     stg, LSKA if tc4 % 2 == 0 else LSKB)
    vop('memset', hf2[0:2, 0:2048], 0.0)
    wt('g', VV)
    godma(kv_in[2048:2050, :], hf2[0:2, 0:2048])
    knb = R_KNOWN.rearrange("o (a s) -> o a s", s=2)
    godma(knb[:, :, 0:1], k2m[0:1, 0:8], ns=True)
    godma(knb[:, :, 1:2], k2m[1:2, 0:8], ns=True)
    godma(kv_in[2049:2050, 0:16], R_KNOWN)
    wt('g', LSKA)
    wt('g', LSKB)
    wt('g', LDO)
    em.run('g', lambda g: g.collective_compute(
        "AllGather", ALU.bypass, replica_groups=[[0, 1], [2, 3], [4, 5], [6, 7]],
        ins=[kv_in], outs=[kv_out]), CC, 1)

    pari = sb("pari", [1, 1], I32)
    yodma(pari[:], parity)
    prow = rows[0:1, 2048:3072]
    _halo_lyo = LYO.n
    _halo_ldo = LDO.n

    def _halo_switch(g):
        g.wait_ge(CC.h, 1)
        g.wait_ge(LYO.h, _halo_lyo)
        g.wait_ge(LDO.h, _halo_ldo)
        with g.register() as r:
            g.load(r, pari[0:1, 0:1])
            idx = g.snap(r, min_val=0, max_val=1)
            for i in g.Switch(idx, 2):
                src = halo_out[2:3, :] if i == 0 else halo_out[1:2, :]
                g.dma_start(out=prow, in_=src).then_inc(LDO.h, 16)
    LDO.n += 16
    em.add('g', _halo_switch)

    vop('memset', mxT[:], 0.0)
    vop('memset', onesw[:], 1.0)
    wt('t', SS)
    wt('t', VV)
    for tc in range(16):
        mm(ps[0:1, tc * 128:tc * 128 + 128], sc1T[:, tc:tc + 1], identF[:], True, True)
    wt('v', MM)
    vop('tensor_copy', rows[0:1, 0:2048], ps[0:1, 0:2048])
    vop('tensor_scalar', out=rows[0:1, 0:2048], in0=rows[0:1, 0:2048],
        scalar1=R_WM[0:1, 0:1], scalar2=None, op0=ALU.mult)
    wt('t', VV)
    for j in range(4):
        mm(ps[:, 2048 + 512 * j:2560 + 512 * j], onesr[0:1, :],
           rows[0:1, 512 * j:512 * j + 512], True, True)
    wt('v', MM)
    cB = POOL2[:, 0:2048]
    vop('tensor_copy', cB, ps[:, 2048:4096])
    wt('v', LDO)
    vop('reduce_max', R_HS[0:1, 0:1], prow, AX.X, apply_absolute_value=True)
    vop('tensor_scalar_max', R_HS[0:1, 1:2], R_HS[0:1, 0:1], 1e-5)
    vop('reciprocal', R_HS[0:1, 2:3], R_HS[0:1, 1:2])
    wt('t', VV)
    for hc in range(8):
        mm(ps[:, BK(6)][:, hc:hc + 1], prow[0:1, 128 * hc:128 * hc + 128],
           onesf[0:1, 0:1], True, True)
    mm(ps[:, BK(7)][:, 0:1], onesr[0:1, :], R_HS[0:1, 2:3], True, True)
    mm(ps[:, BK(7)][:, 1:2], onesr[0:1, :], R_HS[0:1, 1:2], True, True)
    wt('v', MM)
    vop('tensor_copy', stat[:, 25:26], ps[:, BK(7)][:, 0:1])
    vop('tensor_copy', stat[:, 26:27], ps[:, BK(7)][:, 1:2])
    vop('tensor_scalar', out=xt0[:, 0:8], in0=ps[:, BK(6)][:, 0:8],
        scalar1=stat[:, 25:26], scalar2=127.0, op0=ALU.mult, op1=ALU.mult)
    vop('tensor_copy', xti[:, 0:8], xt0[:, 0:8])
    vop('tensor_copy', xqh[:, 0:8], xti[:, 0:8])
    wt('t', VV)
    mm(ps[:, BK(7)][:, 2:3], onesr[0:1, :], R_WM[0:1, 0:1], True, True)
    wt('v', MM)
    vop('tensor_copy', stat[:, 29:30], ps[:, BK(7)][:, 2:3])
    vop('tensor_scalar', out=stat[:, 28:29], in0=stat[:, 26:27],
        scalar1=stat[:, 29:30], scalar2=1.0 / 127.0, op0=ALU.mult, op1=ALU.mult)
    for fc in range(32):
        sl = fc % 2
        LW = LWA if sl == 0 else LWB
        em.wait('y', MM, ring_cons.get(("w1q", sl), 0))
        em.run('y', lambda y, fc=fc, sl=sl: y.dma_start(
            out=wblk[:, 1024 * sl:1024 * sl + 1024].rearrange("p (d c) -> p d c", c=128),
            in_=w1q_scr[:, fc * 128:fc * 128 + 128].rearrange("(d p) c -> p d c", p=128)),
            LW, 16)
        wt('t', LW)
        wt('t', VV)
        for dinc in range(8):
            for b4 in range(4):
                mm(ps[:, BK(b4)], wblk[:, 1024 * sl + 128 * dinc:1024 * sl + 128 * dinc + 128],
                   xqF[:, dinc * NTOK + b4 * 512:dinc * NTOK + b4 * 512 + 512],
                   dinc == 0, dinc == 7)
            mm(ps[:, BK(4)][:, 0:8], wblk[:, 1024 * sl + 128 * dinc:1024 * sl + 128 * dinc + 128],
               xqh[:, 0:8], dinc == 0, dinc == 7)
        ring_cons[("w1q", sl)] = MM.n
        wt('v', MM)
        for b4 in range(4):
            vop('tensor_mul', hfc[:, 1 + 512 * b4:1 + 512 * b4 + 512], ps[:, BK(b4)],
                cB[:, 512 * b4:512 * b4 + 512])
        vop('tensor_scalar', out=xt0[:, 8:9], in0=ps[:, BK(4)][:, 0:1],
            scalar1=stat[:, 28:29], scalar2=None, op0=ALU.mult)
        vop('tensor_scalar', out=hfc[:, 0:1], in0=xt0[:, 8:9], scalar1=hmS[:, 0:1],
            scalar2=None, op0=ALU.mult)
        vop('tensor_scalar', out=hfc[:, 2049:2050], in0=xt0[:, 8:9], scalar1=hmS[:, 1:2],
            scalar2=None, op0=ALU.mult)
        vop('tensor_scalar', out=hf2, in0=hfc[:, 0:2048], scalar1=ctap[:, fc:fc + 1],
            scalar2=ctap[:, 96 + fc:97 + fc], op0=ALU.mult, op1=ALU.add)
        vop('scalar_tensor_tensor', out=hf2, in0=hfc[:, 1:2049],
            scalar=ctap[:, 32 + fc:33 + fc], in1=hf2, op0=ALU.mult, op1=ALU.add)
        vop('scalar_tensor_tensor', out=hf2, in0=hfc[:, 2:2050],
            scalar=ctap[:, 64 + fc:65 + fc], in1=hf2, op0=ALU.mult, op1=ALU.add)
        wt('s', VV)
        wt('s', LSH)
        act(kxb[:, 0:2048], hf2, AF.Silu)
        act(kxb[:, 2048:4096], kxb[:, 0:2048], AF.Abs)
        wt('v', SS)
        vop('tensor_max', mxT[:, 0:2048], mxT[:, 0:2048], kxb[:, 2048:4096])
        wt('y', VV)
        ydma(hs_scr[128 * fc:128 * fc + 128, :], kxb[:, 0:2048], LSH)
    if "hs" in taps:
        wt('g', LSH)
        godma(taps["hs"], hs_scr)

    ph9_ldo = LDO.n
    ph9_mm = MM.n

    def _kv_switch(g):
        g.wait_ge(CC.h, 2)
        g.wait_ge(LDO.h, ph9_ldo)
        g.wait_ge(MM.h, ph9_mm)
        with g.register() as r:
            g.load(r, pari[0:1, 0:1])
            idx = g.snap(r, min_val=0, max_val=1)
            for i in g.Switch(idx, 2):
                own, oth = (0, 2050) if i == 0 else (2050, 0)
                g.dma_start(out=k_scr[:, 0:2048], in_=kv_out[own:own + 1024, :]).then_inc(LDO.h, 16)
                g.dma_start(out=k_scr[:, 2048:4096], in_=kv_out[oth:oth + 1024, :]).then_inc(LDO.h, 16)
                g.dma_start(out=v_scr[0:1024, :], in_=kv_out[own + 1024:own + 2048, 0:1024]).then_inc(LDO.h, 16)
                g.dma_start(out=v_scr[1024:2048, :], in_=kv_out[own + 1024:own + 2048, 1024:2048]).then_inc(LDO.h, 16)
                g.dma_start(out=v_scr[2048:3072, :], in_=kv_out[oth + 1024:oth + 2048, 0:1024]).then_inc(LDO.h, 16)
                g.dma_start(out=v_scr[3072:4096, :], in_=kv_out[oth + 1024:oth + 2048, 1024:2048]).then_inc(LDO.h, 16)
                g.dma_start(out=R_KN[0:1, 16:32], in_=kv_out[oth + 2049:oth + 2050, 0:16]).then_inc(LDO.h, 16)
                g.dma_start(out=R_KN[0:1, 0:16], in_=kv_out[own + 2049:own + 2050, 0:16]).then_inc(LDO.h, 16)
    LDO.n += 8 * 16
    em.add('g', _kv_switch)
    KVSW_LD = LDO.n

    wt('v', LDO, KVSW_LD)
    qnb = R_KN[0:1, 32:48].rearrange("o (a s) -> o a s", s=2)
    godma(qnb[:, :, 0:1], q2m[0:1, 0:8], ns=True)
    godma(qnb[:, :, 1:2], q2m[1:2, 0:8], ns=True)
    wt('v', LDO)
    vop('tensor_max', R_KN[0:1, 48:64], R_KN[0:1, 0:16], R_KN[0:1, 16:32])
    vop('tensor_mul', R_KN[0:1, 48:64], R_KN[0:1, 48:64], R_KN[0:1, 32:48])
    wt('s', VV)
    act(R_KN[0:1, 64:80], R_KN[0:1, 48:64], AF.Sqrt)
    wt('v', SS)
    vop('tensor_scalar', out=R_KN[0:1, 64:80], in0=R_KN[0:1, 64:80], scalar1=-0.5,
        scalar2=None, op0=ALU.mult)
    wt('t', VV)
    mm(ps[:, BK(6)][:, 0:16], onesr[0:1, :], R_KN[0:1, 64:80], True, True)
    wt('v', MM)
    vop('tensor_copy', biasH[:, 0:16], ps[:, BK(6)][:, 0:16])
    vop('tensor_scalar', out=biasHK[:, 0:16], in0=biasH[:, 0:16], scalar1=klS[:, 0:1],
        scalar2=None, op0=ALU.add)
    bp1_v = biasH[:, 0:16].rearrange("p (a s) -> p a s", s=2)
    vop('tensor_tensor', biasP[:, 0:8], bp1_v[:, :, 0:1], bp1_v[:, :, 1:2], ALU.min)
    vop('tensor_scalar', out=biasPK[:, 0:8], in0=biasP[:, 0:8], scalar1=klS[:, 0:1],
        scalar2=None, op0=ALU.add)
    yodma(R_MB2, boutr)
    wt('t', LYO)
    for j in range(2):
        wt('t', VV)
        mm(ps[:, BK(6)], onesr[0:1, :], R_MB2[0:1, 512 * j:512 * j + 512], True, True)
        wt('v', MM)
        vop('tensor_copy', bcB[:, 512 * j:512 * j + 512], ps[:, BK(6)])
    kbuf = [kxb[:, 0:4096], kxb[:, 4096:8192]]
    vbuf1 = [POOL1[:, 8064:12224], POOL1[:, 12224:16384]]
    gseq = []
    for qb in range(4):
        for hg in range(8):
            gseq.append((qb, hg))
    wt('v', MM)
    vop('memset', POOL1[:, 8064:16384], 1.0)
    wt('s', LSKA)
    wt('s', LSKB)
    em.wait('g', VV)
    em.wait('g', MM)

    def l1_group_loads(g):
        if g >= 32:
            return
        qb, hg = gseq[g]
        sl = g % 2
        em.wait('g', MM, ring_cons.get(("kv1", sl), 0))
        em.wait('g', LSH)
        em.wait('g', LDO, KVSW_LD)
        gdma(kbuf[sl][:, 0:NFULL], k_scr[128 * hg:128 * hg + 128, :], WQ[sl])
        vv_ = vbuf1[sl][:, 0:4160].rearrange("p (c w) -> p c w", w=130)
        em.run('g', lambda g_, hg=hg, vv_=vv_: g_.dma_start(
            out=vv_[:, :, 0:64],
            in_=v_scr[:, 128 * hg:128 * hg + 64].rearrange("(c p) d -> p c d", p=128)),
            WQ[sl], 16)
        em.run('g', lambda g_, hg=hg, vv_=vv_: g_.dma_start(
            out=vv_[:, :, 65:129],
            in_=v_scr[:, 128 * hg + 64:128 * hg + 128].rearrange("(c p) d -> p c d", p=128)),
            WQ[sl], 16)

    l1_group_loads(0)
    for g in range(32):
        qb, hg = gseq[g]
        sl = g % 2
        seq = list(range(0, 4 * qb + 4)) + list(range(16, 32))
        nch = len(seq)
        diag0 = 4 * qb
        l1_group_loads(g + 1)
        kg, vg = kbuf[sl], vbuf1[sl]

        def qrow1(r0, r1, hg=hg, qb=qb):
            return QF[r0:r1, hg * NTOK + qb * 512: hg * NTOK + qb * 512 + 512]
        wt('t', WQ[sl])
        wt('t', VV)
        ss_after = {}
        vv_after = {}
        eA = lambda st, hh: eT[:, (2 * st + hh) * 512:(2 * st + hh) * 512 + 512]

        def scores1(cix):
            c = seq[cix]
            isB = c >= 16
            isdiag = (not isB) and (c >= diag0)
            bsrcP = biasPK if isB else biasP
            if cix >= 2:
                wt('t', SS, ss_after[cix - 2])
                if (cix - 2) in vv_after:
                    wt('t', VV, vv_after[cix - 2])
            st = cix % 2
            mm(ps[:, BK(2 * st)], kg[0:64, 128 * c:128 * c + 128],
               qrow1(0, 64), True, True, tp=(0, 0))
            mm(ps[:, BK(2 * st + 1)], kg[64:128, 128 * c:128 * c + 128],
               qrow1(64, 128), True, True, tp=(64, 0))
            wt('s', MM)
            act(eT[:, 1024 * st:1024 * st + 1024], ps[:, 1024 * st:1024 * st + 1024],
                AF.Exp, bias=bsrcP[:, hg:hg + 1])
            ss_after[cix] = SS.n
            if isdiag:
                di = c - diag0
                wt('v', SS, ss_after[cix])
                vop('tensor_mul', eA(st, 0), eA(st, 0), dmask[:, 512 * di:512 * di + 512])
                vop('tensor_mul', eA(st, 1), eA(st, 1), dmask[:, 512 * di:512 * di + 512])
                vv_after[cix] = VV.n

        def av1(cix, first, last):
            c = seq[cix]
            st = cix % 2
            wt('t', SS, ss_after[cix])
            if cix in vv_after:
                wt('t', VV, vv_after[cix])
            mm(ps[0:65, BK(4)], vg[:, 130 * c:130 * c + 65], eA(st, 0), first, last)
            mm(ps[64:128, BK(5)], vg[:, 130 * c + 65:130 * c + 129], eA(st, 1),
               first, last, tp=(0, 64), sgc=True)
            mm(ps[:, BK(7)], onesw[:, 0:128], eA(st, 1), first, last)

        scores1(0)
        for cix in range(1, nch):
            scores1(cix)
            av1(cix - 1, cix - 1 == 0, False)
        av1(nch - 1, nch == 1, True)
        ring_cons[("kv1", sl)] = MM.n
        wt('v', MM)
        vop('tensor_copy', R_SUM[0:1, 0:512], ps[64:65, BK(4)])
        vop('reciprocal', R_RCP[0:1, 0:512], R_SUM[0:1, 0:512])
        vop('reciprocal', bcC[:, 512:1024], ps[:, BK(7)])
        wt('t', VV)
        wt('t', SS)
        mm(ps[:, BK(6)], onesr[0:1, :], R_RCP[0:1, 0:512], True, True)
        wt('v', MM)
        vop('tensor_copy', bcC[:, 0:512], ps[:, BK(6)])
        vop('tensor_mul', oF[0:64, 512 * hg:512 * hg + 512], ps[0:64, BK(4)],
            bcC[0:64, 0:512])
        vop('tensor_mul', oF[64:128, 512 * hg:512 * hg + 512], ps[64:128, BK(5)],
            bcC[64:128, 512:1024])
        if hg == 7:
            for col in range(2):
                for vhh in range(8):
                    wsl_ = 4096 + 512 * (vhh % 2)
                    LO = LOA if vhh % 2 == 0 else LOB
                    em.wait('g', MM, ring_cons.get(("wo", vhh % 2), 0))
                    gdma(wblk[:, wsl_:wsl_ + 512],
                         wout[128 * vhh:128 * vhh + 128, 512 * col:512 * col + 512], LO)
                    wt('t', LO)
                    if vhh == 0:
                        wt('t', VV)
                    for tc in range(4):
                        mm(ps[:, BK(tc)], oF[:, 512 * vhh + 128 * tc:512 * vhh + 128 * tc + 128],
                           wblk[:, wsl_:wsl_ + 512], vhh == 0, vhh == 7)
                    ring_cons[("wo", vhh % 2)] = MM.n
                wt('v', MM)
                for tc in range(4):
                    wt('v', LSS)
                    vop('tensor_add', POOL2[:, 1024:1536], ps[:, BK(tc)],
                        bcB[:, 512 * col:512 * col + 512])
                    wt('y', VV)
                    ydma(s_scr[512 * qb + 128 * tc:512 * qb + 128 * tc + 128,
                               512 * col:512 * col + 512], POOL2[:, 1024:1536], LSS)
    if "s_l1" in taps:
        wt('g', LSS)
        godma(taps["s_l1"], s_scr)

    s2b = stat[:, 26:27]
    wt('t', VV)
    mm(ps[:, BK(7)][:, 2:3], onesr[0:1, :], R_WM[0:1, 1:2], True, True)
    wt('v', MM)
    vop('tensor_copy', s2b, ps[:, BK(7)][:, 2:3])
    for tg in range(4):
        wt('t', VV)
        for j in range(4):
            tc = tg * 4 + j
            tr(ps[:, 2048 + 128 * j:2048 + 128 * j + 128], mxT[:, 128 * tc:128 * tc + 128])
        wt('v', MM)
        for j in range(4):
            tc = tg * 4 + j
            vop('reduce_max', stat[:, 22:23], ps[:, 2048 + 128 * j:2048 + 128 * j + 128], AX.X)
            vop('tensor_scalar_max', stat[:, 23:24], stat[:, 22:23], 1e-5)
            vop('tensor_scalar', out=c2T[:, tc:tc + 1], in0=stat[:, 23:24],
                scalar1=s2b, scalar2=1.0 / 127.0, op0=ALU.mult, op1=ALU.mult)
            vop('reciprocal', stat[:, 24:25], stat[:, 23:24])
            vop('tensor_scalar', out=sc1T[:, tc:tc + 1], in0=stat[:, 24:25],
                scalar1=127.0, scalar2=None, op0=ALU.mult)
    wt('t', VV)
    for tc in range(16):
        mm(ps[0:1, tc * 128:tc * 128 + 128], sc1T[:, tc:tc + 1], identF[:], True, True)
    wt('v', MM)
    vop('tensor_copy', rows[0:1, 0:2048], ps[0:1, 0:2048])
    wt('t', VV)
    for j in range(4):
        mm(ps[:, 2048 + 512 * j:2560 + 512 * j], onesr[0:1, :],
           rows[0:1, 512 * j:512 * j + 512], True, True)
    wt('v', MM)
    wt('v', LSS)
    sc2B = POOL2[:, 0:2048]
    vop('tensor_copy', sc2B, ps[:, 2048:4096])
    for fc in range(32):
        wt('y', VV)
        wt('y', LSH)
        ydma(kxb[:, 0:2048], hs_scr[128 * fc:128 * fc + 128, :], LYW)
        wt('v', LYW)
        vop('tensor_tensor', xti[:, 0:2048], kxb[:, 0:2048], sc2B, ALU.mult)
        vop('tensor_copy', kxb[:, 2048:4096], xti[:, 0:2048])
        wt('y', VV)
        ydma(hs_scr[128 * fc:128 * fc + 128, :], kxb[:, 2048:4096], LSH)
    for tb in range(4):
        for fc in range(32):
            sl = fc % 2
            LW = LWA if sl == 0 else LWB
            LW2 = LW2A if sl == 0 else LW2B
            em.wait('y', MM, ring_cons.get(("bl2", sl), 0))
            if tb == 0 and fc == 0:
                em.wait('y', LSH)
            ydma(wblk[:, 1024 * sl:1024 * sl + 512],
                 hs_scr[128 * fc:128 * fc + 128, 512 * tb:512 * tb + 512], LW)
            ydma(wblk[:, 2048 + 1024 * sl:2048 + 1024 * sl + 1024],
                 w2q_scr[128 * fc:128 * fc + 128, 0:1024], LW2)
            wt('t', LW)
            wt('t', LW2)
            if fc == 0:
                wt('t', SS)
                wt('t', VV)
            for tc in range(4):
                for col in range(2):
                    mm(ps[:, BK(tc * 2 + col)],
                       wblk[:, 1024 * sl + 128 * tc:1024 * sl + 128 * tc + 128],
                       wblk[:, 2048 + 1024 * sl + 512 * col:2048 + 1024 * sl + 512 * col + 512],
                       fc == 0, fc == 31)
            ring_cons[("bl2", sl)] = MM.n
        for tc in range(4):
            wt('y', VV)
            wt('y', LSS)
            wt('y', LSX1)
            ydma(xt0, x1_scr[tb * 512 + tc * 128:tb * 512 + tc * 128 + 128, :], LYX)
            ydma(xt1, s_scr[tb * 512 + tc * 128:tb * 512 + tc * 128 + 128, :], LYX)
            for col in range(2):
                wt('s', MM)
                wt('s', VV)
                act(POOL2[:, 2048:2560], ps[:, BK(tc * 2 + col)], AF.Silu,
                    scale=c2T[:, tb * 4 + tc:tb * 4 + tc + 1])
                wt('v', SS)
                wt('v', LYX)
                wt('v', STO)
                vop('tensor_mul', POOL2[:, 2560:3072], POOL2[:, 2048:2560],
                    xt1[:, 512 * col:512 * col + 512])
                vop('tensor_add', POOL2[:, 3072:3584], POOL2[:, 2560:3072],
                    xt0[:, 512 * col:512 * col + 512])
                wt('y', VV)
                em.run('y', lambda y, tb=tb, tc=tc, col=col: y.dma_start(
                    out=out[tb * 512 + tc * 128:tb * 512 + tc * 128 + 128,
                            512 * col:512 * col + 512],
                    in_=POOL2[:, 3072:3584]), STO, 16)
    em.add('g', lambda g: g.wait_ge(STO.h, STO.n))
    em.add('y', lambda y: y.wait_ge(STO.h, STO.n))

    with nc.Block() as block:
        @block.tensor
        def _(t):
            for f in em.TH['t']:
                f(t)

        @block.vector
        def _(v):
            for f in em.TH['v']:
                f(v)

        @block.scalar
        def _(s):
            for f in em.TH['s']:
                f(s)

        @block.gpsimd
        def _(g):
            for f in em.TH['g']:
                f(g)

        @block.sync
        def _(y):
            for f in em.TH['y']:
                f(y)

    es.close()
    return nc


_BUILT = {}


def kernel(**inputs):
    in_maps = [_prep_core(c, inputs) for c in range(8)]
    nc = build()
    res = run_bass_kernel_spmd(nc, in_maps, core_ids=list(range(8)))
    out = np.empty((4, 4096, 1024), np.float32)
    for c in range(8):
        b, h = c >> 1, c & 1
        out[b, 2048 * h:2048 * h + 2048] = np.asarray(res.results[c]["out"])
    return out
